# version 01
# baseline (speedup 1.0000x reference)
import math
from contextlib import ExitStack

import numpy as np

import concourse.bass as bass
import concourse.mybir as mybir
from concourse.bass_utils import run_bass_kernel_spmd

F32 = mybir.dt.float32
BF16 = mybir.dt.bfloat16
I8 = mybir.dt.int8
AF = mybir.ActivationFunctionType
ALU = mybir.AluOpType
AX = mybir.AxisListType

B_TOT, T, D = 16, 2048, 1024
NCORES = 8
B = 2
H, KD, BK = 16, 64, 32
FFN = 4096
LORA = 48
NT = 16
ST = 2
NST = NT // ST
ALPHA = 3.5
PI = math.pi
MAGIC = 12582912.0
EPS_OUT = 1e-5
EPS_IN = 1e-6
CST_COLS = 3200


def host_constants():
    c = np.zeros((128, CST_COLS), np.float32)
    pidx = np.arange(128)
    c[:, 0:128] = np.eye(128, dtype=np.float32)
    c[:, 128:1152] = (ALPHA * (np.arange(1024) % 16))[None, :]
    t = pidx
    d = t[None, :] - pidx[:, None]
    blk_t, blk_p = t[None, :] // 16, (pidx // 16)[:, None]
    md = ((blk_t == blk_p) & (d >= 0)) * np.exp(-ALPHA * np.maximum(d, 0))
    c[:, 1152:1664] = np.tile(md, (1, 4))
    m16 = (blk_t == blk_p + 1) & ((pidx // 16 % 2 == 0))[:, None]
    c[:, 1664:2176] = np.tile(m16, (1, 4))
    m32 = ((t // 32)[None, :] == (pidx // 32)[:, None] + 1) & \
        ((pidx // 32 % 2 == 0))[:, None]
    c[:, 2176:2688] = np.tile(m32, (1, 4))
    m64 = (t[None, :] >= 64) & (pidx[:, None] < 64)
    c[:, 2688:3200] = np.tile(m64, (1, 4))
    return c

ENGS = ("T", "V", "A", "G", "S")


class Prog:

    def __init__(self, nc):
        self.nc = nc
        self.q = {k: [] for k in ENGS}
        self.sems = {}
        self.members = {}
        self.waited = {k: {} for k in ENGS}
        for k in ENGS:
            self.sem("ev" + k)
        self.nbar = 0
        self.sem("bar")

    def sem(self, name):
        if name not in self.sems:
            h = self.nc.alloc_semaphore(name=name)
            self.sems[name] = [h, 0]
        return name

    def tok(self, name):
        out = []
        if name in self.sems and self.sems[name][1] > 0:
            out.append((name, self.sems[name][1]))
        for nm in sorted(self.members.get(name, ())):
            if self.sems[nm][1] > 0:
                out.append((nm, self.sems[nm][1]))
        return tuple(out)

    def wait(self, key, *toks):
        for tk in toks:
            if tk is None:
                continue
            if isinstance(tk, (list, tuple)) and (len(tk) == 0
                                                  or not isinstance(tk[0], str)):
                self.wait(key, *tk)
                continue
            (sname, val) = tk
            if val <= 0:
                continue
            w = self.waited[key]
            if w.get(sname, -1) >= val:
                continue
            w[sname] = val
            h = self.sems[sname][0]
            self.q[key].append(lambda e, h=h, v=val: e.wait_ge(h, v))

    def op(self, key, fn, deps=(), inc=None, dma=None, mark=False):
        self.wait(key, *deps)
        incs = []
        if inc is not None:
            nm = inc + "." + key
            self.sem(nm)
            self.sems[nm][1] += 1
            self.members.setdefault(inc, set()).add(nm)
            incs.append((nm, 1))
        if dma is not None:
            nm = dma + "." + key
            self.sem(nm)
            self.sems[nm][1] += 16
            self.members.setdefault(dma, set()).add(nm)
            incs.append((nm, 16))
        if mark:
            en = "ev" + key
            self.sems[en][1] += 1
            incs.append((en, 1))
        toks = [(n, self.sems[n][1]) for n, _ in incs]

        def runner(e, fn=fn, incs=tuple(incs)):
            inst = fn(e)
            for (n, kk) in incs:
                inst.then_inc(self.sems[n][0], kk)
            return inst

        self.q[key].append(runner)
        return toks[0] if len(toks) == 1 else (toks if toks else None)

    def barrier(self):
        self.nbar += 1
        cur = [(n, s[1]) for n, s in self.sems.items() if s[1] > 0 and n != "bar"]
        bh = self.sems["bar"][0]
        self.sems["bar"][1] += len(ENGS)
        tgt = self.nbar * len(ENGS)
        for key in ENGS:
            self.wait(key, *cur)
            self.q[key].append(lambda e, bh=bh: e.sem_inc(bh, 1))
            self.q[key].append(lambda e, bh=bh, t=tgt: e.wait_ge(bh, t))
            self.waited[key] = {n: v for n, v in cur}
            self.waited[key]["bar"] = tgt

    def replay(self):
        with self.nc.Block() as block:
            @block.tensor
            def _(e):
                for f in self.q["T"]:
                    f(e)

            @block.vector
            def _(e):
                for f in self.q["V"]:
                    f(e)

            @block.scalar
            def _(e):
                for f in self.q["A"]:
                    f(e)

            @block.gpsimd
            def _(e):
                for f in self.q["G"]:
                    f(e)

            @block.sync
            def _(e):
                for f in self.q["S"]:
                    f(e)


def APx(t, off, pat):
    return bass.AP(t, off, pat)


def build_program(debug_taps=(), n_tiles=NT, max_phase=99):
    nc = bass.Bass("TRN2", target_bir_lowering=False, detect_race_conditions=False)
    p = Prog(nc)
    ctx = ExitStack()

    dram = {}

    def din(name, shape):
        dram[name] = nc.dram_tensor(name, list(shape), F32, kind="ExternalInput").ap()
        return dram[name]

    xin = din("xin", (B, T, D))
    for nm, shp in [
        ("aWq", (D, D)), ("abq", (D,)), ("aWk", (D, D)), ("abk", (D,)),
        ("aWv", (D, D)), ("abv", (D,)), ("aWo", (D, D)), ("abo", (D,)),
        ("n1g", (D,)), ("n1b", (D,)), ("n2g", (D,)), ("n2b", (D,)),
        ("thb", (H, BK)), ("tw1", (D, LORA)), ("tw2", (LORA, H * BK)),
        ("lmb", (H, BK)), ("eta", (H, BK)),
        ("aWf1", (D, FFN)), ("abf1", (FFN,)), ("aWf2", (FFN, D)), ("abf2", (D,)),
        ("bWq", (D, D)), ("bbq", (D,)), ("bWk", (D, D)), ("bbk", (D,)),
        ("bWv", (D, D)), ("bbv", (D,)), ("bWo", (D, D)), ("bbo", (D,)),
        ("fw1", (FFN, D)), ("fw2", (D, FFN)),
    ]:
        din(nm, shp)

    xout = nc.dram_tensor("out", [B, T, D], F32, kind="ExternalOutput").ap()

    taps = {}

    def scratch(name, shape, dtype):
        kind = "ExternalOutput" if name in debug_taps else "Internal"
        ap = nc.dram_tensor("tap_" + name if kind == "ExternalOutput" else name,
                            list(shape), dtype, kind=kind).ap()
        if kind == "ExternalOutput":
            taps[name] = ap
        return ap

    x1s = scratch("x1s", (B, T, D), F32)
    h1S = scratch("h1S", (B, T, D), BF16)
    qrS = scratch("qrS", (B, NT, 128, 512), BF16)
    qiS = scratch("qiS", (B, NT, 128, 512), BF16)
    prS = scratch("prS", (B, NT, 128, 512), BF16)
    piS = scratch("piS", (B, NT, 128, 512), BF16)
    thS = scratch("thS", (B, NT, 128, 512), F32)
    nlS = scratch("nlS", (B, NT, 128, 512), F32)
    vS = scratch("vS", (B, T, D), BF16)
    w1tq = scratch("w1tq", (D, FFN), BF16)
    w2tq = scratch("w2tq", (FFN, D), BF16)
    bWT = {nm: scratch(nm + "T", (D, D), BF16)
           for nm in ("bWq", "bWk", "bWv", "bWo")}
    qbS = scratch("qbS", (B, NT, 128, D), BF16)
    kbS = scratch("kbS", (B, 8, 128, T), BF16)
    vbS = scratch("vbS", (B, T, D), BF16)

    def tap(name, shape, dtype=F32):
        if name in debug_taps:
            taps[name] = nc.dram_tensor("tap_" + name, list(shape), dtype,
                                        kind="ExternalOutput").ap()
            return taps[name]
        return None

    sbufs = {}

    def sbuf(name, shape, dtype=F32):
        h = ctx.enter_context(nc.sbuf_tensor(name, list(shape), dtype))
        sbufs[name] = h
        return h

    ones_row = sbuf("ones_row", (1, 128), BF16)
    iden = sbuf("iden", (128, 128), BF16)
    alphat = sbuf("alphat", (128, 1024), F32)
    mask_diag = sbuf("mask_diag", (128, 512), F32)
    mask_16 = sbuf("mask_16", (128, 512), BF16)
    mask_32 = sbuf("mask_32", (128, 512), BF16)
    mask_64 = sbuf("mask_64", (128, 512), BF16)

    thb_t = sbuf("thb_t", (128, 4), F32)
    lmbn_t = sbuf("lmbn_t", (128, 4), F32)
    etan_t = sbuf("etan_t", (128, 4), F32)
    bq_r = sbuf("bq_r", (128, 4), F32)
    bq_i = sbuf("bq_i", (128, 4), F32)
    bk_r = sbuf("bk_r", (128, 4), F32)
    bk_i = sbuf("bk_i", (128, 4), F32)

    g1t = sbuf("g1t", (128, 1024), BF16)
    b1t = sbuf("b1t", (128, 1024), BF16)
    g2t = sbuf("g2t", (128, 1024), BF16)
    b2t = sbuf("b2t", (128, 1024), BF16)
    rows = {}
    for nm in ["abv", "abo", "abf2", "bbq", "bbk", "bbv", "bbo"]:
        rows[nm] = sbuf("row_" + nm, (1, D), BF16)
    grow = sbuf("grow", (1, 4 * D), BF16)
    bf1c = sbuf("bf1c", (128, 32), F32)
    pihalf = sbuf("pihalf", (128, 1), F32)
    epsc = sbuf("epsc", (128, 2), F32)
    magicc = sbuf("magicc", (128, 1), F32)
    onec = sbuf("onec", (128, 1), F32)
    wsc = sbuf("wsc", (128, 4), F32)
    onef = sbuf("onef", (1, 128), F32)
    bqcol = sbuf("bqcol", (128, 8), F32)
    bkcol = sbuf("bkcol", (128, 8), F32)

    psF = ctx.enter_context(nc.psum_tensor("psF", [128, 6, 512], F32))
    psT = ctx.enter_context(nc.psum_tensor("psT", [128, 512], BF16))
    psX = ctx.enter_context(nc.psum_tensor("psX", [128, 512], F32))

    cstT = nc.dram_tensor("cst", [128, CST_COLS], F32, kind="ExternalInput")
    dram["cst"] = cstT.ap()

    def consts():
        def csl(a, b):
            return APx(cstT, a, [[CST_COLS, 128], [1, b - a]])
        p.op("S", lambda e: e.dma_start(out=alphat[:], in_=csl(128, 1152)),
             dma="dcs2")
        p.op("S", lambda e: e.dma_start(out=mask_diag[:], in_=csl(1152, 1664)),
             dma="dcs2")
        p.op("G", lambda e: e.dma_start(out=iden[:], in_=csl(0, 128)), dma="dcst")
        p.op("G", lambda e: e.dma_start(out=mask_16[:], in_=csl(1664, 2176)),
             dma="dcst")
        p.op("G", lambda e: e.dma_start(out=mask_32[:], in_=csl(2176, 2688)),
             dma="dcst")
        p.op("G", lambda e: e.dma_start(out=mask_64[:], in_=csl(2688, 3200)),
             dma="dcst")
        p.op("V", lambda e: e.memset(ones_row[:], 1.0), mark=True)
        p.op("V", lambda e: e.memset(pihalf[:], PI / 2), mark=True)
        p.op("V", lambda e: e.memset(epsc[:, 0:1], EPS_OUT), mark=True)
        p.op("V", lambda e: e.memset(epsc[:, 1:2], EPS_IN), mark=True)
        p.op("V", lambda e: e.memset(magicc[:], MAGIC), mark=True)
        p.op("V", lambda e: e.memset(onec[:], 1.0), mark=True)
        p.op("V", lambda e: e.memset(onef[:], 1.0), mark=True)

        gdmas = []
        for c in range(4):
            gdmas.append(lambda e, c=c: e.dma_start(
                out=APx(thb_t, c, [[4, 128], [1, 1]]),
                in_=APx(dram["thb"].tensor, 128 * c, [[1, 128], [1, 1]])))
            gdmas.append(lambda e, c=c: e.dma_start(
                out=APx(lmbn_t, c, [[4, 128], [1, 1]]),
                in_=APx(dram["lmb"].tensor, 128 * c, [[1, 128], [1, 1]])))
            gdmas.append(lambda e, c=c: e.dma_start(
                out=APx(etan_t, c, [[4, 128], [1, 1]]),
                in_=APx(dram["eta"].tensor, 128 * c, [[1, 128], [1, 1]])))
            for dst, srcn, par in [(bq_r, "abq", 0), (bq_i, "abq", 1),
                                   (bk_r, "abk", 0), (bk_i, "abk", 1)]:
                gdmas.append(lambda e, c=c, dst=dst, srcn=srcn, par=par: e.dma_start(
                    out=APx(dst, c, [[4, 128], [1, 1]]),
                    in_=APx(dram[srcn].tensor, 256 * c + par,
                            [[64, 4], [2, 32], [1, 1]]),
                    allow_slow_non_contiguous=True))
        for nm in ["abv", "abo", "abf2", "bbq", "bbk", "bbv", "bbo"]:
            gdmas.append(lambda e, nm=nm: e.dma_start(
                out=rows[nm][0:1, :], in_=APx(dram[nm].tensor, 0, [[0, 1], [1, D]])))
        for ic in range(8):
            for dst, srcn in [(bqcol, "bbq"), (bkcol, "bbk")]:
                gdmas.append(lambda e, ic=ic, dst=dst, srcn=srcn: e.dma_start(
                    out=APx(dst, ic, [[8, 128], [1, 1]]),
                    in_=APx(dram[srcn].tensor, ic * 128, [[1, 128], [1, 1]])))
        for i, nm in enumerate(["n1g", "n1b", "n2g", "n2b"]):
            gdmas.append(lambda e, i=i, nm=nm: e.dma_start(
                out=grow[0:1, i * D:(i + 1) * D],
                in_=APx(dram[nm].tensor, 0, [[0, 1], [1, D]])))
        gdmas.append(lambda e: e.dma_start(
            out=APx(bf1c, 0, [[32, 128], [1, 32]]),
            in_=APx(dram["abf1"].tensor, 0, [[1, 128], [128, 32], [1, 1]]),
            allow_slow_non_contiguous=True))
        for fn in gdmas:
            p.op("G", fn, dma="dgth")

        def neg(e):
            e.tensor_scalar_mul(lmbn_t[:], lmbn_t[:], -1.0)
            return e.tensor_scalar_mul(etan_t[:], etan_t[:], -1.0)
        p.op("V", neg, deps=[p.tok("dgth")], mark=True)

        dep = [p.tok("dgth"), p.tok("evV")]
        for i, dst in enumerate([g1t, b1t, g2t, b2t]):
            for half in range(2):
                bk_ = (2 * i + half) % 4
                tm = p.op("T", lambda e, i=i, half=half, bk_=bk_: e.matmul(
                    psF[:, bk_, :], ones_row[0:1, :],
                    grow[0:1, i * D + half * 512: i * D + half * 512 + 512],
                    skip_group_check=True, start=True, stop=True), deps=dep + [p.tok("evV")], mark=True)
                p.op("V", lambda e, dst=dst, half=half, bk_=bk_: e.tensor_copy(
                    dst[:, half * 512:(half + 1) * 512], psF[:, bk_, :]),
                    deps=[tm], mark=True)
        p.barrier()

    consts()

    def make_emit_ln(stat, s0, s1, s2, s3):
        def emit_ln(src_ap, dst_ap, eps_col, gain=None, bias=None, dep=(), bn=0):
            c0, c1, c2, c3, c4, c5 = (bn + j for j in range(6))
            t1 = p.op("V", lambda e: e.reduce_sum(stat[:, c0:c0 + 1], src_ap, axis=AX.X),
                      deps=dep, inc="lns")
            t2 = p.op("A", lambda e: e.activation(s0[:], src_ap, AF.Square),
                      deps=dep, inc="lns")
            t3 = p.op("V", lambda e: e.reduce_sum(stat[:, c1:c1 + 1], s0[:], axis=AX.X),
                      deps=[t2], inc="lns")
            t4a = p.op("A", lambda e: e.activation(stat[:, c2:c2 + 1], stat[:, c0:c0 + 1],
                                                   AF.Square, scale=1.0 / 1024.0),
                       deps=[t1], inc="lns")
            t4b = p.op("A", lambda e: e.activation(stat[:, c3:c3 + 1], stat[:, c0:c0 + 1],
                                                   AF.Copy, scale=1.0 / 1024.0),
                       deps=[t1], inc="lns")
            t4d = p.op("A", lambda e: e.activation(stat[:, c4:c4 + 1], stat[:, c1:c1 + 1],
                                                   AF.Copy, scale=1.0 / 1024.0),
                       deps=[t3], inc="lns")
            t4c = p.op("G", lambda e: e.tensor_tensor(
                stat[:, c1:c1 + 1], stat[:, c4:c4 + 1], stat[:, c2:c2 + 1],
                ALU.subtract), deps=[t4a, t4d], inc="lns")
            t5a = p.op("A", lambda e: e.activation(stat[:, c2:c2 + 1], stat[:, c1:c1 + 1],
                                                   AF.Sqrt, bias=epsc[:, eps_col:eps_col + 1]),
                       deps=[t4c], inc="lns")
            t5b = p.op("V", lambda e: e.reciprocal(stat[:, c5:c5 + 1], stat[:, c2:c2 + 1]),
                       deps=[t5a], inc="lns")
            t5 = p.op("G", lambda e: e.tensor_copy(stat[:, c1:c1 + 1],
                                                   stat[:, c5:c5 + 1]),
                      deps=[t5b], inc="lns")
            if gain is None:
                return p.op("V", lambda e: e.tensor_scalar(
                    dst_ap, src_ap, stat[:, c3:c3 + 1], stat[:, c1:c1 + 1],
                    ALU.subtract, ALU.mult), deps=[t5, t4b], inc="lns")
            p.op("V", lambda e: e.tensor_scalar(
                s1[:], src_ap, stat[:, c3:c3 + 1], stat[:, c1:c1 + 1],
                ALU.subtract, ALU.mult), deps=[t5, t4b], inc="lns")
            p.op("V", lambda e: e.tensor_copy(s2[:, 0:1024], s1[:]), inc="lns")
            p.op("V", lambda e: e.tensor_mul(s3[:, 0:1024], s2[:, 0:1024], gain[:]),
                 inc="lns")
            return p.op("V", lambda e: e.tensor_add(dst_ap, s3[:, 0:1024], bias[:]),
                        inc="lns")
        return emit_ln

    if "masks" in debug_taps:
        mt = nc.dram_tensor("tap_masks", [4, 128, 512], F32, kind="ExternalOutput").ap()
        taps["masks"] = mt
        def dump_masks(e):
            e.dma_start(out=mt[0], in_=mask_diag[:])
            e.dma_start(out=mt[1], in_=mask_16[:])
            e.dma_start(out=mt[2], in_=mask_32[:])
            return e.dma_start(out=mt[3], in_=mask_64[:])
        for i in range(4):
            p.op("G", lambda e, i=i: e.dma_start(
                out=taps["masks"][i],
                in_=[mask_diag, mask_16, mask_32, mask_64][i][:]),
                deps=[p.tok("dcst"), p.tok("dcs2")], dma="dmask")
        p.barrier()

    def phase1a():
        pctx = ExitStack()

        def pbuf(name, shape, dtype=F32):
            return pctx.enter_context(nc.sbuf_tensor(name, list(shape), dtype))

        Wq = pbuf("Wq", (128, 8 * D), BF16)
        Wk = pbuf("Wk", (128, 8 * D), BF16)
        Wv = pbuf("Wv", (128, 8 * D), BF16)
        Tw1 = pbuf("Tw1", (128, 8 * LORA), BF16)
        Tw2 = pbuf("Tw2", (LORA, 512), BF16)

        for W, nm in [(Wq, "aWq"), (Wk, "aWk"), (Wv, "aWv")]:
            for dc in range(8):
                p.op("G", lambda e, W=W, nm=nm, dc=dc: e.dma_start(
                    out=W[:, dc * D:(dc + 1) * D],
                    in_=dram[nm][dc * 128:(dc + 1) * 128, :]), dma="dw1")
        for dc in range(8):
            p.op("G", lambda e, dc=dc: e.dma_start(
                out=Tw1[:, dc * LORA:(dc + 1) * LORA],
                in_=dram["tw1"][dc * 128:(dc + 1) * 128, :]), dma="dw1")
        p.op("G", lambda e: e.dma_start(out=Tw2[:], in_=dram["tw2"][:, :]), dma="dw1")
        wtok = p.tok("dw1")

        qr = pbuf("qr", (128, 2048), BF16)
        qi = pbuf("qi", (128, 2048), BF16)
        pr = pbuf("pr", (128, 2048), BF16)
        pi = pbuf("pi", (128, 2048), BF16)
        th = pbuf("th", (128, 2048), F32)
        nl = pbuf("nl", (128, 2048), F32)
        s0 = pbuf("s0", (128, 1024), F32)
        s1 = pbuf("s1", (128, 1024), F32)
        s2 = pbuf("s2", (128, 2048), BF16)
        s3 = pbuf("s3", (128, 2048), BF16)
        s4 = pbuf("s4", (128, 2048), BF16)
        xt0 = pbuf("xt0", (128, 2 * 1024), F32)
        h1f = pbuf("h1f", (128, 1024), F32)
        xn = pbuf("xn", (128, 1024), BF16)
        xnT = pbuf("xnT", (128, 4 * 1024), BF16)
        stat = pbuf("stat", (128, 12), F32)
        l1T = pbuf("l1T", (LORA, 512), BF16)
        vtmp = pbuf("vtmp", (128, 1024), BF16)

        emit_ln = make_emit_ln(stat, s0, s1, s2, s3)

        for si in range(n_tiles // ST):
            tts = [ST * si + jj for jj in range(ST)]
            xn_toks = {}
            for b in range(B):
                for j, tt in enumerate(tts):
                    slot = b * 2 + j
                    lx = p.op("S", lambda e, b=b, tt=tt: e.dma_start(
                        out=xt0[:, b * 1024:(b + 1) * 1024],
                        in_=xin[b, tt * 128:(tt + 1) * 128, :]),
                        deps=[p.tok("lns"), p.tok("phi")], dma="dx")
                    src = xt0[:, b * 1024:(b + 1) * 1024]
                    t_h1 = emit_ln(src, h1f[:], 0, dep=[lx])
                    p.op("G", lambda e, b=b, tt=tt: e.dma_start(
                        out=h1S[b, tt * 128:(tt + 1) * 128, :], in_=h1f[:]),
                        deps=[t_h1], dma="dh1")
                    t_xn = emit_ln(h1f[:], xn[:], 1, gain=g1t, bias=b1t,
                                   dep=[t_h1, p.tok("evT"), p.tok("dh1")], bn=6)
                    for r in range(2):
                        tms = []
                        for blk in range(4):
                            dc = r * 4 + blk
                            tm = p.op("T", lambda e, dc=dc, blk=blk: e.transpose(
                                psT[:, blk * 128:(blk + 1) * 128],
                                xn[:, dc * 128:(dc + 1) * 128], iden[:]),
                                deps=[t_xn, p.tok("xnt"), p.tok("evV")], mark=True)
                            tms.append(tm)
                        p.op("V", lambda e, slot=slot, r=r: e.tensor_copy(
                            xnT[:, slot * 1024 + r * 512: slot * 1024 + r * 512 + 512],
                            psT[:]), deps=tms, inc="xnt")
                    xn_toks[(b, j)] = p.tok("xnt")

            dep_all = [xn_toks[(b, j)] for b in range(B) for j in range(ST)] + [wtok]

            def mov_xnT(b, dc):
                return APx(xnT, (b * 2) * 1024 + dc * 128,
                           [[4096, 128], [1024, 2], [1, 128]])

            bank = [0]
            st_toks = []
            for b in range(B):
                for (plane, Wsb, boff, brAP) in [
                    (qr, Wq, 0, bq_r), (qi, Wq, 1, bq_i),
                    (pr, Wk, 0, bk_r), (pi, Wk, 1, bk_i),
                ]:
                    for c in range(4):
                        bk_ = bank[0] % 4
                        bank[0] += 1
                        for dc in range(8):
                            sel = APx(Wsb, dc * D + 256 * c + boff,
                                      [[8 * D, 128], [64, 4], [2, 32]])
                            p.op("T", lambda e, sel=sel, b=b, dc=dc, bk_=bk_: e.matmul(
                                psF[:, bk_, 0:256], sel, mov_xnT(b, dc),
                                skip_group_check=True, start=(dc == 0), stop=(dc == 7)),
                                deps=(dep_all + [p.tok("qkev")]) if dc == 0 else (),
                                mark=(dc == 7))
                        tm = p.tok("evT")
                        tk = p.op("V", lambda e, plane=plane, b=b, c=c, bk_=bk_, brAP=brAP:
                                  e.tensor_scalar_add(
                                      plane[:, b * 1024 + c * 256: b * 1024 + c * 256 + 256],
                                      psF[:, bk_, 0:256], brAP[:, c:c + 1]),
                                  deps=[tm], inc="qkev")

            lo_toks = []
            for b in range(B):
                for dc in range(8):
                    p.op("T", lambda e, b=b, dc=dc: e.matmul(
                        psX[0:48, 0:256], Tw1[:, dc * LORA:(dc + 1) * LORA],
                        mov_xnT(b, dc), skip_group_check=True, start=(dc == 0), stop=(dc == 7)),
                        deps=(dep_all + [p.tok("lo")]) if dc == 0 else (),
                        mark=(dc == 7))
                tm = p.tok("evT")
                tk = p.op("A", lambda e, b=b: e.activation(
                    l1T[0:48, b * 256:(b + 1) * 256], psX[0:48, 0:256], AF.Tanh),
                    deps=[tm], inc="lo")
                lo_toks.append(tk)

            th_toks = []
            for b in range(B):
                for c in range(4):
                    sel = APx(Tw2, 128 * c, [[512, LORA], [32, 4], [1, 32]])
                    p.op("T", lambda e, sel=sel, b=b: e.matmul(
                        psF[:, 4, 0:256], sel, l1T[0:48, b * 256:(b + 1) * 256],
                        skip_group_check=True, start=True, stop=True),
                        deps=lo_toks + [p.tok("thev")], mark=True)
                    tm = p.tok("evT")

                    def theta_ops(e, b=b, c=c):
                        off = b * 1024 + c * 256
                        e.tensor_scalar(th[:, off:off + 256], psF[:, 4, 0:256],
                                        thb_t[:, c:c + 1], PI / 2, ALU.add, ALU.min)
                        e.tensor_scalar_max(th[:, off:off + 256], th[:, off:off + 256],
                                            -PI / 2)
                        e.tensor_tensor(s0[:, 0:256], th[:, off:off + 256],
                                        th[:, off:off + 256], ALU.mult)
                        return e.tensor_scalar(nl[:, off:off + 256], s0[:, 0:256],
                                               etan_t[:, c:c + 1], lmbn_t[:, c:c + 1],
                                               ALU.mult, ALU.add)
                    tk = p.op("V", theta_ops, deps=[tm], inc="thev")
                    th_toks.append(tk)

            for b in range(B):
                for j, tt in enumerate(tts):
                    slot = b * 2 + j
                    for half in range(2):
                        for dc in range(8):
                            p.op("T", lambda e, slot=slot, dc=dc, half=half: e.matmul(
                                psF[:, 5, 0:512],
                                xnT[:, slot * 1024 + dc * 128: slot * 1024 + dc * 128 + 128],
                                Wv[:, dc * D + half * 512: dc * D + half * 512 + 512],
                                skip_group_check=True, start=(dc == 0), stop=False),
                                deps=(dep_all + [p.tok("vev")]) if dc == 0 else ())
                        p.op("T", lambda e, half=half: e.matmul(
                            psF[:, 5, 0:512], ones_row[0:1, :],
                            rows["abv"][0:1, half * 512: half * 512 + 512],
                            skip_group_check=True, start=False, stop=True), deps=[p.tok("evV")], mark=True)
                        tme = p.tok("evT")
                        p.op("V", lambda e, half=half: e.tensor_copy(
                            vtmp[:, half * 512: half * 512 + 512],
                            psF[:, 5, 0:512]), deps=[tme, p.tok("dvS")], inc="vev")
                    p.op("S", lambda e, b=b, tt=tt: e.dma_start(
                        out=vS[b, tt * 128:(tt + 1) * 128, :], in_=vtmp[:]),
                        deps=[p.tok("vev")], dma="dvS")

            phk = []
            for pl in (pr, pi):
                for half in range(2):
                    hs = slice(half * 1024, half * 1024 + 1024)
                    t1 = p.op("A", lambda e, pl=pl, hs=hs: e.activation(
                        s2[:, 0:1024], pl[:, hs], AF.Exp), deps=[p.tok("qkev")] + phk,
                        inc="phi")
                    t2 = p.op("V", lambda e, pl=pl, hs=hs: e.scalar_tensor_tensor(
                        s3[:, 0:1024], pl[:, hs], 1.0, s2[:, 0:1024],
                        ALU.add, ALU.subtract), deps=[t1], inc="phi")
                    t3 = p.op("V", lambda e, pl=pl, hs=hs: e.tensor_single_scalar(
                        s4[:, 0:1024], pl[:, hs], 0.0, ALU.is_gt),
                        deps=[p.tok("qkev")], inc="phi")
                    t4 = p.op("V", lambda e: e.tensor_mul(s3[:, 0:1024], s3[:, 0:1024],
                                                          s4[:, 0:1024]),
                              deps=[t2, t3], inc="phi")
                    t5 = p.op("V", lambda e, pl=pl, hs=hs: e.tensor_add(
                        pl[:, hs], s2[:, 0:1024], s3[:, 0:1024]), deps=[t4], inc="phi")
                    phk = [t5]

            def pl_slice(ten, b, j):
                return APx(ten, b * 1024 + j * 128, [[2048, 128], [256, 4], [1, 128]])

            def th_slice(ten, b, j):
                return APx(ten, b * 1024 + j * 128, [[2048, 128], [256, 4], [1, 128]])

            for b in range(B):
                for j, tt in enumerate(tts):
                    p.op("S", lambda e, b=b, j=j, tt=tt: e.dma_start(
                        out=qrS[b, tt], in_=pl_slice(qr, b, j)), deps=[p.tok("qkev")],
                        dma="dpl")
                    p.op("S", lambda e, b=b, j=j, tt=tt: e.dma_start(
                        out=qiS[b, tt], in_=pl_slice(qi, b, j)), dma="dpl")
                    p.op("S", lambda e, b=b, j=j, tt=tt: e.dma_start(
                        out=prS[b, tt], in_=pl_slice(pr, b, j)), deps=phk, dma="dpl")
                    p.op("S", lambda e, b=b, j=j, tt=tt: e.dma_start(
                        out=piS[b, tt], in_=pl_slice(pi, b, j)), dma="dpl")
                    p.op("S", lambda e, b=b, j=j, tt=tt: e.dma_start(
                        out=thS[b, tt], in_=th_slice(th, b, j)),
                        deps=[p.tok("thev")], dma="dpl")
                    p.op("S", lambda e, b=b, j=j, tt=tt: e.dma_start(
                        out=nlS[b, tt], in_=th_slice(nl, b, j)), dma="dpl")
            p.op("V", lambda e: e.engine_nop(), deps=[p.tok("dpl")], inc="qkev")
            p.op("V", lambda e: e.engine_nop(), deps=[p.tok("dpl")], inc="thev")
            p.op("V", lambda e: e.engine_nop(), deps=[p.tok("dpl")], inc="phi")

        pctx.close()

    phase1a()
    p.barrier()
    if max_phase < 2:
        p.replay()
        ctx.close()
        return nc, taps

    def phase1b():
        pctx = ExitStack()

        def pbuf(name, shape, dtype=F32):
            return pctx.enter_context(nc.sbuf_tensor("b_" + name, list(shape), dtype))

        Wo = pbuf("Wo", (128, 8 * D), BF16)
        for dc in range(8):
            p.op("G", lambda e, dc=dc: e.dma_start(
                out=Wo[:, dc * D:(dc + 1) * D],
                in_=dram["aWo"][dc * 128:(dc + 1) * 128, :]), dma="dwo")
        wtok = p.tok("dwo")

        qr = pbuf("qr", (128, 1024), BF16)
        qi = pbuf("qi", (128, 1024), BF16)
        pr = pbuf("pr", (128, 1024), BF16)
        pi = pbuf("pi", (128, 1024), BF16)
        th = pbuf("th", (128, 1024), F32)
        nl = pbuf("nl", (128, 1024), F32)
        mre = pbuf("mre", (128, 8 * 129), F32)
        mim = pbuf("mim", (128, 8 * 129), F32)
        s0 = pbuf("s0", (128, 1024), F32)
        s1 = pbuf("s1", (128, 1024), F32)
        s2 = pbuf("s2", (128, 1024), BF16)
        s3 = pbuf("s3", (128, 1024), BF16)
        s4 = pbuf("s4", (128, 1024), BF16)
        f16r = pbuf("f16r", (128, 1024), BF16)
        f16i = pbuf("f16i", (128, 1024), BF16)
        fdgc = pbuf("fdgc", (128, 1024), BF16)
        fdgs = pbuf("fdgs", (128, 1024), BF16)
        fqgc = pbuf("fqgc", (128, 1024), BF16)
        fqgs = pbuf("fqgs", (128, 1024), BF16)
        gsm = pbuf("gsm", (128, 30 * 64), F32)
        gsmb = pbuf("gsmb", (128, 32 * 64), BF16)
        LV = ("dg", "l16", "l32", "l64", "pv")
        Aq = {lv: pbuf("Aq_" + lv, (128, 1024), BF16) for lv in LV[:4]}
        Bq = {lv: pbuf("Bq_" + lv, (128, 1024), BF16) for lv in LV[:4]}
        Aqs = pbuf("Aq_st", (128, 1024), BF16)
        Bqs = pbuf("Bq_st", (128, 1024), BF16)
        Ck = {lv: pbuf("Ck_" + lv, (128, 1024), BF16) for lv in LV[:4]}
        Dk = {lv: pbuf("Dk_" + lv, (128, 1024), BF16) for lv in LV[:4]}
        vrg = pbuf("vrg", (128, 4 * 1024), BF16)
        Cst = pbuf("Cst", (128, 4 * 512), BF16)
        Dst = pbuf("Dst", (128, 4 * 512), BF16)
        h1l = pbuf("h1l", (128, 2 * 1024), BF16)
        xt1 = pbuf("xt1", (128, 1024), F32)
        Psb = {lv: pbuf("Psb" + lv, (128, 512), BF16) for lv in LV}
        yT = pbuf("yT", (128, 1024), BF16)
        x1f = pbuf("x1f", (128, 1024), F32)

        tap_y = tap("y", (B, NT, 128, D), BF16)
        tap_x1 = tap("x1", (B, NT, 128, D), F32)

        for ti in range(n_tiles):
            sl = ti % 2
            for b in range(B):
                l_fns = [
                    lambda e, b=b, ti=ti: e.dma_start(out=qr[:, b * 512:(b + 1) * 512],
                                                      in_=qrS[b, ti]),
                    lambda e, b=b, ti=ti: e.dma_start(out=qi[:, b * 512:(b + 1) * 512],
                                                      in_=qiS[b, ti]),
                    lambda e, b=b, ti=ti: e.dma_start(out=pr[:, b * 512:(b + 1) * 512],
                                                      in_=prS[b, ti]),
                    lambda e, b=b, ti=ti: e.dma_start(out=pi[:, b * 512:(b + 1) * 512],
                                                      in_=piS[b, ti]),
                    lambda e, b=b, ti=ti: e.dma_start(out=th[:, b * 512:(b + 1) * 512],
                                                      in_=thS[b, ti]),
                    lambda e, b=b, ti=ti: e.dma_start(out=nl[:, b * 512:(b + 1) * 512],
                                                      in_=nlS[b, ti]),
                    lambda e, b=b, ti=ti: e.dma_start(
                        out=h1l[:, b * 1024:(b + 1) * 1024],
                        in_=h1S[b, ti * 128:(ti + 1) * 128, :]),
                    lambda e, b=b, ti=ti, sl=sl: e.dma_start(
                        out=vrg[:, (b * 2 + sl) * 1024:(b * 2 + sl) * 1024 + 1024],
                        in_=vS[b, ti * 128:(ti + 1) * 128, :]),
                ]
                for i2, fn in enumerate(l_fns):
                    p.op("S", fn, deps=[p.tok("cmb"), p.tok("ymm"), p.tok("x1e")]
                         if i2 == 0 else (), dma="dld")
            ldtok = p.tok("dld")

            def scan_all(e):
                e.memset(APx(mre, 0, [[1032, 128], [129, 8], [1, 1]]), 0.0)
                e.memset(APx(mim, 0, [[1032, 128], [129, 8], [1, 1]]), 0.0)
                for b in range(B):
                    for c in range(4):
                        src = b * 512 + c * 128
                        dst = (b * 4 + c) * 129 + 1
                        e.tensor_tensor_scan(mre[:, dst:dst + 128],
                                             nl[:, src:src + 128], nl[:, src:src + 128],
                                             0.0, ALU.add, ALU.bypass)
                        inst = e.tensor_tensor_scan(mim[:, dst:dst + 128],
                                                    th[:, src:src + 128],
                                                    th[:, src:src + 128],
                                                    0.0, ALU.add, ALU.bypass)
                return inst
            t_scan = p.op("V", scan_all, deps=[ldtok, p.tok("gf"), p.tok("fb"), p.tok("dxo")], inc="msc")

            def mv4(m, off):
                return APx(m, off, [[1032, 128], [129, 8], [16, 8], [1, 16]])

            def mR(m, off):
                return APx(m, off, [[1032, 128], [129, 8], [16, 8], [0, 16]])

            def s_4(s, off=0):
                return APx(s, off, [[1024, 128], [128, 8], [16, 8], [1, 16]])

            def rangered(dep):
                t1 = p.op("V", lambda e: e.tensor_scalar(
                    s1[:], s0[:], 1.0 / (2 * PI), MAGIC, ALU.mult, ALU.add),
                    deps=dep, inc="rr")
                t2 = p.op("V", lambda e: e.tensor_scalar(
                    s1[:], s1[:], MAGIC, 2 * PI, ALU.subtract, ALU.mult), inc="rr")
                t3 = p.op("V", lambda e: e.tensor_sub(s0[:], s0[:], s1[:]), inc="rr")
                t4 = p.op("A", lambda e: e.activation(s3[:], s0[:], AF.Sin),
                          deps=[t3], inc="rr")
                t5 = p.op("A", lambda e: e.activation(s1[:], s0[:], AF.Abs),
                          deps=[t4], inc="rr")
                t6 = p.op("A", lambda e: e.activation(s4[:], s1[:], AF.Sin,
                                                      scale=-1.0, bias=pihalf[:]),
                          deps=[t5], inc="rr")
                return [t4, t6]

            tq1 = p.op("V", lambda e: e.tensor_tensor(s_4(s0), mv4(mre, 1), mR(mre, 0),
                                                      ALU.subtract), deps=[t_scan], inc="fb")
            tq2 = p.op("A", lambda e: e.activation(s2[:], s0[:], AF.Exp), deps=[tq1],
                       inc="fb")
            tq2b = p.op("V", lambda e: e.tensor_copy(f16r[:], s2[:]), deps=[tq2], inc="fb")
            tqe0 = p.op("V", lambda e: e.tensor_add(s1[:], s0[:], alphat[:]),
                        deps=[tq1], inc="fb")
            tqe1 = p.op("A", lambda e: e.activation(fqgc[:], s1[:], AF.Exp),
                        deps=[tqe0], inc="fb")
            tqd0 = p.op("V", lambda e: e.scalar_tensor_tensor(
                s1[:], s0[:], -1.0, alphat[:], ALU.mult, ALU.subtract),
                deps=[tqe1], inc="fb")
            tqd1 = p.op("A", lambda e: e.activation(s4[:], s1[:], AF.Exp),
                        deps=[tqd0], inc="fb")
            tqd2 = p.op("V", lambda e: e.tensor_copy(fdgc[:], s4[:]), deps=[tqd1], inc="fb")
            tq3b = p.op("V", lambda e: e.tensor_tensor(s_4(s0), mv4(mim, 1), mR(mim, 0),
                                                       ALU.subtract),
                        deps=[t_scan, tqd2], inc="fb")
            rr = rangered([tq3b])
            tqd3 = p.op("V", lambda e: e.tensor_mul(fdgs[:], fdgc[:], s3[:]),
                        deps=rr, inc="fb")
            tqd4 = p.op("V", lambda e: e.tensor_mul(fdgc[:], fdgc[:], s4[:]), inc="fb")
            tqe2 = p.op("V", lambda e: e.tensor_mul(fqgs[:], fqgc[:], s3[:]), inc="fb")
            tqe3 = p.op("V", lambda e: e.tensor_mul(fqgc[:], fqgc[:], s4[:]), inc="fb")
            tq4 = p.op("V", lambda e: e.tensor_mul(f16i[:], f16r[:], s3[:]),
                       deps=[tq2b] + rr, inc="fb")
            tq5 = p.op("V", lambda e: e.tensor_mul(f16r[:], f16r[:], s4[:]), inc="fb")
            ta1 = p.op("V", lambda e: e.tensor_mul(s2[:], qr[:], f16r[:]),
                       deps=[tq4, tq5, p.tok("evT")], inc="cmb")
            ta2 = p.op("V", lambda e: e.tensor_mul(s3[:], qi[:], f16i[:]), inc="cmb")
            ta3 = p.op("V", lambda e: e.tensor_add(Aq["l16"][:], s2[:], s3[:]), inc="cmb")
            tb1 = p.op("V", lambda e: e.tensor_mul(s2[:], qi[:], f16r[:]), inc="cmb")
            tb2 = p.op("V", lambda e: e.tensor_mul(s3[:], qr[:], f16i[:]), inc="cmb")
            tb3 = p.op("V", lambda e: e.tensor_sub(Bq["l16"][:], s2[:], s3[:]), inc="cmb")
            td1a = p.op("V", lambda e: e.tensor_mul(s2[:], qr[:], fqgc[:]),
                        deps=[tb3], inc="cmb")
            td1b = p.op("V", lambda e: e.tensor_mul(s3[:], qi[:], fqgs[:]), inc="cmb")
            td1 = p.op("V", lambda e: e.tensor_add(Aq["dg"][:], s2[:], s3[:]), inc="cmb")
            td2a = p.op("V", lambda e: e.tensor_mul(s2[:], qi[:], fqgc[:]), inc="cmb")
            td2b = p.op("V", lambda e: e.tensor_mul(s3[:], qr[:], fqgs[:]), inc="cmb")
            td2 = p.op("V", lambda e: e.tensor_sub(Bq["dg"][:], s2[:], s3[:]), inc="cmb")

            tk1 = p.op("V", lambda e: e.tensor_tensor(s_4(s0), mR(mre, 16), mv4(mre, 1),
                                                      ALU.subtract),
                       deps=[t_scan, td1, td2, tb3], inc="fb")
            tk2 = p.op("A", lambda e: e.activation(s2[:], s0[:], AF.Exp), deps=[tk1],
                       inc="fb")
            tk2b = p.op("V", lambda e: e.tensor_copy(f16r[:], s2[:]), deps=[tk2], inc="fb")
            tk3 = p.op("V", lambda e: e.tensor_tensor(s_4(s0), mR(mim, 16), mv4(mim, 1),
                                                      ALU.subtract), deps=[tk2b], inc="fb")
            rrk = rangered([tk3])
            tk4 = p.op("V", lambda e: e.tensor_mul(f16i[:], f16r[:], s3[:]),
                       deps=[tk2b] + rrk, inc="fb")
            tk5 = p.op("V", lambda e: e.tensor_mul(f16r[:], f16r[:], s4[:]), inc="fb")
            tc1 = p.op("V", lambda e: e.tensor_mul(s2[:], pr[:], f16r[:]),
                       deps=[tk4, tk5, p.tok("evT")], inc="cmb")
            tc2 = p.op("V", lambda e: e.tensor_mul(s3[:], pi[:], f16i[:]), inc="cmb")
            tc3 = p.op("V", lambda e: e.tensor_sub(Ck["l16"][:], s2[:], s3[:]), inc="cmb")
            te1 = p.op("V", lambda e: e.tensor_mul(s2[:], pr[:], f16i[:]), inc="cmb")
            te2 = p.op("V", lambda e: e.tensor_mul(s3[:], pi[:], f16r[:]), inc="cmb")
            te3 = p.op("V", lambda e: e.tensor_add(Dk["l16"][:], s2[:], s3[:]), inc="cmb")
            tf0a = p.op("V", lambda e: e.tensor_mul(s2[:], pr[:], fdgc[:]),
                        deps=[te3, tqd3, tqd4], inc="cmb")
            tf0b = p.op("V", lambda e: e.tensor_mul(s3[:], pi[:], fdgs[:]), inc="cmb")
            tf1 = p.op("V", lambda e: e.tensor_add(Ck["dg"][:], s2[:], s3[:]), inc="cmb")
            tf0c = p.op("V", lambda e: e.tensor_mul(s2[:], pi[:], fdgc[:]), inc="cmb")
            tf0d = p.op("V", lambda e: e.tensor_mul(s3[:], pr[:], fdgs[:]), inc="cmb")
            tf2 = p.op("V", lambda e: e.tensor_sub(Dk["dg"][:], s2[:], s3[:]), inc="cmb")
            base_tok = p.tok("cmb")

            def gsub_ops(kind):
                if kind == "q32":
                    return [(w, [[8, 8], [2, 4]], 16 * w, [[129, 8], [32, 4]],
                             0, [[129, 8], [32, 4]]) for w in range(2)]
                if kind == "k32":
                    return [(w, [[8, 8], [2, 4]], 32, [[129, 8], [32, 4]],
                             16 + 16 * w, [[129, 8], [32, 4]]) for w in range(2)]
                if kind == "q64":
                    return [(2 * v, [[8, 8], [4, 2], [1, 2]],
                             32 * v, [[129, 8], [64, 2], [0, 2]],
                             0, [[129, 8], [64, 2], [0, 2]]) for v in range(2)]
                if kind == "k64":
                    return [(2 * v, [[8, 8], [4, 2], [1, 2]],
                             64, [[129, 8], [64, 2], [0, 2]],
                             32 * v + 32, [[129, 8], [64, 2], [0, 2]]) for v in range(2)]
                if kind == "qst":
                    return [(0, [[8, 8], [4, 2], [1, 4]],
                             0, [[129, 8], [64, 2], [0, 4]],
                             0, [[129, 8], [0, 2], [0, 4]])]
                if kind == "kst":
                    return [(0, [[8, 8], [4, 2], [1, 4]],
                             128, [[129, 8], [0, 2], [0, 4]],
                             64, [[129, 8], [64, 2], [0, 4]])]
                raise ValueError(kind)

            KINDS = [("q32", 0, 1), ("q64", 2, 3), ("qst", 4, 5),
                     ("k32", 6, 7), ("k64", 8, 9), ("kst", 10, 11)]
            NKI = len(KINDS)

            def gsl(ki, j):
                o = (ki * 5 + j) * 64
                return gsm[:, o:o + 64]

            def mk_subs(kind, ki, m):
                def do(e, kind=kind, ki=ki, m=m):
                    inst = None
                    for (doff, dpat, hoff, hpat, loff, lpat) in gsub_ops(kind):
                        dst = APx(gsm, ki * 320 + doff, [[1920, 128]] + dpat)
                        hi = APx(m, hoff, [[1032, 128]] + hpat)
                        lo = APx(m, loff, [[1032, 128]] + lpat)
                        inst = e.tensor_tensor(dst, hi, lo, ALU.subtract)
                    return inst
                return do

            smb = lambda i: gsmb[:, i * 64:(i + 1) * 64]
            u1t, u2t, u3t, u4t, u5t, u6t, u7t, u8t, u10t = ({} for _ in range(9))
            for ki, (kind, slr, sli) in enumerate(KINDS):
                u1t[ki] = p.op("V", mk_subs(kind, ki, mre),
                               deps=([t_scan, p.tok("gf")] if ki == 0 else ()),
                               inc="gf")
            for ki in range(NKI):
                u2t[ki] = p.op("A", lambda e, ki=ki: e.activation(
                    gsl(ki, 1), gsl(ki, 0), AF.Exp), deps=[u1t[ki]], inc="gf")
            for ki, (kind, slr, sli) in enumerate(KINDS):
                u3t[ki] = p.op("V", mk_subs(kind, ki, mim), deps=[u2t[ki]], inc="gf")
            for ki in range(NKI):
                u4t[ki] = p.op("A", lambda e, ki=ki: e.activation(
                    gsl(ki, 2), gsl(ki, 0), AF.Copy, scale=1.0 / (2 * PI),
                    bias=MAGIC), deps=[u3t[ki]], inc="gf")
            for ki in range(NKI):
                u5t[ki] = p.op("A", lambda e, ki=ki: e.activation(
                    gsl(ki, 3), gsl(ki, 0), AF.Copy, scale=1.0 / (2 * PI)),
                    deps=[u3t[ki]], inc="gf")
            for ki in range(NKI):
                u6t[ki] = p.op("V", lambda e, ki=ki: e.scalar_tensor_tensor(
                    gsl(ki, 4), gsl(ki, 2), MAGIC, gsl(ki, 3),
                    ALU.subtract, ALU.subtract), deps=[u4t[ki], u5t[ki]], inc="gf")
            for ki, (kind, slr, sli) in enumerate(KINDS):
                u7t[ki] = p.op("A", lambda e, ki=ki, sli=sli: e.activation(
                    smb(sli), gsl(ki, 4), AF.Sin, scale=-2 * PI),
                    deps=[u6t[ki]] + ([p.tok("ch")] if ki == 0 else []), inc="gf")
            for ki in range(NKI):
                u8t[ki] = p.op("A", lambda e, ki=ki: e.activation(
                    gsl(ki, 2), gsl(ki, 4), AF.Abs), deps=[u6t[ki]], inc="gf")
            for ki, (kind, slr, sli) in enumerate(KINDS):
                u10t[ki] = p.op("A", lambda e, ki=ki, slr=slr: e.activation(
                    smb(slr), gsl(ki, 2), AF.Sin, scale=-2 * PI,
                    bias=pihalf[:]), deps=[u8t[ki]], inc="gf")
            gq_out = {}
            for ki, (kind, slr, sli) in enumerate(KINDS):
                ta = p.op("V", lambda e, ki=ki, sli=sli: e.tensor_mul(
                    smb(sli), smb(sli), gsl(ki, 1)),
                    deps=[u2t[ki], u7t[ki]], inc="gf")
                tb = p.op("V", lambda e, ki=ki, slr=slr: e.tensor_mul(
                    smb(slr), smb(slr), gsl(ki, 1)),
                    deps=[u10t[ki]], inc="gf")
                gq_out[kind] = [ta, tb]
            gq32, gq64, gqst = gq_out["q32"], gq_out["q64"], gq_out["qst"]
            gk32, gk64, gkst = gq_out["k32"], gq_out["k64"], gq_out["kst"]

            def gbb(slot):
                return APx(gsmb, slot * 64, [[2048, 128], [8, 8], [1, 8], [0, 16]])

            def chain_q(Asrc, Bsrc, Adst, Bdst, slr, sli, dep):
                t1 = p.op("V", lambda e: e.tensor_mul(s2[:], Asrc[:], gbb(slr)),
                          deps=list(dep) + [p.tok("evT")], inc="ch")
                t2 = p.op("V", lambda e: e.tensor_mul(s3[:], Bsrc[:], gbb(sli)), inc="ch")
                t3 = p.op("V", lambda e: e.tensor_add(Adst[:], s2[:], s3[:]), inc="ch")
                t4 = p.op("V", lambda e: e.tensor_mul(s2[:], Bsrc[:], gbb(slr)), inc="ch")
                t5 = p.op("V", lambda e: e.tensor_mul(s3[:], Asrc[:], gbb(sli)), inc="ch")
                t6 = p.op("V", lambda e: e.tensor_sub(Bdst[:], s2[:], s3[:]), inc="ch")
                return [t3, t6]

            def chain_k(Csrc, Dsrc, CdstAP, DdstAP, slr, sli, dep):
                t1 = p.op("V", lambda e: e.tensor_mul(s2[:], Csrc[:], gbb(slr)),
                          deps=list(dep) + [p.tok("evT")], inc="ch")
                t2 = p.op("V", lambda e: e.tensor_mul(s3[:], Dsrc[:], gbb(sli)), inc="ch")
                t3 = p.op("V", lambda e: e.tensor_tensor(CdstAP, s2[:], s3[:],
                                                         ALU.subtract), inc="ch")
                t4 = p.op("V", lambda e: e.tensor_mul(s2[:], Dsrc[:], gbb(slr)), inc="ch")
                t5 = p.op("V", lambda e: e.tensor_mul(s3[:], Csrc[:], gbb(sli)), inc="ch")
                t6 = p.op("V", lambda e: e.tensor_tensor(DdstAP, s2[:], s3[:], ALU.add),
                          inc="ch")
                return [t3, t6]

            cq32 = chain_q(Aq["l16"], Bq["l16"], Aq["l32"], Bq["l32"], 0, 1,
                           gq32 + [base_tok])
            cq64 = chain_q(Aq["l32"], Bq["l32"], Aq["l64"], Bq["l64"], 2, 3, gq64 + cq32)
            cqst = chain_q(Aq["l64"], Bq["l64"], Aqs, Bqs, 4, 5, gqst + cq64)
            ck32 = chain_k(Ck["l16"], Dk["l16"], Ck["l32"][:], Dk["l32"][:], 6, 7,
                           gk32 + [base_tok])
            ck64 = chain_k(Ck["l32"], Dk["l32"], Ck["l64"][:], Dk["l64"][:], 8, 9,
                           gk64 + ck32)
            CstV = APx(Cst, sl * 512, [[2048, 128], [1024, 2], [128, 4], [1, 128]])
            DstV = APx(Dst, sl * 512, [[2048, 128], [1024, 2], [128, 4], [1, 128]])
            ckst = chain_k(Ck["l64"], Dk["l64"], CstV, DstV, 10, 11,
                           gkst + ck64 + [p.tok("evT")])

            masks = {"dg": mask_diag, "l16": mask_16, "l32": mask_32, "l64": mask_64}
            lv_ready = {"dg": [td1, td2, tf1, tf2], "l16": [base_tok],
                        "l32": cq32 + ck32, "l64": cq64 + ck64,
                        "pv": cqst + ckst}
            gctr = [0]
            for b in range(B):
                for c in range(4):
                    ev_toks = {}
                    for lv in LV:
                        if lv == "pv" and ti == 0:
                            continue
                        g = gctr[0]
                        gctr[0] += 1
                        colg = (g % 4) * 128
                        foff = b * 512 + c * 128
                        if lv == "pv":
                            soff = (b * 2 + ((ti - 1) % 2)) * 512 + c * 128
                            Ssrc, Msrc = [Cst, Dst], [Aqs, Bqs]
                        else:
                            soff = foff
                            Ssrc, Msrc = [Ck[lv], Dk[lv]], [Aq[lv], Bq[lv]]
                        first = True
                        for pi_ in range(2):
                            for hc in range(4):
                                Ssl = Ssrc[pi_][32 * hc:32 * hc + 32, soff:soff + 128]
                                Msl = Msrc[pi_][32 * hc:32 * hc + 32, foff:foff + 128]
                                dps = [d for d in (lv_ready[lv] + [p.tok("pev"),
                                                                   p.tok("x1e")])
                                       if d is not None] if first and hc == 0 else ()
                                p.op("T", lambda e, Ssl=Ssl, Msl=Msl, hc=hc, colg=colg,
                                     first=first: e.matmul(
                                         psF[:, hc, colg:colg + 128], Ssl, Msl,
                                         skip_group_check=True, start=first, stop=(not first),
                                         tile_position=(32 * hc, 0)),
                                     deps=dps, mark=(pi_ == 1 and hc == 3))
                            first = False
                        tmm = p.tok("evT")
                        if lv == "pv":
                            def evac_pv(e, colg=colg):
                                for hc2 in range(4):
                                    inst = e.activation(
                                        Psb["pv"][:, hc2 * 128:(hc2 + 1) * 128],
                                        psF[:, hc2, colg:colg + 128], AF.Copy)
                                return inst
                            tk = p.op("A", evac_pv, deps=[tmm, p.tok("ymm")], inc="pev")
                        else:
                            def evac_lv(e, colg=colg, lv=lv):
                                for hc2 in range(4):
                                    inst = e.tensor_mul(
                                        Psb[lv][:, hc2 * 128:(hc2 + 1) * 128],
                                        psF[:, hc2, colg:colg + 128],
                                        masks[lv][:, hc2 * 128:(hc2 + 1) * 128])
                                return inst
                            tk = p.op("V", evac_lv, deps=[tmm, p.tok("ymm")], inc="pev")
                        ev_toks[lv] = tk
                    ydeps = [ev_toks.get(lv) for lv in LV if ev_toks.get(lv)]
                    vsl = b * 2 + sl
                    pvs = b * 2 + ((ti - 1) % 2)
                    for hc in range(4):
                        h = 4 * c + hc
                        yb = 4 + (h // 8)
                        ycol = ((h % 8) // 2) * 128
                        pofs = 64 * (h % 2)
                        lvlist = ["dg", "l16", "l32", "l64"] + (["pv"] if ti > 0 else [])
                        for li, lv in enumerate(lvlist):
                            vs = pvs if lv == "pv" else vsl
                            vst = vrg[:, vs * 1024 + 64 * h: vs * 1024 + 64 * h + 64]
                            mv = Psb[lv][:, hc * 128:hc * 128 + 128]
                            p.op("T", lambda e, vst=vst, mv=mv, yb=yb, ycol=ycol,
                                 li=li, pofs=pofs, last=(li == len(lvlist) - 1):
                                 e.matmul(
                                     psF[pofs:pofs + 64, yb, ycol:ycol + 128],
                                     vst, mv, skip_group_check=True, start=(li == 0), stop=last,
                                     tile_position=(0, pofs)),
                                 deps=(ydeps + [ldtok, p.tok("yev")]) if li == 0 else (),
                                 inc="ymm")
                ycht = [p.tok("ymm"), p.tok("evT")]
                te1 = p.op("V", lambda e: e.tensor_copy(yT[:, 0:512], psF[:, 4, :]),
                           deps=ycht, inc="yev")
                te2 = p.op("V", lambda e: e.tensor_copy(yT[:, 512:1024], psF[:, 5, :]),
                           deps=ycht, inc="yev")
                if tap_y is not None:
                    p.op("S", lambda e, b=b, ti=ti: e.dma_start(
                        out=taps["y"][b, ti], in_=yT[:]), deps=[te1, te2], dma="dtap")
                for half in range(2):
                    for dc in range(8):
                        p.op("T", lambda e, dc=dc, half=half: e.matmul(
                            psF[:, half, 0:512],
                            yT[:, dc * 128:(dc + 1) * 128],
                            Wo[:, dc * D + half * 512: dc * D + half * 512 + 512],
                            skip_group_check=True, start=(dc == 0), stop=False),
                            deps=[te1, te2, p.tok("x1e"), wtok] if dc == 0 else ())
                    p.op("T", lambda e, half=half: e.matmul(
                        psF[:, half, 0:512], ones_row[0:1, :],
                        rows["abo"][0:1, half * 512: half * 512 + 512],
                        skip_group_check=True, start=False, stop=True), mark=True)
                two = p.tok("evT")
                tx1 = p.op("V", lambda e, b=b: e.tensor_copy(
                    s1[:], h1l[:, b * 1024:(b + 1) * 1024]),
                    deps=[two, p.tok("dxs"), p.tok("dtap")], inc="x1e")
                tx2 = p.op("V", lambda e: e.tensor_tensor(
                    x1f[:, 0:512], psF[:, 0, 0:512], s1[:, 0:512], ALU.add),
                    deps=[tx1], inc="x1e")
                tx3 = p.op("V", lambda e: e.tensor_tensor(
                    x1f[:, 512:1024], psF[:, 1, 0:512], s1[:, 512:1024], ALU.add),
                    inc="x1e")
                if tap_x1 is not None:
                    p.op("S", lambda e, b=b, ti=ti: e.dma_start(
                        out=taps["x1"][b, ti], in_=x1f[:]), deps=[tx2, tx3], dma="dtap")
                p.op("G", lambda e, b=b, ti=ti: e.dma_start(
                    out=x1s[b, ti * 128:(ti + 1) * 128, :], in_=x1f[:]),
                    deps=[tx2, tx3], dma="dxs")
                lx2 = p.op("S", lambda e, b=b, ti=ti: e.dma_start(
                    out=xt1[:], in_=xin[b, ti * 128:(ti + 1) * 128, :]),
                    deps=[tx2, tx3, p.tok("x1e")], dma="dx2")
                tx4 = p.op("V", lambda e: e.tensor_add(s0[:], x1f[:], xt1[:]),
                           deps=[lx2], inc="x1e")
                p.op("S", lambda e, b=b, ti=ti: e.dma_start(
                    out=xout[b, ti * 128:(ti + 1) * 128, :], in_=s0[:]),
                    deps=[tx4], dma="dxo")

        pctx.close()

    phase1b()
    p.barrier()

    def phase2():
        pctx = ExitStack()

        def pbuf(name, shape, dtype=F32):
            return pctx.enter_context(nc.sbuf_tensor("c_" + name, list(shape), dtype))

        Wf1 = pbuf("Wf1", (128, 8 * FFN), BF16)
        Wf2 = pbuf("Wf2", (128, 32 * D), BF16)
        for jc in range(8):
            p.op("G", lambda e, jc=jc: e.dma_start(
                out=Wf1[:, jc * FFN:(jc + 1) * FFN],
                in_=dram["aWf1"][jc * 128:(jc + 1) * 128, :]), dma="dwf")
        for fc in range(32):
            p.op("G", lambda e, fc=fc: e.dma_start(
                out=Wf2[:, fc * D:(fc + 1) * D],
                in_=dram["aWf2"][fc * 128:(fc + 1) * 128, :]), dma="dwf")
        wtok2 = p.tok("dwf")

        x1l = pbuf("x1l", (128, 1024), F32)
        xol = pbuf("xol", (128, 1024), F32)
        h2 = pbuf("h2", (128, 1024), BF16)
        h2T = pbuf("h2T", (128, 1024), BF16)
        hg = pbuf("hg", (128, FFN), BF16)
        xs1f = pbuf("xs1f", (128, 1024), F32)
        stat = pbuf("stat", (128, 12), F32)
        s0 = pbuf("s0", (128, 1024), F32)
        s1 = pbuf("s1", (128, 1024), F32)
        s2 = pbuf("s2", (128, 1024), BF16)
        s3 = pbuf("s3", (128, 1024), BF16)
        eln = make_emit_ln(stat, s0, s1, s2, s3)
        tap_xs1 = tap("xs1", (B, NT, 128, D), F32)

        for ti in range(n_tiles):
            for b in range(B):
                lx = p.op("S", lambda e, b=b, ti=ti: e.dma_start(
                    out=x1l[:], in_=x1s[b, ti * 128:(ti + 1) * 128, :]),
                    deps=[p.tok("lns")], dma="dl2")
                lxo = p.op("S", lambda e, b=b, ti=ti: e.dma_start(
                    out=xol[:], in_=xout[b, ti * 128:(ti + 1) * 128, :]),
                    deps=[p.tok("x2e")], dma="dl2")
                t_h2 = eln(x1l[:], h2[:], 1, gain=g2t, bias=b2t,
                           dep=[lx, p.tok("evT")], bn=0)
                ttoks = []
                for r in range(2):
                    tms = []
                    for blk in range(4):
                        dc = r * 4 + blk
                        tm = p.op("T", lambda e, dc=dc, blk=blk: e.transpose(
                            psT[:, blk * 128:(blk + 1) * 128],
                            h2[:, dc * 128:(dc + 1) * 128], iden[:]),
                            deps=[t_h2, p.tok("xnt")], mark=True)
                        tms.append(tm)
                    ttoks.append(p.op("V", lambda e, r=r: e.tensor_copy(
                        h2T[:, r * 512:(r + 1) * 512], psT[:]), deps=tms, inc="xnt"))
                f1toks = {}
                for fb in range(32):
                    slot = fb % 16
                    bank, col = slot // 4, (slot % 4) * 128
                    for jc in range(8):
                        dps = ([ttoks[0], ttoks[1], wtok2, p.tok("f1e")]
                               if jc == 0 else ())
                        p.op("T", lambda e, fb=fb, jc=jc, bank=bank, col=col: e.matmul(
                            psF[:, bank, col:col + 128],
                            Wf1[:, jc * FFN + fb * 128: jc * FFN + fb * 128 + 128],
                            h2T[:, jc * 128:(jc + 1) * 128],
                            skip_group_check=True, start=(jc == 0), stop=(jc == 7)),
                            deps=dps, mark=(jc == 7))
                    tmm = p.tok("evT")
                    f1toks[fb] = p.op("A", lambda e, fb=fb, bank=bank, col=col:
                                      e.activation(
                                          hg[:, fb * 128:(fb + 1) * 128],
                                          psF[:, bank, col:col + 128],
                                          AF.Gelu_apprx_tanh,
                                          bias=bf1c[:, fb:fb + 1]),
                                      deps=[tmm], inc="f1e")
                for half in range(2):
                    for fc in range(32):
                        dps = ([f1toks[fb] for fb in range(32)] + [p.tok("x2e")]) \
                            if fc == 0 else ()
                        p.op("T", lambda e, half=half, fc=fc: e.matmul(
                            psF[:, 4 + half, 0:512],
                            hg[:, fc * 128:(fc + 1) * 128],
                            Wf2[:, fc * D + half * 512: fc * D + half * 512 + 512],
                            skip_group_check=True, start=(fc == 0), stop=False), deps=dps)
                    p.op("T", lambda e, half=half: e.matmul(
                        psF[:, 4 + half, 0:512], ones_row[0:1, :],
                        rows["abf2"][0:1, half * 512: half * 512 + 512],
                        skip_group_check=True, start=False, stop=True), mark=True)
                t2w = p.tok("evT")
                tx1 = p.op("V", lambda e: e.tensor_tensor(
                    xs1f[:, 0:512], psF[:, 4, 0:512], xol[:, 0:512], ALU.add),
                    deps=[t2w, lxo, p.tok("dso"), p.tok("dtap")], inc="x2e")
                tx2 = p.op("V", lambda e: e.tensor_tensor(
                    xs1f[:, 512:1024], psF[:, 5, 0:512], xol[:, 512:1024],
                    ALU.add), inc="x2e")
                if tap_xs1 is not None:
                    p.op("S", lambda e, b=b, ti=ti: e.dma_start(
                        out=taps["xs1"][b, ti], in_=xs1f[:]),
                        deps=[tx1, tx2], dma="dtap")
                p.op("S", lambda e, b=b, ti=ti: e.dma_start(
                    out=xout[b, ti * 128:(ti + 1) * 128, :], in_=xs1f[:]),
                    deps=[tx1, tx2, p.tok("dtap")], dma="dso")
        pctx.close()

    if max_phase >= 2:
        phase2()
        p.barrier()

    def phase0():
        pctx = ExitStack()

        def pbuf(name, shape, dtype=F32):
            return pctx.enter_context(nc.sbuf_tensor("d_" + name, list(shape), dtype))

        wt = pbuf("wt", (128, 2 * FFN), F32)
        wa = pbuf("wa", (128, FFN), F32)
        acc = pbuf("acc", (128, 40), F32)
        acc2 = pbuf("acc2", (128, 40), F32)
        tn = pbuf("tn", (128, FFN), BF16)
        tb = pbuf("tb", (128, FFN), BF16)
        sc2 = pbuf("sc2", (128, 4), F32)
        scB = pbuf("scB", (1, 2), F32)

        def stats(src_name, nrt, ncols, col0):
            for rt in range(nrt):
                sl = rt % 2
                la = p.op("S", lambda e, rt=rt, sl=sl, src_name=src_name,
                          ncols=ncols: e.dma_start(
                              out=wt[:, sl * FFN: sl * FFN + ncols],
                              in_=dram[src_name][rt * 128:(rt + 1) * 128, :]),
                          deps=[p.tok("p0a")], dma="dp0l")
                a1 = p.op("A", lambda e, ncols=ncols, sl=sl: e.activation(
                    wa[:, 0:ncols], wt[:, sl * FFN: sl * FFN + ncols], AF.Abs),
                    deps=[la, p.tok("p0r")], inc="p0a")
                p.op("V", lambda e, rt=rt, ncols=ncols, col0=col0:
                     e.reduce_sum(acc[:, col0 + rt:col0 + rt + 1],
                                  wa[:, 0:ncols], axis=AX.X),
                     deps=[a1], inc="p0r")

        stats("fw1", 32, D, 0)
        stats("fw2", 8, FFN, 32)

        def scalar_pipeline(col0, n, wcol, pc, si):
            ca = p.op("A", lambda e, col0=col0, n=n: e.activation(
                acc2[:, col0:col0 + n], acc[:, col0:col0 + n], AF.Copy),
                deps=[p.tok("p0r")], inc="p0s")
            cv = p.op("V", lambda e, col0=col0, n=n, si=si: e.reduce_sum(
                sc2[:, si:si + 1], acc2[:, col0:col0 + n], axis=AX.X),
                deps=[ca], inc="p0s")
            tm1 = p.op("T", lambda e, pc=pc, si=si: e.matmul(
                psX[0:1, pc:pc + 1], sc2[:, si:si + 1], onec[:],
                skip_group_check=True, start=True, stop=True),
                deps=[cv, p.tok("p0s")], mark=True)
            cc1 = p.op("V", lambda e, pc=pc, si=si: e.tensor_copy(
                scB[0:1, si // 2:si // 2 + 1], psX[0:1, pc:pc + 1]),
                deps=[tm1], inc="p0s")
            tm2 = p.op("T", lambda e, pc=pc, si=si: e.matmul(
                psX[:, pc + 8:pc + 9], onef[0:1, :],
                scB[0:1, si // 2:si // 2 + 1],
                skip_group_check=True, start=True, stop=True), deps=[cc1], mark=True)
            cc2 = p.op("A", lambda e, pc=pc, si=si: e.activation(
                sc2[:, si:si + 1], psX[:, pc + 8:pc + 9], AF.Copy,
                scale=1.0 / (D * FFN)), deps=[tm2], inc="p0s")
            cm = p.op("V", lambda e, wcol=wcol, si=si: e.tensor_scalar_max(
                wsc[:, wcol:wcol + 1], sc2[:, si:si + 1], 1e-5),
                deps=[cc2], inc="p0s")
            ch = p.op("A", lambda e, wcol=wcol, si=si: e.activation(
                sc2[:, si + 1:si + 2], wsc[:, wcol:wcol + 1], AF.Copy),
                deps=[cm], inc="p0s")
            p.op("V", lambda e, wcol=wcol, si=si: e.reciprocal(
                wsc[:, wcol + 1:wcol + 2], sc2[:, si + 1:si + 2]),
                deps=[ch], inc="p0s")

        scalar_pipeline(0, 32, 0, 0, 0)
        scalar_pipeline(32, 8, 2, 4, 2)
        sc_tok = p.tok("p0s")

        def quant_store(src_name, nrt, ncols, wcol, dstT):
            nblk = ncols // 128
            for rt in range(nrt):
                sl = rt % 2
                la = p.op("S", lambda e, rt=rt, sl=sl, src_name=src_name,
                          ncols=ncols: e.dma_start(
                              out=wt[:, sl * FFN: sl * FFN + ncols],
                              in_=dram[src_name][rt * 128:(rt + 1) * 128, :]),
                          deps=[p.tok("p0q")], dma="dp0l")
                v1 = p.op("V", lambda e, sl=sl, ncols=ncols, wcol=wcol:
                          e.tensor_scalar(wa[:, 0:ncols],
                                          wt[:, sl * FFN: sl * FFN + ncols],
                                          wsc[:, wcol + 1:wcol + 2], MAGIC,
                                          ALU.mult, ALU.add),
                          deps=[la, sc_tok, p.tok("evT")], inc="p0q")
                p.op("V", lambda e, ncols=ncols: e.tensor_scalar(
                    wa[:, 0:ncols], wa[:, 0:ncols], MAGIC, 1.0,
                    ALU.subtract, ALU.min), inc="p0q")
                v3 = p.op("V", lambda e, ncols=ncols: e.tensor_scalar_max(
                    tn[:, 0:ncols], wa[:, 0:ncols], -1.0), inc="p0q")
                for dc in range(nblk):
                    blk = dc % 4
                    tm = p.op("T", lambda e, dc=dc, blk=blk: e.transpose(
                        psT[:, blk * 128:(blk + 1) * 128],
                        tn[:, dc * 128:(dc + 1) * 128], iden[:]),
                        deps=[v3, p.tok("xnt")], mark=True)
                    cv = p.op("V", lambda e, dc=dc, blk=blk: e.tensor_copy(
                        tb[:, dc * 128:(dc + 1) * 128],
                        psT[:, blk * 128:(blk + 1) * 128]),
                        deps=[tm, p.tok("dp0")], inc="xnt")
                    p.op("G", lambda e, dc=dc, rt=rt, dstT=dstT: e.dma_start(
                        out=dstT[dc * 128:(dc + 1) * 128,
                                 rt * 128:(rt + 1) * 128],
                        in_=tb[:, dc * 128:(dc + 1) * 128]),
                        deps=[cv], dma="dp0")

        quant_store("fw1", 32, D, 0, w1tq)
        quant_store("fw2", 8, FFN, 2, w2tq)

        for nm in ("bWq", "bWk", "bWv", "bWo"):
            for rt in range(8):
                lb = p.op("G", lambda e, nm=nm, rt=rt: e.dma_start(
                    out=tn[:, 0:1024],
                    in_=dram[nm][rt * 128:(rt + 1) * 128, :]),
                    deps=[p.tok("evT")], dma="dp0b")
                for dc in range(8):
                    blk = dc % 4
                    tm = p.op("T", lambda e, dc=dc, blk=blk: e.transpose(
                        psT[:, blk * 128:(blk + 1) * 128],
                        tn[:, dc * 128:(dc + 1) * 128], iden[:]),
                        deps=[lb, p.tok("xnt")], mark=True)
                    cv = p.op("V", lambda e, dc=dc, blk=blk: e.tensor_copy(
                        tb[:, dc * 128:(dc + 1) * 128],
                        psT[:, blk * 128:(blk + 1) * 128]),
                        deps=[tm, p.tok("dp0")], inc="xnt")
                    p.op("G", lambda e, dc=dc, rt=rt, nm=nm: e.dma_start(
                        out=bWT[nm][dc * 128:(dc + 1) * 128,
                                    rt * 128:(rt + 1) * 128],
                        in_=tb[:, dc * 128:(dc + 1) * 128]),
                        deps=[cv], dma="dp0")
        pctx.close()

    def phase_bitffn(tapname):
        pctx = ExitStack()

        def pbuf(name, shape, dtype=F32):
            return pctx.enter_context(
                nc.sbuf_tensor(tapname + "_" + name, list(shape), dtype))

        W1 = pbuf("W1", (128, 8 * FFN), BF16)
        W2 = pbuf("W2", (128, 32 * D), BF16)
        for jc in range(8):
            p.op("S", lambda e, jc=jc: e.dma_start(
                out=W1[:, jc * FFN:(jc + 1) * FFN],
                in_=w1tq[jc * 128:(jc + 1) * 128, :]), dma="dwb")
        for fc in range(32):
            p.op("S", lambda e, fc=fc: e.dma_start(
                out=W2[:, fc * D:(fc + 1) * D],
                in_=w2tq[fc * 128:(fc + 1) * 128, :]), dma="dwb")
        wtokb = p.tok("dwb")

        xol = pbuf("xol", (128, 1024), F32)
        zf = pbuf("zf", (128, 1024), F32)
        zq = pbuf("zq", (128, 1024), BF16)
        zqT = pbuf("zqT", (128, 1024), BF16)
        hg = pbuf("hg", (128, FFN), BF16)
        z2q = pbuf("z2q", (128, 1024), BF16)
        z2T = pbuf("z2T", (128, 1024), BF16)
        xsf = pbuf("xsf", (128, 1024), F32)
        stat = pbuf("stat", (128, 40), F32)
        s0 = pbuf("s0", (128, 1024), F32)
        s1 = s0
        eln = make_emit_ln(stat, s0, s1, s1, s1)
        tap_xs = tap(tapname, (B, NT, 128, D), F32)

        for ti in range(n_tiles):
            for b in range(B):
                lxo = p.op("S", lambda e, b=b, ti=ti: e.dma_start(
                    out=xol[:], in_=xout[b, ti * 128:(ti + 1) * 128, :]),
                    deps=[p.tok("bfe"), p.tok("lns")], dma="dl3")
                t_z = eln(xol[:], zf[:], 0,
                          dep=[lxo, p.tok("evT"), p.tok("bfq")], bn=0)
                q1 = p.op("A", lambda e: e.activation(s0[:], zf[:], AF.Abs),
                          deps=[t_z], inc="bfq")
                q2 = p.op("V", lambda e: e.reduce_max(stat[:, 12:13], s0[:],
                                                      axis=AX.X),
                          deps=[q1], inc="bfq")
                q3 = p.op("A", lambda e: e.activation(stat[:, 13:14],
                                                      stat[:, 12:13], AF.Copy,
                                                      scale=1.0 / 127.0),
                          deps=[q2], inc="bfq")
                q4 = p.op("V", lambda e: e.tensor_scalar_max(
                    stat[:, 14:15], stat[:, 13:14], 1e-5 / 127.0),
                    deps=[q3], inc="bfq")
                q5 = p.op("A", lambda e: e.activation(stat[:, 15:16],
                                                      stat[:, 14:15], AF.Copy),
                          deps=[q4], inc="bfq")
                q6 = p.op("V", lambda e: e.reciprocal(stat[:, 16:17],
                                                      stat[:, 15:16]),
                          deps=[q5], inc="bfq")
                q7 = p.op("A", lambda e: e.activation(stat[:, 17:18],
                                                      stat[:, 16:17], AF.Copy),
                          deps=[q6], inc="bfq")
                se1 = p.op("V", lambda e: e.tensor_scalar_mul(
                    stat[:, 18:19], stat[:, 15:16], wsc[:, 0:1]),
                    deps=[q5, p.tok("f1e")], inc="bfq")
                v1 = p.op("V", lambda e: e.tensor_scalar(
                    s1[:], zf[:], stat[:, 17:18], MAGIC, ALU.mult, ALU.add),
                    deps=[q7, p.tok("evT")], inc="bfq")
                p.op("V", lambda e: e.tensor_scalar(
                    s1[:], s1[:], MAGIC, 127.0, ALU.subtract, ALU.min),
                    inc="bfq")
                v3 = p.op("V", lambda e: e.tensor_scalar_max(
                    zq[:], s1[:], -128.0), inc="bfq")
                zt = []
                for dc in range(8):
                    blk = dc % 4
                    tm = p.op("T", lambda e, dc=dc, blk=blk: e.transpose(
                        psT[:, blk * 128:(blk + 1) * 128],
                        zq[:, dc * 128:(dc + 1) * 128], iden[:]),
                        deps=[v3, p.tok("xnt")], mark=True)
                    zt.append(p.op("V", lambda e, dc=dc, blk=blk: e.tensor_copy(
                        zqT[:, dc * 128:(dc + 1) * 128],
                        psT[:, blk * 128:(blk + 1) * 128]),
                        deps=[tm], inc="xnt"))
                qtoks = []
                for qk in range(4):
                    bks = (0, 1) if qk % 2 == 0 else (2, 3)
                    etoks = []
                    for bi, bank in enumerate(bks):
                        fcol0 = qk * 1024 + bi * 512
                        for jc in range(8):
                            dps = (zt + [wtokb, p.tok("f1e")]) if jc == 0 else ()
                            p.op("T", lambda e, jc=jc, bank=bank, fcol0=fcol0:
                                 e.matmul(
                                     psF[:, bank, 0:512],
                                     zqT[:, jc * 128:(jc + 1) * 128],
                                     W1[:, jc * FFN + fcol0:
                                        jc * FFN + fcol0 + 512],
                                     skip_group_check=True, start=(jc == 0), stop=(jc == 7)),
                                 deps=dps, mark=(jc == 7))
                        tmm = p.tok("evT")
                        etoks.append(p.op("A", lambda e, bank=bank, fcol0=fcol0:
                                          e.activation(
                                              hg[:, fcol0:fcol0 + 512],
                                              psF[:, bank, 0:512], AF.Gelu,
                                              scale=stat[:, 18:19]),
                                          deps=[tmm, se1, p.tok("bfq")],
                                          inc="f1e"))
                    qa = p.op("A", lambda e, qk=qk: e.activation(
                        s0[:], hg[:, qk * 1024:(qk + 1) * 1024], AF.Abs),
                        deps=etoks + [q2] + (qtoks[-1:] if qtoks else []),
                        inc="f1e")
                    qtoks.append(p.op("V", lambda e, qk=qk: e.reduce_max(
                        stat[:, 19 + qk:20 + qk], s0[:], axis=AX.X),
                        deps=[qa], inc="bfq"))
                hA = p.op("A", lambda e: e.activation(stat[:, 24:28],
                                                      stat[:, 19:23], AF.Copy),
                          deps=qtoks, inc="bfq")
                hV = p.op("V", lambda e: e.reduce_max(stat[:, 28:29],
                                                      stat[:, 24:28], axis=AX.X),
                          deps=[hA], inc="bfq")
                hB = p.op("A", lambda e: e.activation(stat[:, 29:30],
                                                      stat[:, 28:29], AF.Copy,
                                                      scale=1.0 / 127.0),
                          deps=[hV], inc="bfq")
                hC = p.op("V", lambda e: e.tensor_scalar_max(
                    stat[:, 30:31], stat[:, 29:30], 1e-5 / 127.0),
                    deps=[hB], inc="bfq")
                hD = p.op("A", lambda e: e.activation(stat[:, 31:32],
                                                      stat[:, 30:31], AF.Copy),
                          deps=[hC], inc="bfq")
                hE = p.op("V", lambda e: e.reciprocal(stat[:, 32:33],
                                                      stat[:, 31:32]),
                          deps=[hD], inc="bfq")
                hF = p.op("A", lambda e: e.activation(stat[:, 33:34],
                                                      stat[:, 32:33], AF.Copy),
                          deps=[hE], inc="bfq")
                se2 = p.op("V", lambda e: e.tensor_scalar_mul(
                    stat[:, 34:35], stat[:, 31:32], wsc[:, 2:3]),
                    deps=[hD], inc="bfq")
                for qk in range(4):
                    w1q = p.op("V", lambda e, qk=qk: e.tensor_scalar(
                        s1[:], hg[:, qk * 1024:(qk + 1) * 1024],
                        stat[:, 33:34], MAGIC, ALU.mult, ALU.add),
                        deps=[hF, p.tok("evT")], inc="bfq")
                    p.op("V", lambda e: e.tensor_scalar(
                        s1[:], s1[:], MAGIC, 127.0, ALU.subtract, ALU.min),
                        inc="bfq")
                    w3q = p.op("V", lambda e: e.tensor_scalar_max(
                        z2q[:], s1[:], -128.0), inc="bfq")
                    z2toks = []
                    for dc in range(8):
                        blk = dc % 4
                        tm = p.op("T", lambda e, dc=dc, blk=blk: e.transpose(
                            psT[:, blk * 128:(blk + 1) * 128],
                            z2q[:, dc * 128:(dc + 1) * 128], iden[:]),
                            deps=[w3q, p.tok("xnt")], mark=True)
                        z2toks.append(p.op("V", lambda e, dc=dc, blk=blk:
                                           e.tensor_copy(
                                               z2T[:, dc * 128:(dc + 1) * 128],
                                               psT[:, blk * 128:(blk + 1) * 128]),
                                           deps=[tm], inc="xnt"))
                    for half in range(2):
                        for dci in range(8):
                            fc = qk * 8 + dci
                            dps = (z2toks + [p.tok("bfe")]) if fc == 0 else \
                                (z2toks if dci == 0 else ())
                            p.op("T", lambda e, half=half, fc=fc, dci=dci:
                                 e.matmul(
                                     psF[:, 4 + half, 0:512],
                                     z2T[:, dci * 128:(dci + 1) * 128],
                                     W2[:, fc * D + half * 512:
                                        fc * D + half * 512 + 512],
                                     skip_group_check=True, start=(fc == 0), stop=(fc == 31)),
                                 deps=dps, mark=(fc == 31 and half == 1))
                t2w = p.tok("evT")
                ev = []
                for half in range(2):
                    e1 = p.op("V", lambda e, half=half: e.tensor_scalar_mul(
                        s1[:, half * 512:(half + 1) * 512],
                        psF[:, 4 + half, 0:512], stat[:, 34:35]),
                        deps=[t2w, se2, lxo, p.tok("dso2")], inc="bfe")
                    ev.append(p.op("V", lambda e, half=half: e.tensor_tensor(
                        xsf[:, half * 512:(half + 1) * 512],
                        s1[:, half * 512:(half + 1) * 512],
                        xol[:, half * 512:(half + 1) * 512], ALU.add),
                        inc="bfe"))
                if tap_xs is not None:
                    p.op("S", lambda e, b=b, ti=ti: e.dma_start(
                        out=taps[tapname][b, ti], in_=xsf[:]),
                        deps=ev, dma="dtap")
                p.op("S", lambda e, b=b, ti=ti: e.dma_start(
                    out=xout[b, ti * 128:(ti + 1) * 128, :], in_=xsf[:]),
                    deps=ev + [p.tok("dtap")], dma="dso2")
        pctx.close()

    def phase4():
        pctx = ExitStack()

        def pbuf(name, shape, dtype=F32):
            return pctx.enter_context(nc.sbuf_tensor("e_" + name, list(shape), dtype))

        WqT = pbuf("WqT", (128, 8 * D), BF16)
        WkT = pbuf("WkT", (128, 8 * D), BF16)
        WvT = pbuf("WvT", (128, 8 * D), BF16)
        for W, nm in [(WqT, "bWq"), (WkT, "bWk"), (WvT, "bWv")]:
            for jc in range(8):
                p.op("S", lambda e, W=W, nm=nm, jc=jc: e.dma_start(
                    out=W[:, jc * D:(jc + 1) * D],
                    in_=bWT[nm][jc * 128:(jc + 1) * 128, :]), dma="dw4")
        wtok4 = p.tok("dw4")
        xol = pbuf("xol", (128, 1024), F32)
        zb = pbuf("zb", (128, 1024), BF16)
        zbT = pbuf("zbT", (128, 1024), BF16)
        QTt = pbuf("QTt", (128, 1024), BF16)
        KTt = pbuf("KTt", (128, 1024), BF16)
        Vt = pbuf("Vt", (128, 1024), BF16)
        stat = pbuf("stat", (128, 12), F32)
        s0 = pbuf("s0", (128, 1024), F32)
        eln = make_emit_ln(stat, s0, s0, s0, s0)

        cut = globals().get("P4_CUT", 99)
        for ti in range(n_tiles):
            for b in range(B):
                lxo = p.op("S", lambda e, b=b, ti=ti: e.dma_start(
                    out=xol[:], in_=xout[b, ti * 128:(ti + 1) * 128, :]),
                    deps=[p.tok("lns")], dma="dl4")
                t_z = eln(xol[:], zb[:], 0, dep=[lxo, p.tok("evT")], bn=0)
                if cut < 1:
                    continue
                zt = []
                for dc in range(8):
                    blk = dc % 4
                    tm = p.op("T", lambda e, dc=dc, blk=blk: e.transpose(
                        psT[:, blk * 128:(blk + 1) * 128],
                        zb[:, dc * 128:(dc + 1) * 128], iden[:]),
                        deps=[t_z, p.tok("xnt")], mark=True)
                    zt.append(p.op("V", lambda e, dc=dc, blk=blk: e.tensor_copy(
                        zbT[:, dc * 128:(dc + 1) * 128],
                        psT[:, blk * 128:(blk + 1) * 128]),
                        deps=[tm], inc="xnt"))
                if cut < 2:
                    continue
                ev4 = []
                for (W, dstT, bcol, bank0) in [(WqT, QTt, bqcol, 0),
                                               (WkT, KTt, bkcol, 2)]:
                    for ic in range(8):
                        bank = bank0 + ic // 4
                        col = (ic % 4) * 128
                        for jc in range(8):
                            dps = (zt + [wtok4, p.tok("q4e")]) \
                                if jc == 0 else ()
                            p.op("T", lambda e, W=W, ic=ic, jc=jc, bank=bank,
                                 col=col: e.matmul(
                                     psF[:, bank, col:col + 128],
                                     W[:, jc * D + ic * 128:
                                       jc * D + ic * 128 + 128],
                                     zbT[:, jc * 128:(jc + 1) * 128],
                                     skip_group_check=True, start=(jc == 0), stop=(jc == 7)),
                                 deps=dps, mark=(jc == 7))
                        tmm = p.tok("evT")
                        ev4.append(p.op("V", lambda e, dstT=dstT, bcol=bcol,
                                        ic=ic, bank=bank, col=col:
                                        e.tensor_scalar_add(
                                            dstT[:, ic * 128:(ic + 1) * 128],
                                            psF[:, bank, col:col + 128],
                                            bcol[:, ic:ic + 1]),
                                        deps=[tmm, p.tok("ds4")], inc="q4e"))
                if cut < 3:
                    continue
                for half in range(2):
                    for jc in range(8):
                        dps = (zt + [p.tok("q4e")]) if jc == 0 else ()
                        p.op("T", lambda e, half=half, jc=jc: e.matmul(
                            psF[:, 4 + half, 0:512],
                            zbT[:, jc * 128:(jc + 1) * 128],
                            WvT[:, jc * D + half * 512:
                                jc * D + half * 512 + 512],
                            skip_group_check=True, start=(jc == 0), stop=False), deps=dps)
                    p.op("T", lambda e, half=half: e.matmul(
                        psF[:, 4 + half, 0:512], ones_row[0:1, :],
                        rows["bbv"][0:1, half * 512: half * 512 + 512],
                        skip_group_check=True, start=False, stop=True), mark=True)
                tvm = p.tok("evT")
                ev4.append(p.op("V", lambda e: e.tensor_copy(
                    Vt[:, 0:512], psF[:, 4, 0:512]),
                    deps=[tvm, p.tok("ds4")], inc="q4e"))
                ev4.append(p.op("V", lambda e: e.tensor_copy(
                    Vt[:, 512:1024], psF[:, 5, 0:512]),
                    deps=[tvm], inc="q4e"))
                if cut < 4:
                    continue
                p.op("G", lambda e, b=b, ti=ti: e.dma_start(
                    out=qbS[b, ti], in_=QTt[:]), deps=ev4, dma="ds4")
                for ic in range(8):
                    p.op("G", lambda e, b=b, ti=ti, ic=ic: e.dma_start(
                        out=kbS[b, ic, :, ti * 128:(ti + 1) * 128],
                        in_=KTt[:, ic * 128:(ic + 1) * 128]), dma="ds4")
                p.op("G", lambda e, b=b, ti=ti: e.dma_start(
                    out=vbS[b, ti * 128:(ti + 1) * 128, :], in_=Vt[:]),
                    dma="ds4")
        pctx.close()

    def phase5():
        pctx = ExitStack()

        def pbuf(name, shape, dtype=F32):
            return pctx.enter_context(nc.sbuf_tensor("f_" + name, list(shape), dtype))

        Kpl = pbuf("Kpl", (128, 8 * T), BF16)
        Vpl = pbuf("Vpl", (128, NT * 1024), BF16)
        WoT = pbuf("WoT", (128, 8 * D), BF16)
        for jc in range(8):
            p.op("S", lambda e, jc=jc: e.dma_start(
                out=WoT[:, jc * D:(jc + 1) * D],
                in_=bWT["bWo"][jc * 128:(jc + 1) * 128, :]), dma="dw5")
        wtok5 = p.tok("dw5")
        Qt = pbuf("Qt", (128, 1024), BF16)
        Pex = pbuf("Pex", (128, 2048), BF16)
        PnT = pbuf("PnT", (128, 2048), BF16)
        avT = pbuf("avT", (128, 1024), BF16)
        xol = pbuf("xol", (128, 1024), F32)
        xsf = pbuf("xsf", (128, 1024), F32)
        stat = pbuf("stat", (128, 16), F32)
        tap_xs3 = tap("xs3", (B, NT, 128, D), F32)
        RS = 1.0 / math.sqrt(512.0)
        Ts = n_tiles * 128
        nsb = (Ts + 511) // 512

        for b in range(B):
            kvdeps = [p.tok("evT")]
            for ic in range(8):
                p.op("S", lambda e, b=b, ic=ic: e.dma_start(
                    out=Kpl[:, ic * T:ic * T + Ts], in_=kbS[b, ic, :, 0:Ts]),
                    deps=kvdeps if ic == 0 else (), dma="dkv")
            for st in range(n_tiles):
                p.op("S", lambda e, b=b, st=st: e.dma_start(
                    out=Vpl[:, st * 1024:(st + 1) * 1024],
                    in_=vbS[b, st * 128:(st + 1) * 128, :]), dma="dkv")
            kvtok = p.tok("dkv")
            for ti in range(n_tiles):
                lq = p.op("S", lambda e, b=b, ti=ti: e.dma_start(
                    out=Qt[:], in_=qbS[b, ti]),
                    deps=[p.tok("evT")], dma="dq5")
                lxo = p.op("S", lambda e, b=b, ti=ti: e.dma_start(
                    out=xol[:], in_=xout[b, ti * 128:(ti + 1) * 128, :]),
                    deps=[p.tok("x5e")], dma="dq5")
                for sb in range(nsb):
                    Nn = min(512, Ts - sb * 512)
                    for ic in range(8):
                        dps = ([lq, kvtok, p.tok("sfe")]
                               if (sb == 0 and ic == 0) else ())
                        p.op("T", lambda e, sb=sb, ic=ic, Nn=Nn: e.matmul(
                            psF[:, sb, 0:Nn],
                            Qt[:, ic * 128:(ic + 1) * 128],
                            Kpl[:, ic * T + sb * 512: ic * T + sb * 512 + Nn],
                            skip_group_check=True, start=(ic == 0), stop=(ic == 7)),
                            deps=dps, mark=(ic == 7))
                smk = p.tok("evT")
                mt = []
                for sb in range(nsb):
                    Nn = min(512, Ts - sb * 512)
                    mt.append(p.op("V", lambda e, sb=sb, Nn=Nn: e.reduce_max(
                        stat[:, sb:sb + 1], psF[:, sb, 0:Nn], axis=AX.X),
                        deps=[smk] if sb == 0 else (), inc="sfq"))
                mA = p.op("A", lambda e: e.activation(stat[:, 4:4 + nsb],
                                                      stat[:, 0:nsb], AF.Copy),
                          deps=mt, inc="sfq")
                mV = p.op("V", lambda e: e.reduce_max(stat[:, 8:9],
                                                      stat[:, 4:4 + nsb],
                                                      axis=AX.X),
                          deps=[mA], inc="sfq")
                mN = p.op("A", lambda e: e.activation(stat[:, 9:10],
                                                      stat[:, 8:9], AF.Copy,
                                                      scale=-RS),
                          deps=[mV], inc="sfq")
                mH = p.op("V", lambda e: e.tensor_copy(stat[:, 10:11],
                                                       stat[:, 9:10]),
                          deps=[mN], inc="sfq")
                ext = []
                for sb in range(nsb):
                    Nn = min(512, Ts - sb * 512)
                    ext.append(p.op("A", lambda e, sb=sb, Nn=Nn: e.activation(
                        Pex[:, sb * 512:sb * 512 + Nn], psF[:, sb, 0:Nn],
                        AF.Exp, scale=RS, bias=stat[:, 10:11]),
                        deps=[mH, p.tok("sfq")] if sb == 0 else (), inc="sfe"))
                r1 = p.op("V", lambda e: e.reduce_sum(stat[:, 11:12],
                                                      Pex[:, 0:Ts], axis=AX.X),
                          deps=ext, inc="sfq")
                r2 = p.op("A", lambda e: e.activation(stat[:, 12:13],
                                                      stat[:, 11:12], AF.Copy),
                          deps=[r1], inc="sfq")
                r3 = p.op("V", lambda e: e.reciprocal(stat[:, 13:14],
                                                      stat[:, 12:13]),
                          deps=[r2], inc="sfq")
                r4 = p.op("A", lambda e: e.activation(stat[:, 14:15],
                                                      stat[:, 13:14], AF.Copy),
                          deps=[r3], inc="sfq")
                nrm = p.op("V", lambda e: e.tensor_scalar_mul(
                    Pex[:, 0:Ts], Pex[:, 0:Ts], stat[:, 14:15]),
                    deps=[r4], inc="sfq")
                ptoks = []
                for st in range(n_tiles):
                    blk = st % 4
                    tm = p.op("T", lambda e, st=st, blk=blk: e.transpose(
                        psT[:, blk * 128:(blk + 1) * 128],
                        Pex[:, st * 128:(st + 1) * 128], iden[:]),
                        deps=[nrm, p.tok("xnt")], mark=True)
                    ptoks.append(p.op("V", lambda e, st=st, blk=blk:
                                      e.tensor_copy(
                                          PnT[:, st * 128:(st + 1) * 128],
                                          psT[:, blk * 128:(blk + 1) * 128]),
                                      deps=[tm], inc="xnt"))
                for vc in range(8):
                    bank = 4 + vc // 4
                    col = (vc % 4) * 128
                    for st in range(n_tiles):
                        dps = (ptoks + [p.tok("ave")]) \
                            if (vc == 0 and st == 0) else ()
                        p.op("T", lambda e, vc=vc, st=st, bank=bank, col=col:
                             e.matmul(
                                 psF[:, bank, col:col + 128],
                                 Vpl[:, st * 1024 + vc * 128:
                                     st * 1024 + vc * 128 + 128],
                                 PnT[:, st * 128:(st + 1) * 128],
                                 skip_group_check=True, start=(st == 0),
                                 stop=(st == n_tiles - 1)),
                             deps=dps, mark=(st == n_tiles - 1))
                avk = p.tok("evT")
                avt = []
                for vc in range(8):
                    bank = 4 + vc // 4
                    col = (vc % 4) * 128
                    avt.append(p.op("V", lambda e, vc=vc, bank=bank, col=col:
                                    e.tensor_copy(
                                        avT[:, vc * 128:(vc + 1) * 128],
                                        psF[:, bank, col:col + 128]),
                                    deps=[avk] if vc == 0 else (), inc="ave"))
                for half in range(2):
                    for vc in range(8):
                        dps = (avt + ext + [wtok5, p.tok("x5e")]) \
                            if (half == 0 and vc == 0) else ()
                        p.op("T", lambda e, half=half, vc=vc: e.matmul(
                            psF[:, half, 0:512],
                            avT[:, vc * 128:(vc + 1) * 128],
                            WoT[:, vc * D + half * 512:
                                vc * D + half * 512 + 512],
                            skip_group_check=True, start=(vc == 0), stop=False), deps=dps)
                    p.op("T", lambda e, half=half: e.matmul(
                        psF[:, half, 0:512], ones_row[0:1, :],
                        rows["bbo"][0:1, half * 512: half * 512 + 512],
                        skip_group_check=True, start=False, stop=True), mark=True)
                t5w = p.tok("evT")
                ex1 = p.op("V", lambda e: e.tensor_tensor(
                    xsf[:, 0:512], psF[:, 0, 0:512], xol[:, 0:512], ALU.add),
                    deps=[t5w, lxo, p.tok("dso5"), p.tok("dtap")], inc="x5e")
                ex2 = p.op("V", lambda e: e.tensor_tensor(
                    xsf[:, 512:1024], psF[:, 1, 0:512], xol[:, 512:1024],
                    ALU.add), inc="x5e")
                if tap_xs3 is not None:
                    p.op("S", lambda e, b=b, ti=ti: e.dma_start(
                        out=taps["xs3"][b, ti], in_=xsf[:]),
                        deps=[ex1, ex2], dma="dtap")
                p.op("S", lambda e, b=b, ti=ti: e.dma_start(
                    out=xout[b, ti * 128:(ti + 1) * 128, :], in_=xsf[:]),
                    deps=[ex1, ex2, p.tok("dtap")], dma="dso5")
        pctx.close()

    if max_phase >= 3:
        phase0()
        p.barrier()
        phase_bitffn("xs2")
        p.barrier()
    if max_phase >= 4:
        phase4()
        p.barrier()
    if max_phase >= 5:
        phase5()
        p.barrier()
    if max_phase >= 6:
        phase_bitffn("xs4")
        p.barrier()

    p.replay()
    ctx.close()
    return nc, taps


NMAP = {
    "xin": "L0_attn_a_x",
    "aWq": "L0_attn_a_Wq", "abq": "L0_attn_a_bq", "aWk": "L0_attn_a_Wk",
    "abk": "L0_attn_a_bk", "aWv": "L0_attn_a_Wv", "abv": "L0_attn_a_bv",
    "aWo": "L0_attn_a_Wo", "abo": "L0_attn_a_bo",
    "n1g": "L0_attn_a_n1g", "n1b": "L0_attn_a_n1b",
    "n2g": "L0_attn_a_n2g", "n2b": "L0_attn_a_n2b",
    "thb": "L0_attn_a_theta_base", "tw1": "L0_attn_a_theta_w1",
    "tw2": "L0_attn_a_theta_w2", "lmb": "L0_attn_a_lambda_base",
    "eta": "L0_attn_a_eta",
    "aWf1": "L0_attn_a_Wf1", "abf1": "L0_attn_a_bf1",
    "aWf2": "L0_attn_a_Wf2", "abf2": "L0_attn_a_bf2",
    "bWq": "L0_attn_b_Wq", "bbq": "L0_attn_b_bq", "bWk": "L0_attn_b_Wk",
    "bbk": "L0_attn_b_bk", "bWv": "L0_attn_b_Wv", "bbv": "L0_attn_b_bv",
    "bWo": "L0_attn_b_Wo", "bbo": "L0_attn_b_bo",
    "fw1": "L0_ffn_w1", "fw2": "L0_ffn_w2",
}


def make_in_maps(inputs):
    x = np.ascontiguousarray(inputs[NMAP["xin"]], dtype=np.float32)
    in_maps = []
    cst = host_constants()
    for core in range(NCORES):
        m = {"xin": x[core * B:(core + 1) * B], "cst": cst}
        for k2, v2 in NMAP.items():
            if k2 == "xin":
                continue
            m[k2] = np.ascontiguousarray(inputs[v2], dtype=np.float32)
        in_maps.append(m)
    return in_maps


def kernel(**inputs):
    nc, _ = build_program()
    in_maps = make_in_maps(inputs)
    res = run_bass_kernel_spmd(nc, in_maps, core_ids=list(range(NCORES)))
    return np.concatenate([res.results[i]["out"] for i in range(NCORES)], axis=0)


# version 02
# speedup vs baseline: 1.2604x; 1.2604x over previous
import math
from contextlib import ExitStack

import numpy as np

import concourse.bass as bass
import concourse.mybir as mybir
from concourse.bass_utils import run_bass_kernel_spmd

F32 = mybir.dt.float32
BF16 = mybir.dt.bfloat16
I8 = mybir.dt.int8
AF = mybir.ActivationFunctionType
ALU = mybir.AluOpType
AX = mybir.AxisListType

B_TOT, T, D = 16, 2048, 1024
NCORES = 8
B = 2
H, KD, BK = 16, 64, 32
FFN = 4096
LORA = 48
NT = 16
ST = 2
NST = NT // ST
ALPHA = 3.5
PI = math.pi
MAGIC = 12582912.0
EPS_OUT = 1e-5
EPS_IN = 1e-6
CST_COLS = 3200


def host_constants():
    c = np.zeros((128, CST_COLS), np.float32)
    pidx = np.arange(128)
    c[:, 0:128] = np.eye(128, dtype=np.float32)
    c[:, 128:1152] = (ALPHA * (np.arange(1024) % 16))[None, :]
    t = pidx
    d = t[None, :] - pidx[:, None]
    blk_t, blk_p = t[None, :] // 16, (pidx // 16)[:, None]
    md = ((blk_t == blk_p) & (d >= 0)) * np.exp(-ALPHA * np.maximum(d, 0))
    c[:, 1152:1664] = np.tile(md, (1, 4))
    m16 = (blk_t == blk_p + 1) & ((pidx // 16 % 2 == 0))[:, None]
    c[:, 1664:2176] = np.tile(m16, (1, 4))
    m32 = ((t // 32)[None, :] == (pidx // 32)[:, None] + 1) & \
        ((pidx // 32 % 2 == 0))[:, None]
    c[:, 2176:2688] = np.tile(m32, (1, 4))
    m64 = (t[None, :] >= 64) & (pidx[:, None] < 64)
    c[:, 2688:3200] = np.tile(m64, (1, 4))
    return c

ENGS = ("T", "V", "A", "G", "S")


class Prog:

    def __init__(self, nc):
        self.nc = nc
        self.q = {k: [] for k in ENGS}
        self.sems = {}
        self.members = {}
        self.waited = {k: {} for k in ENGS}
        for k in ENGS:
            self.sem("ev" + k)
        self.nbar = 0
        self.sem("bar")

    def sem(self, name):
        if name not in self.sems:
            h = self.nc.alloc_semaphore(name=name)
            self.sems[name] = [h, 0]
        return name

    def tok(self, name):
        out = []
        if name in self.sems and self.sems[name][1] > 0:
            out.append((name, self.sems[name][1]))
        for nm in sorted(self.members.get(name, ())):
            if self.sems[nm][1] > 0:
                out.append((nm, self.sems[nm][1]))
        return tuple(out)

    def wait(self, key, *toks):
        for tk in toks:
            if tk is None:
                continue
            if isinstance(tk, (list, tuple)) and (len(tk) == 0
                                                  or not isinstance(tk[0], str)):
                self.wait(key, *tk)
                continue
            (sname, val) = tk
            if val <= 0:
                continue
            w = self.waited[key]
            if w.get(sname, -1) >= val:
                continue
            w[sname] = val
            h = self.sems[sname][0]
            self.q[key].append(lambda e, h=h, v=val: e.wait_ge(h, v))

    def op(self, key, fn, deps=(), inc=None, dma=None, mark=False):
        self.wait(key, *deps)
        incs = []
        if inc is not None:
            nm = inc + "." + key
            self.sem(nm)
            self.sems[nm][1] += 1
            self.members.setdefault(inc, set()).add(nm)
            incs.append((nm, 1))
        if dma is not None:
            nm = dma + "." + key
            self.sem(nm)
            self.sems[nm][1] += 16
            self.members.setdefault(dma, set()).add(nm)
            incs.append((nm, 16))
        if mark:
            en = "ev" + key
            self.sems[en][1] += 1
            incs.append((en, 1))
        toks = [(n, self.sems[n][1]) for n, _ in incs]

        def runner(e, fn=fn, incs=tuple(incs)):
            inst = fn(e)
            for (n, kk) in incs:
                inst.then_inc(self.sems[n][0], kk)
            return inst

        self.q[key].append(runner)
        return toks[0] if len(toks) == 1 else (toks if toks else None)

    def barrier(self):
        self.nbar += 1
        cur = [(n, s[1]) for n, s in self.sems.items() if s[1] > 0 and n != "bar"]
        bh = self.sems["bar"][0]
        self.sems["bar"][1] += len(ENGS)
        tgt = self.nbar * len(ENGS)
        for key in ENGS:
            self.wait(key, *cur)
            self.q[key].append(lambda e, bh=bh: e.sem_inc(bh, 1))
            self.q[key].append(lambda e, bh=bh, t=tgt: e.wait_ge(bh, t))
            self.waited[key] = {n: v for n, v in cur}
            self.waited[key]["bar"] = tgt

    def replay(self):
        with self.nc.Block() as block:
            @block.tensor
            def _(e):
                for f in self.q["T"]:
                    f(e)

            @block.vector
            def _(e):
                for f in self.q["V"]:
                    f(e)

            @block.scalar
            def _(e):
                for f in self.q["A"]:
                    f(e)

            @block.gpsimd
            def _(e):
                for f in self.q["G"]:
                    f(e)

            @block.sync
            def _(e):
                for f in self.q["S"]:
                    f(e)


def APx(t, off, pat):
    return bass.AP(t, off, pat)


def build_program(debug_taps=(), n_tiles=NT, max_phase=99):
    nc = bass.Bass("TRN2", target_bir_lowering=False, detect_race_conditions=False)
    p = Prog(nc)
    ctx = ExitStack()

    dram = {}

    def din(name, shape):
        dram[name] = nc.dram_tensor(name, list(shape), F32, kind="ExternalInput").ap()
        return dram[name]

    xin = din("xin", (B, T, D))
    for nm, shp in [
        ("aWq", (D, D)), ("abq", (D,)), ("aWk", (D, D)), ("abk", (D,)),
        ("aWv", (D, D)), ("abv", (D,)), ("aWo", (D, D)), ("abo", (D,)),
        ("n1g", (D,)), ("n1b", (D,)), ("n2g", (D,)), ("n2b", (D,)),
        ("thb", (H, BK)), ("tw1", (D, LORA)), ("tw2", (LORA, H * BK)),
        ("lmb", (H, BK)), ("eta", (H, BK)),
        ("aWf1", (D, FFN)), ("abf1", (FFN,)), ("aWf2", (FFN, D)), ("abf2", (D,)),
        ("bWq", (D, D)), ("bbq", (D,)), ("bWk", (D, D)), ("bbk", (D,)),
        ("bWv", (D, D)), ("bbv", (D,)), ("bWo", (D, D)), ("bbo", (D,)),
        ("fw1", (FFN, D)), ("fw2", (D, FFN)),
    ]:
        din(nm, shp)

    xout = nc.dram_tensor("out", [B, T, D], F32, kind="ExternalOutput").ap()

    taps = {}

    def scratch(name, shape, dtype):
        kind = "ExternalOutput" if name in debug_taps else "Internal"
        ap = nc.dram_tensor("tap_" + name if kind == "ExternalOutput" else name,
                            list(shape), dtype, kind=kind).ap()
        if kind == "ExternalOutput":
            taps[name] = ap
        return ap

    x1s = scratch("x1s", (B, T, D), F32)
    h1S = scratch("h1S", (B, T, D), BF16)
    qrS = scratch("qrS", (B, NT, 128, 512), BF16)
    qiS = scratch("qiS", (B, NT, 128, 512), BF16)
    prS = scratch("prS", (B, NT, 128, 512), BF16)
    piS = scratch("piS", (B, NT, 128, 512), BF16)
    thS = scratch("thS", (B, NT, 128, 512), F32)
    nlS = scratch("nlS", (B, NT, 128, 512), F32)
    vS = scratch("vS", (B, T, D), BF16)
    w1tq = scratch("w1tq", (D, FFN), BF16)
    w2tq = scratch("w2tq", (FFN, D), BF16)
    bWT = {nm: scratch(nm + "T", (D, D), BF16)
           for nm in ("bWq", "bWk", "bWv", "bWo")}
    qbS = scratch("qbS", (B, NT, 128, D), BF16)
    kbS = scratch("kbS", (B, 8, 128, T), BF16)
    vbS = scratch("vbS", (B, T, D), BF16)

    def tap(name, shape, dtype=F32):
        if name in debug_taps:
            taps[name] = nc.dram_tensor("tap_" + name, list(shape), dtype,
                                        kind="ExternalOutput").ap()
            return taps[name]
        return None

    sbufs = {}

    def sbuf(name, shape, dtype=F32):
        h = ctx.enter_context(nc.sbuf_tensor(name, list(shape), dtype))
        sbufs[name] = h
        return h

    ones_row = sbuf("ones_row", (1, 128), BF16)
    iden = sbuf("iden", (128, 128), BF16)
    alphat = sbuf("alphat", (128, 1024), F32)
    mask_diag = sbuf("mask_diag", (128, 512), F32)
    mask_16 = sbuf("mask_16", (128, 512), BF16)
    mask_32 = sbuf("mask_32", (128, 512), BF16)
    mask_64 = sbuf("mask_64", (128, 512), BF16)

    thb_t = sbuf("thb_t", (128, 4), F32)
    lmbn_t = sbuf("lmbn_t", (128, 4), F32)
    etan_t = sbuf("etan_t", (128, 4), F32)
    bq_r = sbuf("bq_r", (128, 4), F32)
    bq_i = sbuf("bq_i", (128, 4), F32)
    bk_r = sbuf("bk_r", (128, 4), F32)
    bk_i = sbuf("bk_i", (128, 4), F32)

    g1t = sbuf("g1t", (128, 1024), BF16)
    b1t = sbuf("b1t", (128, 1024), BF16)
    g2t = sbuf("g2t", (128, 1024), BF16)
    b2t = sbuf("b2t", (128, 1024), BF16)
    rows = {}
    for nm in ["abv", "abo", "abf2", "bbq", "bbk", "bbv", "bbo"]:
        rows[nm] = sbuf("row_" + nm, (1, D), BF16)
    grow = sbuf("grow", (1, 4 * D), BF16)
    bf1c = sbuf("bf1c", (128, 32), F32)
    pihalf = sbuf("pihalf", (128, 1), F32)
    epsc = sbuf("epsc", (128, 2), F32)
    magicc = sbuf("magicc", (128, 1), F32)
    onec = sbuf("onec", (128, 1), F32)
    wsc = sbuf("wsc", (128, 4), F32)
    onef = sbuf("onef", (1, 128), F32)
    bqcol = sbuf("bqcol", (128, 8), F32)
    bkcol = sbuf("bkcol", (128, 8), F32)

    psF = ctx.enter_context(nc.psum_tensor("psF", [128, 6, 512], F32))
    psT = ctx.enter_context(nc.psum_tensor("psT", [128, 512], BF16))
    psX = ctx.enter_context(nc.psum_tensor("psX", [128, 512], F32))

    cstT = nc.dram_tensor("cst", [128, CST_COLS], F32, kind="ExternalInput")
    dram["cst"] = cstT.ap()

    def consts():
        def csl(a, b):
            return APx(cstT, a, [[CST_COLS, 128], [1, b - a]])
        p.op("S", lambda e: e.dma_start(out=alphat[:], in_=csl(128, 1152)),
             dma="dcs2")
        p.op("S", lambda e: e.dma_start(out=mask_diag[:], in_=csl(1152, 1664)),
             dma="dcs2")
        p.op("G", lambda e: e.dma_start(out=iden[:], in_=csl(0, 128)), dma="dcst")
        p.op("G", lambda e: e.dma_start(out=mask_16[:], in_=csl(1664, 2176)),
             dma="dcst")
        p.op("G", lambda e: e.dma_start(out=mask_32[:], in_=csl(2176, 2688)),
             dma="dcst")
        p.op("G", lambda e: e.dma_start(out=mask_64[:], in_=csl(2688, 3200)),
             dma="dcst")
        p.op("V", lambda e: e.memset(ones_row[:], 1.0), mark=True)
        p.op("V", lambda e: e.memset(pihalf[:], PI / 2), mark=True)
        p.op("V", lambda e: e.memset(epsc[:, 0:1], EPS_OUT), mark=True)
        p.op("V", lambda e: e.memset(epsc[:, 1:2], EPS_IN), mark=True)
        p.op("V", lambda e: e.memset(magicc[:], MAGIC), mark=True)
        p.op("V", lambda e: e.memset(onec[:], 1.0), mark=True)
        p.op("V", lambda e: e.memset(onef[:], 1.0), mark=True)

        gdmas = []
        for c in range(4):
            gdmas.append(lambda e, c=c: e.dma_start(
                out=APx(thb_t, c, [[4, 128], [1, 1]]),
                in_=APx(dram["thb"].tensor, 128 * c, [[1, 128], [1, 1]])))
            gdmas.append(lambda e, c=c: e.dma_start(
                out=APx(lmbn_t, c, [[4, 128], [1, 1]]),
                in_=APx(dram["lmb"].tensor, 128 * c, [[1, 128], [1, 1]])))
            gdmas.append(lambda e, c=c: e.dma_start(
                out=APx(etan_t, c, [[4, 128], [1, 1]]),
                in_=APx(dram["eta"].tensor, 128 * c, [[1, 128], [1, 1]])))
            for dst, srcn, par in [(bq_r, "abq", 0), (bq_i, "abq", 1),
                                   (bk_r, "abk", 0), (bk_i, "abk", 1)]:
                gdmas.append(lambda e, c=c, dst=dst, srcn=srcn, par=par: e.dma_start(
                    out=APx(dst, c, [[4, 128], [1, 1]]),
                    in_=APx(dram[srcn].tensor, 256 * c + par,
                            [[64, 4], [2, 32], [1, 1]]),
                    allow_slow_non_contiguous=True))
        for nm in ["abv", "abo", "abf2", "bbq", "bbk", "bbv", "bbo"]:
            gdmas.append(lambda e, nm=nm: e.dma_start(
                out=rows[nm][0:1, :], in_=APx(dram[nm].tensor, 0, [[0, 1], [1, D]])))
        for ic in range(8):
            for dst, srcn in [(bqcol, "bbq"), (bkcol, "bbk")]:
                gdmas.append(lambda e, ic=ic, dst=dst, srcn=srcn: e.dma_start(
                    out=APx(dst, ic, [[8, 128], [1, 1]]),
                    in_=APx(dram[srcn].tensor, ic * 128, [[1, 128], [1, 1]])))
        for i, nm in enumerate(["n1g", "n1b", "n2g", "n2b"]):
            gdmas.append(lambda e, i=i, nm=nm: e.dma_start(
                out=grow[0:1, i * D:(i + 1) * D],
                in_=APx(dram[nm].tensor, 0, [[0, 1], [1, D]])))
        gdmas.append(lambda e: e.dma_start(
            out=APx(bf1c, 0, [[32, 128], [1, 32]]),
            in_=APx(dram["abf1"].tensor, 0, [[1, 128], [128, 32], [1, 1]]),
            allow_slow_non_contiguous=True))
        for fn in gdmas:
            p.op("G", fn, dma="dgth")

        def neg(e):
            e.tensor_scalar_mul(lmbn_t[:], lmbn_t[:], -1.0)
            return e.tensor_scalar_mul(etan_t[:], etan_t[:], -1.0)
        p.op("V", neg, deps=[p.tok("dgth")], mark=True)

        dep = [p.tok("dgth"), p.tok("evV")]
        for i, dst in enumerate([g1t, b1t, g2t, b2t]):
            for half in range(2):
                bk_ = (2 * i + half) % 4
                tm = p.op("T", lambda e, i=i, half=half, bk_=bk_: e.matmul(
                    psF[:, bk_, :], ones_row[0:1, :],
                    grow[0:1, i * D + half * 512: i * D + half * 512 + 512],
                    skip_group_check=True, start=True, stop=True), deps=dep + [p.tok("evV")], mark=True)
                p.op("V", lambda e, dst=dst, half=half, bk_=bk_: e.tensor_copy(
                    dst[:, half * 512:(half + 1) * 512], psF[:, bk_, :]),
                    deps=[tm], mark=True)
        p.barrier()

    consts()

    def make_emit_ln(stat, s0, s1, s2, s3):
        def emit_ln(src_ap, dst_ap, eps_col, gain=None, bias=None, dep=(), bn=0):
            c0, c1, c2, c3, c4, c5 = (bn + j for j in range(6))
            t1 = p.op("V", lambda e: e.reduce_sum(stat[:, c0:c0 + 1], src_ap, axis=AX.X),
                      deps=dep, inc="lns")
            t2 = p.op("A", lambda e: e.activation(s0[:], src_ap, AF.Square),
                      deps=dep, inc="lns")
            t3 = p.op("V", lambda e: e.reduce_sum(stat[:, c1:c1 + 1], s0[:], axis=AX.X),
                      deps=[t2], inc="lns")
            t4a = p.op("A", lambda e: e.activation(stat[:, c2:c2 + 1], stat[:, c0:c0 + 1],
                                                   AF.Square, scale=1.0 / 1024.0),
                       deps=[t1], inc="lns")
            t4b = p.op("A", lambda e: e.activation(stat[:, c3:c3 + 1], stat[:, c0:c0 + 1],
                                                   AF.Copy, scale=1.0 / 1024.0),
                       deps=[t1], inc="lns")
            t4d = p.op("A", lambda e: e.activation(stat[:, c4:c4 + 1], stat[:, c1:c1 + 1],
                                                   AF.Copy, scale=1.0 / 1024.0),
                       deps=[t3], inc="lns")
            t4c = p.op("G", lambda e: e.tensor_tensor(
                stat[:, c1:c1 + 1], stat[:, c4:c4 + 1], stat[:, c2:c2 + 1],
                ALU.subtract), deps=[t4a, t4d], inc="lns")
            t5a = p.op("A", lambda e: e.activation(stat[:, c2:c2 + 1], stat[:, c1:c1 + 1],
                                                   AF.Sqrt, bias=epsc[:, eps_col:eps_col + 1]),
                       deps=[t4c], inc="lns")
            t5b = p.op("V", lambda e: e.reciprocal(stat[:, c5:c5 + 1], stat[:, c2:c2 + 1]),
                       deps=[t5a], inc="lns")
            t5 = p.op("G", lambda e: e.tensor_copy(stat[:, c1:c1 + 1],
                                                   stat[:, c5:c5 + 1]),
                      deps=[t5b], inc="lns")
            if gain is None:
                return p.op("V", lambda e: e.tensor_scalar(
                    dst_ap, src_ap, stat[:, c3:c3 + 1], stat[:, c1:c1 + 1],
                    ALU.subtract, ALU.mult), deps=[t5, t4b], inc="lns")
            p.op("V", lambda e: e.tensor_scalar(
                s1[:], src_ap, stat[:, c3:c3 + 1], stat[:, c1:c1 + 1],
                ALU.subtract, ALU.mult), deps=[t5, t4b], inc="lns")
            p.op("V", lambda e: e.tensor_copy(s2[:, 0:1024], s1[:]), inc="lns")
            p.op("V", lambda e: e.tensor_mul(s3[:, 0:1024], s2[:, 0:1024], gain[:]),
                 inc="lns")
            return p.op("V", lambda e: e.tensor_add(dst_ap, s3[:, 0:1024], bias[:]),
                        inc="lns")
        return emit_ln

    if "masks" in debug_taps:
        mt = nc.dram_tensor("tap_masks", [4, 128, 512], F32, kind="ExternalOutput").ap()
        taps["masks"] = mt
        def dump_masks(e):
            e.dma_start(out=mt[0], in_=mask_diag[:])
            e.dma_start(out=mt[1], in_=mask_16[:])
            e.dma_start(out=mt[2], in_=mask_32[:])
            return e.dma_start(out=mt[3], in_=mask_64[:])
        for i in range(4):
            p.op("G", lambda e, i=i: e.dma_start(
                out=taps["masks"][i],
                in_=[mask_diag, mask_16, mask_32, mask_64][i][:]),
                deps=[p.tok("dcst"), p.tok("dcs2")], dma="dmask")
        p.barrier()

    def phase1a():
        pctx = ExitStack()

        def pbuf(name, shape, dtype=F32):
            return pctx.enter_context(nc.sbuf_tensor(name, list(shape), dtype))

        Wq = pbuf("Wq", (128, 8 * D), BF16)
        Wk = pbuf("Wk", (128, 8 * D), BF16)
        Wv = pbuf("Wv", (128, 8 * D), BF16)
        Tw1 = pbuf("Tw1", (128, 8 * LORA), BF16)
        Tw2 = pbuf("Tw2", (LORA, 512), BF16)

        for W, nm in [(Wq, "aWq"), (Wk, "aWk"), (Wv, "aWv")]:
            for dc in range(8):
                p.op("G", lambda e, W=W, nm=nm, dc=dc: e.dma_start(
                    out=W[:, dc * D:(dc + 1) * D],
                    in_=dram[nm][dc * 128:(dc + 1) * 128, :]), dma="dw1")
        for dc in range(8):
            p.op("G", lambda e, dc=dc: e.dma_start(
                out=Tw1[:, dc * LORA:(dc + 1) * LORA],
                in_=dram["tw1"][dc * 128:(dc + 1) * 128, :]), dma="dw1")
        p.op("G", lambda e: e.dma_start(out=Tw2[:], in_=dram["tw2"][:, :]), dma="dw1")
        wtok = p.tok("dw1")

        qr = pbuf("qr", (128, 2048), BF16)
        qi = pbuf("qi", (128, 2048), BF16)
        pr = pbuf("pr", (128, 2048), BF16)
        pi = pbuf("pi", (128, 2048), BF16)
        th = pbuf("th", (128, 2048), F32)
        nl = pbuf("nl", (128, 2048), F32)
        s0 = pbuf("s0", (128, 1024), F32)
        s1 = pbuf("s1", (128, 1024), F32)
        s2 = pbuf("s2", (128, 2048), BF16)
        s3 = pbuf("s3", (128, 2048), BF16)
        s4 = pbuf("s4", (128, 2048), BF16)
        xt0 = pbuf("xt0", (128, 2 * 1024), F32)
        h1f = pbuf("h1f", (128, 1024), F32)
        xn = pbuf("xn", (128, 1024), BF16)
        xnT = pbuf("xnT", (128, 4 * 1024), BF16)
        stat = pbuf("stat", (128, 12), F32)
        l1T = pbuf("l1T", (LORA, 512), BF16)
        vtmp = pbuf("vtmp", (128, 1024), BF16)

        emit_ln = make_emit_ln(stat, s0, s1, s2, s3)

        for si in range(n_tiles // ST):
            tts = [ST * si + jj for jj in range(ST)]
            xn_toks = {}
            for b in range(B):
                for j, tt in enumerate(tts):
                    slot = b * 2 + j
                    lx = p.op("S", lambda e, b=b, tt=tt: e.dma_start(
                        out=xt0[:, b * 1024:(b + 1) * 1024],
                        in_=xin[b, tt * 128:(tt + 1) * 128, :]),
                        deps=[p.tok("lns"), p.tok("phi")], dma="dx")
                    src = xt0[:, b * 1024:(b + 1) * 1024]
                    t_h1 = emit_ln(src, h1f[:], 0, dep=[lx])
                    p.op("G", lambda e, b=b, tt=tt: e.dma_start(
                        out=h1S[b, tt * 128:(tt + 1) * 128, :], in_=h1f[:]),
                        deps=[t_h1], dma="dh1")
                    t_xn = emit_ln(h1f[:], xn[:], 1, gain=g1t, bias=b1t,
                                   dep=[t_h1, p.tok("evT"), p.tok("dh1")], bn=6)
                    for r in range(2):
                        tms = []
                        for blk in range(4):
                            dc = r * 4 + blk
                            tm = p.op("T", lambda e, dc=dc, blk=blk: e.transpose(
                                psT[:, blk * 128:(blk + 1) * 128],
                                xn[:, dc * 128:(dc + 1) * 128], iden[:]),
                                deps=[t_xn, p.tok("xnt"), p.tok("evV")], mark=True)
                            tms.append(tm)
                        p.op("V", lambda e, slot=slot, r=r: e.tensor_copy(
                            xnT[:, slot * 1024 + r * 512: slot * 1024 + r * 512 + 512],
                            psT[:]), deps=tms, inc="xnt")
                    xn_toks[(b, j)] = p.tok("xnt")

            dep_all = [xn_toks[(b, j)] for b in range(B) for j in range(ST)] + [wtok]

            def mov_xnT(b, dc):
                return APx(xnT, (b * 2) * 1024 + dc * 128,
                           [[4096, 128], [1024, 2], [1, 128]])

            bank = [0]
            st_toks = []
            for b in range(B):
                for (plane, Wsb, boff, brAP) in [
                    (qr, Wq, 0, bq_r), (qi, Wq, 1, bq_i),
                    (pr, Wk, 0, bk_r), (pi, Wk, 1, bk_i),
                ]:
                    for c in range(4):
                        bk_ = bank[0] % 4
                        bank[0] += 1
                        for dc in range(8):
                            sel = APx(Wsb, dc * D + 256 * c + boff,
                                      [[8 * D, 128], [64, 4], [2, 32]])
                            p.op("T", lambda e, sel=sel, b=b, dc=dc, bk_=bk_: e.matmul(
                                psF[:, bk_, 0:256], sel, mov_xnT(b, dc),
                                skip_group_check=True, start=(dc == 0), stop=(dc == 7)),
                                deps=(dep_all + [p.tok("qkev")]) if dc == 0 else (),
                                mark=(dc == 7))
                        tm = p.tok("evT")
                        tk = p.op("V", lambda e, plane=plane, b=b, c=c, bk_=bk_, brAP=brAP:
                                  e.tensor_scalar_add(
                                      plane[:, b * 1024 + c * 256: b * 1024 + c * 256 + 256],
                                      psF[:, bk_, 0:256], brAP[:, c:c + 1]),
                                  deps=[tm], inc="qkev")

            lo_toks = []
            for b in range(B):
                for dc in range(8):
                    p.op("T", lambda e, b=b, dc=dc: e.matmul(
                        psX[0:48, 0:256], Tw1[:, dc * LORA:(dc + 1) * LORA],
                        mov_xnT(b, dc), skip_group_check=True, start=(dc == 0), stop=(dc == 7)),
                        deps=(dep_all + [p.tok("lo")]) if dc == 0 else (),
                        mark=(dc == 7))
                tm = p.tok("evT")
                tk = p.op("A", lambda e, b=b: e.activation(
                    l1T[0:48, b * 256:(b + 1) * 256], psX[0:48, 0:256], AF.Tanh),
                    deps=[tm], inc="lo")
                lo_toks.append(tk)

            th_toks = []
            for b in range(B):
                for c in range(4):
                    sel = APx(Tw2, 128 * c, [[512, LORA], [32, 4], [1, 32]])
                    p.op("T", lambda e, sel=sel, b=b: e.matmul(
                        psF[:, 4, 0:256], sel, l1T[0:48, b * 256:(b + 1) * 256],
                        skip_group_check=True, start=True, stop=True),
                        deps=lo_toks + [p.tok("thev")], mark=True)
                    tm = p.tok("evT")

                    def theta_ops(e, b=b, c=c):
                        off = b * 1024 + c * 256
                        e.tensor_scalar(th[:, off:off + 256], psF[:, 4, 0:256],
                                        thb_t[:, c:c + 1], PI / 2, ALU.add, ALU.min)
                        e.tensor_scalar_max(th[:, off:off + 256], th[:, off:off + 256],
                                            -PI / 2)
                        e.tensor_tensor(s0[:, 0:256], th[:, off:off + 256],
                                        th[:, off:off + 256], ALU.mult)
                        return e.tensor_scalar(nl[:, off:off + 256], s0[:, 0:256],
                                               etan_t[:, c:c + 1], lmbn_t[:, c:c + 1],
                                               ALU.mult, ALU.add)
                    tk = p.op("V", theta_ops, deps=[tm], inc="thev")
                    th_toks.append(tk)

            for b in range(B):
                for j, tt in enumerate(tts):
                    slot = b * 2 + j
                    for half in range(2):
                        for dc in range(8):
                            p.op("T", lambda e, slot=slot, dc=dc, half=half: e.matmul(
                                psF[:, 5, 0:512],
                                xnT[:, slot * 1024 + dc * 128: slot * 1024 + dc * 128 + 128],
                                Wv[:, dc * D + half * 512: dc * D + half * 512 + 512],
                                skip_group_check=True, start=(dc == 0), stop=False),
                                deps=(dep_all + [p.tok("vev")]) if dc == 0 else ())
                        p.op("T", lambda e, half=half: e.matmul(
                            psF[:, 5, 0:512], ones_row[0:1, :],
                            rows["abv"][0:1, half * 512: half * 512 + 512],
                            skip_group_check=True, start=False, stop=True), deps=[p.tok("evV")], mark=True)
                        tme = p.tok("evT")
                        p.op("V", lambda e, half=half: e.tensor_copy(
                            vtmp[:, half * 512: half * 512 + 512],
                            psF[:, 5, 0:512]), deps=[tme, p.tok("dvS")], inc="vev")
                    p.op("S", lambda e, b=b, tt=tt: e.dma_start(
                        out=vS[b, tt * 128:(tt + 1) * 128, :], in_=vtmp[:]),
                        deps=[p.tok("vev")], dma="dvS")

            phk = []
            for pl in (pr, pi):
                for half in range(2):
                    hs = slice(half * 1024, half * 1024 + 1024)
                    t1 = p.op("A", lambda e, pl=pl, hs=hs: e.activation(
                        s2[:, 0:1024], pl[:, hs], AF.Exp), deps=[p.tok("qkev")] + phk,
                        inc="phi")
                    t2 = p.op("V", lambda e, pl=pl, hs=hs: e.scalar_tensor_tensor(
                        s3[:, 0:1024], pl[:, hs], 1.0, s2[:, 0:1024],
                        ALU.add, ALU.subtract), deps=[t1], inc="phi")
                    t3 = p.op("V", lambda e, pl=pl, hs=hs: e.tensor_single_scalar(
                        s4[:, 0:1024], pl[:, hs], 0.0, ALU.is_gt),
                        deps=[p.tok("qkev")], inc="phi")
                    t4 = p.op("V", lambda e: e.tensor_mul(s3[:, 0:1024], s3[:, 0:1024],
                                                          s4[:, 0:1024]),
                              deps=[t2, t3], inc="phi")
                    t5 = p.op("V", lambda e, pl=pl, hs=hs: e.tensor_add(
                        pl[:, hs], s2[:, 0:1024], s3[:, 0:1024]), deps=[t4], inc="phi")
                    phk = [t5]

            def pl_slice(ten, b, j):
                return APx(ten, b * 1024 + j * 128, [[2048, 128], [256, 4], [1, 128]])

            def th_slice(ten, b, j):
                return APx(ten, b * 1024 + j * 128, [[2048, 128], [256, 4], [1, 128]])

            for b in range(B):
                for j, tt in enumerate(tts):
                    p.op("S", lambda e, b=b, j=j, tt=tt: e.dma_start(
                        out=qrS[b, tt], in_=pl_slice(qr, b, j)), deps=[p.tok("qkev")],
                        dma="dpl")
                    p.op("S", lambda e, b=b, j=j, tt=tt: e.dma_start(
                        out=qiS[b, tt], in_=pl_slice(qi, b, j)), dma="dpl")
                    p.op("S", lambda e, b=b, j=j, tt=tt: e.dma_start(
                        out=prS[b, tt], in_=pl_slice(pr, b, j)), deps=phk, dma="dpl")
                    p.op("S", lambda e, b=b, j=j, tt=tt: e.dma_start(
                        out=piS[b, tt], in_=pl_slice(pi, b, j)), dma="dpl")
                    p.op("S", lambda e, b=b, j=j, tt=tt: e.dma_start(
                        out=thS[b, tt], in_=th_slice(th, b, j)),
                        deps=[p.tok("thev")], dma="dpl")
                    p.op("S", lambda e, b=b, j=j, tt=tt: e.dma_start(
                        out=nlS[b, tt], in_=th_slice(nl, b, j)), dma="dpl")
            p.op("V", lambda e: e.engine_nop(), deps=[p.tok("dpl")], inc="qkev")
            p.op("V", lambda e: e.engine_nop(), deps=[p.tok("dpl")], inc="thev")
            p.op("V", lambda e: e.engine_nop(), deps=[p.tok("dpl")], inc="phi")

        pctx.close()

    phase1a()
    p.barrier()
    if max_phase < 2:
        p.replay()
        ctx.close()
        return nc, taps

    def phase1b():
        pctx = ExitStack()

        def pbuf(name, shape, dtype=F32):
            return pctx.enter_context(nc.sbuf_tensor("b_" + name, list(shape), dtype))

        Wo = pbuf("Wo", (128, 8 * D), BF16)
        for dc in range(8):
            p.op("G", lambda e, dc=dc: e.dma_start(
                out=Wo[:, dc * D:(dc + 1) * D],
                in_=dram["aWo"][dc * 128:(dc + 1) * 128, :]), dma="dwo")
        wtok = p.tok("dwo")

        qr = pbuf("qr", (128, 1024), BF16)
        qi = pbuf("qi", (128, 1024), BF16)
        pr = pbuf("pr", (128, 1024), BF16)
        pi = pbuf("pi", (128, 1024), BF16)
        th = pbuf("th", (128, 1024), F32)
        nl = pbuf("nl", (128, 1024), F32)
        mre = pbuf("mre", (128, 8 * 129), F32)
        mim = pbuf("mim", (128, 8 * 129), F32)
        s0 = pbuf("s0", (128, 1024), F32)
        s1 = pbuf("s1", (128, 1024), F32)
        s2 = pbuf("s2", (128, 1024), BF16)
        s3 = pbuf("s3", (128, 1024), BF16)
        s4 = pbuf("s4", (128, 1024), BF16)
        f16r = pbuf("f16r", (128, 1024), BF16)
        f16i = pbuf("f16i", (128, 1024), BF16)
        fdgc = pbuf("fdgc", (128, 1024), BF16)
        fdgs = pbuf("fdgs", (128, 1024), BF16)
        fqgc = pbuf("fqgc", (128, 1024), BF16)
        fqgs = pbuf("fqgs", (128, 1024), BF16)
        gsm = pbuf("gsm", (128, 30 * 64), F32)
        gsmb = pbuf("gsmb", (128, 32 * 64), BF16)
        LV = ("dg", "l16", "l32", "l64", "pv")
        Aq = {lv: pbuf("Aq_" + lv, (128, 1024), BF16) for lv in LV[:4]}
        Bq = {lv: pbuf("Bq_" + lv, (128, 1024), BF16) for lv in LV[:4]}
        Aqs = pbuf("Aq_st", (128, 1024), BF16)
        Bqs = pbuf("Bq_st", (128, 1024), BF16)
        Ck = {lv: pbuf("Ck_" + lv, (128, 1024), BF16) for lv in LV[:4]}
        Dk = {lv: pbuf("Dk_" + lv, (128, 1024), BF16) for lv in LV[:4]}
        vrg = pbuf("vrg", (128, 4 * 1024), BF16)
        Cst = pbuf("Cst", (128, 4 * 512), BF16)
        Dst = pbuf("Dst", (128, 4 * 512), BF16)
        h1l = pbuf("h1l", (128, 2 * 1024), BF16)
        xt1 = pbuf("xt1", (128, 1024), F32)
        Psb = {lv: pbuf("Psb" + lv, (128, 512), BF16) for lv in LV}
        yT = pbuf("yT", (128, 1024), BF16)
        x1f = pbuf("x1f", (128, 1024), F32)

        tap_y = tap("y", (B, NT, 128, D), BF16)
        tap_x1 = tap("x1", (B, NT, 128, D), F32)

        for ti in range(n_tiles):
            sl = ti % 2
            for b in range(B):
                l_fns = [
                    lambda e, b=b, ti=ti: e.dma_start(out=qr[:, b * 512:(b + 1) * 512],
                                                      in_=qrS[b, ti]),
                    lambda e, b=b, ti=ti: e.dma_start(out=qi[:, b * 512:(b + 1) * 512],
                                                      in_=qiS[b, ti]),
                    lambda e, b=b, ti=ti: e.dma_start(out=pr[:, b * 512:(b + 1) * 512],
                                                      in_=prS[b, ti]),
                    lambda e, b=b, ti=ti: e.dma_start(out=pi[:, b * 512:(b + 1) * 512],
                                                      in_=piS[b, ti]),
                    lambda e, b=b, ti=ti: e.dma_start(out=th[:, b * 512:(b + 1) * 512],
                                                      in_=thS[b, ti]),
                    lambda e, b=b, ti=ti: e.dma_start(out=nl[:, b * 512:(b + 1) * 512],
                                                      in_=nlS[b, ti]),
                    lambda e, b=b, ti=ti: e.dma_start(
                        out=h1l[:, b * 1024:(b + 1) * 1024],
                        in_=h1S[b, ti * 128:(ti + 1) * 128, :]),
                    lambda e, b=b, ti=ti, sl=sl: e.dma_start(
                        out=vrg[:, (b * 2 + sl) * 1024:(b * 2 + sl) * 1024 + 1024],
                        in_=vS[b, ti * 128:(ti + 1) * 128, :]),
                ]
                for i2, fn in enumerate(l_fns):
                    p.op("S", fn, deps=[p.tok("cmb"), p.tok("ymm"), p.tok("x1e")]
                         if i2 == 0 else (), dma="dld")
            ldtok = p.tok("dld")

            def scan_all(e):
                e.memset(APx(mre, 0, [[1032, 128], [129, 8], [1, 1]]), 0.0)
                e.memset(APx(mim, 0, [[1032, 128], [129, 8], [1, 1]]), 0.0)
                for b in range(B):
                    for c in range(4):
                        src = b * 512 + c * 128
                        dst = (b * 4 + c) * 129 + 1
                        e.tensor_tensor_scan(mre[:, dst:dst + 128],
                                             nl[:, src:src + 128], nl[:, src:src + 128],
                                             0.0, ALU.add, ALU.bypass)
                        inst = e.tensor_tensor_scan(mim[:, dst:dst + 128],
                                                    th[:, src:src + 128],
                                                    th[:, src:src + 128],
                                                    0.0, ALU.add, ALU.bypass)
                return inst
            t_scan = p.op("V", scan_all, deps=[ldtok, p.tok("gf"), p.tok("fb"), p.tok("dxo")], inc="msc")

            def mv4(m, off):
                return APx(m, off, [[1032, 128], [129, 8], [16, 8], [1, 16]])

            def mR(m, off):
                return APx(m, off, [[1032, 128], [129, 8], [16, 8], [0, 16]])

            def s_4(s, off=0):
                return APx(s, off, [[1024, 128], [128, 8], [16, 8], [1, 16]])

            def rangered(dep):
                t1 = p.op("V", lambda e: e.tensor_scalar(
                    s1[:], s0[:], 1.0 / (2 * PI), MAGIC, ALU.mult, ALU.add),
                    deps=dep, inc="rr")
                t2 = p.op("V", lambda e: e.tensor_scalar(
                    s1[:], s1[:], MAGIC, 2 * PI, ALU.subtract, ALU.mult), inc="rr")
                t3 = p.op("V", lambda e: e.tensor_sub(s0[:], s0[:], s1[:]), inc="rr")
                t4 = p.op("A", lambda e: e.activation(s3[:], s0[:], AF.Sin),
                          deps=[t3], inc="rr")
                t5 = p.op("A", lambda e: e.activation(s1[:], s0[:], AF.Abs),
                          deps=[t4], inc="rr")
                t6 = p.op("A", lambda e: e.activation(s4[:], s1[:], AF.Sin,
                                                      scale=-1.0, bias=pihalf[:]),
                          deps=[t5], inc="rr")
                return [t4, t6]

            tq1 = p.op("V", lambda e: e.tensor_tensor(s_4(s0), mv4(mre, 1), mR(mre, 0),
                                                      ALU.subtract), deps=[t_scan], inc="fb")
            tq2 = p.op("A", lambda e: e.activation(s2[:], s0[:], AF.Exp), deps=[tq1],
                       inc="fb")
            tq2b = p.op("V", lambda e: e.tensor_copy(f16r[:], s2[:]), deps=[tq2], inc="fb")
            tqe0 = p.op("V", lambda e: e.tensor_add(s1[:], s0[:], alphat[:]),
                        deps=[tq1], inc="fb")
            tqe1 = p.op("A", lambda e: e.activation(fqgc[:], s1[:], AF.Exp),
                        deps=[tqe0], inc="fb")
            tqd0 = p.op("V", lambda e: e.scalar_tensor_tensor(
                s1[:], s0[:], -1.0, alphat[:], ALU.mult, ALU.subtract),
                deps=[tqe1], inc="fb")
            tqd1 = p.op("A", lambda e: e.activation(s4[:], s1[:], AF.Exp),
                        deps=[tqd0], inc="fb")
            tqd2 = p.op("V", lambda e: e.tensor_copy(fdgc[:], s4[:]), deps=[tqd1], inc="fb")
            tq3b = p.op("V", lambda e: e.tensor_tensor(s_4(s0), mv4(mim, 1), mR(mim, 0),
                                                       ALU.subtract),
                        deps=[t_scan, tqd2], inc="fb")
            rr = rangered([tq3b])
            tqd3 = p.op("V", lambda e: e.tensor_mul(fdgs[:], fdgc[:], s3[:]),
                        deps=rr, inc="fb")
            tqd4 = p.op("V", lambda e: e.tensor_mul(fdgc[:], fdgc[:], s4[:]), inc="fb")
            tqe2 = p.op("V", lambda e: e.tensor_mul(fqgs[:], fqgc[:], s3[:]), inc="fb")
            tqe3 = p.op("V", lambda e: e.tensor_mul(fqgc[:], fqgc[:], s4[:]), inc="fb")
            tq4 = p.op("V", lambda e: e.tensor_mul(f16i[:], f16r[:], s3[:]),
                       deps=[tq2b] + rr, inc="fb")
            tq5 = p.op("V", lambda e: e.tensor_mul(f16r[:], f16r[:], s4[:]), inc="fb")
            ta1 = p.op("V", lambda e: e.tensor_mul(s2[:], qr[:], f16r[:]),
                       deps=[tq4, tq5, p.tok("evT")], inc="cmb")
            ta2 = p.op("V", lambda e: e.tensor_mul(s3[:], qi[:], f16i[:]), inc="cmb")
            ta3 = p.op("V", lambda e: e.tensor_add(Aq["l16"][:], s2[:], s3[:]), inc="cmb")
            tb1 = p.op("V", lambda e: e.tensor_mul(s2[:], qi[:], f16r[:]), inc="cmb")
            tb2 = p.op("V", lambda e: e.tensor_mul(s3[:], qr[:], f16i[:]), inc="cmb")
            tb3 = p.op("V", lambda e: e.tensor_sub(Bq["l16"][:], s2[:], s3[:]), inc="cmb")
            td1a = p.op("V", lambda e: e.tensor_mul(s2[:], qr[:], fqgc[:]),
                        deps=[tb3], inc="cmb")
            td1b = p.op("V", lambda e: e.tensor_mul(s3[:], qi[:], fqgs[:]), inc="cmb")
            td1 = p.op("V", lambda e: e.tensor_add(Aq["dg"][:], s2[:], s3[:]), inc="cmb")
            td2a = p.op("V", lambda e: e.tensor_mul(s2[:], qi[:], fqgc[:]), inc="cmb")
            td2b = p.op("V", lambda e: e.tensor_mul(s3[:], qr[:], fqgs[:]), inc="cmb")
            td2 = p.op("V", lambda e: e.tensor_sub(Bq["dg"][:], s2[:], s3[:]), inc="cmb")

            tk1 = p.op("V", lambda e: e.tensor_tensor(s_4(s0), mR(mre, 16), mv4(mre, 1),
                                                      ALU.subtract),
                       deps=[t_scan, td1, td2, tb3], inc="fb")
            tk2 = p.op("A", lambda e: e.activation(s2[:], s0[:], AF.Exp), deps=[tk1],
                       inc="fb")
            tk2b = p.op("V", lambda e: e.tensor_copy(f16r[:], s2[:]), deps=[tk2], inc="fb")
            tk3 = p.op("V", lambda e: e.tensor_tensor(s_4(s0), mR(mim, 16), mv4(mim, 1),
                                                      ALU.subtract), deps=[tk2b], inc="fb")
            rrk = rangered([tk3])
            tk4 = p.op("V", lambda e: e.tensor_mul(f16i[:], f16r[:], s3[:]),
                       deps=[tk2b] + rrk, inc="fb")
            tk5 = p.op("V", lambda e: e.tensor_mul(f16r[:], f16r[:], s4[:]), inc="fb")
            tc1 = p.op("V", lambda e: e.tensor_mul(s2[:], pr[:], f16r[:]),
                       deps=[tk4, tk5, p.tok("evT")], inc="cmb")
            tc2 = p.op("V", lambda e: e.tensor_mul(s3[:], pi[:], f16i[:]), inc="cmb")
            tc3 = p.op("V", lambda e: e.tensor_sub(Ck["l16"][:], s2[:], s3[:]), inc="cmb")
            te1 = p.op("V", lambda e: e.tensor_mul(s2[:], pr[:], f16i[:]), inc="cmb")
            te2 = p.op("V", lambda e: e.tensor_mul(s3[:], pi[:], f16r[:]), inc="cmb")
            te3 = p.op("V", lambda e: e.tensor_add(Dk["l16"][:], s2[:], s3[:]), inc="cmb")
            tf0a = p.op("V", lambda e: e.tensor_mul(s2[:], pr[:], fdgc[:]),
                        deps=[te3, tqd3, tqd4], inc="cmb")
            tf0b = p.op("V", lambda e: e.tensor_mul(s3[:], pi[:], fdgs[:]), inc="cmb")
            tf1 = p.op("V", lambda e: e.tensor_add(Ck["dg"][:], s2[:], s3[:]), inc="cmb")
            tf0c = p.op("V", lambda e: e.tensor_mul(s2[:], pi[:], fdgc[:]), inc="cmb")
            tf0d = p.op("V", lambda e: e.tensor_mul(s3[:], pr[:], fdgs[:]), inc="cmb")
            tf2 = p.op("V", lambda e: e.tensor_sub(Dk["dg"][:], s2[:], s3[:]), inc="cmb")
            base_tok = p.tok("cmb")

            def gsub_ops(kind):
                if kind == "q32":
                    return [(w, [[8, 8], [2, 4]], 16 * w, [[129, 8], [32, 4]],
                             0, [[129, 8], [32, 4]]) for w in range(2)]
                if kind == "k32":
                    return [(w, [[8, 8], [2, 4]], 32, [[129, 8], [32, 4]],
                             16 + 16 * w, [[129, 8], [32, 4]]) for w in range(2)]
                if kind == "q64":
                    return [(2 * v, [[8, 8], [4, 2], [1, 2]],
                             32 * v, [[129, 8], [64, 2], [0, 2]],
                             0, [[129, 8], [64, 2], [0, 2]]) for v in range(2)]
                if kind == "k64":
                    return [(2 * v, [[8, 8], [4, 2], [1, 2]],
                             64, [[129, 8], [64, 2], [0, 2]],
                             32 * v + 32, [[129, 8], [64, 2], [0, 2]]) for v in range(2)]
                if kind == "qst":
                    return [(0, [[8, 8], [4, 2], [1, 4]],
                             0, [[129, 8], [64, 2], [0, 4]],
                             0, [[129, 8], [0, 2], [0, 4]])]
                if kind == "kst":
                    return [(0, [[8, 8], [4, 2], [1, 4]],
                             128, [[129, 8], [0, 2], [0, 4]],
                             64, [[129, 8], [64, 2], [0, 4]])]
                raise ValueError(kind)

            KINDS = [("q32", 0, 1), ("q64", 2, 3), ("qst", 4, 5),
                     ("k32", 6, 7), ("k64", 8, 9), ("kst", 10, 11)]
            NKI = len(KINDS)

            def gsl(ki, j):
                o = (ki * 5 + j) * 64
                return gsm[:, o:o + 64]

            def mk_subs(kind, ki, m):
                def do(e, kind=kind, ki=ki, m=m):
                    inst = None
                    for (doff, dpat, hoff, hpat, loff, lpat) in gsub_ops(kind):
                        dst = APx(gsm, ki * 320 + doff, [[1920, 128]] + dpat)
                        hi = APx(m, hoff, [[1032, 128]] + hpat)
                        lo = APx(m, loff, [[1032, 128]] + lpat)
                        inst = e.tensor_tensor(dst, hi, lo, ALU.subtract)
                    return inst
                return do

            smb = lambda i: gsmb[:, i * 64:(i + 1) * 64]
            u1t, u2t, u3t, u4t, u5t, u6t, u7t, u8t, u10t = ({} for _ in range(9))
            for ki, (kind, slr, sli) in enumerate(KINDS):
                u1t[ki] = p.op("V", mk_subs(kind, ki, mre),
                               deps=([t_scan, p.tok("gf")] if ki == 0 else ()),
                               inc="gf")
            for ki in range(NKI):
                u2t[ki] = p.op("A", lambda e, ki=ki: e.activation(
                    gsl(ki, 1), gsl(ki, 0), AF.Exp), deps=[u1t[ki]], inc="gf")
            for ki, (kind, slr, sli) in enumerate(KINDS):
                u3t[ki] = p.op("V", mk_subs(kind, ki, mim), deps=[u2t[ki]], inc="gf")
            for ki in range(NKI):
                u4t[ki] = p.op("A", lambda e, ki=ki: e.activation(
                    gsl(ki, 2), gsl(ki, 0), AF.Copy, scale=1.0 / (2 * PI),
                    bias=MAGIC), deps=[u3t[ki]], inc="gf")
            for ki in range(NKI):
                u5t[ki] = p.op("A", lambda e, ki=ki: e.activation(
                    gsl(ki, 3), gsl(ki, 0), AF.Copy, scale=1.0 / (2 * PI)),
                    deps=[u3t[ki]], inc="gf")
            for ki in range(NKI):
                u6t[ki] = p.op("V", lambda e, ki=ki: e.scalar_tensor_tensor(
                    gsl(ki, 4), gsl(ki, 2), MAGIC, gsl(ki, 3),
                    ALU.subtract, ALU.subtract), deps=[u4t[ki], u5t[ki]], inc="gf")
            for ki, (kind, slr, sli) in enumerate(KINDS):
                u7t[ki] = p.op("A", lambda e, ki=ki, sli=sli: e.activation(
                    smb(sli), gsl(ki, 4), AF.Sin, scale=-2 * PI),
                    deps=[u6t[ki]] + ([p.tok("ch")] if ki == 0 else []), inc="gf")
            for ki in range(NKI):
                u8t[ki] = p.op("A", lambda e, ki=ki: e.activation(
                    gsl(ki, 2), gsl(ki, 4), AF.Abs), deps=[u6t[ki]], inc="gf")
            for ki, (kind, slr, sli) in enumerate(KINDS):
                u10t[ki] = p.op("A", lambda e, ki=ki, slr=slr: e.activation(
                    smb(slr), gsl(ki, 2), AF.Sin, scale=-2 * PI,
                    bias=pihalf[:]), deps=[u8t[ki]], inc="gf")
            gq_out = {}
            for ki, (kind, slr, sli) in enumerate(KINDS):
                ta = p.op("V", lambda e, ki=ki, sli=sli: e.tensor_mul(
                    smb(sli), smb(sli), gsl(ki, 1)),
                    deps=[u2t[ki], u7t[ki]], inc="gf")
                tb = p.op("V", lambda e, ki=ki, slr=slr: e.tensor_mul(
                    smb(slr), smb(slr), gsl(ki, 1)),
                    deps=[u10t[ki]], inc="gf")
                gq_out[kind] = [ta, tb]
            gq32, gq64, gqst = gq_out["q32"], gq_out["q64"], gq_out["qst"]
            gk32, gk64, gkst = gq_out["k32"], gq_out["k64"], gq_out["kst"]

            def gbb(slot):
                return APx(gsmb, slot * 64, [[2048, 128], [8, 8], [1, 8], [0, 16]])

            def chain_q(Asrc, Bsrc, Adst, Bdst, slr, sli, dep):
                t1 = p.op("V", lambda e: e.tensor_mul(s2[:], Asrc[:], gbb(slr)),
                          deps=list(dep) + [p.tok("evT")], inc="ch")
                t2 = p.op("V", lambda e: e.tensor_mul(s3[:], Bsrc[:], gbb(sli)), inc="ch")
                t3 = p.op("V", lambda e: e.tensor_add(Adst[:], s2[:], s3[:]), inc="ch")
                t4 = p.op("V", lambda e: e.tensor_mul(s2[:], Bsrc[:], gbb(slr)), inc="ch")
                t5 = p.op("V", lambda e: e.tensor_mul(s3[:], Asrc[:], gbb(sli)), inc="ch")
                t6 = p.op("V", lambda e: e.tensor_sub(Bdst[:], s2[:], s3[:]), inc="ch")
                return [t3, t6]

            def chain_k(Csrc, Dsrc, CdstAP, DdstAP, slr, sli, dep):
                t1 = p.op("V", lambda e: e.tensor_mul(s2[:], Csrc[:], gbb(slr)),
                          deps=list(dep) + [p.tok("evT")], inc="ch")
                t2 = p.op("V", lambda e: e.tensor_mul(s3[:], Dsrc[:], gbb(sli)), inc="ch")
                t3 = p.op("V", lambda e: e.tensor_tensor(CdstAP, s2[:], s3[:],
                                                         ALU.subtract), inc="ch")
                t4 = p.op("V", lambda e: e.tensor_mul(s2[:], Dsrc[:], gbb(slr)), inc="ch")
                t5 = p.op("V", lambda e: e.tensor_mul(s3[:], Csrc[:], gbb(sli)), inc="ch")
                t6 = p.op("V", lambda e: e.tensor_tensor(DdstAP, s2[:], s3[:], ALU.add),
                          inc="ch")
                return [t3, t6]

            cq32 = chain_q(Aq["l16"], Bq["l16"], Aq["l32"], Bq["l32"], 0, 1,
                           gq32 + [base_tok])
            cq64 = chain_q(Aq["l32"], Bq["l32"], Aq["l64"], Bq["l64"], 2, 3, gq64 + cq32)
            cqst = chain_q(Aq["l64"], Bq["l64"], Aqs, Bqs, 4, 5, gqst + cq64)
            ck32 = chain_k(Ck["l16"], Dk["l16"], Ck["l32"][:], Dk["l32"][:], 6, 7,
                           gk32 + [base_tok])
            ck64 = chain_k(Ck["l32"], Dk["l32"], Ck["l64"][:], Dk["l64"][:], 8, 9,
                           gk64 + ck32)
            CstV = APx(Cst, sl * 512, [[2048, 128], [1024, 2], [128, 4], [1, 128]])
            DstV = APx(Dst, sl * 512, [[2048, 128], [1024, 2], [128, 4], [1, 128]])
            ckst = chain_k(Ck["l64"], Dk["l64"], CstV, DstV, 10, 11,
                           gkst + ck64 + [p.tok("evT")])

            masks = {"dg": mask_diag, "l16": mask_16, "l32": mask_32, "l64": mask_64}
            lv_ready = {"dg": [td1, td2, tf1, tf2], "l16": [base_tok],
                        "l32": cq32 + ck32, "l64": cq64 + ck64,
                        "pv": cqst + ckst}
            gctr = [0]
            for b in range(B):
                for c in range(4):
                    ev_toks = {}
                    for lv in LV:
                        if lv == "pv" and ti == 0:
                            continue
                        g = gctr[0]
                        gctr[0] += 1
                        colg = (g % 4) * 128
                        foff = b * 512 + c * 128
                        if lv == "pv":
                            soff = (b * 2 + ((ti - 1) % 2)) * 512 + c * 128
                            Ssrc, Msrc = [Cst, Dst], [Aqs, Bqs]
                        else:
                            soff = foff
                            Ssrc, Msrc = [Ck[lv], Dk[lv]], [Aq[lv], Bq[lv]]
                        first = True
                        for pi_ in range(2):
                            for hc in range(4):
                                Ssl = Ssrc[pi_][32 * hc:32 * hc + 32, soff:soff + 128]
                                Msl = Msrc[pi_][32 * hc:32 * hc + 32, foff:foff + 128]
                                dps = [d for d in (lv_ready[lv] + [p.tok("pev"),
                                                                   p.tok("x1e")])
                                       if d is not None] if first and hc == 0 else ()
                                p.op("T", lambda e, Ssl=Ssl, Msl=Msl, hc=hc, colg=colg,
                                     first=first: e.matmul(
                                         psF[:, hc, colg:colg + 128], Ssl, Msl,
                                         skip_group_check=True, start=first, stop=(not first),
                                         tile_position=(32 * hc, 0)),
                                     deps=dps, mark=(pi_ == 1 and hc == 3))
                            first = False
                        tmm = p.tok("evT")
                        if lv == "pv":
                            def evac_pv(e, colg=colg):
                                for hc2 in range(4):
                                    inst = e.activation(
                                        Psb["pv"][:, hc2 * 128:(hc2 + 1) * 128],
                                        psF[:, hc2, colg:colg + 128], AF.Copy)
                                return inst
                            tk = p.op("A", evac_pv, deps=[tmm, p.tok("ymm")], inc="pev")
                        else:
                            def evac_lv(e, colg=colg, lv=lv):
                                for hc2 in range(4):
                                    inst = e.tensor_mul(
                                        Psb[lv][:, hc2 * 128:(hc2 + 1) * 128],
                                        psF[:, hc2, colg:colg + 128],
                                        masks[lv][:, hc2 * 128:(hc2 + 1) * 128])
                                return inst
                            tk = p.op("V", evac_lv, deps=[tmm, p.tok("ymm")], inc="pev")
                        ev_toks[lv] = tk
                    ydeps = [ev_toks.get(lv) for lv in LV if ev_toks.get(lv)]
                    vsl = b * 2 + sl
                    pvs = b * 2 + ((ti - 1) % 2)
                    for hc in range(4):
                        h = 4 * c + hc
                        yb = 4 + (h // 8)
                        ycol = ((h % 8) // 2) * 128
                        pofs = 64 * (h % 2)
                        lvlist = ["dg", "l16", "l32", "l64"] + (["pv"] if ti > 0 else [])
                        for li, lv in enumerate(lvlist):
                            vs = pvs if lv == "pv" else vsl
                            vst = vrg[:, vs * 1024 + 64 * h: vs * 1024 + 64 * h + 64]
                            mv = Psb[lv][:, hc * 128:hc * 128 + 128]
                            p.op("T", lambda e, vst=vst, mv=mv, yb=yb, ycol=ycol,
                                 li=li, pofs=pofs, last=(li == len(lvlist) - 1):
                                 e.matmul(
                                     psF[pofs:pofs + 64, yb, ycol:ycol + 128],
                                     vst, mv, skip_group_check=True, start=(li == 0), stop=last,
                                     tile_position=(0, pofs)),
                                 deps=(ydeps + [ldtok, p.tok("yev")]) if li == 0 else (),
                                 inc="ymm")
                ycht = [p.tok("ymm"), p.tok("evT")]
                te1 = p.op("V", lambda e: e.tensor_copy(yT[:, 0:512], psF[:, 4, :]),
                           deps=ycht, inc="yev")
                te2 = p.op("V", lambda e: e.tensor_copy(yT[:, 512:1024], psF[:, 5, :]),
                           deps=ycht, inc="yev")
                if tap_y is not None:
                    p.op("S", lambda e, b=b, ti=ti: e.dma_start(
                        out=taps["y"][b, ti], in_=yT[:]), deps=[te1, te2], dma="dtap")
                for half in range(2):
                    for dc in range(8):
                        p.op("T", lambda e, dc=dc, half=half: e.matmul(
                            psF[:, half, 0:512],
                            yT[:, dc * 128:(dc + 1) * 128],
                            Wo[:, dc * D + half * 512: dc * D + half * 512 + 512],
                            skip_group_check=True, start=(dc == 0), stop=False),
                            deps=[te1, te2, p.tok("x1e"), wtok] if dc == 0 else ())
                    p.op("T", lambda e, half=half: e.matmul(
                        psF[:, half, 0:512], ones_row[0:1, :],
                        rows["abo"][0:1, half * 512: half * 512 + 512],
                        skip_group_check=True, start=False, stop=True), mark=True)
                two = p.tok("evT")
                tx1 = p.op("V", lambda e, b=b: e.tensor_copy(
                    s1[:], h1l[:, b * 1024:(b + 1) * 1024]),
                    deps=[two, p.tok("dxs"), p.tok("dtap")], inc="x1e")
                tx2 = p.op("V", lambda e: e.tensor_tensor(
                    x1f[:, 0:512], psF[:, 0, 0:512], s1[:, 0:512], ALU.add),
                    deps=[tx1], inc="x1e")
                tx3 = p.op("V", lambda e: e.tensor_tensor(
                    x1f[:, 512:1024], psF[:, 1, 0:512], s1[:, 512:1024], ALU.add),
                    inc="x1e")
                if tap_x1 is not None:
                    p.op("S", lambda e, b=b, ti=ti: e.dma_start(
                        out=taps["x1"][b, ti], in_=x1f[:]), deps=[tx2, tx3], dma="dtap")
                p.op("G", lambda e, b=b, ti=ti: e.dma_start(
                    out=x1s[b, ti * 128:(ti + 1) * 128, :], in_=x1f[:]),
                    deps=[tx2, tx3], dma="dxs")
                lx2 = p.op("S", lambda e, b=b, ti=ti: e.dma_start(
                    out=xt1[:], in_=xin[b, ti * 128:(ti + 1) * 128, :]),
                    deps=[tx2, tx3, p.tok("x1e")], dma="dx2")
                tx4 = p.op("V", lambda e: e.tensor_add(s0[:], x1f[:], xt1[:]),
                           deps=[lx2], inc="x1e")
                p.op("S", lambda e, b=b, ti=ti: e.dma_start(
                    out=xout[b, ti * 128:(ti + 1) * 128, :], in_=s0[:]),
                    deps=[tx4], dma="dxo")

        pctx.close()

    phase1b()
    p.barrier()

    def phase2():
        pctx = ExitStack()

        def pbuf(name, shape, dtype=F32):
            return pctx.enter_context(nc.sbuf_tensor("c_" + name, list(shape), dtype))

        Wf1 = pbuf("Wf1", (128, 8 * FFN), BF16)
        Wf2 = pbuf("Wf2", (128, 32 * D), BF16)
        for jc in range(8):
            p.op("G", lambda e, jc=jc: e.dma_start(
                out=Wf1[:, jc * FFN:(jc + 1) * FFN],
                in_=dram["aWf1"][jc * 128:(jc + 1) * 128, :]), dma="dwf")
        for fc in range(32):
            p.op("G", lambda e, fc=fc: e.dma_start(
                out=Wf2[:, fc * D:(fc + 1) * D],
                in_=dram["aWf2"][fc * 128:(fc + 1) * 128, :]), dma="dwf")
        wtok2 = p.tok("dwf")

        x1l = pbuf("x1l", (128, 1024), F32)
        xol = pbuf("xol", (128, 1024), F32)
        h2 = pbuf("h2", (128, 1024), BF16)
        h2T = pbuf("h2T", (128, 1024), BF16)
        hg = pbuf("hg", (128, FFN), BF16)
        xs1f = pbuf("xs1f", (128, 1024), F32)
        stat = pbuf("stat", (128, 12), F32)
        s0 = pbuf("s0", (128, 1024), F32)
        s1 = pbuf("s1", (128, 1024), F32)
        s2 = pbuf("s2", (128, 1024), BF16)
        s3 = pbuf("s3", (128, 1024), BF16)
        eln = make_emit_ln(stat, s0, s1, s2, s3)
        tap_xs1 = tap("xs1", (B, NT, 128, D), F32)
        evq = []

        for ti in range(n_tiles):
            for b in range(B):
                lx = p.op("S", lambda e, b=b, ti=ti: e.dma_start(
                    out=x1l[:], in_=x1s[b, ti * 128:(ti + 1) * 128, :]),
                    deps=[p.tok("lns")], dma="dl2")
                lxo = p.op("S", lambda e, b=b, ti=ti: e.dma_start(
                    out=xol[:], in_=xout[b, ti * 128:(ti + 1) * 128, :]),
                    deps=[p.tok("x2e")], dma="dl2")
                t_h2 = eln(x1l[:], h2[:], 1, gain=g2t, bias=b2t,
                           dep=[lx, p.tok("evT")], bn=0)
                ttoks = []
                for r in range(2):
                    tms = []
                    for blk in range(4):
                        dc = r * 4 + blk
                        tm = p.op("T", lambda e, dc=dc, blk=blk: e.transpose(
                            psT[:, blk * 128:(blk + 1) * 128],
                            h2[:, dc * 128:(dc + 1) * 128], iden[:]),
                            deps=[t_h2, p.tok("xnt")], mark=True)
                        tms.append(tm)
                    ttoks.append(p.op("V", lambda e, r=r: e.tensor_copy(
                        h2T[:, r * 512:(r + 1) * 512], psT[:]), deps=tms, inc="xnt"))
                f1toks = {}
                for fb in range(32):
                    bank, col = fb % 4, ((fb // 4) % 4) * 128
                    for jc in range(8):
                        dps = []
                        if jc == 0:
                            if fb == 0:
                                dps += [ttoks[0], ttoks[1], wtok2]
                            if len(evq) >= 4:
                                dps.append(evq[-4])
                        p.op("T", lambda e, fb=fb, jc=jc, bank=bank, col=col: e.matmul(
                            psF[:, bank, col:col + 128],
                            Wf1[:, jc * FFN + fb * 128: jc * FFN + fb * 128 + 128],
                            h2T[:, jc * 128:(jc + 1) * 128],
                            skip_group_check=True, start=(jc == 0), stop=(jc == 7)),
                            deps=dps, mark=(jc == 7))
                    tmm = p.tok("evT")
                    f1toks[fb] = p.op("A", lambda e, fb=fb, bank=bank, col=col:
                                      e.activation(
                                          hg[:, fb * 128:(fb + 1) * 128],
                                          psF[:, bank, col:col + 128],
                                          AF.Gelu_apprx_tanh,
                                          bias=bf1c[:, fb:fb + 1]),
                                      deps=[tmm], inc="f1e")
                    evq.append(f1toks[fb])
                for half in range(2):
                    for fc in range(32):
                        dps = ([f1toks[fb] for fb in range(32)] + [p.tok("x2e")]) \
                            if fc == 0 else ()
                        p.op("T", lambda e, half=half, fc=fc: e.matmul(
                            psF[:, 4 + half, 0:512],
                            hg[:, fc * 128:(fc + 1) * 128],
                            Wf2[:, fc * D + half * 512: fc * D + half * 512 + 512],
                            skip_group_check=True, start=(fc == 0), stop=False), deps=dps)
                    p.op("T", lambda e, half=half: e.matmul(
                        psF[:, 4 + half, 0:512], ones_row[0:1, :],
                        rows["abf2"][0:1, half * 512: half * 512 + 512],
                        skip_group_check=True, start=False, stop=True), mark=True)
                t2w = p.tok("evT")
                tx1 = p.op("V", lambda e: e.tensor_tensor(
                    xs1f[:, 0:512], psF[:, 4, 0:512], xol[:, 0:512], ALU.add),
                    deps=[t2w, lxo, p.tok("dso"), p.tok("dtap")], inc="x2e")
                tx2 = p.op("V", lambda e: e.tensor_tensor(
                    xs1f[:, 512:1024], psF[:, 5, 0:512], xol[:, 512:1024],
                    ALU.add), inc="x2e")
                if tap_xs1 is not None:
                    p.op("S", lambda e, b=b, ti=ti: e.dma_start(
                        out=taps["xs1"][b, ti], in_=xs1f[:]),
                        deps=[tx1, tx2], dma="dtap")
                p.op("S", lambda e, b=b, ti=ti: e.dma_start(
                    out=xout[b, ti * 128:(ti + 1) * 128, :], in_=xs1f[:]),
                    deps=[tx1, tx2, p.tok("dtap")], dma="dso")
        pctx.close()

    if max_phase >= 2:
        phase2()
        p.barrier()

    def phase0():
        pctx = ExitStack()

        def pbuf(name, shape, dtype=F32):
            return pctx.enter_context(nc.sbuf_tensor("d_" + name, list(shape), dtype))

        wt = pbuf("wt", (128, 2 * FFN), F32)
        wa = pbuf("wa", (128, 2 * FFN), F32)
        acc = pbuf("acc", (128, 40), F32)
        acc2 = pbuf("acc2", (128, 40), F32)
        tn = pbuf("tn", (128, 2 * FFN), BF16)
        tb = pbuf("tb", (128, 2 * FFN), BF16)
        sc2 = pbuf("sc2", (128, 4), F32)
        scB = pbuf("scB", (1, 2), F32)

        def stats(src_name, nrt, ncols, col0):
            atoks, rtoks = [], []
            for rt in range(nrt):
                sl = rt % 2
                la = p.op("S", lambda e, rt=rt, sl=sl, src_name=src_name,
                          ncols=ncols: e.dma_start(
                              out=wt[:, sl * FFN: sl * FFN + ncols],
                              in_=dram[src_name][rt * 128:(rt + 1) * 128, :]),
                          deps=[atoks[rt - 2]] if rt >= 2 else [p.tok("p0a")],
                          dma="dp0l")
                atoks.append(p.op("A", lambda e, ncols=ncols, sl=sl: e.activation(
                    wa[:, sl * FFN: sl * FFN + ncols],
                    wt[:, sl * FFN: sl * FFN + ncols], AF.Abs),
                    deps=[la] + ([rtoks[rt - 2]] if rt >= 2
                                 else [p.tok("p0r")]), inc="p0a"))
                rtoks.append(p.op("V", lambda e, rt=rt, ncols=ncols, col0=col0,
                                  sl=sl:
                                  e.reduce_sum(acc[:, col0 + rt:col0 + rt + 1],
                                               wa[:, sl * FFN: sl * FFN + ncols],
                                               axis=AX.X),
                                  deps=[atoks[rt]], inc="p0r"))

        stats("fw1", 32, D, 0)
        stats("fw2", 8, FFN, 32)

        def scalar_pipeline(col0, n, wcol, pc, si):
            ca = p.op("A", lambda e, col0=col0, n=n: e.activation(
                acc2[:, col0:col0 + n], acc[:, col0:col0 + n], AF.Copy),
                deps=[p.tok("p0r")], inc="p0s")
            cv = p.op("V", lambda e, col0=col0, n=n, si=si: e.reduce_sum(
                sc2[:, si:si + 1], acc2[:, col0:col0 + n], axis=AX.X),
                deps=[ca], inc="p0s")
            tm1 = p.op("T", lambda e, pc=pc, si=si: e.matmul(
                psX[0:1, pc:pc + 1], sc2[:, si:si + 1], onec[:],
                skip_group_check=True, start=True, stop=True),
                deps=[cv, p.tok("p0s")], mark=True)
            cc1 = p.op("V", lambda e, pc=pc, si=si: e.tensor_copy(
                scB[0:1, si // 2:si // 2 + 1], psX[0:1, pc:pc + 1]),
                deps=[tm1], inc="p0s")
            tm2 = p.op("T", lambda e, pc=pc, si=si: e.matmul(
                psX[:, pc + 8:pc + 9], onef[0:1, :],
                scB[0:1, si // 2:si // 2 + 1],
                skip_group_check=True, start=True, stop=True), deps=[cc1], mark=True)
            cc2 = p.op("A", lambda e, pc=pc, si=si: e.activation(
                sc2[:, si:si + 1], psX[:, pc + 8:pc + 9], AF.Copy,
                scale=1.0 / (D * FFN)), deps=[tm2], inc="p0s")
            cm = p.op("V", lambda e, wcol=wcol, si=si: e.tensor_scalar_max(
                wsc[:, wcol:wcol + 1], sc2[:, si:si + 1], 1e-5),
                deps=[cc2], inc="p0s")
            ch = p.op("A", lambda e, wcol=wcol, si=si: e.activation(
                sc2[:, si + 1:si + 2], wsc[:, wcol:wcol + 1], AF.Copy),
                deps=[cm], inc="p0s")
            p.op("V", lambda e, wcol=wcol, si=si: e.reciprocal(
                wsc[:, wcol + 1:wcol + 2], sc2[:, si + 1:si + 2]),
                deps=[ch], inc="p0s")

        scalar_pipeline(0, 32, 0, 0, 0)
        scalar_pipeline(32, 8, 2, 4, 2)
        sc_tok = p.tok("p0s")

        def quant_store(src_name, nrt, ncols, wcol, dstT, vtoks, ttoks, dtoks):
            nblk = ncols // 128
            for rt in range(nrt):
                sl = rt % 2
                la = p.op("S", lambda e, rt=rt, sl=sl, src_name=src_name,
                          ncols=ncols: e.dma_start(
                              out=wt[:, sl * FFN: sl * FFN + ncols],
                              in_=dram[src_name][rt * 128:(rt + 1) * 128, :]),
                          deps=[vtoks[-2]] if len(vtoks) >= 2
                          else [p.tok("p0q")], dma="dp0l")
                v1 = p.op("V", lambda e, sl=sl, ncols=ncols, wcol=wcol:
                          e.tensor_scalar(wa[:, sl * FFN: sl * FFN + ncols],
                                          wt[:, sl * FFN: sl * FFN + ncols],
                                          wsc[:, wcol + 1:wcol + 2], MAGIC,
                                          ALU.mult, ALU.add),
                          deps=[la, sc_tok] + ([ttoks[-2]] if len(ttoks) >= 2
                                               else [p.tok("evT")]), inc="p0q")
                vtoks.append(v1)
                p.op("V", lambda e, sl=sl, ncols=ncols: e.tensor_scalar(
                    wa[:, sl * FFN: sl * FFN + ncols],
                    wa[:, sl * FFN: sl * FFN + ncols], MAGIC, 1.0,
                    ALU.subtract, ALU.min), inc="p0q")
                v3 = p.op("V", lambda e, sl=sl, ncols=ncols: e.tensor_scalar_max(
                    tn[:, sl * FFN: sl * FFN + ncols],
                    wa[:, sl * FFN: sl * FFN + ncols], -1.0), inc="p0q")
                tmlast = None
                for dc in range(nblk):
                    blk = dc % 4
                    tm = p.op("T", lambda e, dc=dc, blk=blk, sl=sl: e.transpose(
                        psT[:, blk * 128:(blk + 1) * 128],
                        tn[:, sl * FFN + dc * 128: sl * FFN + (dc + 1) * 128],
                        iden[:]),
                        deps=[v3, p.tok("xnt")], mark=True)
                    cv = p.op("V", lambda e, dc=dc, blk=blk, sl=sl: e.tensor_copy(
                        tb[:, sl * FFN + dc * 128: sl * FFN + (dc + 1) * 128],
                        psT[:, blk * 128:(blk + 1) * 128]),
                        deps=[tm] + ([dtoks[-2]] if len(dtoks) >= 2 and dc == 0
                                     else []), inc="xnt")
                    dt_ = p.op("G", lambda e, dc=dc, rt=rt, dstT=dstT, sl=sl:
                               e.dma_start(
                                   out=dstT[dc * 128:(dc + 1) * 128,
                                            rt * 128:(rt + 1) * 128],
                                   in_=tb[:, sl * FFN + dc * 128:
                                          sl * FFN + (dc + 1) * 128]),
                               deps=[cv], dma="dp0")
                    tmlast = tm
                ttoks.append(tmlast)
                dtoks.append(dt_)

        vtoks, ttoks, dtoks = [], [], []
        quant_store("fw1", 32, D, 0, w1tq, vtoks, ttoks, dtoks)
        quant_store("fw2", 8, FFN, 2, w2tq, vtoks, ttoks, dtoks)

        for wi, nm in enumerate(("bWq", "bWk", "bWv", "bWo")):
            for rt in range(8):
                gi = wi * 8 + rt
                sl = gi % 2
                lb = p.op("G", lambda e, nm=nm, rt=rt, sl=sl: e.dma_start(
                    out=tn[:, sl * FFN: sl * FFN + 1024],
                    in_=dram[nm][rt * 128:(rt + 1) * 128, :]),
                    deps=[ttoks[-2]] if len(ttoks) >= 2 else [p.tok("evT")],
                    dma="dp0b")
                tmlast = None
                for dc in range(8):
                    blk = dc % 4
                    tm = p.op("T", lambda e, dc=dc, blk=blk, sl=sl: e.transpose(
                        psT[:, blk * 128:(blk + 1) * 128],
                        tn[:, sl * FFN + dc * 128: sl * FFN + (dc + 1) * 128],
                        iden[:]),
                        deps=[lb, p.tok("xnt")], mark=True)
                    cv = p.op("V", lambda e, dc=dc, blk=blk, sl=sl: e.tensor_copy(
                        tb[:, sl * FFN + dc * 128: sl * FFN + (dc + 1) * 128],
                        psT[:, blk * 128:(blk + 1) * 128]),
                        deps=[tm] + ([dtoks[-2]] if len(dtoks) >= 2 and dc == 0
                                     else []), inc="xnt")
                    dt_ = p.op("G", lambda e, dc=dc, rt=rt, nm=nm, sl=sl:
                               e.dma_start(
                                   out=bWT[nm][dc * 128:(dc + 1) * 128,
                                               rt * 128:(rt + 1) * 128],
                                   in_=tb[:, sl * FFN + dc * 128:
                                          sl * FFN + (dc + 1) * 128]),
                               deps=[cv], dma="dp0")
                    tmlast = tm
                ttoks.append(tmlast)
                dtoks.append(dt_)
        pctx.close()

    def phase_bitffn(tapname):
        pctx = ExitStack()

        def pbuf(name, shape, dtype=F32):
            return pctx.enter_context(
                nc.sbuf_tensor(tapname + "_" + name, list(shape), dtype))

        W1 = pbuf("W1", (128, 8 * FFN), BF16)
        W2 = pbuf("W2", (128, 32 * D), BF16)
        for jc in range(8):
            p.op("S", lambda e, jc=jc: e.dma_start(
                out=W1[:, jc * FFN:(jc + 1) * FFN],
                in_=w1tq[jc * 128:(jc + 1) * 128, :]), dma="dwb")
        for fc in range(32):
            p.op("S", lambda e, fc=fc: e.dma_start(
                out=W2[:, fc * D:(fc + 1) * D],
                in_=w2tq[fc * 128:(fc + 1) * 128, :]), dma="dwb")
        wtokb = p.tok("dwb")

        xol = pbuf("xol", (128, 1024), F32)
        zf = pbuf("zf", (128, 1024), F32)
        zq = pbuf("zq", (128, 1024), BF16)
        zqT = pbuf("zqT", (128, 1024), BF16)
        hg = pbuf("hg", (128, FFN), BF16)
        z2q = pbuf("z2q", (128, 1024), BF16)
        z2T = pbuf("z2T", (128, 1024), BF16)
        xsf = pbuf("xsf", (128, 1024), F32)
        stat = pbuf("stat", (128, 40), F32)
        s0 = pbuf("s0", (128, 1024), F32)
        s1 = s0
        eln = make_emit_ln(stat, s0, s1, s1, s1)
        tap_xs = tap(tapname, (B, NT, 128, D), F32)
        evroll = []

        for ti in range(n_tiles):
            for b in range(B):
                lxo = p.op("S", lambda e, b=b, ti=ti: e.dma_start(
                    out=xol[:], in_=xout[b, ti * 128:(ti + 1) * 128, :]),
                    deps=[p.tok("bfe"), p.tok("lns")], dma="dl3")
                t_z = eln(xol[:], zf[:], 0,
                          dep=[lxo, p.tok("evT"), p.tok("bfq")], bn=0)
                q1 = p.op("A", lambda e: e.activation(s0[:], zf[:], AF.Abs),
                          deps=[t_z], inc="bfq")
                q2 = p.op("V", lambda e: e.reduce_max(stat[:, 12:13], s0[:],
                                                      axis=AX.X),
                          deps=[q1], inc="bfq")
                q3 = p.op("A", lambda e: e.activation(stat[:, 13:14],
                                                      stat[:, 12:13], AF.Copy,
                                                      scale=1.0 / 127.0),
                          deps=[q2], inc="bfq")
                q4 = p.op("V", lambda e: e.tensor_scalar_max(
                    stat[:, 14:15], stat[:, 13:14], 1e-5 / 127.0),
                    deps=[q3], inc="bfq")
                q5 = p.op("A", lambda e: e.activation(stat[:, 15:16],
                                                      stat[:, 14:15], AF.Copy),
                          deps=[q4], inc="bfq")
                q6 = p.op("V", lambda e: e.reciprocal(stat[:, 16:17],
                                                      stat[:, 15:16]),
                          deps=[q5], inc="bfq")
                q7 = p.op("A", lambda e: e.activation(stat[:, 17:18],
                                                      stat[:, 16:17], AF.Copy),
                          deps=[q6], inc="bfq")
                se1 = p.op("V", lambda e: e.tensor_scalar_mul(
                    stat[:, 18:19], stat[:, 15:16], wsc[:, 0:1]),
                    deps=[q5, p.tok("f1e")], inc="bfq")
                v1 = p.op("V", lambda e: e.tensor_scalar(
                    s1[:], zf[:], stat[:, 17:18], MAGIC, ALU.mult, ALU.add),
                    deps=[q7, p.tok("evT")], inc="bfq")
                p.op("V", lambda e: e.tensor_scalar(
                    s1[:], s1[:], MAGIC, 127.0, ALU.subtract, ALU.min),
                    inc="bfq")
                v3 = p.op("V", lambda e: e.tensor_scalar_max(
                    zq[:], s1[:], -128.0), inc="bfq")
                zt = []
                for dc in range(8):
                    blk = dc % 4
                    tm = p.op("T", lambda e, dc=dc, blk=blk: e.transpose(
                        psT[:, blk * 128:(blk + 1) * 128],
                        zq[:, dc * 128:(dc + 1) * 128], iden[:]),
                        deps=[v3, p.tok("xnt")], mark=True)
                    zt.append(p.op("V", lambda e, dc=dc, blk=blk: e.tensor_copy(
                        zqT[:, dc * 128:(dc + 1) * 128],
                        psT[:, blk * 128:(blk + 1) * 128]),
                        deps=[tm], inc="xnt"))
                qtoks = []
                for qk in range(4):
                    bks = (0, 1) if qk % 2 == 0 else (2, 3)
                    etoks = []
                    for bi, bank in enumerate(bks):
                        fcol0 = qk * 1024 + bi * 512
                        for jc in range(8):
                            dps = []
                            if jc == 0:
                                if qk == 0 and bi == 0:
                                    dps += zt + [wtokb]
                                if len(evroll) >= 4:
                                    dps.append(evroll[-4])
                            p.op("T", lambda e, jc=jc, bank=bank, fcol0=fcol0:
                                 e.matmul(
                                     psF[:, bank, 0:512],
                                     zqT[:, jc * 128:(jc + 1) * 128],
                                     W1[:, jc * FFN + fcol0:
                                        jc * FFN + fcol0 + 512],
                                     skip_group_check=True, start=(jc == 0), stop=(jc == 7)),
                                 deps=dps, mark=(jc == 7))
                        tmm = p.tok("evT")
                        etoks.append(p.op("A", lambda e, bank=bank, fcol0=fcol0:
                                          e.activation(
                                              hg[:, fcol0:fcol0 + 512],
                                              psF[:, bank, 0:512], AF.Gelu,
                                              scale=stat[:, 18:19]),
                                          deps=[tmm, se1, p.tok("bfq")],
                                          inc="f1e"))
                        evroll.append(etoks[-1])
                    qa = p.op("A", lambda e, qk=qk: e.activation(
                        s0[:], hg[:, qk * 1024:(qk + 1) * 1024], AF.Abs),
                        deps=etoks + [q2] + (qtoks[-1:] if qtoks else []),
                        inc="f1e")
                    qtoks.append(p.op("V", lambda e, qk=qk: e.reduce_max(
                        stat[:, 19 + qk:20 + qk], s0[:], axis=AX.X),
                        deps=[qa], inc="bfq"))
                hA = p.op("A", lambda e: e.activation(stat[:, 24:28],
                                                      stat[:, 19:23], AF.Copy),
                          deps=qtoks, inc="bfq")
                hV = p.op("V", lambda e: e.reduce_max(stat[:, 28:29],
                                                      stat[:, 24:28], axis=AX.X),
                          deps=[hA], inc="bfq")
                hB = p.op("A", lambda e: e.activation(stat[:, 29:30],
                                                      stat[:, 28:29], AF.Copy,
                                                      scale=1.0 / 127.0),
                          deps=[hV], inc="bfq")
                hC = p.op("V", lambda e: e.tensor_scalar_max(
                    stat[:, 30:31], stat[:, 29:30], 1e-5 / 127.0),
                    deps=[hB], inc="bfq")
                hD = p.op("A", lambda e: e.activation(stat[:, 31:32],
                                                      stat[:, 30:31], AF.Copy),
                          deps=[hC], inc="bfq")
                hE = p.op("V", lambda e: e.reciprocal(stat[:, 32:33],
                                                      stat[:, 31:32]),
                          deps=[hD], inc="bfq")
                hF = p.op("A", lambda e: e.activation(stat[:, 33:34],
                                                      stat[:, 32:33], AF.Copy),
                          deps=[hE], inc="bfq")
                se2 = p.op("V", lambda e: e.tensor_scalar_mul(
                    stat[:, 34:35], stat[:, 31:32], wsc[:, 2:3]),
                    deps=[hD], inc="bfq")
                for qk in range(4):
                    w1q = p.op("V", lambda e, qk=qk: e.tensor_scalar(
                        s1[:], hg[:, qk * 1024:(qk + 1) * 1024],
                        stat[:, 33:34], MAGIC, ALU.mult, ALU.add),
                        deps=[hF, p.tok("evT")], inc="bfq")
                    p.op("V", lambda e: e.tensor_scalar(
                        s1[:], s1[:], MAGIC, 127.0, ALU.subtract, ALU.min),
                        inc="bfq")
                    w3q = p.op("V", lambda e: e.tensor_scalar_max(
                        z2q[:], s1[:], -128.0), inc="bfq")
                    z2toks = []
                    for dc in range(8):
                        blk = dc % 4
                        tm = p.op("T", lambda e, dc=dc, blk=blk: e.transpose(
                            psT[:, blk * 128:(blk + 1) * 128],
                            z2q[:, dc * 128:(dc + 1) * 128], iden[:]),
                            deps=[w3q, p.tok("xnt")], mark=True)
                        z2toks.append(p.op("V", lambda e, dc=dc, blk=blk:
                                           e.tensor_copy(
                                               z2T[:, dc * 128:(dc + 1) * 128],
                                               psT[:, blk * 128:(blk + 1) * 128]),
                                           deps=[tm], inc="xnt"))
                    for half in range(2):
                        for dci in range(8):
                            fc = qk * 8 + dci
                            dps = (z2toks + [p.tok("bfe")]) if fc == 0 else \
                                (z2toks if dci == 0 else ())
                            p.op("T", lambda e, half=half, fc=fc, dci=dci:
                                 e.matmul(
                                     psF[:, 4 + half, 0:512],
                                     z2T[:, dci * 128:(dci + 1) * 128],
                                     W2[:, fc * D + half * 512:
                                        fc * D + half * 512 + 512],
                                     skip_group_check=True, start=(fc == 0), stop=(fc == 31)),
                                 deps=dps, mark=(fc == 31 and half == 1))
                t2w = p.tok("evT")
                ev = []
                for half in range(2):
                    e1 = p.op("V", lambda e, half=half: e.tensor_scalar_mul(
                        s1[:, half * 512:(half + 1) * 512],
                        psF[:, 4 + half, 0:512], stat[:, 34:35]),
                        deps=[t2w, se2, lxo, p.tok("dso2")], inc="bfe")
                    ev.append(p.op("V", lambda e, half=half: e.tensor_tensor(
                        xsf[:, half * 512:(half + 1) * 512],
                        s1[:, half * 512:(half + 1) * 512],
                        xol[:, half * 512:(half + 1) * 512], ALU.add),
                        inc="bfe"))
                if tap_xs is not None:
                    p.op("S", lambda e, b=b, ti=ti: e.dma_start(
                        out=taps[tapname][b, ti], in_=xsf[:]),
                        deps=ev, dma="dtap")
                p.op("S", lambda e, b=b, ti=ti: e.dma_start(
                    out=xout[b, ti * 128:(ti + 1) * 128, :], in_=xsf[:]),
                    deps=ev + [p.tok("dtap")], dma="dso2")
        pctx.close()

    def phase4():
        pctx = ExitStack()

        def pbuf(name, shape, dtype=F32):
            return pctx.enter_context(nc.sbuf_tensor("e_" + name, list(shape), dtype))

        WqT = pbuf("WqT", (128, 8 * D), BF16)
        WkT = pbuf("WkT", (128, 8 * D), BF16)
        WvT = pbuf("WvT", (128, 8 * D), BF16)
        for W, nm in [(WqT, "bWq"), (WkT, "bWk"), (WvT, "bWv")]:
            for jc in range(8):
                p.op("S", lambda e, W=W, nm=nm, jc=jc: e.dma_start(
                    out=W[:, jc * D:(jc + 1) * D],
                    in_=bWT[nm][jc * 128:(jc + 1) * 128, :]), dma="dw4")
        wtok4 = p.tok("dw4")
        xol = pbuf("xol", (128, 1024), F32)
        zb = pbuf("zb", (128, 1024), BF16)
        zbT = pbuf("zbT", (128, 1024), BF16)
        QTt = pbuf("QTt", (128, 1024), BF16)
        KTt = pbuf("KTt", (128, 1024), BF16)
        Vt = pbuf("Vt", (128, 1024), BF16)
        stat = pbuf("stat", (128, 12), F32)
        s0 = pbuf("s0", (128, 1024), F32)
        eln = make_emit_ln(stat, s0, s0, s0, s0)
        evq4 = []

        cut = globals().get("P4_CUT", 99)
        for ti in range(n_tiles):
            for b in range(B):
                lxo = p.op("S", lambda e, b=b, ti=ti: e.dma_start(
                    out=xol[:], in_=xout[b, ti * 128:(ti + 1) * 128, :]),
                    deps=[p.tok("lns")], dma="dl4")
                t_z = eln(xol[:], zb[:], 0, dep=[lxo, p.tok("evT")], bn=0)
                if cut < 1:
                    continue
                zt = []
                for dc in range(8):
                    blk = dc % 4
                    tm = p.op("T", lambda e, dc=dc, blk=blk: e.transpose(
                        psT[:, blk * 128:(blk + 1) * 128],
                        zb[:, dc * 128:(dc + 1) * 128], iden[:]),
                        deps=[t_z, p.tok("xnt")], mark=True)
                    zt.append(p.op("V", lambda e, dc=dc, blk=blk: e.tensor_copy(
                        zbT[:, dc * 128:(dc + 1) * 128],
                        psT[:, blk * 128:(blk + 1) * 128]),
                        deps=[tm], inc="xnt"))
                if cut < 2:
                    continue
                ev4 = []
                for gi, (W, dstT, bcol) in enumerate(
                        [(WqT, QTt, bqcol)] * 8 + [(WkT, KTt, bkcol)] * 8):
                    ic = gi % 8
                    if True:
                        bank = gi % 4
                        col = ((gi // 4) % 4) * 128
                        for jc in range(8):
                            dps = []
                            if jc == 0:
                                if gi == 0:
                                    dps += zt + [wtok4]
                                if len(evq4) >= 4:
                                    dps.append(evq4[-4])
                            p.op("T", lambda e, W=W, ic=ic, jc=jc, bank=bank,
                                 col=col: e.matmul(
                                     psF[:, bank, col:col + 128],
                                     W[:, jc * D + ic * 128:
                                       jc * D + ic * 128 + 128],
                                     zbT[:, jc * 128:(jc + 1) * 128],
                                     skip_group_check=True, start=(jc == 0), stop=(jc == 7)),
                                 deps=dps, mark=(jc == 7))
                        tmm = p.tok("evT")
                        ev4.append(p.op("V", lambda e, dstT=dstT, bcol=bcol,
                                        ic=ic, bank=bank, col=col:
                                        e.tensor_scalar_add(
                                            dstT[:, ic * 128:(ic + 1) * 128],
                                            psF[:, bank, col:col + 128],
                                            bcol[:, ic:ic + 1]),
                                        deps=[tmm, p.tok("ds4")], inc="q4e"))
                        evq4.append(ev4[-1])
                if cut < 3:
                    continue
                for half in range(2):
                    for jc in range(8):
                        dps = (zt + [p.tok("q4e")]) if jc == 0 else ()
                        p.op("T", lambda e, half=half, jc=jc: e.matmul(
                            psF[:, 4 + half, 0:512],
                            zbT[:, jc * 128:(jc + 1) * 128],
                            WvT[:, jc * D + half * 512:
                                jc * D + half * 512 + 512],
                            skip_group_check=True, start=(jc == 0), stop=False), deps=dps)
                    p.op("T", lambda e, half=half: e.matmul(
                        psF[:, 4 + half, 0:512], ones_row[0:1, :],
                        rows["bbv"][0:1, half * 512: half * 512 + 512],
                        skip_group_check=True, start=False, stop=True), mark=True)
                tvm = p.tok("evT")
                ev4.append(p.op("V", lambda e: e.tensor_copy(
                    Vt[:, 0:512], psF[:, 4, 0:512]),
                    deps=[tvm, p.tok("ds4")], inc="q4e"))
                ev4.append(p.op("V", lambda e: e.tensor_copy(
                    Vt[:, 512:1024], psF[:, 5, 0:512]),
                    deps=[tvm], inc="q4e"))
                if cut < 4:
                    continue
                p.op("G", lambda e, b=b, ti=ti: e.dma_start(
                    out=qbS[b, ti], in_=QTt[:]), deps=ev4, dma="ds4")
                for ic in range(8):
                    p.op("G", lambda e, b=b, ti=ti, ic=ic: e.dma_start(
                        out=kbS[b, ic, :, ti * 128:(ti + 1) * 128],
                        in_=KTt[:, ic * 128:(ic + 1) * 128]), dma="ds4")
                p.op("G", lambda e, b=b, ti=ti: e.dma_start(
                    out=vbS[b, ti * 128:(ti + 1) * 128, :], in_=Vt[:]),
                    dma="ds4")
        pctx.close()

    def phase5():
        pctx = ExitStack()

        def pbuf(name, shape, dtype=F32):
            return pctx.enter_context(nc.sbuf_tensor("f_" + name, list(shape), dtype))

        Kpl = pbuf("Kpl", (128, 8 * T), BF16)
        Vpl = pbuf("Vpl", (128, NT * 1024), BF16)
        WoT = pbuf("WoT", (128, 8 * D), BF16)
        for jc in range(8):
            p.op("S", lambda e, jc=jc: e.dma_start(
                out=WoT[:, jc * D:(jc + 1) * D],
                in_=bWT["bWo"][jc * 128:(jc + 1) * 128, :]), dma="dw5")
        wtok5 = p.tok("dw5")
        Qt = pbuf("Qt", (128, 1024), BF16)
        Pex = pbuf("Pex", (128, 2048), BF16)
        PnT = pbuf("PnT", (128, 2048), BF16)
        avT = pbuf("avT", (128, 1024), BF16)
        xol = pbuf("xol", (128, 1024), F32)
        xsf = pbuf("xsf", (128, 1024), F32)
        stat = pbuf("stat", (128, 16), F32)
        tap_xs3 = tap("xs3", (B, NT, 128, D), F32)
        RS = 1.0 / math.sqrt(512.0)
        Ts = n_tiles * 128
        nsb = (Ts + 511) // 512

        for b in range(B):
            kvdeps = [p.tok("evT")]
            for ic in range(8):
                p.op("S", lambda e, b=b, ic=ic: e.dma_start(
                    out=Kpl[:, ic * T:ic * T + Ts], in_=kbS[b, ic, :, 0:Ts]),
                    deps=kvdeps if ic == 0 else (), dma="dkv")
            for st in range(n_tiles):
                p.op("S", lambda e, b=b, st=st: e.dma_start(
                    out=Vpl[:, st * 1024:(st + 1) * 1024],
                    in_=vbS[b, st * 128:(st + 1) * 128, :]), dma="dkv")
            kvtok = p.tok("dkv")
            for ti in range(n_tiles):
                lq = p.op("S", lambda e, b=b, ti=ti: e.dma_start(
                    out=Qt[:], in_=qbS[b, ti]),
                    deps=[p.tok("evT")], dma="dq5")
                lxo = p.op("S", lambda e, b=b, ti=ti: e.dma_start(
                    out=xol[:], in_=xout[b, ti * 128:(ti + 1) * 128, :]),
                    deps=[p.tok("x5e")], dma="dq5")
                for sb in range(nsb):
                    Nn = min(512, Ts - sb * 512)
                    for ic in range(8):
                        dps = ([lq, kvtok, p.tok("sfe")]
                               if (sb == 0 and ic == 0) else ())
                        p.op("T", lambda e, sb=sb, ic=ic, Nn=Nn: e.matmul(
                            psF[:, sb, 0:Nn],
                            Qt[:, ic * 128:(ic + 1) * 128],
                            Kpl[:, ic * T + sb * 512: ic * T + sb * 512 + Nn],
                            skip_group_check=True, start=(ic == 0), stop=(ic == 7)),
                            deps=dps, mark=(ic == 7))
                smk = p.tok("evT")
                mt = []
                for sb in range(nsb):
                    Nn = min(512, Ts - sb * 512)
                    mt.append(p.op("V", lambda e, sb=sb, Nn=Nn: e.reduce_max(
                        stat[:, sb:sb + 1], psF[:, sb, 0:Nn], axis=AX.X),
                        deps=[smk] if sb == 0 else (), inc="sfq"))
                mA = p.op("A", lambda e: e.activation(stat[:, 4:4 + nsb],
                                                      stat[:, 0:nsb], AF.Copy),
                          deps=mt, inc="sfq")
                mV = p.op("V", lambda e: e.reduce_max(stat[:, 8:9],
                                                      stat[:, 4:4 + nsb],
                                                      axis=AX.X),
                          deps=[mA], inc="sfq")
                mN = p.op("A", lambda e: e.activation(stat[:, 9:10],
                                                      stat[:, 8:9], AF.Copy,
                                                      scale=-RS),
                          deps=[mV], inc="sfq")
                mH = p.op("V", lambda e: e.tensor_copy(stat[:, 10:11],
                                                       stat[:, 9:10]),
                          deps=[mN], inc="sfq")
                ext = []
                for sb in range(nsb):
                    Nn = min(512, Ts - sb * 512)
                    ext.append(p.op("A", lambda e, sb=sb, Nn=Nn: e.activation(
                        Pex[:, sb * 512:sb * 512 + Nn], psF[:, sb, 0:Nn],
                        AF.Exp, scale=RS, bias=stat[:, 10:11]),
                        deps=[mH, p.tok("sfq")] if sb == 0 else (), inc="sfe"))
                r1 = p.op("V", lambda e: e.reduce_sum(stat[:, 11:12],
                                                      Pex[:, 0:Ts], axis=AX.X),
                          deps=ext, inc="sfq")
                r2 = p.op("A", lambda e: e.activation(stat[:, 12:13],
                                                      stat[:, 11:12], AF.Copy),
                          deps=[r1], inc="sfq")
                r3 = p.op("V", lambda e: e.reciprocal(stat[:, 13:14],
                                                      stat[:, 12:13]),
                          deps=[r2], inc="sfq")
                r4 = p.op("A", lambda e: e.activation(stat[:, 14:15],
                                                      stat[:, 13:14], AF.Copy),
                          deps=[r3], inc="sfq")
                nrm = p.op("V", lambda e: e.tensor_scalar_mul(
                    Pex[:, 0:Ts], Pex[:, 0:Ts], stat[:, 14:15]),
                    deps=[r4], inc="sfq")
                ptoks = []
                for st in range(n_tiles):
                    blk = st % 4
                    tm = p.op("T", lambda e, st=st, blk=blk: e.transpose(
                        psT[:, blk * 128:(blk + 1) * 128],
                        Pex[:, st * 128:(st + 1) * 128], iden[:]),
                        deps=[nrm, p.tok("xnt")], mark=True)
                    ptoks.append(p.op("V", lambda e, st=st, blk=blk:
                                      e.tensor_copy(
                                          PnT[:, st * 128:(st + 1) * 128],
                                          psT[:, blk * 128:(blk + 1) * 128]),
                                      deps=[tm], inc="xnt"))
                for vc in range(8):
                    bank = 4 + vc // 4
                    col = (vc % 4) * 128
                    for st in range(n_tiles):
                        dps = (ptoks + [p.tok("ave")]) \
                            if (vc == 0 and st == 0) else ()
                        p.op("T", lambda e, vc=vc, st=st, bank=bank, col=col:
                             e.matmul(
                                 psF[:, bank, col:col + 128],
                                 Vpl[:, st * 1024 + vc * 128:
                                     st * 1024 + vc * 128 + 128],
                                 PnT[:, st * 128:(st + 1) * 128],
                                 skip_group_check=True, start=(st == 0),
                                 stop=(st == n_tiles - 1)),
                             deps=dps, mark=(st == n_tiles - 1))
                avk = p.tok("evT")
                avt = []
                for vc in range(8):
                    bank = 4 + vc // 4
                    col = (vc % 4) * 128
                    avt.append(p.op("V", lambda e, vc=vc, bank=bank, col=col:
                                    e.tensor_copy(
                                        avT[:, vc * 128:(vc + 1) * 128],
                                        psF[:, bank, col:col + 128]),
                                    deps=[avk] if vc == 0 else (), inc="ave"))
                for half in range(2):
                    for vc in range(8):
                        dps = (avt + ext + [wtok5, p.tok("x5e")]) \
                            if (half == 0 and vc == 0) else ()
                        p.op("T", lambda e, half=half, vc=vc: e.matmul(
                            psF[:, half, 0:512],
                            avT[:, vc * 128:(vc + 1) * 128],
                            WoT[:, vc * D + half * 512:
                                vc * D + half * 512 + 512],
                            skip_group_check=True, start=(vc == 0), stop=False), deps=dps)
                    p.op("T", lambda e, half=half: e.matmul(
                        psF[:, half, 0:512], ones_row[0:1, :],
                        rows["bbo"][0:1, half * 512: half * 512 + 512],
                        skip_group_check=True, start=False, stop=True), mark=True)
                t5w = p.tok("evT")
                ex1 = p.op("V", lambda e: e.tensor_tensor(
                    xsf[:, 0:512], psF[:, 0, 0:512], xol[:, 0:512], ALU.add),
                    deps=[t5w, lxo, p.tok("dso5"), p.tok("dtap")], inc="x5e")
                ex2 = p.op("V", lambda e: e.tensor_tensor(
                    xsf[:, 512:1024], psF[:, 1, 0:512], xol[:, 512:1024],
                    ALU.add), inc="x5e")
                if tap_xs3 is not None:
                    p.op("S", lambda e, b=b, ti=ti: e.dma_start(
                        out=taps["xs3"][b, ti], in_=xsf[:]),
                        deps=[ex1, ex2], dma="dtap")
                p.op("S", lambda e, b=b, ti=ti: e.dma_start(
                    out=xout[b, ti * 128:(ti + 1) * 128, :], in_=xsf[:]),
                    deps=[ex1, ex2, p.tok("dtap")], dma="dso5")
        pctx.close()

    if max_phase >= 3:
        phase0()
        p.barrier()
        phase_bitffn("xs2")
        p.barrier()
    if max_phase >= 4:
        phase4()
        p.barrier()
    if max_phase >= 5:
        phase5()
        p.barrier()
    if max_phase >= 6:
        phase_bitffn("xs4")
        p.barrier()

    p.replay()
    ctx.close()
    return nc, taps


NMAP = {
    "xin": "L0_attn_a_x",
    "aWq": "L0_attn_a_Wq", "abq": "L0_attn_a_bq", "aWk": "L0_attn_a_Wk",
    "abk": "L0_attn_a_bk", "aWv": "L0_attn_a_Wv", "abv": "L0_attn_a_bv",
    "aWo": "L0_attn_a_Wo", "abo": "L0_attn_a_bo",
    "n1g": "L0_attn_a_n1g", "n1b": "L0_attn_a_n1b",
    "n2g": "L0_attn_a_n2g", "n2b": "L0_attn_a_n2b",
    "thb": "L0_attn_a_theta_base", "tw1": "L0_attn_a_theta_w1",
    "tw2": "L0_attn_a_theta_w2", "lmb": "L0_attn_a_lambda_base",
    "eta": "L0_attn_a_eta",
    "aWf1": "L0_attn_a_Wf1", "abf1": "L0_attn_a_bf1",
    "aWf2": "L0_attn_a_Wf2", "abf2": "L0_attn_a_bf2",
    "bWq": "L0_attn_b_Wq", "bbq": "L0_attn_b_bq", "bWk": "L0_attn_b_Wk",
    "bbk": "L0_attn_b_bk", "bWv": "L0_attn_b_Wv", "bbv": "L0_attn_b_bv",
    "bWo": "L0_attn_b_Wo", "bbo": "L0_attn_b_bo",
    "fw1": "L0_ffn_w1", "fw2": "L0_ffn_w2",
}


def make_in_maps(inputs):
    x = np.ascontiguousarray(inputs[NMAP["xin"]], dtype=np.float32)
    in_maps = []
    cst = host_constants()
    for core in range(NCORES):
        m = {"xin": x[core * B:(core + 1) * B], "cst": cst}
        for k2, v2 in NMAP.items():
            if k2 == "xin":
                continue
            m[k2] = np.ascontiguousarray(inputs[v2], dtype=np.float32)
        in_maps.append(m)
    return in_maps


def kernel(**inputs):
    nc, _ = build_program()
    in_maps = make_in_maps(inputs)
    res = run_bass_kernel_spmd(nc, in_maps, core_ids=list(range(NCORES)))
    return np.concatenate([res.results[i]["out"] for i in range(NCORES)], axis=0)


# version 03
# speedup vs baseline: 1.3262x; 1.0522x over previous
import math
from contextlib import ExitStack

import numpy as np

import concourse.bass as bass
import concourse.mybir as mybir
from concourse.bass_utils import run_bass_kernel_spmd

F32 = mybir.dt.float32
BF16 = mybir.dt.bfloat16
I8 = mybir.dt.int8
AF = mybir.ActivationFunctionType
ALU = mybir.AluOpType
AX = mybir.AxisListType

B_TOT, T, D = 16, 2048, 1024
NCORES = 8
B = 2
H, KD, BK = 16, 64, 32
FFN = 4096
LORA = 48
NT = 16
ST = 2
NST = NT // ST
ALPHA = 3.5
PI = math.pi
MAGIC = 12582912.0
EPS_OUT = 1e-5
EPS_IN = 1e-6
CST_COLS = 3200


def host_constants():
    c = np.zeros((128, CST_COLS), np.float32)
    pidx = np.arange(128)
    c[:, 0:128] = np.eye(128, dtype=np.float32)
    c[:, 128:1152] = (ALPHA * (np.arange(1024) % 16))[None, :]
    t = pidx
    d = t[None, :] - pidx[:, None]
    blk_t, blk_p = t[None, :] // 16, (pidx // 16)[:, None]
    md = ((blk_t == blk_p) & (d >= 0)) * np.exp(-ALPHA * np.maximum(d, 0))
    c[:, 1152:1664] = np.tile(md, (1, 4))
    m16 = (blk_t == blk_p + 1) & ((pidx // 16 % 2 == 0))[:, None]
    c[:, 1664:2176] = np.tile(m16, (1, 4))
    m32 = ((t // 32)[None, :] == (pidx // 32)[:, None] + 1) & \
        ((pidx // 32 % 2 == 0))[:, None]
    c[:, 2176:2688] = np.tile(m32, (1, 4))
    m64 = (t[None, :] >= 64) & (pidx[:, None] < 64)
    c[:, 2688:3200] = np.tile(m64, (1, 4))
    return c

ENGS = ("T", "V", "A", "G", "S")


class Prog:

    def __init__(self, nc):
        self.nc = nc
        self.q = {k: [] for k in ENGS}
        self.sems = {}
        self.members = {}
        self.waited = {k: {} for k in ENGS}
        for k in ENGS:
            self.sem("ev" + k)
        self.nbar = 0
        self.sem("bar")

    def sem(self, name):
        if name not in self.sems:
            h = self.nc.alloc_semaphore(name=name)
            self.sems[name] = [h, 0]
        return name

    def tok(self, name):
        out = []
        if name in self.sems and self.sems[name][1] > 0:
            out.append((name, self.sems[name][1]))
        for nm in sorted(self.members.get(name, ())):
            if self.sems[nm][1] > 0:
                out.append((nm, self.sems[nm][1]))
        return tuple(out)

    def wait(self, key, *toks):
        for tk in toks:
            if tk is None:
                continue
            if isinstance(tk, (list, tuple)) and (len(tk) == 0
                                                  or not isinstance(tk[0], str)):
                self.wait(key, *tk)
                continue
            (sname, val) = tk
            if val <= 0:
                continue
            w = self.waited[key]
            if w.get(sname, -1) >= val:
                continue
            w[sname] = val
            h = self.sems[sname][0]
            self.q[key].append(lambda e, h=h, v=val: e.wait_ge(h, v))

    def op(self, key, fn, deps=(), inc=None, dma=None, mark=False):
        self.wait(key, *deps)
        incs = []
        if inc is not None:
            nm = inc + "." + key
            self.sem(nm)
            self.sems[nm][1] += 1
            self.members.setdefault(inc, set()).add(nm)
            incs.append((nm, 1))
        if dma is not None:
            nm = dma + "." + key
            self.sem(nm)
            self.sems[nm][1] += 16
            self.members.setdefault(dma, set()).add(nm)
            incs.append((nm, 16))
        if mark:
            en = "ev" + key
            self.sems[en][1] += 1
            incs.append((en, 1))
        toks = [(n, self.sems[n][1]) for n, _ in incs]

        def runner(e, fn=fn, incs=tuple(incs)):
            inst = fn(e)
            for (n, kk) in incs:
                inst.then_inc(self.sems[n][0], kk)
            return inst

        self.q[key].append(runner)
        return toks[0] if len(toks) == 1 else (toks if toks else None)

    def barrier(self):
        self.nbar += 1
        cur = [(n, s[1]) for n, s in self.sems.items() if s[1] > 0 and n != "bar"]
        bh = self.sems["bar"][0]
        self.sems["bar"][1] += len(ENGS)
        tgt = self.nbar * len(ENGS)
        for key in ENGS:
            self.wait(key, *cur)
            self.q[key].append(lambda e, bh=bh: e.sem_inc(bh, 1))
            self.q[key].append(lambda e, bh=bh, t=tgt: e.wait_ge(bh, t))
            self.waited[key] = {n: v for n, v in cur}
            self.waited[key]["bar"] = tgt

    def replay(self):
        with self.nc.Block() as block:
            @block.tensor
            def _(e):
                for f in self.q["T"]:
                    f(e)

            @block.vector
            def _(e):
                for f in self.q["V"]:
                    f(e)

            @block.scalar
            def _(e):
                for f in self.q["A"]:
                    f(e)

            @block.gpsimd
            def _(e):
                for f in self.q["G"]:
                    f(e)

            @block.sync
            def _(e):
                for f in self.q["S"]:
                    f(e)


def APx(t, off, pat):
    return bass.AP(t, off, pat)


def build_program(debug_taps=(), n_tiles=NT, max_phase=99):
    nc = bass.Bass("TRN2", target_bir_lowering=False, detect_race_conditions=False)
    p = Prog(nc)
    ctx = ExitStack()

    dram = {}

    def din(name, shape):
        dram[name] = nc.dram_tensor(name, list(shape), F32, kind="ExternalInput").ap()
        return dram[name]

    xin = din("xin", (B, T, D))
    for nm, shp in [
        ("aWq", (D, D)), ("abq", (D,)), ("aWk", (D, D)), ("abk", (D,)),
        ("aWv", (D, D)), ("abv", (D,)), ("aWo", (D, D)), ("abo", (D,)),
        ("n1g", (D,)), ("n1b", (D,)), ("n2g", (D,)), ("n2b", (D,)),
        ("thb", (H, BK)), ("tw1", (D, LORA)), ("tw2", (LORA, H * BK)),
        ("lmb", (H, BK)), ("eta", (H, BK)),
        ("aWf1", (D, FFN)), ("abf1", (FFN,)), ("aWf2", (FFN, D)), ("abf2", (D,)),
        ("bWq", (D, D)), ("bbq", (D,)), ("bWk", (D, D)), ("bbk", (D,)),
        ("bWv", (D, D)), ("bbv", (D,)), ("bWo", (D, D)), ("bbo", (D,)),
        ("fw1", (FFN, D)), ("fw2", (D, FFN)),
    ]:
        din(nm, shp)

    xout = nc.dram_tensor("out", [B, T, D], F32, kind="ExternalOutput").ap()

    taps = {}

    def scratch(name, shape, dtype):
        kind = "ExternalOutput" if name in debug_taps else "Internal"
        ap = nc.dram_tensor("tap_" + name if kind == "ExternalOutput" else name,
                            list(shape), dtype, kind=kind).ap()
        if kind == "ExternalOutput":
            taps[name] = ap
        return ap

    x1s = scratch("x1s", (B, T, D), F32)
    h1S = scratch("h1S", (B, T, D), BF16)
    qrS = scratch("qrS", (B, NT, 128, 512), BF16)
    qiS = scratch("qiS", (B, NT, 128, 512), BF16)
    prS = scratch("prS", (B, NT, 128, 512), BF16)
    piS = scratch("piS", (B, NT, 128, 512), BF16)
    thS = scratch("thS", (B, NT, 128, 512), F32)
    nlS = scratch("nlS", (B, NT, 128, 512), F32)
    vS = scratch("vS", (B, T, D), BF16)
    w1tq = scratch("w1tq", (D, FFN), BF16)
    w2tq = scratch("w2tq", (FFN, D), BF16)
    bWT = {nm: scratch(nm + "T", (D, D), BF16)
           for nm in ("bWq", "bWk", "bWv", "bWo")}
    qbS = scratch("qbS", (B, NT, 128, D), BF16)
    kbS = scratch("kbS", (B, 8, 128, T), BF16)
    vbS = scratch("vbS", (B, T, D), BF16)

    def tap(name, shape, dtype=F32):
        if name in debug_taps:
            taps[name] = nc.dram_tensor("tap_" + name, list(shape), dtype,
                                        kind="ExternalOutput").ap()
            return taps[name]
        return None

    sbufs = {}

    def sbuf(name, shape, dtype=F32):
        h = ctx.enter_context(nc.sbuf_tensor(name, list(shape), dtype))
        sbufs[name] = h
        return h

    ones_row = sbuf("ones_row", (1, 128), BF16)
    iden = sbuf("iden", (128, 128), BF16)
    alphat = sbuf("alphat", (128, 1024), F32)
    mask_diag = sbuf("mask_diag", (128, 512), F32)
    mask_16 = sbuf("mask_16", (128, 512), BF16)
    mask_32 = sbuf("mask_32", (128, 512), BF16)
    mask_64 = sbuf("mask_64", (128, 512), BF16)

    thb_t = sbuf("thb_t", (128, 4), F32)
    lmbn_t = sbuf("lmbn_t", (128, 4), F32)
    etan_t = sbuf("etan_t", (128, 4), F32)
    bq_r = sbuf("bq_r", (128, 4), F32)
    bq_i = sbuf("bq_i", (128, 4), F32)
    bk_r = sbuf("bk_r", (128, 4), F32)
    bk_i = sbuf("bk_i", (128, 4), F32)

    g1t = sbuf("g1t", (128, 1024), BF16)
    b1t = sbuf("b1t", (128, 1024), BF16)
    g2t = sbuf("g2t", (128, 1024), BF16)
    b2t = sbuf("b2t", (128, 1024), BF16)
    rows = {}
    for nm in ["abv", "abo", "abf2", "bbq", "bbk", "bbv", "bbo"]:
        rows[nm] = sbuf("row_" + nm, (1, D), BF16)
    grow = sbuf("grow", (1, 4 * D), BF16)
    bf1c = sbuf("bf1c", (128, 32), F32)
    pihalf = sbuf("pihalf", (128, 1), F32)
    epsc = sbuf("epsc", (128, 2), F32)
    magicc = sbuf("magicc", (128, 1), F32)
    onec = sbuf("onec", (128, 1), F32)
    wsc = sbuf("wsc", (128, 4), F32)
    onef = sbuf("onef", (1, 128), F32)
    bqcol = sbuf("bqcol", (128, 8), F32)
    bkcol = sbuf("bkcol", (128, 8), F32)

    psF = ctx.enter_context(nc.psum_tensor("psF", [128, 6, 512], F32))
    psT = ctx.enter_context(nc.psum_tensor("psT", [128, 512], BF16))
    psX = ctx.enter_context(nc.psum_tensor("psX", [128, 512], F32))

    cstT = nc.dram_tensor("cst", [128, CST_COLS], F32, kind="ExternalInput")
    dram["cst"] = cstT.ap()

    def consts():
        def csl(a, b):
            return APx(cstT, a, [[CST_COLS, 128], [1, b - a]])
        p.op("S", lambda e: e.dma_start(out=alphat[:], in_=csl(128, 1152)),
             dma="dcs2")
        p.op("S", lambda e: e.dma_start(out=mask_diag[:], in_=csl(1152, 1664)),
             dma="dcs2")
        p.op("G", lambda e: e.dma_start(out=iden[:], in_=csl(0, 128)), dma="dcst")
        p.op("G", lambda e: e.dma_start(out=mask_16[:], in_=csl(1664, 2176)),
             dma="dcst")
        p.op("G", lambda e: e.dma_start(out=mask_32[:], in_=csl(2176, 2688)),
             dma="dcst")
        p.op("G", lambda e: e.dma_start(out=mask_64[:], in_=csl(2688, 3200)),
             dma="dcst")
        p.op("V", lambda e: e.memset(ones_row[:], 1.0), mark=True)
        p.op("V", lambda e: e.memset(pihalf[:], PI / 2), mark=True)
        p.op("V", lambda e: e.memset(epsc[:, 0:1], EPS_OUT), mark=True)
        p.op("V", lambda e: e.memset(epsc[:, 1:2], EPS_IN), mark=True)
        p.op("V", lambda e: e.memset(magicc[:], MAGIC), mark=True)
        p.op("V", lambda e: e.memset(onec[:], 1.0), mark=True)
        p.op("V", lambda e: e.memset(onef[:], 1.0), mark=True)

        gdmas = []
        for c in range(4):
            gdmas.append(lambda e, c=c: e.dma_start(
                out=APx(thb_t, c, [[4, 128], [1, 1]]),
                in_=APx(dram["thb"].tensor, 128 * c, [[1, 128], [1, 1]])))
            gdmas.append(lambda e, c=c: e.dma_start(
                out=APx(lmbn_t, c, [[4, 128], [1, 1]]),
                in_=APx(dram["lmb"].tensor, 128 * c, [[1, 128], [1, 1]])))
            gdmas.append(lambda e, c=c: e.dma_start(
                out=APx(etan_t, c, [[4, 128], [1, 1]]),
                in_=APx(dram["eta"].tensor, 128 * c, [[1, 128], [1, 1]])))
            for dst, srcn, par in [(bq_r, "abq", 0), (bq_i, "abq", 1),
                                   (bk_r, "abk", 0), (bk_i, "abk", 1)]:
                gdmas.append(lambda e, c=c, dst=dst, srcn=srcn, par=par: e.dma_start(
                    out=APx(dst, c, [[4, 128], [1, 1]]),
                    in_=APx(dram[srcn].tensor, 256 * c + par,
                            [[64, 4], [2, 32], [1, 1]]),
                    allow_slow_non_contiguous=True))
        for nm in ["abv", "abo", "abf2", "bbq", "bbk", "bbv", "bbo"]:
            gdmas.append(lambda e, nm=nm: e.dma_start(
                out=rows[nm][0:1, :], in_=APx(dram[nm].tensor, 0, [[0, 1], [1, D]])))
        for ic in range(8):
            for dst, srcn in [(bqcol, "bbq"), (bkcol, "bbk")]:
                gdmas.append(lambda e, ic=ic, dst=dst, srcn=srcn: e.dma_start(
                    out=APx(dst, ic, [[8, 128], [1, 1]]),
                    in_=APx(dram[srcn].tensor, ic * 128, [[1, 128], [1, 1]])))
        for i, nm in enumerate(["n1g", "n1b", "n2g", "n2b"]):
            gdmas.append(lambda e, i=i, nm=nm: e.dma_start(
                out=grow[0:1, i * D:(i + 1) * D],
                in_=APx(dram[nm].tensor, 0, [[0, 1], [1, D]])))
        gdmas.append(lambda e: e.dma_start(
            out=APx(bf1c, 0, [[32, 128], [1, 32]]),
            in_=APx(dram["abf1"].tensor, 0, [[1, 128], [128, 32], [1, 1]]),
            allow_slow_non_contiguous=True))
        for fn in gdmas:
            p.op("G", fn, dma="dgth")

        def neg(e):
            e.tensor_scalar_mul(lmbn_t[:], lmbn_t[:], -1.0)
            return e.tensor_scalar_mul(etan_t[:], etan_t[:], -1.0)
        p.op("V", neg, deps=[p.tok("dgth")], mark=True)

        dep = [p.tok("dgth"), p.tok("evV")]
        for i, dst in enumerate([g1t, b1t, g2t, b2t]):
            for half in range(2):
                bk_ = (2 * i + half) % 4
                tm = p.op("T", lambda e, i=i, half=half, bk_=bk_: e.matmul(
                    psF[:, bk_, :], ones_row[0:1, :],
                    grow[0:1, i * D + half * 512: i * D + half * 512 + 512],
                    skip_group_check=True, start=True, stop=True), deps=dep + [p.tok("evV")], mark=True)
                p.op("V", lambda e, dst=dst, half=half, bk_=bk_: e.tensor_copy(
                    dst[:, half * 512:(half + 1) * 512], psF[:, bk_, :]),
                    deps=[tm], mark=True)
        p.barrier()

    consts()

    def make_emit_ln(stat, s0, s1, s2, s3):
        def emit_ln(src_ap, dst_ap, eps_col, gain=None, bias=None, dep=(), bn=0):
            c0, c1, c2, c3, c4, c5 = (bn + j for j in range(6))
            t1 = p.op("V", lambda e: e.reduce_sum(stat[:, c0:c0 + 1], src_ap, axis=AX.X),
                      deps=dep, inc="lns")
            t2 = p.op("A", lambda e: e.activation(s0[:], src_ap, AF.Square),
                      deps=dep, inc="lns")
            t3 = p.op("V", lambda e: e.reduce_sum(stat[:, c1:c1 + 1], s0[:], axis=AX.X),
                      deps=[t2], inc="lns")
            t4a = p.op("A", lambda e: e.activation(stat[:, c2:c2 + 1], stat[:, c0:c0 + 1],
                                                   AF.Square, scale=1.0 / 1024.0),
                       deps=[t1], inc="lns")
            t4b = p.op("A", lambda e: e.activation(stat[:, c3:c3 + 1], stat[:, c0:c0 + 1],
                                                   AF.Copy, scale=1.0 / 1024.0),
                       deps=[t1], inc="lns")
            t4d = p.op("A", lambda e: e.activation(stat[:, c4:c4 + 1], stat[:, c1:c1 + 1],
                                                   AF.Copy, scale=1.0 / 1024.0),
                       deps=[t3], inc="lns")
            t4c = p.op("G", lambda e: e.tensor_tensor(
                stat[:, c1:c1 + 1], stat[:, c4:c4 + 1], stat[:, c2:c2 + 1],
                ALU.subtract), deps=[t4a, t4d], inc="lns")
            t5a = p.op("A", lambda e: e.activation(stat[:, c2:c2 + 1], stat[:, c1:c1 + 1],
                                                   AF.Sqrt, bias=epsc[:, eps_col:eps_col + 1]),
                       deps=[t4c], inc="lns")
            t5b = p.op("V", lambda e: e.reciprocal(stat[:, c5:c5 + 1], stat[:, c2:c2 + 1]),
                       deps=[t5a], inc="lns")
            t5 = p.op("G", lambda e: e.tensor_copy(stat[:, c1:c1 + 1],
                                                   stat[:, c5:c5 + 1]),
                      deps=[t5b], inc="lns")
            if gain is None:
                return p.op("V", lambda e: e.tensor_scalar(
                    dst_ap, src_ap, stat[:, c3:c3 + 1], stat[:, c1:c1 + 1],
                    ALU.subtract, ALU.mult), deps=[t5, t4b], inc="lns")
            p.op("V", lambda e: e.tensor_scalar(
                s1[:, 0:1024], src_ap, stat[:, c3:c3 + 1], stat[:, c1:c1 + 1],
                ALU.subtract, ALU.mult), deps=[t5, t4b], inc="lns")
            p.op("V", lambda e: e.tensor_mul(s1[:, 0:1024], s1[:, 0:1024],
                                             gain[:]), inc="lns")
            return p.op("V", lambda e: e.tensor_add(dst_ap, s1[:, 0:1024],
                                                    bias[:]), inc="lns")
        return emit_ln

    if "masks" in debug_taps:
        mt = nc.dram_tensor("tap_masks", [4, 128, 512], F32, kind="ExternalOutput").ap()
        taps["masks"] = mt
        def dump_masks(e):
            e.dma_start(out=mt[0], in_=mask_diag[:])
            e.dma_start(out=mt[1], in_=mask_16[:])
            e.dma_start(out=mt[2], in_=mask_32[:])
            return e.dma_start(out=mt[3], in_=mask_64[:])
        for i in range(4):
            p.op("G", lambda e, i=i: e.dma_start(
                out=taps["masks"][i],
                in_=[mask_diag, mask_16, mask_32, mask_64][i][:]),
                deps=[p.tok("dcst"), p.tok("dcs2")], dma="dmask")
        p.barrier()

    def phase1a():
        pctx = ExitStack()

        def pbuf(name, shape, dtype=F32):
            return pctx.enter_context(nc.sbuf_tensor(name, list(shape), dtype))

        Wq = pbuf("Wq", (128, 8 * D), BF16)
        Wk = pbuf("Wk", (128, 8 * D), BF16)
        Wv = pbuf("Wv", (128, 8 * D), BF16)
        Tw1 = pbuf("Tw1", (128, 8 * LORA), BF16)
        Tw2 = pbuf("Tw2", (LORA, 512), BF16)

        for W, nm in [(Wq, "aWq"), (Wk, "aWk"), (Wv, "aWv")]:
            for dc in range(8):
                p.op("G", lambda e, W=W, nm=nm, dc=dc: e.dma_start(
                    out=W[:, dc * D:(dc + 1) * D],
                    in_=dram[nm][dc * 128:(dc + 1) * 128, :]), dma="dw1")
        for dc in range(8):
            p.op("G", lambda e, dc=dc: e.dma_start(
                out=Tw1[:, dc * LORA:(dc + 1) * LORA],
                in_=dram["tw1"][dc * 128:(dc + 1) * 128, :]), dma="dw1")
        p.op("G", lambda e: e.dma_start(out=Tw2[:], in_=dram["tw2"][:, :]), dma="dw1")
        wtok = p.tok("dw1")

        qr = pbuf("qr", (128, 2048), BF16)
        qi = pbuf("qi", (128, 2048), BF16)
        pr = pbuf("pr", (128, 2048), BF16)
        pi = pbuf("pi", (128, 2048), BF16)
        th = pbuf("th", (128, 2048), F32)
        nl = pbuf("nl", (128, 2048), F32)
        s0 = pbuf("s0", (128, 1024), F32)
        s1 = pbuf("s1", (128, 1024), F32)
        s2 = pbuf("s2", (128, 2048), BF16)
        s3 = pbuf("s3", (128, 2048), BF16)
        s4 = pbuf("s4", (128, 2048), BF16)
        xt0 = pbuf("xt0", (128, 2 * 1024), F32)
        h1f = pbuf("h1f", (128, 1024), F32)
        xn = pbuf("xn", (128, 1024), BF16)
        xnT = pbuf("xnT", (128, 4 * 1024), BF16)
        stat = pbuf("stat", (128, 12), F32)
        l1T = pbuf("l1T", (LORA, 512), BF16)
        vtmp = pbuf("vtmp", (128, 1024), BF16)

        emit_ln = make_emit_ln(stat, s0, s1, s2, s3)

        for si in range(n_tiles // ST):
            tts = [ST * si + jj for jj in range(ST)]
            xn_toks = {}
            for b in range(B):
                for j, tt in enumerate(tts):
                    slot = b * 2 + j
                    lx = p.op("S", lambda e, b=b, tt=tt: e.dma_start(
                        out=xt0[:, b * 1024:(b + 1) * 1024],
                        in_=xin[b, tt * 128:(tt + 1) * 128, :]),
                        deps=[p.tok("lns"), p.tok("phi")], dma="dx")
                    src = xt0[:, b * 1024:(b + 1) * 1024]
                    t_h1 = emit_ln(src, h1f[:], 0, dep=[lx])
                    p.op("G", lambda e, b=b, tt=tt: e.dma_start(
                        out=h1S[b, tt * 128:(tt + 1) * 128, :], in_=h1f[:]),
                        deps=[t_h1], dma="dh1")
                    t_xn = emit_ln(h1f[:], xn[:], 1, gain=g1t, bias=b1t,
                                   dep=[t_h1, p.tok("evT"), p.tok("dh1")], bn=6)
                    for r in range(2):
                        tms = []
                        for blk in range(4):
                            dc = r * 4 + blk
                            tm = p.op("T", lambda e, dc=dc, blk=blk: e.transpose(
                                psT[:, blk * 128:(blk + 1) * 128],
                                xn[:, dc * 128:(dc + 1) * 128], iden[:]),
                                deps=[t_xn, p.tok("xnt"), p.tok("evV")], mark=True)
                            tms.append(tm)
                        p.op("V", lambda e, slot=slot, r=r: e.tensor_copy(
                            xnT[:, slot * 1024 + r * 512: slot * 1024 + r * 512 + 512],
                            psT[:]), deps=tms, inc="xnt")
                    xn_toks[(b, j)] = p.tok("xnt")

            dep_all = [xn_toks[(b, j)] for b in range(B) for j in range(ST)] + [wtok]

            def mov_xnT(b, dc):
                return APx(xnT, (b * 2) * 1024 + dc * 128,
                           [[4096, 128], [1024, 2], [1, 128]])

            bank = [0]
            st_toks = []
            for b in range(B):
                for (plane, Wsb, boff, brAP) in [
                    (qr, Wq, 0, bq_r), (qi, Wq, 1, bq_i),
                    (pr, Wk, 0, bk_r), (pi, Wk, 1, bk_i),
                ]:
                    for c in range(4):
                        bk_ = bank[0] % 4
                        bank[0] += 1
                        for dc in range(8):
                            sel = APx(Wsb, dc * D + 256 * c + boff,
                                      [[8 * D, 128], [64, 4], [2, 32]])
                            p.op("T", lambda e, sel=sel, b=b, dc=dc, bk_=bk_: e.matmul(
                                psF[:, bk_, 0:256], sel, mov_xnT(b, dc),
                                skip_group_check=True, start=(dc == 0), stop=(dc == 7)),
                                deps=(dep_all + [p.tok("qkev")]) if dc == 0 else (),
                                mark=(dc == 7))
                        tm = p.tok("evT")
                        tk = p.op("V", lambda e, plane=plane, b=b, c=c, bk_=bk_, brAP=brAP:
                                  e.tensor_scalar_add(
                                      plane[:, b * 1024 + c * 256: b * 1024 + c * 256 + 256],
                                      psF[:, bk_, 0:256], brAP[:, c:c + 1]),
                                  deps=[tm], inc="qkev")

            lo_toks = []
            for b in range(B):
                for dc in range(8):
                    p.op("T", lambda e, b=b, dc=dc: e.matmul(
                        psX[0:48, 0:256], Tw1[:, dc * LORA:(dc + 1) * LORA],
                        mov_xnT(b, dc), skip_group_check=True, start=(dc == 0), stop=(dc == 7)),
                        deps=(dep_all + [p.tok("lo")]) if dc == 0 else (),
                        mark=(dc == 7))
                tm = p.tok("evT")
                tk = p.op("A", lambda e, b=b: e.activation(
                    l1T[0:48, b * 256:(b + 1) * 256], psX[0:48, 0:256], AF.Tanh),
                    deps=[tm], inc="lo")
                lo_toks.append(tk)

            th_toks = []
            for b in range(B):
                for c in range(4):
                    sel = APx(Tw2, 128 * c, [[512, LORA], [32, 4], [1, 32]])
                    p.op("T", lambda e, sel=sel, b=b: e.matmul(
                        psF[:, 4, 0:256], sel, l1T[0:48, b * 256:(b + 1) * 256],
                        skip_group_check=True, start=True, stop=True),
                        deps=lo_toks + [p.tok("thev")], mark=True)
                    tm = p.tok("evT")

                    def theta_ops(e, b=b, c=c):
                        off = b * 1024 + c * 256
                        e.tensor_scalar(th[:, off:off + 256], psF[:, 4, 0:256],
                                        thb_t[:, c:c + 1], PI / 2, ALU.add, ALU.min)
                        e.tensor_scalar_max(th[:, off:off + 256], th[:, off:off + 256],
                                            -PI / 2)
                        e.tensor_tensor(s0[:, 0:256], th[:, off:off + 256],
                                        th[:, off:off + 256], ALU.mult)
                        return e.tensor_scalar(nl[:, off:off + 256], s0[:, 0:256],
                                               etan_t[:, c:c + 1], lmbn_t[:, c:c + 1],
                                               ALU.mult, ALU.add)
                    tk = p.op("V", theta_ops, deps=[tm], inc="thev")
                    th_toks.append(tk)

            for b in range(B):
                for j, tt in enumerate(tts):
                    slot = b * 2 + j
                    for half in range(2):
                        for dc in range(8):
                            p.op("T", lambda e, slot=slot, dc=dc, half=half: e.matmul(
                                psF[:, 5, 0:512],
                                xnT[:, slot * 1024 + dc * 128: slot * 1024 + dc * 128 + 128],
                                Wv[:, dc * D + half * 512: dc * D + half * 512 + 512],
                                skip_group_check=True, start=(dc == 0), stop=False),
                                deps=(dep_all + [p.tok("vev")]) if dc == 0 else ())
                        p.op("T", lambda e, half=half: e.matmul(
                            psF[:, 5, 0:512], ones_row[0:1, :],
                            rows["abv"][0:1, half * 512: half * 512 + 512],
                            skip_group_check=True, start=False, stop=True), deps=[p.tok("evV")], mark=True)
                        tme = p.tok("evT")
                        p.op("V", lambda e, half=half: e.tensor_copy(
                            vtmp[:, half * 512: half * 512 + 512],
                            psF[:, 5, 0:512]), deps=[tme, p.tok("dvS")], inc="vev")
                    p.op("S", lambda e, b=b, tt=tt: e.dma_start(
                        out=vS[b, tt * 128:(tt + 1) * 128, :], in_=vtmp[:]),
                        deps=[p.tok("vev")], dma="dvS")

            phk = []
            for pl in (pr, pi):
                for half in range(2):
                    hs = slice(half * 1024, half * 1024 + 1024)
                    t1 = p.op("A", lambda e, pl=pl, hs=hs: e.activation(
                        s2[:, 0:1024], pl[:, hs], AF.Exp), deps=[p.tok("qkev")] + phk,
                        inc="phi")
                    t2 = p.op("V", lambda e, pl=pl, hs=hs: e.scalar_tensor_tensor(
                        s3[:, 0:1024], pl[:, hs], 1.0, s2[:, 0:1024],
                        ALU.add, ALU.subtract), deps=[t1], inc="phi")
                    t3 = p.op("V", lambda e, pl=pl, hs=hs: e.tensor_single_scalar(
                        s4[:, 0:1024], pl[:, hs], 0.0, ALU.is_gt),
                        deps=[p.tok("qkev")], inc="phi")
                    t4 = p.op("V", lambda e: e.tensor_mul(s3[:, 0:1024], s3[:, 0:1024],
                                                          s4[:, 0:1024]),
                              deps=[t2, t3], inc="phi")
                    t5 = p.op("V", lambda e, pl=pl, hs=hs: e.tensor_add(
                        pl[:, hs], s2[:, 0:1024], s3[:, 0:1024]), deps=[t4], inc="phi")
                    phk = [t5]

            def pl_slice(ten, b, j):
                return APx(ten, b * 1024 + j * 128, [[2048, 128], [256, 4], [1, 128]])

            def th_slice(ten, b, j):
                return APx(ten, b * 1024 + j * 128, [[2048, 128], [256, 4], [1, 128]])

            for b in range(B):
                for j, tt in enumerate(tts):
                    p.op("S", lambda e, b=b, j=j, tt=tt: e.dma_start(
                        out=qrS[b, tt], in_=pl_slice(qr, b, j)), deps=[p.tok("qkev")],
                        dma="dpl")
                    p.op("S", lambda e, b=b, j=j, tt=tt: e.dma_start(
                        out=qiS[b, tt], in_=pl_slice(qi, b, j)), dma="dpl")
                    p.op("S", lambda e, b=b, j=j, tt=tt: e.dma_start(
                        out=prS[b, tt], in_=pl_slice(pr, b, j)), deps=phk, dma="dpl")
                    p.op("S", lambda e, b=b, j=j, tt=tt: e.dma_start(
                        out=piS[b, tt], in_=pl_slice(pi, b, j)), dma="dpl")
                    p.op("S", lambda e, b=b, j=j, tt=tt: e.dma_start(
                        out=thS[b, tt], in_=th_slice(th, b, j)),
                        deps=[p.tok("thev")], dma="dpl")
                    p.op("S", lambda e, b=b, j=j, tt=tt: e.dma_start(
                        out=nlS[b, tt], in_=th_slice(nl, b, j)), dma="dpl")
            p.op("V", lambda e: e.engine_nop(), deps=[p.tok("dpl")], inc="qkev")
            p.op("V", lambda e: e.engine_nop(), deps=[p.tok("dpl")], inc="thev")
            p.op("V", lambda e: e.engine_nop(), deps=[p.tok("dpl")], inc="phi")

        pctx.close()

    phase1a()
    p.barrier()
    if max_phase < 2:
        p.replay()
        ctx.close()
        return nc, taps

    def phase1b():
        pctx = ExitStack()

        def pbuf(name, shape, dtype=F32):
            return pctx.enter_context(nc.sbuf_tensor("b_" + name, list(shape), dtype))

        Wo = pbuf("Wo", (128, 8 * D), BF16)
        for dc in range(8):
            p.op("G", lambda e, dc=dc: e.dma_start(
                out=Wo[:, dc * D:(dc + 1) * D],
                in_=dram["aWo"][dc * 128:(dc + 1) * 128, :]), dma="dwo")
        wtok = p.tok("dwo")

        qr = pbuf("qr", (128, 1024), BF16)
        qi = pbuf("qi", (128, 1024), BF16)
        pr = pbuf("pr", (128, 1024), BF16)
        pi = pbuf("pi", (128, 1024), BF16)
        th = pbuf("th", (128, 1024), F32)
        nl = pbuf("nl", (128, 1024), F32)
        mre = pbuf("mre", (128, 8 * 129), F32)
        mim = pbuf("mim", (128, 8 * 129), F32)
        s0 = pbuf("s0", (128, 1024), F32)
        s1 = pbuf("s1", (128, 1024), F32)
        s2 = pbuf("s2", (128, 1024), BF16)
        s3 = pbuf("s3", (128, 1024), BF16)
        s4 = pbuf("s4", (128, 1024), BF16)
        f16r = pbuf("f16r", (128, 1024), BF16)
        f16i = pbuf("f16i", (128, 1024), BF16)
        fdgc = pbuf("fdgc", (128, 1024), BF16)
        fdgs = pbuf("fdgs", (128, 1024), BF16)
        fqgc = pbuf("fqgc", (128, 1024), BF16)
        fqgs = pbuf("fqgs", (128, 1024), BF16)
        gsm = pbuf("gsm", (128, 30 * 64), F32)
        gsmb = pbuf("gsmb", (128, 32 * 64), BF16)
        LV = ("dg", "l16", "l32", "l64", "pv")
        Aq = {lv: pbuf("Aq_" + lv, (128, 1024), BF16) for lv in LV[:4]}
        Bq = {lv: pbuf("Bq_" + lv, (128, 1024), BF16) for lv in LV[:4]}
        Aqs = pbuf("Aq_st", (128, 1024), BF16)
        Bqs = pbuf("Bq_st", (128, 1024), BF16)
        Ck = {lv: pbuf("Ck_" + lv, (128, 1024), BF16) for lv in LV[:4]}
        Dk = {lv: pbuf("Dk_" + lv, (128, 1024), BF16) for lv in LV[:4]}
        vrg = pbuf("vrg", (128, 4 * 1024), BF16)
        Cst = pbuf("Cst", (128, 4 * 512), BF16)
        Dst = pbuf("Dst", (128, 4 * 512), BF16)
        h1l = pbuf("h1l", (128, 2 * 1024), BF16)
        xt1 = pbuf("xt1", (128, 1024), F32)
        Psb = {lv: pbuf("Psb" + lv, (128, 512), BF16) for lv in LV}
        yT = pbuf("yT", (128, 1024), BF16)
        x1f = pbuf("x1f", (128, 1024), F32)

        tap_y = tap("y", (B, NT, 128, D), BF16)
        tap_x1 = tap("x1", (B, NT, 128, D), F32)

        for ti in range(n_tiles):
            sl = ti % 2
            for b in range(B):
                l_fns = [
                    lambda e, b=b, ti=ti: e.dma_start(out=qr[:, b * 512:(b + 1) * 512],
                                                      in_=qrS[b, ti]),
                    lambda e, b=b, ti=ti: e.dma_start(out=qi[:, b * 512:(b + 1) * 512],
                                                      in_=qiS[b, ti]),
                    lambda e, b=b, ti=ti: e.dma_start(out=pr[:, b * 512:(b + 1) * 512],
                                                      in_=prS[b, ti]),
                    lambda e, b=b, ti=ti: e.dma_start(out=pi[:, b * 512:(b + 1) * 512],
                                                      in_=piS[b, ti]),
                    lambda e, b=b, ti=ti: e.dma_start(out=th[:, b * 512:(b + 1) * 512],
                                                      in_=thS[b, ti]),
                    lambda e, b=b, ti=ti: e.dma_start(out=nl[:, b * 512:(b + 1) * 512],
                                                      in_=nlS[b, ti]),
                    lambda e, b=b, ti=ti: e.dma_start(
                        out=h1l[:, b * 1024:(b + 1) * 1024],
                        in_=h1S[b, ti * 128:(ti + 1) * 128, :]),
                    lambda e, b=b, ti=ti, sl=sl: e.dma_start(
                        out=vrg[:, (b * 2 + sl) * 1024:(b * 2 + sl) * 1024 + 1024],
                        in_=vS[b, ti * 128:(ti + 1) * 128, :]),
                ]
                for i2, fn in enumerate(l_fns):
                    p.op("S", fn, deps=[p.tok("cmb"), p.tok("ymm"), p.tok("x1e")]
                         if i2 == 0 else (), dma="dld")
            ldtok = p.tok("dld")

            def scan_all(e):
                e.memset(APx(mre, 0, [[1032, 128], [129, 8], [1, 1]]), 0.0)
                e.memset(APx(mim, 0, [[1032, 128], [129, 8], [1, 1]]), 0.0)
                for b in range(B):
                    for c in range(4):
                        src = b * 512 + c * 128
                        dst = (b * 4 + c) * 129 + 1
                        e.tensor_tensor_scan(mre[:, dst:dst + 128],
                                             nl[:, src:src + 128], nl[:, src:src + 128],
                                             0.0, ALU.add, ALU.bypass)
                        inst = e.tensor_tensor_scan(mim[:, dst:dst + 128],
                                                    th[:, src:src + 128],
                                                    th[:, src:src + 128],
                                                    0.0, ALU.add, ALU.bypass)
                return inst
            t_scan = p.op("V", scan_all, deps=[ldtok, p.tok("gf"), p.tok("fb"), p.tok("dxo")], inc="msc")

            def mv4(m, off):
                return APx(m, off, [[1032, 128], [129, 8], [16, 8], [1, 16]])

            def mR(m, off):
                return APx(m, off, [[1032, 128], [129, 8], [16, 8], [0, 16]])

            def s_4(s, off=0):
                return APx(s, off, [[1024, 128], [128, 8], [16, 8], [1, 16]])

            def rangered(dep):
                t1 = p.op("V", lambda e: e.tensor_scalar(
                    s1[:], s0[:], 1.0 / (2 * PI), MAGIC, ALU.mult, ALU.add),
                    deps=dep, inc="rr")
                t2 = p.op("V", lambda e: e.tensor_scalar(
                    s1[:], s1[:], MAGIC, 2 * PI, ALU.subtract, ALU.mult), inc="rr")
                t3 = p.op("V", lambda e: e.tensor_sub(s0[:], s0[:], s1[:]), inc="rr")
                t4 = p.op("A", lambda e: e.activation(s3[:], s0[:], AF.Sin),
                          deps=[t3], inc="rr")
                t5 = p.op("A", lambda e: e.activation(s1[:], s0[:], AF.Abs),
                          deps=[t4], inc="rr")
                t6 = p.op("A", lambda e: e.activation(s4[:], s1[:], AF.Sin,
                                                      scale=-1.0, bias=pihalf[:]),
                          deps=[t5], inc="rr")
                return [t4, t6]

            tq1 = p.op("V", lambda e: e.tensor_tensor(s_4(s0), mv4(mre, 1), mR(mre, 0),
                                                      ALU.subtract), deps=[t_scan], inc="fb")
            tq2 = p.op("A", lambda e: e.activation(s2[:], s0[:], AF.Exp), deps=[tq1],
                       inc="fb")
            tq2b = p.op("V", lambda e: e.tensor_copy(f16r[:], s2[:]), deps=[tq2], inc="fb")
            tqe0 = p.op("V", lambda e: e.tensor_add(s1[:], s0[:], alphat[:]),
                        deps=[tq1], inc="fb")
            tqe1 = p.op("A", lambda e: e.activation(fqgc[:], s1[:], AF.Exp),
                        deps=[tqe0], inc="fb")
            tqd0 = p.op("V", lambda e: e.scalar_tensor_tensor(
                s1[:], s0[:], -1.0, alphat[:], ALU.mult, ALU.subtract),
                deps=[tqe1], inc="fb")
            tqd1 = p.op("A", lambda e: e.activation(s4[:], s1[:], AF.Exp),
                        deps=[tqd0], inc="fb")
            tqd2 = p.op("V", lambda e: e.tensor_copy(fdgc[:], s4[:]), deps=[tqd1], inc="fb")
            tq3b = p.op("V", lambda e: e.tensor_tensor(s_4(s0), mv4(mim, 1), mR(mim, 0),
                                                       ALU.subtract),
                        deps=[t_scan, tqd2], inc="fb")
            rr = rangered([tq3b])
            tqd3 = p.op("V", lambda e: e.tensor_mul(fdgs[:], fdgc[:], s3[:]),
                        deps=rr, inc="fb")
            tqd4 = p.op("V", lambda e: e.tensor_mul(fdgc[:], fdgc[:], s4[:]), inc="fb")
            tqe2 = p.op("V", lambda e: e.tensor_mul(fqgs[:], fqgc[:], s3[:]), inc="fb")
            tqe3 = p.op("V", lambda e: e.tensor_mul(fqgc[:], fqgc[:], s4[:]), inc="fb")
            tq4 = p.op("V", lambda e: e.tensor_mul(f16i[:], f16r[:], s3[:]),
                       deps=[tq2b] + rr, inc="fb")
            tq5 = p.op("V", lambda e: e.tensor_mul(f16r[:], f16r[:], s4[:]), inc="fb")
            ta1 = p.op("V", lambda e: e.tensor_mul(s2[:], qr[:], f16r[:]),
                       deps=[tq4, tq5, p.tok("evT")], inc="cmb")
            ta2 = p.op("V", lambda e: e.tensor_mul(s3[:], qi[:], f16i[:]), inc="cmb")
            ta3 = p.op("V", lambda e: e.tensor_add(Aq["l16"][:], s2[:], s3[:]), inc="cmb")
            tb1 = p.op("V", lambda e: e.tensor_mul(s2[:], qi[:], f16r[:]), inc="cmb")
            tb2 = p.op("V", lambda e: e.tensor_mul(s3[:], qr[:], f16i[:]), inc="cmb")
            tb3 = p.op("V", lambda e: e.tensor_sub(Bq["l16"][:], s2[:], s3[:]), inc="cmb")
            td1a = p.op("V", lambda e: e.tensor_mul(s2[:], qr[:], fqgc[:]),
                        deps=[tb3], inc="cmb")
            td1b = p.op("V", lambda e: e.tensor_mul(s3[:], qi[:], fqgs[:]), inc="cmb")
            td1 = p.op("V", lambda e: e.tensor_add(Aq["dg"][:], s2[:], s3[:]), inc="cmb")
            td2a = p.op("V", lambda e: e.tensor_mul(s2[:], qi[:], fqgc[:]), inc="cmb")
            td2b = p.op("V", lambda e: e.tensor_mul(s3[:], qr[:], fqgs[:]), inc="cmb")
            td2 = p.op("V", lambda e: e.tensor_sub(Bq["dg"][:], s2[:], s3[:]), inc="cmb")

            tk1 = p.op("V", lambda e: e.tensor_tensor(s_4(s0), mR(mre, 16), mv4(mre, 1),
                                                      ALU.subtract),
                       deps=[t_scan, td1, td2, tb3], inc="fb")
            tk2 = p.op("A", lambda e: e.activation(s2[:], s0[:], AF.Exp), deps=[tk1],
                       inc="fb")
            tk2b = p.op("V", lambda e: e.tensor_copy(f16r[:], s2[:]), deps=[tk2], inc="fb")
            tk3 = p.op("V", lambda e: e.tensor_tensor(s_4(s0), mR(mim, 16), mv4(mim, 1),
                                                      ALU.subtract), deps=[tk2b], inc="fb")
            rrk = rangered([tk3])
            tk4 = p.op("V", lambda e: e.tensor_mul(f16i[:], f16r[:], s3[:]),
                       deps=[tk2b] + rrk, inc="fb")
            tk5 = p.op("V", lambda e: e.tensor_mul(f16r[:], f16r[:], s4[:]), inc="fb")
            tc1 = p.op("V", lambda e: e.tensor_mul(s2[:], pr[:], f16r[:]),
                       deps=[tk4, tk5, p.tok("evT")], inc="cmb")
            tc2 = p.op("V", lambda e: e.tensor_mul(s3[:], pi[:], f16i[:]), inc="cmb")
            tc3 = p.op("V", lambda e: e.tensor_sub(Ck["l16"][:], s2[:], s3[:]), inc="cmb")
            te1 = p.op("V", lambda e: e.tensor_mul(s2[:], pr[:], f16i[:]), inc="cmb")
            te2 = p.op("V", lambda e: e.tensor_mul(s3[:], pi[:], f16r[:]), inc="cmb")
            te3 = p.op("V", lambda e: e.tensor_add(Dk["l16"][:], s2[:], s3[:]), inc="cmb")
            tf0a = p.op("V", lambda e: e.tensor_mul(s2[:], pr[:], fdgc[:]),
                        deps=[te3, tqd3, tqd4], inc="cmb")
            tf0b = p.op("V", lambda e: e.tensor_mul(s3[:], pi[:], fdgs[:]), inc="cmb")
            tf1 = p.op("V", lambda e: e.tensor_add(Ck["dg"][:], s2[:], s3[:]), inc="cmb")
            tf0c = p.op("V", lambda e: e.tensor_mul(s2[:], pi[:], fdgc[:]), inc="cmb")
            tf0d = p.op("V", lambda e: e.tensor_mul(s3[:], pr[:], fdgs[:]), inc="cmb")
            tf2 = p.op("V", lambda e: e.tensor_sub(Dk["dg"][:], s2[:], s3[:]), inc="cmb")
            base_tok = p.tok("cmb")

            def gsub_ops(kind):
                if kind == "q32":
                    return [(w, [[8, 8], [2, 4]], 16 * w, [[129, 8], [32, 4]],
                             0, [[129, 8], [32, 4]]) for w in range(2)]
                if kind == "k32":
                    return [(w, [[8, 8], [2, 4]], 32, [[129, 8], [32, 4]],
                             16 + 16 * w, [[129, 8], [32, 4]]) for w in range(2)]
                if kind == "q64":
                    return [(2 * v, [[8, 8], [4, 2], [1, 2]],
                             32 * v, [[129, 8], [64, 2], [0, 2]],
                             0, [[129, 8], [64, 2], [0, 2]]) for v in range(2)]
                if kind == "k64":
                    return [(2 * v, [[8, 8], [4, 2], [1, 2]],
                             64, [[129, 8], [64, 2], [0, 2]],
                             32 * v + 32, [[129, 8], [64, 2], [0, 2]]) for v in range(2)]
                if kind == "qst":
                    return [(0, [[8, 8], [4, 2], [1, 4]],
                             0, [[129, 8], [64, 2], [0, 4]],
                             0, [[129, 8], [0, 2], [0, 4]])]
                if kind == "kst":
                    return [(0, [[8, 8], [4, 2], [1, 4]],
                             128, [[129, 8], [0, 2], [0, 4]],
                             64, [[129, 8], [64, 2], [0, 4]])]
                raise ValueError(kind)

            KINDS = [("q32", 0, 1), ("q64", 2, 3), ("qst", 4, 5),
                     ("k32", 6, 7), ("k64", 8, 9), ("kst", 10, 11)]
            NKI = len(KINDS)

            def gsl(ki, j):
                o = (ki * 5 + j) * 64
                return gsm[:, o:o + 64]

            def mk_subs(kind, ki, m):
                def do(e, kind=kind, ki=ki, m=m):
                    inst = None
                    for (doff, dpat, hoff, hpat, loff, lpat) in gsub_ops(kind):
                        dst = APx(gsm, ki * 320 + doff, [[1920, 128]] + dpat)
                        hi = APx(m, hoff, [[1032, 128]] + hpat)
                        lo = APx(m, loff, [[1032, 128]] + lpat)
                        inst = e.tensor_tensor(dst, hi, lo, ALU.subtract)
                    return inst
                return do

            smb = lambda i: gsmb[:, i * 64:(i + 1) * 64]
            u1t, u2t, u3t, u4t, u5t, u6t, u7t, u8t, u10t = ({} for _ in range(9))
            for ki, (kind, slr, sli) in enumerate(KINDS):
                u1t[ki] = p.op("V", mk_subs(kind, ki, mre),
                               deps=([t_scan, p.tok("gf")] if ki == 0 else ()),
                               inc="gf")
            for ki in range(NKI):
                u2t[ki] = p.op("A", lambda e, ki=ki: e.activation(
                    gsl(ki, 1), gsl(ki, 0), AF.Exp), deps=[u1t[ki]], inc="gf")
            for ki, (kind, slr, sli) in enumerate(KINDS):
                u3t[ki] = p.op("V", mk_subs(kind, ki, mim), deps=[u2t[ki]], inc="gf")
            for ki in range(NKI):
                u4t[ki] = p.op("A", lambda e, ki=ki: e.activation(
                    gsl(ki, 2), gsl(ki, 0), AF.Copy, scale=1.0 / (2 * PI),
                    bias=MAGIC), deps=[u3t[ki]], inc="gf")
            for ki in range(NKI):
                u5t[ki] = p.op("A", lambda e, ki=ki: e.activation(
                    gsl(ki, 3), gsl(ki, 0), AF.Copy, scale=1.0 / (2 * PI)),
                    deps=[u3t[ki]], inc="gf")
            for ki in range(NKI):
                u6t[ki] = p.op("V", lambda e, ki=ki: e.scalar_tensor_tensor(
                    gsl(ki, 4), gsl(ki, 2), MAGIC, gsl(ki, 3),
                    ALU.subtract, ALU.subtract), deps=[u4t[ki], u5t[ki]], inc="gf")
            for ki, (kind, slr, sli) in enumerate(KINDS):
                u7t[ki] = p.op("A", lambda e, ki=ki, sli=sli: e.activation(
                    smb(sli), gsl(ki, 4), AF.Sin, scale=-2 * PI),
                    deps=[u6t[ki]] + ([p.tok("ch")] if ki == 0 else []), inc="gf")
            for ki in range(NKI):
                u8t[ki] = p.op("A", lambda e, ki=ki: e.activation(
                    gsl(ki, 2), gsl(ki, 4), AF.Abs), deps=[u6t[ki]], inc="gf")
            for ki, (kind, slr, sli) in enumerate(KINDS):
                u10t[ki] = p.op("A", lambda e, ki=ki, slr=slr: e.activation(
                    smb(slr), gsl(ki, 2), AF.Sin, scale=-2 * PI,
                    bias=pihalf[:]), deps=[u8t[ki]], inc="gf")
            gq_out = {}
            for ki, (kind, slr, sli) in enumerate(KINDS):
                ta = p.op("V", lambda e, ki=ki, sli=sli: e.tensor_mul(
                    smb(sli), smb(sli), gsl(ki, 1)),
                    deps=[u2t[ki], u7t[ki]], inc="gf")
                tb = p.op("V", lambda e, ki=ki, slr=slr: e.tensor_mul(
                    smb(slr), smb(slr), gsl(ki, 1)),
                    deps=[u10t[ki]], inc="gf")
                gq_out[kind] = [ta, tb]
            gq32, gq64, gqst = gq_out["q32"], gq_out["q64"], gq_out["qst"]
            gk32, gk64, gkst = gq_out["k32"], gq_out["k64"], gq_out["kst"]

            def gbb(slot):
                return APx(gsmb, slot * 64, [[2048, 128], [8, 8], [1, 8], [0, 16]])

            def chain_q(Asrc, Bsrc, Adst, Bdst, slr, sli, dep):
                t1 = p.op("V", lambda e: e.tensor_mul(s2[:], Asrc[:], gbb(slr)),
                          deps=list(dep) + [p.tok("evT")], inc="ch")
                t2 = p.op("V", lambda e: e.tensor_mul(s3[:], Bsrc[:], gbb(sli)), inc="ch")
                t3 = p.op("V", lambda e: e.tensor_add(Adst[:], s2[:], s3[:]), inc="ch")
                t4 = p.op("V", lambda e: e.tensor_mul(s2[:], Bsrc[:], gbb(slr)), inc="ch")
                t5 = p.op("V", lambda e: e.tensor_mul(s3[:], Asrc[:], gbb(sli)), inc="ch")
                t6 = p.op("V", lambda e: e.tensor_sub(Bdst[:], s2[:], s3[:]), inc="ch")
                return [t3, t6]

            def chain_k(Csrc, Dsrc, CdstAP, DdstAP, slr, sli, dep):
                t1 = p.op("V", lambda e: e.tensor_mul(s2[:], Csrc[:], gbb(slr)),
                          deps=list(dep) + [p.tok("evT")], inc="ch")
                t2 = p.op("V", lambda e: e.tensor_mul(s3[:], Dsrc[:], gbb(sli)), inc="ch")
                t3 = p.op("V", lambda e: e.tensor_tensor(CdstAP, s2[:], s3[:],
                                                         ALU.subtract), inc="ch")
                t4 = p.op("V", lambda e: e.tensor_mul(s2[:], Dsrc[:], gbb(slr)), inc="ch")
                t5 = p.op("V", lambda e: e.tensor_mul(s3[:], Csrc[:], gbb(sli)), inc="ch")
                t6 = p.op("V", lambda e: e.tensor_tensor(DdstAP, s2[:], s3[:], ALU.add),
                          inc="ch")
                return [t3, t6]

            cq32 = chain_q(Aq["l16"], Bq["l16"], Aq["l32"], Bq["l32"], 0, 1,
                           gq32 + [base_tok])
            cq64 = chain_q(Aq["l32"], Bq["l32"], Aq["l64"], Bq["l64"], 2, 3, gq64 + cq32)
            cqst = chain_q(Aq["l64"], Bq["l64"], Aqs, Bqs, 4, 5, gqst + cq64)
            ck32 = chain_k(Ck["l16"], Dk["l16"], Ck["l32"][:], Dk["l32"][:], 6, 7,
                           gk32 + [base_tok])
            ck64 = chain_k(Ck["l32"], Dk["l32"], Ck["l64"][:], Dk["l64"][:], 8, 9,
                           gk64 + ck32)
            CstV = APx(Cst, sl * 512, [[2048, 128], [1024, 2], [128, 4], [1, 128]])
            DstV = APx(Dst, sl * 512, [[2048, 128], [1024, 2], [128, 4], [1, 128]])
            ckst = chain_k(Ck["l64"], Dk["l64"], CstV, DstV, 10, 11,
                           gkst + ck64 + [p.tok("evT")])

            masks = {"dg": mask_diag, "l16": mask_16, "l32": mask_32, "l64": mask_64}
            lv_ready = {"dg": [td1, td2, tf1, tf2], "l16": [base_tok],
                        "l32": cq32 + ck32, "l64": cq64 + ck64,
                        "pv": cqst + ckst}
            gctr = [0]
            for b in range(B):
                for c in range(4):
                    ev_toks = {}
                    for lv in LV:
                        if lv == "pv" and ti == 0:
                            continue
                        g = gctr[0]
                        gctr[0] += 1
                        colg = (g % 4) * 128
                        foff = b * 512 + c * 128
                        if lv == "pv":
                            soff = (b * 2 + ((ti - 1) % 2)) * 512 + c * 128
                            Ssrc, Msrc = [Cst, Dst], [Aqs, Bqs]
                        else:
                            soff = foff
                            Ssrc, Msrc = [Ck[lv], Dk[lv]], [Aq[lv], Bq[lv]]
                        first = True
                        for pi_ in range(2):
                            for hc in range(4):
                                Ssl = Ssrc[pi_][32 * hc:32 * hc + 32, soff:soff + 128]
                                Msl = Msrc[pi_][32 * hc:32 * hc + 32, foff:foff + 128]
                                dps = [d for d in (lv_ready[lv] + [p.tok("pev"),
                                                                   p.tok("x1e")])
                                       if d is not None] if first and hc == 0 else ()
                                p.op("T", lambda e, Ssl=Ssl, Msl=Msl, hc=hc, colg=colg,
                                     first=first: e.matmul(
                                         psF[:, hc, colg:colg + 128], Ssl, Msl,
                                         skip_group_check=True, start=first, stop=(not first),
                                         tile_position=(32 * hc, 0)),
                                     deps=dps, mark=(pi_ == 1 and hc == 3))
                            first = False
                        tmm = p.tok("evT")
                        if lv == "pv":
                            def evac_pv(e, colg=colg):
                                for hc2 in range(4):
                                    inst = e.activation(
                                        Psb["pv"][:, hc2 * 128:(hc2 + 1) * 128],
                                        psF[:, hc2, colg:colg + 128], AF.Copy)
                                return inst
                            tk = p.op("A", evac_pv, deps=[tmm, p.tok("ymm")], inc="pev")
                        else:
                            def evac_lv(e, colg=colg, lv=lv):
                                for hc2 in range(4):
                                    inst = e.tensor_mul(
                                        Psb[lv][:, hc2 * 128:(hc2 + 1) * 128],
                                        psF[:, hc2, colg:colg + 128],
                                        masks[lv][:, hc2 * 128:(hc2 + 1) * 128])
                                return inst
                            tk = p.op("V", evac_lv, deps=[tmm, p.tok("ymm")], inc="pev")
                        ev_toks[lv] = tk
                    ydeps = [ev_toks.get(lv) for lv in LV if ev_toks.get(lv)]
                    vsl = b * 2 + sl
                    pvs = b * 2 + ((ti - 1) % 2)
                    for hc in range(4):
                        h = 4 * c + hc
                        yb = 4 + (h // 8)
                        ycol = ((h % 8) // 2) * 128
                        pofs = 64 * (h % 2)
                        lvlist = ["dg", "l16", "l32", "l64"] + (["pv"] if ti > 0 else [])
                        for li, lv in enumerate(lvlist):
                            vs = pvs if lv == "pv" else vsl
                            vst = vrg[:, vs * 1024 + 64 * h: vs * 1024 + 64 * h + 64]
                            mv = Psb[lv][:, hc * 128:hc * 128 + 128]
                            p.op("T", lambda e, vst=vst, mv=mv, yb=yb, ycol=ycol,
                                 li=li, pofs=pofs, last=(li == len(lvlist) - 1):
                                 e.matmul(
                                     psF[pofs:pofs + 64, yb, ycol:ycol + 128],
                                     vst, mv, skip_group_check=True, start=(li == 0), stop=last,
                                     tile_position=(0, pofs)),
                                 deps=(ydeps + [ldtok, p.tok("yev")]) if li == 0 else (),
                                 inc="ymm")
                ycht = [p.tok("ymm"), p.tok("evT")]
                te1 = p.op("V", lambda e: e.tensor_copy(yT[:, 0:512], psF[:, 4, :]),
                           deps=ycht, inc="yev")
                te2 = p.op("V", lambda e: e.tensor_copy(yT[:, 512:1024], psF[:, 5, :]),
                           deps=ycht, inc="yev")
                if tap_y is not None:
                    p.op("S", lambda e, b=b, ti=ti: e.dma_start(
                        out=taps["y"][b, ti], in_=yT[:]), deps=[te1, te2], dma="dtap")
                for half in range(2):
                    for dc in range(8):
                        p.op("T", lambda e, dc=dc, half=half: e.matmul(
                            psF[:, half, 0:512],
                            yT[:, dc * 128:(dc + 1) * 128],
                            Wo[:, dc * D + half * 512: dc * D + half * 512 + 512],
                            skip_group_check=True, start=(dc == 0), stop=False),
                            deps=[te1, te2, p.tok("x1e"), wtok] if dc == 0 else ())
                    p.op("T", lambda e, half=half: e.matmul(
                        psF[:, half, 0:512], ones_row[0:1, :],
                        rows["abo"][0:1, half * 512: half * 512 + 512],
                        skip_group_check=True, start=False, stop=True), mark=True)
                two = p.tok("evT")
                tx1 = p.op("V", lambda e, b=b: e.tensor_copy(
                    s1[:], h1l[:, b * 1024:(b + 1) * 1024]),
                    deps=[two, p.tok("dxs"), p.tok("dtap")], inc="x1e")
                tx2 = p.op("V", lambda e: e.tensor_tensor(
                    x1f[:, 0:512], psF[:, 0, 0:512], s1[:, 0:512], ALU.add),
                    deps=[tx1], inc="x1e")
                tx3 = p.op("V", lambda e: e.tensor_tensor(
                    x1f[:, 512:1024], psF[:, 1, 0:512], s1[:, 512:1024], ALU.add),
                    inc="x1e")
                if tap_x1 is not None:
                    p.op("S", lambda e, b=b, ti=ti: e.dma_start(
                        out=taps["x1"][b, ti], in_=x1f[:]), deps=[tx2, tx3], dma="dtap")
                p.op("G", lambda e, b=b, ti=ti: e.dma_start(
                    out=x1s[b, ti * 128:(ti + 1) * 128, :], in_=x1f[:]),
                    deps=[tx2, tx3], dma="dxs")
                lx2 = p.op("S", lambda e, b=b, ti=ti: e.dma_start(
                    out=xt1[:], in_=xin[b, ti * 128:(ti + 1) * 128, :]),
                    deps=[tx2, tx3, p.tok("x1e")], dma="dx2")
                tx4 = p.op("V", lambda e: e.tensor_add(s0[:], x1f[:], xt1[:]),
                           deps=[lx2], inc="x1e")
                p.op("S", lambda e, b=b, ti=ti: e.dma_start(
                    out=xout[b, ti * 128:(ti + 1) * 128, :], in_=s0[:]),
                    deps=[tx4], dma="dxo")

        pctx.close()

    phase1b()
    p.barrier()

    def phase2():
        pctx = ExitStack()

        def pbuf(name, shape, dtype=F32):
            return pctx.enter_context(nc.sbuf_tensor("c_" + name, list(shape), dtype))

        Wf1 = pbuf("Wf1", (128, 8 * FFN), BF16)
        Wf2 = pbuf("Wf2", (128, 32 * D), BF16)
        for jc in range(8):
            p.op("G", lambda e, jc=jc: e.dma_start(
                out=Wf1[:, jc * FFN:(jc + 1) * FFN],
                in_=dram["aWf1"][jc * 128:(jc + 1) * 128, :]), dma="dwf")
        for fc in range(32):
            p.op("G", lambda e, fc=fc: e.dma_start(
                out=Wf2[:, fc * D:(fc + 1) * D],
                in_=dram["aWf2"][fc * 128:(fc + 1) * 128, :]), dma="dwf")
        wtok2 = p.tok("dwf")

        x1l = pbuf("x1l", (128, 1024), F32)
        xol = pbuf("xol", (128, 2 * 1024), F32)
        h2 = pbuf("h2", (128, 1024), BF16)
        h2T = pbuf("h2T", (128, 1024), BF16)
        hg = pbuf("hg", (128, FFN), BF16)
        xs1f = pbuf("xs1f", (128, 1024), F32)
        stat = pbuf("stat", (128, 24), F32)
        s0 = pbuf("s0", (128, 1024), F32)
        s1 = pbuf("s1", (128, 1024), F32)
        eln = make_emit_ln(stat, s0, s1, s1, s1)
        tap_xs1 = tap("xs1", (B, NT, 128, D), F32)
        evq = []
        seq = [(ti, b) for ti in range(n_tiles) for b in range(B)]
        ctxs = {}

        def frontV(i):
            ti, b = seq[i]
            pr = (i % 2) * 1024
            lx = p.op("S", lambda e, b=b, ti=ti: e.dma_start(
                out=x1l[:],
                in_=x1s[b, ti * 128:(ti + 1) * 128, :]),
                deps=[p.tok("lns")], dma="dl2")
            lxo = p.op("S", lambda e, b=b, ti=ti, pr=pr: e.dma_start(
                out=xol[:, pr:pr + 1024],
                in_=xout[b, ti * 128:(ti + 1) * 128, :]),
                deps=[p.tok("x2e")], dma="dl2")
            t_h2 = eln(x1l[:], h2[:], 1, gain=g2t, bias=b2t,
                       dep=[lx] + ctxs.get(i - 1, {}).get("tms", []),
                       bn=12 * (i % 2))
            ctxs[i] = {"t_h2": t_h2, "lxo": lxo, "pr": pr}

        def frontT(i):
            c = ctxs[i]
            tms, ttoks = [], []
            for r in range(2):
                tms_r = []
                for blk in range(4):
                    dc = r * 4 + blk
                    tm = p.op("T", lambda e, dc=dc, blk=blk: e.transpose(
                        psT[:, blk * 128:(blk + 1) * 128],
                        h2[:, dc * 128:(dc + 1) * 128], iden[:]),
                        deps=[c["t_h2"], p.tok("xnt")], mark=True)
                    tms_r.append(tm)
                tms += tms_r
                ttoks.append(p.op("V", lambda e, r=r: e.tensor_copy(
                    h2T[:, r * 512:(r + 1) * 512], psT[:]),
                    deps=tms_r, inc="xnt"))
            c["tms"] = tms
            c["ttoks"] = ttoks

        def back(i):
            ti, b = seq[i]
            c = ctxs[i]
            pr = c["pr"]
            f1toks = {}
            for fb in range(32):
                bank, col = fb % 4, ((fb // 4) % 4) * 128
                for jc in range(8):
                    dps = []
                    if jc == 0:
                        if fb == 0:
                            dps += c["ttoks"] + [wtok2]
                        if len(evq) >= 4:
                            dps.append(evq[-4])
                    p.op("T", lambda e, fb=fb, jc=jc, bank=bank, col=col: e.matmul(
                        psF[:, bank, col:col + 128],
                        Wf1[:, jc * FFN + fb * 128: jc * FFN + fb * 128 + 128],
                        h2T[:, jc * 128:(jc + 1) * 128],
                        skip_group_check=True, start=(jc == 0), stop=(jc == 7)),
                        deps=dps, mark=(jc == 7))
                tmm = p.tok("evT")
                f1toks[fb] = p.op("A", lambda e, fb=fb, bank=bank, col=col:
                                  e.activation(
                                      hg[:, fb * 128:(fb + 1) * 128],
                                      psF[:, bank, col:col + 128],
                                      AF.Gelu_apprx_tanh,
                                      bias=bf1c[:, fb:fb + 1]),
                                  deps=[tmm], inc="f1e")
                evq.append(f1toks[fb])
            for half in range(2):
                for fc in range(32):
                    dps = ([f1toks[fb] for fb in range(32)] + [p.tok("x2e")]) \
                        if fc == 0 else ()
                    p.op("T", lambda e, half=half, fc=fc: e.matmul(
                        psF[:, 4 + half, 0:512],
                        hg[:, fc * 128:(fc + 1) * 128],
                        Wf2[:, fc * D + half * 512: fc * D + half * 512 + 512],
                        skip_group_check=True, start=(fc == 0), stop=False),
                        deps=dps)
                p.op("T", lambda e, half=half: e.matmul(
                    psF[:, 4 + half, 0:512], ones_row[0:1, :],
                    rows["abf2"][0:1, half * 512: half * 512 + 512],
                    skip_group_check=True, start=False, stop=True), mark=True)
            t2w = p.tok("evT")
            tx1 = p.op("V", lambda e, pr=pr: e.tensor_tensor(
                xs1f[:, 0:512], psF[:, 4, 0:512], xol[:, pr:pr + 512],
                ALU.add),
                deps=[t2w, c["lxo"], p.tok("dso"), p.tok("dtap")], inc="x2e")
            tx2 = p.op("V", lambda e, pr=pr: e.tensor_tensor(
                xs1f[:, 512:1024], psF[:, 5, 0:512],
                xol[:, pr + 512:pr + 1024], ALU.add), inc="x2e")
            if tap_xs1 is not None:
                p.op("S", lambda e, b=b, ti=ti: e.dma_start(
                    out=taps["xs1"][b, ti], in_=xs1f[:]),
                    deps=[tx1, tx2], dma="dtap")
            p.op("S", lambda e, b=b, ti=ti: e.dma_start(
                out=xout[b, ti * 128:(ti + 1) * 128, :], in_=xs1f[:]),
                deps=[tx1, tx2, p.tok("dtap")], dma="dso")

        frontV(0)
        frontT(0)
        for i in range(len(seq)):
            if i + 1 < len(seq):
                frontV(i + 1)
            back(i)
            if i + 1 < len(seq):
                frontT(i + 1)
        pctx.close()

    if max_phase >= 2:
        phase2()
        p.barrier()

    def phase0():
        pctx = ExitStack()

        def pbuf(name, shape, dtype=F32):
            return pctx.enter_context(nc.sbuf_tensor("d_" + name, list(shape), dtype))

        wt = pbuf("wt", (128, 2 * FFN), F32)
        wa = pbuf("wa", (128, 2 * FFN), F32)
        acc = pbuf("acc", (128, 40), F32)
        acc2 = pbuf("acc2", (128, 40), F32)
        tn = pbuf("tn", (128, 2 * FFN), BF16)
        tb = pbuf("tb", (128, 2 * FFN), BF16)
        sc2 = pbuf("sc2", (128, 4), F32)
        scB = pbuf("scB", (1, 2), F32)

        def stats(src_name, nrt, ncols, col0):
            atoks, rtoks = [], []
            for rt in range(nrt):
                sl = rt % 2
                la = p.op("S", lambda e, rt=rt, sl=sl, src_name=src_name,
                          ncols=ncols: e.dma_start(
                              out=wt[:, sl * FFN: sl * FFN + ncols],
                              in_=dram[src_name][rt * 128:(rt + 1) * 128, :]),
                          deps=[atoks[rt - 2]] if rt >= 2 else [p.tok("p0a")],
                          dma="dp0l")
                atoks.append(p.op("A", lambda e, ncols=ncols, sl=sl: e.activation(
                    wa[:, sl * FFN: sl * FFN + ncols],
                    wt[:, sl * FFN: sl * FFN + ncols], AF.Abs),
                    deps=[la] + ([rtoks[rt - 2]] if rt >= 2
                                 else [p.tok("p0r")]), inc="p0a"))
                rtoks.append(p.op("V", lambda e, rt=rt, ncols=ncols, col0=col0,
                                  sl=sl:
                                  e.reduce_sum(acc[:, col0 + rt:col0 + rt + 1],
                                               wa[:, sl * FFN: sl * FFN + ncols],
                                               axis=AX.X),
                                  deps=[atoks[rt]], inc="p0r"))

        stats("fw1", 32, D, 0)
        stats("fw2", 8, FFN, 32)

        def scalar_pipeline(col0, n, wcol, pc, si):
            ca = p.op("A", lambda e, col0=col0, n=n: e.activation(
                acc2[:, col0:col0 + n], acc[:, col0:col0 + n], AF.Copy),
                deps=[p.tok("p0r")], inc="p0s")
            cv = p.op("V", lambda e, col0=col0, n=n, si=si: e.reduce_sum(
                sc2[:, si:si + 1], acc2[:, col0:col0 + n], axis=AX.X),
                deps=[ca], inc="p0s")
            tm1 = p.op("T", lambda e, pc=pc, si=si: e.matmul(
                psX[0:1, pc:pc + 1], sc2[:, si:si + 1], onec[:],
                skip_group_check=True, start=True, stop=True),
                deps=[cv, p.tok("p0s")], mark=True)
            cc1 = p.op("V", lambda e, pc=pc, si=si: e.tensor_copy(
                scB[0:1, si // 2:si // 2 + 1], psX[0:1, pc:pc + 1]),
                deps=[tm1], inc="p0s")
            tm2 = p.op("T", lambda e, pc=pc, si=si: e.matmul(
                psX[:, pc + 8:pc + 9], onef[0:1, :],
                scB[0:1, si // 2:si // 2 + 1],
                skip_group_check=True, start=True, stop=True), deps=[cc1], mark=True)
            cc2 = p.op("A", lambda e, pc=pc, si=si: e.activation(
                sc2[:, si:si + 1], psX[:, pc + 8:pc + 9], AF.Copy,
                scale=1.0 / (D * FFN)), deps=[tm2], inc="p0s")
            cm = p.op("V", lambda e, wcol=wcol, si=si: e.tensor_scalar_max(
                wsc[:, wcol:wcol + 1], sc2[:, si:si + 1], 1e-5),
                deps=[cc2], inc="p0s")
            ch = p.op("A", lambda e, wcol=wcol, si=si: e.activation(
                sc2[:, si + 1:si + 2], wsc[:, wcol:wcol + 1], AF.Copy),
                deps=[cm], inc="p0s")
            p.op("V", lambda e, wcol=wcol, si=si: e.reciprocal(
                wsc[:, wcol + 1:wcol + 2], sc2[:, si + 1:si + 2]),
                deps=[ch], inc="p0s")

        scalar_pipeline(0, 32, 0, 0, 0)
        scalar_pipeline(32, 8, 2, 4, 2)
        sc_tok = p.tok("p0s")

        def quant_store(src_name, nrt, ncols, wcol, dstT, vtoks, ttoks, dtoks):
            nblk = ncols // 128
            for rt in range(nrt):
                sl = rt % 2
                la = p.op("S", lambda e, rt=rt, sl=sl, src_name=src_name,
                          ncols=ncols: e.dma_start(
                              out=wt[:, sl * FFN: sl * FFN + ncols],
                              in_=dram[src_name][rt * 128:(rt + 1) * 128, :]),
                          deps=[vtoks[-2]] if len(vtoks) >= 2
                          else [p.tok("p0q")], dma="dp0l")
                v1 = p.op("V", lambda e, sl=sl, ncols=ncols, wcol=wcol:
                          e.tensor_scalar(wa[:, sl * FFN: sl * FFN + ncols],
                                          wt[:, sl * FFN: sl * FFN + ncols],
                                          wsc[:, wcol + 1:wcol + 2], MAGIC,
                                          ALU.mult, ALU.add),
                          deps=[la, sc_tok] + ([ttoks[-2]] if len(ttoks) >= 2
                                               else [p.tok("evT")]), inc="p0q")
                vtoks.append(v1)
                p.op("V", lambda e, sl=sl, ncols=ncols: e.tensor_scalar(
                    wa[:, sl * FFN: sl * FFN + ncols],
                    wa[:, sl * FFN: sl * FFN + ncols], MAGIC, 1.0,
                    ALU.subtract, ALU.min), inc="p0q")
                v3 = p.op("V", lambda e, sl=sl, ncols=ncols: e.tensor_scalar_max(
                    tn[:, sl * FFN: sl * FFN + ncols],
                    wa[:, sl * FFN: sl * FFN + ncols], -1.0), inc="p0q")
                tmlast = None
                for dc in range(nblk):
                    blk = dc % 4
                    tm = p.op("T", lambda e, dc=dc, blk=blk, sl=sl: e.transpose(
                        psT[:, blk * 128:(blk + 1) * 128],
                        tn[:, sl * FFN + dc * 128: sl * FFN + (dc + 1) * 128],
                        iden[:]),
                        deps=[v3, p.tok("xnt")], mark=True)
                    cv = p.op("V", lambda e, dc=dc, blk=blk, sl=sl: e.tensor_copy(
                        tb[:, sl * FFN + dc * 128: sl * FFN + (dc + 1) * 128],
                        psT[:, blk * 128:(blk + 1) * 128]),
                        deps=[tm] + ([dtoks[-2]] if len(dtoks) >= 2 and dc == 0
                                     else []), inc="xnt")
                    dt_ = p.op("G", lambda e, dc=dc, rt=rt, dstT=dstT, sl=sl:
                               e.dma_start(
                                   out=dstT[dc * 128:(dc + 1) * 128,
                                            rt * 128:(rt + 1) * 128],
                                   in_=tb[:, sl * FFN + dc * 128:
                                          sl * FFN + (dc + 1) * 128]),
                               deps=[cv], dma="dp0")
                    tmlast = tm
                ttoks.append(tmlast)
                dtoks.append(dt_)

        vtoks, ttoks, dtoks = [], [], []
        quant_store("fw1", 32, D, 0, w1tq, vtoks, ttoks, dtoks)
        quant_store("fw2", 8, FFN, 2, w2tq, vtoks, ttoks, dtoks)

        for wi, nm in enumerate(("bWq", "bWk", "bWv", "bWo")):
            for rt in range(8):
                gi = wi * 8 + rt
                sl = gi % 2
                lb = p.op("G", lambda e, nm=nm, rt=rt, sl=sl: e.dma_start(
                    out=tn[:, sl * FFN: sl * FFN + 1024],
                    in_=dram[nm][rt * 128:(rt + 1) * 128, :]),
                    deps=[ttoks[-2]] if len(ttoks) >= 2 else [p.tok("evT")],
                    dma="dp0b")
                tmlast = None
                for dc in range(8):
                    blk = dc % 4
                    tm = p.op("T", lambda e, dc=dc, blk=blk, sl=sl: e.transpose(
                        psT[:, blk * 128:(blk + 1) * 128],
                        tn[:, sl * FFN + dc * 128: sl * FFN + (dc + 1) * 128],
                        iden[:]),
                        deps=[lb, p.tok("xnt")], mark=True)
                    cv = p.op("V", lambda e, dc=dc, blk=blk, sl=sl: e.tensor_copy(
                        tb[:, sl * FFN + dc * 128: sl * FFN + (dc + 1) * 128],
                        psT[:, blk * 128:(blk + 1) * 128]),
                        deps=[tm] + ([dtoks[-2]] if len(dtoks) >= 2 and dc == 0
                                     else []), inc="xnt")
                    dt_ = p.op("G", lambda e, dc=dc, rt=rt, nm=nm, sl=sl:
                               e.dma_start(
                                   out=bWT[nm][dc * 128:(dc + 1) * 128,
                                               rt * 128:(rt + 1) * 128],
                                   in_=tb[:, sl * FFN + dc * 128:
                                          sl * FFN + (dc + 1) * 128]),
                               deps=[cv], dma="dp0")
                    tmlast = tm
                ttoks.append(tmlast)
                dtoks.append(dt_)
        pctx.close()

    def phase_bitffn(tapname):
        pctx = ExitStack()

        def pbuf(name, shape, dtype=F32):
            return pctx.enter_context(
                nc.sbuf_tensor(tapname + "_" + name, list(shape), dtype))

        W1 = pbuf("W1", (128, 8 * FFN), BF16)
        W2 = pbuf("W2", (128, 32 * D), BF16)
        for jc in range(8):
            p.op("S", lambda e, jc=jc: e.dma_start(
                out=W1[:, jc * FFN:(jc + 1) * FFN],
                in_=w1tq[jc * 128:(jc + 1) * 128, :]), dma="dwb")
        for fc in range(32):
            p.op("S", lambda e, fc=fc: e.dma_start(
                out=W2[:, fc * D:(fc + 1) * D],
                in_=w2tq[fc * 128:(fc + 1) * 128, :]), dma="dwb")
        wtokb = p.tok("dwb")

        xol = pbuf("xol", (128, 2 * 1024), F32)
        zf = pbuf("zf", (128, 1024), F32)
        zq = pbuf("zq", (128, 1024), BF16)
        zqT = pbuf("zqT", (128, 1024), BF16)
        hg = pbuf("hg", (128, FFN), BF16)
        z2q = pbuf("z2q", (128, 1024), BF16)
        z2T = pbuf("z2T", (128, 1024), BF16)
        stat = pbuf("stat", (128, 64), F32)
        s0 = pbuf("s0", (128, 1024), F32)
        s1 = pbuf("s1", (128, 1024), F32)
        eln = make_emit_ln(stat, s0, s0, s0, s0)
        tap_xs = tap(tapname, (B, NT, 128, D), F32)
        evroll = []
        seq = [(ti, b) for ti in range(n_tiles) for b in range(B)]
        ctxs = {}

        def frontV(i):
            ti, b = seq[i]
            pr = (i % 2) * 1024
            fp = (i % 2) * 20
            lxo = p.op("S", lambda e, b=b, ti=ti, pr=pr: e.dma_start(
                out=xol[:, pr:pr + 1024],
                in_=xout[b, ti * 128:(ti + 1) * 128, :]),
                deps=[p.tok("bfe"), p.tok("lns"), p.tok("dso2")], dma="dl3")
            t_z = eln(xol[:, pr:pr + 1024], zf[:], 0,
                      dep=[lxo, p.tok("bfq")], bn=fp)
            q1 = p.op("A", lambda e: e.activation(
                s0[:], zf[:], AF.Abs), deps=[t_z], inc="bfq")
            q2 = p.op("V", lambda e, fp=fp: e.reduce_max(
                stat[:, fp + 6:fp + 7], s0[:], axis=AX.X), deps=[q1], inc="bfq")
            q3 = p.op("A", lambda e, fp=fp: e.activation(
                stat[:, fp + 7:fp + 8], stat[:, fp + 6:fp + 7], AF.Copy,
                scale=1.0 / 127.0), deps=[q2], inc="bfq")
            q4 = p.op("V", lambda e, fp=fp: e.tensor_scalar_max(
                stat[:, fp + 8:fp + 9], stat[:, fp + 7:fp + 8], 1e-5 / 127.0),
                deps=[q3], inc="bfq")
            q5 = p.op("A", lambda e, fp=fp: e.activation(
                stat[:, fp + 9:fp + 10], stat[:, fp + 8:fp + 9], AF.Copy),
                deps=[q4], inc="bfq")
            q6 = p.op("V", lambda e, fp=fp: e.reciprocal(
                stat[:, fp + 10:fp + 11], stat[:, fp + 9:fp + 10]),
                deps=[q5], inc="bfq")
            q7 = p.op("A", lambda e, fp=fp: e.activation(
                stat[:, fp + 11:fp + 12], stat[:, fp + 10:fp + 11], AF.Copy),
                deps=[q6], inc="bfq")
            se1 = p.op("V", lambda e, fp=fp: e.tensor_scalar_mul(
                stat[:, fp + 12:fp + 13], stat[:, fp + 9:fp + 10],
                wsc[:, 0:1]), deps=[q5, p.tok("f1e")], inc="bfq")
            v1 = p.op("V", lambda e, fp=fp: e.tensor_scalar(
                s0[:], zf[:], stat[:, fp + 11:fp + 12], MAGIC,
                ALU.mult, ALU.add),
                deps=[q7] + ctxs.get(i - 1, {}).get("ztm", []), inc="bfq")
            p.op("V", lambda e: e.tensor_scalar(
                s0[:], s0[:], MAGIC, 127.0, ALU.subtract, ALU.min), inc="bfq")
            v3 = p.op("V", lambda e: e.tensor_scalar_max(
                zq[:], s0[:], -128.0), inc="bfq")
            ctxs[i] = {"lxo": lxo, "q2": q2, "se1": se1, "v3": v3,
                       "pr": pr, "fp": fp}

        def frontT(i):
            c = ctxs[i]
            zt, ztm = [], []
            for dc in range(8):
                blk = dc % 4
                tm = p.op("T", lambda e, dc=dc, blk=blk: e.transpose(
                    psT[:, blk * 128:(blk + 1) * 128],
                    zq[:, dc * 128:(dc + 1) * 128], iden[:]),
                    deps=[c["v3"], p.tok("xnt")], mark=True)
                ztm.append(tm)
                zt.append(p.op("V", lambda e, dc=dc, blk=blk: e.tensor_copy(
                    zqT[:, dc * 128:(dc + 1) * 128],
                    psT[:, blk * 128:(blk + 1) * 128]),
                    deps=[tm], inc="xnt"))
            c["zt"] = zt
            c["ztm"] = ztm

        def back(i):
            ti, b = seq[i]
            c = ctxs[i]
            pr, fp = c["pr"], c["fp"]
            qtoks = []
            rounds_prev = ctxs.get(i - 1, {}).get("hgr", [])
            for qk in range(4):
                bks = (0, 1) if qk % 2 == 0 else (2, 3)
                etoks = []
                for bi, bank in enumerate(bks):
                    fcol0 = qk * 1024 + bi * 512
                    for jc in range(8):
                        dps = []
                        if jc == 0:
                            if qk == 0 and bi == 0:
                                dps += c["zt"] + [wtokb]
                            if len(evroll) >= 4:
                                dps.append(evroll[-4])
                        p.op("T", lambda e, jc=jc, bank=bank, fcol0=fcol0:
                             e.matmul(
                                 psF[:, bank, 0:512],
                                 zqT[:, jc * 128:(jc + 1) * 128],
                                 W1[:, jc * FFN + fcol0:
                                    jc * FFN + fcol0 + 512],
                                 skip_group_check=True, start=(jc == 0),
                                 stop=(jc == 7)),
                             deps=dps, mark=(jc == 7))
                    tmm = p.tok("evT")
                    etoks.append(p.op("A", lambda e, bank=bank, fcol0=fcol0,
                                      fp=fp: e.activation(
                                          hg[:, fcol0:fcol0 + 512],
                                          psF[:, bank, 0:512], AF.Gelu,
                                          scale=stat[:, fp + 12:fp + 13]),
                                      deps=[tmm, c["se1"]] + rounds_prev,
                                      inc="f1e"))
                    evroll.append(etoks[-1])
                qa = p.op("A", lambda e, qk=qk: e.activation(
                    s1[:], hg[:, qk * 1024:(qk + 1) * 1024], AF.Abs),
                    deps=etoks + [p.tok("bfe")] + (qtoks[-1:] if qtoks else []),
                    inc="f1e")
                qtoks.append(p.op("V", lambda e, qk=qk: e.reduce_max(
                    stat[:, 40 + qk:41 + qk], s1[:], axis=AX.X),
                    deps=[qa], inc="bfq"))
            hA = p.op("A", lambda e: e.activation(stat[:, 44:48],
                                                  stat[:, 40:44], AF.Copy),
                      deps=qtoks, inc="bfq")
            hV = p.op("V", lambda e: e.reduce_max(stat[:, 48:49],
                                                  stat[:, 44:48], axis=AX.X),
                      deps=[hA], inc="bfq")
            hB = p.op("A", lambda e: e.activation(stat[:, 49:50],
                                                  stat[:, 48:49], AF.Copy,
                                                  scale=1.0 / 127.0),
                      deps=[hV], inc="bfq")
            hC = p.op("V", lambda e: e.tensor_scalar_max(
                stat[:, 50:51], stat[:, 49:50], 1e-5 / 127.0),
                deps=[hB], inc="bfq")
            hD = p.op("A", lambda e: e.activation(stat[:, 51:52],
                                                  stat[:, 50:51], AF.Copy),
                      deps=[hC], inc="bfq")
            hE = p.op("V", lambda e: e.reciprocal(stat[:, 52:53],
                                                  stat[:, 51:52]),
                      deps=[hD], inc="bfq")
            hF = p.op("A", lambda e: e.activation(stat[:, 53:54],
                                                  stat[:, 52:53], AF.Copy),
                      deps=[hE], inc="bfq")
            se2 = p.op("V", lambda e: e.tensor_scalar_mul(
                stat[:, 54:55], stat[:, 51:52], wsc[:, 2:3]),
                deps=[hD], inc="bfq")
            hgr = []
            for qk in range(4):
                w1q = p.op("V", lambda e, qk=qk: e.tensor_scalar(
                    s1[:], hg[:, qk * 1024:(qk + 1) * 1024],
                    stat[:, 53:54], MAGIC, ALU.mult, ALU.add),
                    deps=[hF, p.tok("evT")], inc="bfq")
                p.op("V", lambda e: e.tensor_scalar(
                    s1[:], s1[:], MAGIC, 127.0, ALU.subtract, ALU.min),
                    inc="bfq")
                w3q = p.op("V", lambda e: e.tensor_scalar_max(
                    z2q[:], s1[:], -128.0), inc="bfq")
                hgr.append(w3q)
                z2toks = []
                for dc in range(8):
                    blk = dc % 4
                    tm = p.op("T", lambda e, dc=dc, blk=blk: e.transpose(
                        psT[:, blk * 128:(blk + 1) * 128],
                        z2q[:, dc * 128:(dc + 1) * 128], iden[:]),
                        deps=[w3q, p.tok("xnt")], mark=True)
                    z2toks.append(p.op("V", lambda e, dc=dc, blk=blk:
                                       e.tensor_copy(
                                           z2T[:, dc * 128:(dc + 1) * 128],
                                           psT[:, blk * 128:(blk + 1) * 128]),
                                       deps=[tm], inc="xnt"))
                for half in range(2):
                    for dci in range(8):
                        fc = qk * 8 + dci
                        dps = (z2toks + [p.tok("bfe")]) if fc == 0 else \
                            (z2toks if dci == 0 else ())
                        p.op("T", lambda e, half=half, fc=fc, dci=dci:
                             e.matmul(
                                 psF[:, 4 + half, 0:512],
                                 z2T[:, dci * 128:(dci + 1) * 128],
                                 W2[:, fc * D + half * 512:
                                    fc * D + half * 512 + 512],
                                 skip_group_check=True, start=(fc == 0),
                                 stop=(fc == 31)),
                             deps=dps, mark=(fc == 31 and half == 1))
            c["hgr"] = hgr
            t2w = p.tok("evT")
            ev = []
            for half in range(2):
                p.op("V", lambda e, half=half: e.tensor_scalar_mul(
                    s1[:, half * 512:(half + 1) * 512],
                    psF[:, 4 + half, 0:512], stat[:, 54:55]),
                    deps=[t2w, se2, c["lxo"], p.tok("dso2")], inc="bfe")
                ev.append(p.op("V", lambda e, half=half, pr=pr: e.tensor_tensor(
                    s0[:, half * 512:(half + 1) * 512],
                    s1[:, half * 512:(half + 1) * 512],
                    xol[:, pr + half * 512:pr + (half + 1) * 512], ALU.add),
                    inc="bfe"))
            if tap_xs is not None:
                p.op("S", lambda e, b=b, ti=ti: e.dma_start(
                    out=taps[tapname][b, ti], in_=s0[:]),
                    deps=ev, dma="dtap")
            p.op("S", lambda e, b=b, ti=ti: e.dma_start(
                out=xout[b, ti * 128:(ti + 1) * 128, :], in_=s0[:]),
                deps=ev + [p.tok("dtap")], dma="dso2")

        frontV(0)
        frontT(0)
        for i in range(len(seq)):
            if i + 1 < len(seq):
                frontV(i + 1)
            back(i)
            if i + 1 < len(seq):
                frontT(i + 1)
        pctx.close()

    def phase4():
        pctx = ExitStack()

        def pbuf(name, shape, dtype=F32):
            return pctx.enter_context(nc.sbuf_tensor("e_" + name, list(shape), dtype))

        WqT = pbuf("WqT", (128, 8 * D), BF16)
        WkT = pbuf("WkT", (128, 8 * D), BF16)
        WvT = pbuf("WvT", (128, 8 * D), BF16)
        for W, nm in [(WqT, "bWq"), (WkT, "bWk"), (WvT, "bWv")]:
            for jc in range(8):
                p.op("S", lambda e, W=W, nm=nm, jc=jc: e.dma_start(
                    out=W[:, jc * D:(jc + 1) * D],
                    in_=bWT[nm][jc * 128:(jc + 1) * 128, :]), dma="dw4")
        wtok4 = p.tok("dw4")
        xol = pbuf("xol", (128, 1024), F32)
        zb = pbuf("zb", (128, 1024), BF16)
        zbT = pbuf("zbT", (128, 1024), BF16)
        QTt = pbuf("QTt", (128, 1024), BF16)
        KTt = pbuf("KTt", (128, 1024), BF16)
        Vt = pbuf("Vt", (128, 1024), BF16)
        stat = pbuf("stat", (128, 12), F32)
        s0 = pbuf("s0", (128, 1024), F32)
        eln = make_emit_ln(stat, s0, s0, s0, s0)
        evq4 = []

        cut = globals().get("P4_CUT", 99)
        for ti in range(n_tiles):
            for b in range(B):
                lxo = p.op("S", lambda e, b=b, ti=ti: e.dma_start(
                    out=xol[:], in_=xout[b, ti * 128:(ti + 1) * 128, :]),
                    deps=[p.tok("lns")], dma="dl4")
                t_z = eln(xol[:], zb[:], 0, dep=[lxo, p.tok("evT")], bn=0)
                if cut < 1:
                    continue
                zt = []
                for dc in range(8):
                    blk = dc % 4
                    tm = p.op("T", lambda e, dc=dc, blk=blk: e.transpose(
                        psT[:, blk * 128:(blk + 1) * 128],
                        zb[:, dc * 128:(dc + 1) * 128], iden[:]),
                        deps=[t_z, p.tok("xnt")], mark=True)
                    zt.append(p.op("V", lambda e, dc=dc, blk=blk: e.tensor_copy(
                        zbT[:, dc * 128:(dc + 1) * 128],
                        psT[:, blk * 128:(blk + 1) * 128]),
                        deps=[tm], inc="xnt"))
                if cut < 2:
                    continue
                ev4 = []
                for gi, (W, dstT, bcol) in enumerate(
                        [(WqT, QTt, bqcol)] * 8 + [(WkT, KTt, bkcol)] * 8):
                    ic = gi % 8
                    if True:
                        bank = gi % 4
                        col = ((gi // 4) % 4) * 128
                        for jc in range(8):
                            dps = []
                            if jc == 0:
                                if gi == 0:
                                    dps += zt + [wtok4]
                                if len(evq4) >= 4:
                                    dps.append(evq4[-4])
                            p.op("T", lambda e, W=W, ic=ic, jc=jc, bank=bank,
                                 col=col: e.matmul(
                                     psF[:, bank, col:col + 128],
                                     W[:, jc * D + ic * 128:
                                       jc * D + ic * 128 + 128],
                                     zbT[:, jc * 128:(jc + 1) * 128],
                                     skip_group_check=True, start=(jc == 0), stop=(jc == 7)),
                                 deps=dps, mark=(jc == 7))
                        tmm = p.tok("evT")
                        ev4.append(p.op("V", lambda e, dstT=dstT, bcol=bcol,
                                        ic=ic, bank=bank, col=col:
                                        e.tensor_scalar_add(
                                            dstT[:, ic * 128:(ic + 1) * 128],
                                            psF[:, bank, col:col + 128],
                                            bcol[:, ic:ic + 1]),
                                        deps=[tmm, p.tok("ds4")], inc="q4e"))
                        evq4.append(ev4[-1])
                if cut < 3:
                    continue
                for half in range(2):
                    for jc in range(8):
                        dps = (zt + [p.tok("q4e")]) if jc == 0 else ()
                        p.op("T", lambda e, half=half, jc=jc: e.matmul(
                            psF[:, 4 + half, 0:512],
                            zbT[:, jc * 128:(jc + 1) * 128],
                            WvT[:, jc * D + half * 512:
                                jc * D + half * 512 + 512],
                            skip_group_check=True, start=(jc == 0), stop=False), deps=dps)
                    p.op("T", lambda e, half=half: e.matmul(
                        psF[:, 4 + half, 0:512], ones_row[0:1, :],
                        rows["bbv"][0:1, half * 512: half * 512 + 512],
                        skip_group_check=True, start=False, stop=True), mark=True)
                tvm = p.tok("evT")
                ev4.append(p.op("V", lambda e: e.tensor_copy(
                    Vt[:, 0:512], psF[:, 4, 0:512]),
                    deps=[tvm, p.tok("ds4")], inc="q4e"))
                ev4.append(p.op("V", lambda e: e.tensor_copy(
                    Vt[:, 512:1024], psF[:, 5, 0:512]),
                    deps=[tvm], inc="q4e"))
                if cut < 4:
                    continue
                p.op("G", lambda e, b=b, ti=ti: e.dma_start(
                    out=qbS[b, ti], in_=QTt[:]), deps=ev4, dma="ds4")
                for ic in range(8):
                    p.op("G", lambda e, b=b, ti=ti, ic=ic: e.dma_start(
                        out=kbS[b, ic, :, ti * 128:(ti + 1) * 128],
                        in_=KTt[:, ic * 128:(ic + 1) * 128]), dma="ds4")
                p.op("G", lambda e, b=b, ti=ti: e.dma_start(
                    out=vbS[b, ti * 128:(ti + 1) * 128, :], in_=Vt[:]),
                    dma="ds4")
        pctx.close()

    def phase5():
        pctx = ExitStack()

        def pbuf(name, shape, dtype=F32):
            return pctx.enter_context(nc.sbuf_tensor("f_" + name, list(shape), dtype))

        Kpl = pbuf("Kpl", (128, 8 * T), BF16)
        Vpl = pbuf("Vpl", (128, NT * 1024), BF16)
        WoT = pbuf("WoT", (128, 8 * D), BF16)
        for jc in range(8):
            p.op("S", lambda e, jc=jc: e.dma_start(
                out=WoT[:, jc * D:(jc + 1) * D],
                in_=bWT["bWo"][jc * 128:(jc + 1) * 128, :]), dma="dw5")
        wtok5 = p.tok("dw5")
        Qt = pbuf("Qt", (128, 1024), BF16)
        Pex = pbuf("Pex", (128, 2048), BF16)
        PnT = pbuf("PnT", (128, 2048), BF16)
        avT = pbuf("avT", (128, 1024), BF16)
        xol = pbuf("xol", (128, 1024), F32)
        xsf = pbuf("xsf", (128, 1024), F32)
        stat = pbuf("stat", (128, 16), F32)
        tap_xs3 = tap("xs3", (B, NT, 128, D), F32)
        RS = 1.0 / math.sqrt(512.0)
        Ts = n_tiles * 128
        nsb = (Ts + 511) // 512

        for b in range(B):
            kvdeps = [p.tok("evT")]
            for ic in range(8):
                p.op("S", lambda e, b=b, ic=ic: e.dma_start(
                    out=Kpl[:, ic * T:ic * T + Ts], in_=kbS[b, ic, :, 0:Ts]),
                    deps=kvdeps if ic == 0 else (), dma="dkv")
            for st in range(n_tiles):
                p.op("S", lambda e, b=b, st=st: e.dma_start(
                    out=Vpl[:, st * 1024:(st + 1) * 1024],
                    in_=vbS[b, st * 128:(st + 1) * 128, :]), dma="dkv")
            kvtok = p.tok("dkv")
            for ti in range(n_tiles):
                lq = p.op("S", lambda e, b=b, ti=ti: e.dma_start(
                    out=Qt[:], in_=qbS[b, ti]),
                    deps=[p.tok("evT")], dma="dq5")
                lxo = p.op("S", lambda e, b=b, ti=ti: e.dma_start(
                    out=xol[:], in_=xout[b, ti * 128:(ti + 1) * 128, :]),
                    deps=[p.tok("x5e")], dma="dq5")
                for sb in range(nsb):
                    Nn = min(512, Ts - sb * 512)
                    for ic in range(8):
                        dps = ([lq, kvtok, p.tok("sfe")]
                               if (sb == 0 and ic == 0) else ())
                        p.op("T", lambda e, sb=sb, ic=ic, Nn=Nn: e.matmul(
                            psF[:, sb, 0:Nn],
                            Qt[:, ic * 128:(ic + 1) * 128],
                            Kpl[:, ic * T + sb * 512: ic * T + sb * 512 + Nn],
                            skip_group_check=True, start=(ic == 0), stop=(ic == 7)),
                            deps=dps, mark=(ic == 7))
                smk = p.tok("evT")
                mt = []
                for sb in range(nsb):
                    Nn = min(512, Ts - sb * 512)
                    mt.append(p.op("V", lambda e, sb=sb, Nn=Nn: e.reduce_max(
                        stat[:, sb:sb + 1], psF[:, sb, 0:Nn], axis=AX.X),
                        deps=[smk] if sb == 0 else (), inc="sfq"))
                mA = p.op("A", lambda e: e.activation(stat[:, 4:4 + nsb],
                                                      stat[:, 0:nsb], AF.Copy),
                          deps=mt, inc="sfq")
                mV = p.op("V", lambda e: e.reduce_max(stat[:, 8:9],
                                                      stat[:, 4:4 + nsb],
                                                      axis=AX.X),
                          deps=[mA], inc="sfq")
                mN = p.op("A", lambda e: e.activation(stat[:, 9:10],
                                                      stat[:, 8:9], AF.Copy,
                                                      scale=-RS),
                          deps=[mV], inc="sfq")
                mH = p.op("V", lambda e: e.tensor_copy(stat[:, 10:11],
                                                       stat[:, 9:10]),
                          deps=[mN], inc="sfq")
                ext = []
                for sb in range(nsb):
                    Nn = min(512, Ts - sb * 512)
                    ext.append(p.op("A", lambda e, sb=sb, Nn=Nn: e.activation(
                        Pex[:, sb * 512:sb * 512 + Nn], psF[:, sb, 0:Nn],
                        AF.Exp, scale=RS, bias=stat[:, 10:11]),
                        deps=[mH, p.tok("sfq")] if sb == 0 else (), inc="sfe"))
                r1 = p.op("V", lambda e: e.reduce_sum(stat[:, 11:12],
                                                      Pex[:, 0:Ts], axis=AX.X),
                          deps=ext, inc="sfq")
                r2 = p.op("A", lambda e: e.activation(stat[:, 12:13],
                                                      stat[:, 11:12], AF.Copy),
                          deps=[r1], inc="sfq")
                r3 = p.op("V", lambda e: e.reciprocal(stat[:, 13:14],
                                                      stat[:, 12:13]),
                          deps=[r2], inc="sfq")
                r4 = p.op("A", lambda e: e.activation(stat[:, 14:15],
                                                      stat[:, 13:14], AF.Copy),
                          deps=[r3], inc="sfq")
                nrm = p.op("V", lambda e: e.tensor_scalar_mul(
                    Pex[:, 0:Ts], Pex[:, 0:Ts], stat[:, 14:15]),
                    deps=[r4], inc="sfq")
                ptoks = []
                for st in range(n_tiles):
                    blk = st % 4
                    tm = p.op("T", lambda e, st=st, blk=blk: e.transpose(
                        psT[:, blk * 128:(blk + 1) * 128],
                        Pex[:, st * 128:(st + 1) * 128], iden[:]),
                        deps=[nrm, p.tok("xnt")], mark=True)
                    ptoks.append(p.op("V", lambda e, st=st, blk=blk:
                                      e.tensor_copy(
                                          PnT[:, st * 128:(st + 1) * 128],
                                          psT[:, blk * 128:(blk + 1) * 128]),
                                      deps=[tm], inc="xnt"))
                for vc in range(8):
                    bank = 4 + vc // 4
                    col = (vc % 4) * 128
                    for st in range(n_tiles):
                        dps = (ptoks + [p.tok("ave")]) \
                            if (vc == 0 and st == 0) else ()
                        p.op("T", lambda e, vc=vc, st=st, bank=bank, col=col:
                             e.matmul(
                                 psF[:, bank, col:col + 128],
                                 Vpl[:, st * 1024 + vc * 128:
                                     st * 1024 + vc * 128 + 128],
                                 PnT[:, st * 128:(st + 1) * 128],
                                 skip_group_check=True, start=(st == 0),
                                 stop=(st == n_tiles - 1)),
                             deps=dps, mark=(st == n_tiles - 1))
                avk = p.tok("evT")
                avt = []
                for vc in range(8):
                    bank = 4 + vc // 4
                    col = (vc % 4) * 128
                    avt.append(p.op("V", lambda e, vc=vc, bank=bank, col=col:
                                    e.tensor_copy(
                                        avT[:, vc * 128:(vc + 1) * 128],
                                        psF[:, bank, col:col + 128]),
                                    deps=[avk] if vc == 0 else (), inc="ave"))
                for half in range(2):
                    for vc in range(8):
                        dps = (avt + ext + [wtok5, p.tok("x5e")]) \
                            if (half == 0 and vc == 0) else ()
                        p.op("T", lambda e, half=half, vc=vc: e.matmul(
                            psF[:, half, 0:512],
                            avT[:, vc * 128:(vc + 1) * 128],
                            WoT[:, vc * D + half * 512:
                                vc * D + half * 512 + 512],
                            skip_group_check=True, start=(vc == 0), stop=False), deps=dps)
                    p.op("T", lambda e, half=half: e.matmul(
                        psF[:, half, 0:512], ones_row[0:1, :],
                        rows["bbo"][0:1, half * 512: half * 512 + 512],
                        skip_group_check=True, start=False, stop=True), mark=True)
                t5w = p.tok("evT")
                ex1 = p.op("V", lambda e: e.tensor_tensor(
                    xsf[:, 0:512], psF[:, 0, 0:512], xol[:, 0:512], ALU.add),
                    deps=[t5w, lxo, p.tok("dso5"), p.tok("dtap")], inc="x5e")
                ex2 = p.op("V", lambda e: e.tensor_tensor(
                    xsf[:, 512:1024], psF[:, 1, 0:512], xol[:, 512:1024],
                    ALU.add), inc="x5e")
                if tap_xs3 is not None:
                    p.op("S", lambda e, b=b, ti=ti: e.dma_start(
                        out=taps["xs3"][b, ti], in_=xsf[:]),
                        deps=[ex1, ex2], dma="dtap")
                p.op("S", lambda e, b=b, ti=ti: e.dma_start(
                    out=xout[b, ti * 128:(ti + 1) * 128, :], in_=xsf[:]),
                    deps=[ex1, ex2, p.tok("dtap")], dma="dso5")
        pctx.close()

    if max_phase >= 3:
        phase0()
        p.barrier()
        phase_bitffn("xs2")
        p.barrier()
    if max_phase >= 4:
        phase4()
        p.barrier()
    if max_phase >= 5:
        phase5()
        p.barrier()
    if max_phase >= 6:
        phase_bitffn("xs4")
        p.barrier()

    p.replay()
    ctx.close()
    return nc, taps


NMAP = {
    "xin": "L0_attn_a_x",
    "aWq": "L0_attn_a_Wq", "abq": "L0_attn_a_bq", "aWk": "L0_attn_a_Wk",
    "abk": "L0_attn_a_bk", "aWv": "L0_attn_a_Wv", "abv": "L0_attn_a_bv",
    "aWo": "L0_attn_a_Wo", "abo": "L0_attn_a_bo",
    "n1g": "L0_attn_a_n1g", "n1b": "L0_attn_a_n1b",
    "n2g": "L0_attn_a_n2g", "n2b": "L0_attn_a_n2b",
    "thb": "L0_attn_a_theta_base", "tw1": "L0_attn_a_theta_w1",
    "tw2": "L0_attn_a_theta_w2", "lmb": "L0_attn_a_lambda_base",
    "eta": "L0_attn_a_eta",
    "aWf1": "L0_attn_a_Wf1", "abf1": "L0_attn_a_bf1",
    "aWf2": "L0_attn_a_Wf2", "abf2": "L0_attn_a_bf2",
    "bWq": "L0_attn_b_Wq", "bbq": "L0_attn_b_bq", "bWk": "L0_attn_b_Wk",
    "bbk": "L0_attn_b_bk", "bWv": "L0_attn_b_Wv", "bbv": "L0_attn_b_bv",
    "bWo": "L0_attn_b_Wo", "bbo": "L0_attn_b_bo",
    "fw1": "L0_ffn_w1", "fw2": "L0_ffn_w2",
}


def make_in_maps(inputs):
    x = np.ascontiguousarray(inputs[NMAP["xin"]], dtype=np.float32)
    in_maps = []
    cst = host_constants()
    for core in range(NCORES):
        m = {"xin": x[core * B:(core + 1) * B], "cst": cst}
        for k2, v2 in NMAP.items():
            if k2 == "xin":
                continue
            m[k2] = np.ascontiguousarray(inputs[v2], dtype=np.float32)
        in_maps.append(m)
    return in_maps


def kernel(**inputs):
    nc, _ = build_program()
    in_maps = make_in_maps(inputs)
    res = run_bass_kernel_spmd(nc, in_maps, core_ids=list(range(NCORES)))
    return np.concatenate([res.results[i]["out"] for i in range(NCORES)], axis=0)


# version 04
# speedup vs baseline: 1.3503x; 1.0182x over previous
import math
from contextlib import ExitStack

import numpy as np

import concourse.bass as bass
import concourse.mybir as mybir
from concourse.bass_utils import run_bass_kernel_spmd

F32 = mybir.dt.float32
BF16 = mybir.dt.bfloat16
I8 = mybir.dt.int8
AF = mybir.ActivationFunctionType
ALU = mybir.AluOpType
AX = mybir.AxisListType

B_TOT, T, D = 16, 2048, 1024
NCORES = 8
B = 2
H, KD, BK = 16, 64, 32
FFN = 4096
LORA = 48
NT = 16
ST = 2
NST = NT // ST
ALPHA = 3.5
PI = math.pi
MAGIC = 12582912.0
EPS_OUT = 1e-5
EPS_IN = 1e-6
CST_COLS = 3200


def host_constants():
    c = np.zeros((128, CST_COLS), np.float32)
    pidx = np.arange(128)
    c[:, 0:128] = np.eye(128, dtype=np.float32)
    c[:, 128:1152] = (ALPHA * (np.arange(1024) % 16))[None, :]
    t = pidx
    d = t[None, :] - pidx[:, None]
    blk_t, blk_p = t[None, :] // 16, (pidx // 16)[:, None]
    md = ((blk_t == blk_p) & (d >= 0)) * np.exp(-ALPHA * np.maximum(d, 0))
    c[:, 1152:1664] = np.tile(md, (1, 4))
    m16 = (blk_t == blk_p + 1) & ((pidx // 16 % 2 == 0))[:, None]
    c[:, 1664:2176] = np.tile(m16, (1, 4))
    m32 = ((t // 32)[None, :] == (pidx // 32)[:, None] + 1) & \
        ((pidx // 32 % 2 == 0))[:, None]
    c[:, 2176:2688] = np.tile(m32, (1, 4))
    m64 = (t[None, :] >= 64) & (pidx[:, None] < 64)
    c[:, 2688:3200] = np.tile(m64, (1, 4))
    return c

ENGS = ("T", "V", "A", "G", "S")


class Prog:

    def __init__(self, nc):
        self.nc = nc
        self.q = {k: [] for k in ENGS}
        self.sems = {}
        self.members = {}
        self.waited = {k: {} for k in ENGS}
        for k in ENGS:
            self.sem("ev" + k)
        self.nbar = 0
        self.sem("bar")

    def sem(self, name):
        if name not in self.sems:
            h = self.nc.alloc_semaphore(name=name)
            self.sems[name] = [h, 0]
        return name

    def tok(self, name):
        out = []
        if name in self.sems and self.sems[name][1] > 0:
            out.append((name, self.sems[name][1]))
        for nm in sorted(self.members.get(name, ())):
            if self.sems[nm][1] > 0:
                out.append((nm, self.sems[nm][1]))
        return tuple(out)

    def wait(self, key, *toks):
        for tk in toks:
            if tk is None:
                continue
            if isinstance(tk, (list, tuple)) and (len(tk) == 0
                                                  or not isinstance(tk[0], str)):
                self.wait(key, *tk)
                continue
            (sname, val) = tk
            if val <= 0:
                continue
            w = self.waited[key]
            if w.get(sname, -1) >= val:
                continue
            w[sname] = val
            h = self.sems[sname][0]
            self.q[key].append(lambda e, h=h, v=val: e.wait_ge(h, v))

    def op(self, key, fn, deps=(), inc=None, dma=None, mark=False):
        self.wait(key, *deps)
        incs = []
        if inc is not None:
            nm = inc + "." + key
            self.sem(nm)
            self.sems[nm][1] += 1
            self.members.setdefault(inc, set()).add(nm)
            incs.append((nm, 1))
        if dma is not None:
            nm = dma + "." + key
            self.sem(nm)
            self.sems[nm][1] += 16
            self.members.setdefault(dma, set()).add(nm)
            incs.append((nm, 16))
        if mark:
            en = "ev" + key
            self.sems[en][1] += 1
            incs.append((en, 1))
        toks = [(n, self.sems[n][1]) for n, _ in incs]

        def runner(e, fn=fn, incs=tuple(incs)):
            inst = fn(e)
            for (n, kk) in incs:
                inst.then_inc(self.sems[n][0], kk)
            return inst

        self.q[key].append(runner)
        return toks[0] if len(toks) == 1 else (toks if toks else None)

    def barrier(self):
        self.nbar += 1
        cur = [(n, s[1]) for n, s in self.sems.items() if s[1] > 0 and n != "bar"]
        bh = self.sems["bar"][0]
        self.sems["bar"][1] += len(ENGS)
        tgt = self.nbar * len(ENGS)
        for key in ENGS:
            self.wait(key, *cur)
            self.q[key].append(lambda e, bh=bh: e.sem_inc(bh, 1))
            self.q[key].append(lambda e, bh=bh, t=tgt: e.wait_ge(bh, t))
            self.waited[key] = {n: v for n, v in cur}
            self.waited[key]["bar"] = tgt

    def replay(self):
        with self.nc.Block() as block:
            @block.tensor
            def _(e):
                for f in self.q["T"]:
                    f(e)

            @block.vector
            def _(e):
                for f in self.q["V"]:
                    f(e)

            @block.scalar
            def _(e):
                for f in self.q["A"]:
                    f(e)

            @block.gpsimd
            def _(e):
                for f in self.q["G"]:
                    f(e)

            @block.sync
            def _(e):
                for f in self.q["S"]:
                    f(e)


def APx(t, off, pat):
    return bass.AP(t, off, pat)


def build_program(debug_taps=(), n_tiles=NT, max_phase=99):
    nc = bass.Bass("TRN2", target_bir_lowering=False, detect_race_conditions=False)
    p = Prog(nc)
    ctx = ExitStack()

    dram = {}

    def din(name, shape):
        dram[name] = nc.dram_tensor(name, list(shape), F32, kind="ExternalInput").ap()
        return dram[name]

    xin = din("xin", (B, T, D))
    for nm, shp in [
        ("aWq", (D, D)), ("abq", (D,)), ("aWk", (D, D)), ("abk", (D,)),
        ("aWv", (D, D)), ("abv", (D,)), ("aWo", (D, D)), ("abo", (D,)),
        ("n1g", (D,)), ("n1b", (D,)), ("n2g", (D,)), ("n2b", (D,)),
        ("thb", (H, BK)), ("tw1", (D, LORA)), ("tw2", (LORA, H * BK)),
        ("lmb", (H, BK)), ("eta", (H, BK)),
        ("aWf1", (D, FFN)), ("abf1", (FFN,)), ("aWf2", (FFN, D)), ("abf2", (D,)),
        ("bWq", (D, D)), ("bbq", (D,)), ("bWk", (D, D)), ("bbk", (D,)),
        ("bWv", (D, D)), ("bbv", (D,)), ("bWo", (D, D)), ("bbo", (D,)),
        ("fw1", (FFN, D)), ("fw2", (D, FFN)),
    ]:
        din(nm, shp)

    xout = nc.dram_tensor("out", [B, T, D], F32, kind="ExternalOutput").ap()

    taps = {}

    def scratch(name, shape, dtype):
        kind = "ExternalOutput" if name in debug_taps else "Internal"
        ap = nc.dram_tensor("tap_" + name if kind == "ExternalOutput" else name,
                            list(shape), dtype, kind=kind).ap()
        if kind == "ExternalOutput":
            taps[name] = ap
        return ap

    x1s = scratch("x1s", (B, T, D), F32)
    h1S = scratch("h1S", (B, T, D), BF16)
    qrS = scratch("qrS", (B, NT, 128, 512), BF16)
    qiS = scratch("qiS", (B, NT, 128, 512), BF16)
    prS = scratch("prS", (B, NT, 128, 512), BF16)
    piS = scratch("piS", (B, NT, 128, 512), BF16)
    thS = scratch("thS", (B, NT, 128, 512), F32)
    nlS = scratch("nlS", (B, NT, 128, 512), F32)
    vS = scratch("vS", (B, T, D), BF16)
    w1tq = scratch("w1tq", (D, FFN), BF16)
    w2tq = scratch("w2tq", (FFN, D), BF16)
    bWT = {nm: scratch(nm + "T", (D, D), BF16)
           for nm in ("bWq", "bWk", "bWv", "bWo")}
    qbS = scratch("qbS", (B, NT, 128, D), BF16)
    kbS = scratch("kbS", (B, 8, 128, T), BF16)
    vbS = scratch("vbS", (B, T, D), BF16)

    def tap(name, shape, dtype=F32):
        if name in debug_taps:
            taps[name] = nc.dram_tensor("tap_" + name, list(shape), dtype,
                                        kind="ExternalOutput").ap()
            return taps[name]
        return None

    sbufs = {}

    def sbuf(name, shape, dtype=F32):
        h = ctx.enter_context(nc.sbuf_tensor(name, list(shape), dtype))
        sbufs[name] = h
        return h

    ones_row = sbuf("ones_row", (1, 128), BF16)
    iden = sbuf("iden", (128, 128), BF16)
    alphat = sbuf("alphat", (128, 1024), F32)
    mask_diag = sbuf("mask_diag", (128, 512), F32)
    mask_16 = sbuf("mask_16", (128, 512), BF16)
    mask_32 = sbuf("mask_32", (128, 512), BF16)
    mask_64 = sbuf("mask_64", (128, 512), BF16)

    thb_t = sbuf("thb_t", (128, 4), F32)
    lmbn_t = sbuf("lmbn_t", (128, 4), F32)
    etan_t = sbuf("etan_t", (128, 4), F32)
    bq_r = sbuf("bq_r", (128, 4), F32)
    bq_i = sbuf("bq_i", (128, 4), F32)
    bk_r = sbuf("bk_r", (128, 4), F32)
    bk_i = sbuf("bk_i", (128, 4), F32)

    g1t = sbuf("g1t", (128, 1024), BF16)
    b1t = sbuf("b1t", (128, 1024), BF16)
    g2t = sbuf("g2t", (128, 1024), BF16)
    b2t = sbuf("b2t", (128, 1024), BF16)
    rows = {}
    for nm in ["abv", "abo", "abf2", "bbq", "bbk", "bbv", "bbo"]:
        rows[nm] = sbuf("row_" + nm, (1, D), BF16)
    grow = sbuf("grow", (1, 4 * D), BF16)
    bf1c = sbuf("bf1c", (128, 32), F32)
    pihalf = sbuf("pihalf", (128, 1), F32)
    epsc = sbuf("epsc", (128, 2), F32)
    magicc = sbuf("magicc", (128, 1), F32)
    onec = sbuf("onec", (128, 1), F32)
    wsc = sbuf("wsc", (128, 4), F32)
    onef = sbuf("onef", (1, 128), F32)
    bqcol = sbuf("bqcol", (128, 8), F32)
    bkcol = sbuf("bkcol", (128, 8), F32)

    psF = ctx.enter_context(nc.psum_tensor("psF", [128, 6, 512], F32))
    psT = ctx.enter_context(nc.psum_tensor("psT", [128, 512], BF16))
    psX = ctx.enter_context(nc.psum_tensor("psX", [128, 512], F32))

    cstT = nc.dram_tensor("cst", [128, CST_COLS], F32, kind="ExternalInput")
    dram["cst"] = cstT.ap()

    def consts():
        def csl(a, b):
            return APx(cstT, a, [[CST_COLS, 128], [1, b - a]])
        p.op("S", lambda e: e.dma_start(out=alphat[:], in_=csl(128, 1152)),
             dma="dcs2")
        p.op("S", lambda e: e.dma_start(out=mask_diag[:], in_=csl(1152, 1664)),
             dma="dcs2")
        p.op("G", lambda e: e.dma_start(out=iden[:], in_=csl(0, 128)), dma="dcst")
        p.op("G", lambda e: e.dma_start(out=mask_16[:], in_=csl(1664, 2176)),
             dma="dcst")
        p.op("G", lambda e: e.dma_start(out=mask_32[:], in_=csl(2176, 2688)),
             dma="dcst")
        p.op("G", lambda e: e.dma_start(out=mask_64[:], in_=csl(2688, 3200)),
             dma="dcst")
        p.op("V", lambda e: e.memset(ones_row[:], 1.0), mark=True)
        p.op("V", lambda e: e.memset(pihalf[:], PI / 2), mark=True)
        p.op("V", lambda e: e.memset(epsc[:, 0:1], EPS_OUT), mark=True)
        p.op("V", lambda e: e.memset(epsc[:, 1:2], EPS_IN), mark=True)
        p.op("V", lambda e: e.memset(magicc[:], MAGIC), mark=True)
        p.op("V", lambda e: e.memset(onec[:], 1.0), mark=True)
        p.op("V", lambda e: e.memset(onef[:], 1.0), mark=True)

        gdmas = []
        for c in range(4):
            gdmas.append(lambda e, c=c: e.dma_start(
                out=APx(thb_t, c, [[4, 128], [1, 1]]),
                in_=APx(dram["thb"].tensor, 128 * c, [[1, 128], [1, 1]])))
            gdmas.append(lambda e, c=c: e.dma_start(
                out=APx(lmbn_t, c, [[4, 128], [1, 1]]),
                in_=APx(dram["lmb"].tensor, 128 * c, [[1, 128], [1, 1]])))
            gdmas.append(lambda e, c=c: e.dma_start(
                out=APx(etan_t, c, [[4, 128], [1, 1]]),
                in_=APx(dram["eta"].tensor, 128 * c, [[1, 128], [1, 1]])))
            for dst, srcn, par in [(bq_r, "abq", 0), (bq_i, "abq", 1),
                                   (bk_r, "abk", 0), (bk_i, "abk", 1)]:
                gdmas.append(lambda e, c=c, dst=dst, srcn=srcn, par=par: e.dma_start(
                    out=APx(dst, c, [[4, 128], [1, 1]]),
                    in_=APx(dram[srcn].tensor, 256 * c + par,
                            [[64, 4], [2, 32], [1, 1]]),
                    allow_slow_non_contiguous=True))
        for nm in ["abv", "abo", "abf2", "bbq", "bbk", "bbv", "bbo"]:
            gdmas.append(lambda e, nm=nm: e.dma_start(
                out=rows[nm][0:1, :], in_=APx(dram[nm].tensor, 0, [[0, 1], [1, D]])))
        for ic in range(8):
            for dst, srcn in [(bqcol, "bbq"), (bkcol, "bbk")]:
                gdmas.append(lambda e, ic=ic, dst=dst, srcn=srcn: e.dma_start(
                    out=APx(dst, ic, [[8, 128], [1, 1]]),
                    in_=APx(dram[srcn].tensor, ic * 128, [[1, 128], [1, 1]])))
        for i, nm in enumerate(["n1g", "n1b", "n2g", "n2b"]):
            gdmas.append(lambda e, i=i, nm=nm: e.dma_start(
                out=grow[0:1, i * D:(i + 1) * D],
                in_=APx(dram[nm].tensor, 0, [[0, 1], [1, D]])))
        gdmas.append(lambda e: e.dma_start(
            out=APx(bf1c, 0, [[32, 128], [1, 32]]),
            in_=APx(dram["abf1"].tensor, 0, [[1, 128], [128, 32], [1, 1]]),
            allow_slow_non_contiguous=True))
        for fn in gdmas:
            p.op("G", fn, dma="dgth")

        def neg(e):
            e.tensor_scalar_mul(lmbn_t[:], lmbn_t[:], -1.0)
            return e.tensor_scalar_mul(etan_t[:], etan_t[:], -1.0)
        p.op("V", neg, deps=[p.tok("dgth")], mark=True)

        dep = [p.tok("dgth"), p.tok("evV")]
        for i, dst in enumerate([g1t, b1t, g2t, b2t]):
            for half in range(2):
                bk_ = (2 * i + half) % 4
                tm = p.op("T", lambda e, i=i, half=half, bk_=bk_: e.matmul(
                    psF[:, bk_, :], ones_row[0:1, :],
                    grow[0:1, i * D + half * 512: i * D + half * 512 + 512],
                    skip_group_check=True, start=True, stop=True), deps=dep + [p.tok("evV")], mark=True)
                p.op("V", lambda e, dst=dst, half=half, bk_=bk_: e.tensor_copy(
                    dst[:, half * 512:(half + 1) * 512], psF[:, bk_, :]),
                    deps=[tm], mark=True)
        p.barrier()

    consts()

    def make_emit_ln(stat, s0, s1, s2, s3):
        def emit_ln(src_ap, dst_ap, eps_col, gain=None, bias=None, dep=(), bn=0):
            c0, c1, c2, c3, c4, c5 = (bn + j for j in range(6))
            t1 = p.op("V", lambda e: e.reduce_sum(stat[:, c0:c0 + 1], src_ap, axis=AX.X),
                      deps=dep, inc="lns")
            t2 = p.op("A", lambda e: e.activation(s0[:], src_ap, AF.Square),
                      deps=dep, inc="lns")
            t3 = p.op("V", lambda e: e.reduce_sum(stat[:, c1:c1 + 1], s0[:], axis=AX.X),
                      deps=[t2], inc="lns")
            t4a = p.op("A", lambda e: e.activation(stat[:, c2:c2 + 1], stat[:, c0:c0 + 1],
                                                   AF.Square, scale=1.0 / 1024.0),
                       deps=[t1], inc="lns")
            t4b = p.op("A", lambda e: e.activation(stat[:, c3:c3 + 1], stat[:, c0:c0 + 1],
                                                   AF.Copy, scale=1.0 / 1024.0),
                       deps=[t1], inc="lns")
            t4d = p.op("A", lambda e: e.activation(stat[:, c4:c4 + 1], stat[:, c1:c1 + 1],
                                                   AF.Copy, scale=1.0 / 1024.0),
                       deps=[t3], inc="lns")
            t4c = p.op("G", lambda e: e.tensor_tensor(
                stat[:, c1:c1 + 1], stat[:, c4:c4 + 1], stat[:, c2:c2 + 1],
                ALU.subtract), deps=[t4a, t4d], inc="lns")
            t5a = p.op("A", lambda e: e.activation(stat[:, c2:c2 + 1], stat[:, c1:c1 + 1],
                                                   AF.Sqrt, bias=epsc[:, eps_col:eps_col + 1]),
                       deps=[t4c], inc="lns")
            t5b = p.op("V", lambda e: e.reciprocal(stat[:, c5:c5 + 1], stat[:, c2:c2 + 1]),
                       deps=[t5a], inc="lns")
            t5 = p.op("G", lambda e: e.tensor_copy(stat[:, c1:c1 + 1],
                                                   stat[:, c5:c5 + 1]),
                      deps=[t5b], inc="lns")
            if gain is None:
                return p.op("V", lambda e: e.tensor_scalar(
                    dst_ap, src_ap, stat[:, c3:c3 + 1], stat[:, c1:c1 + 1],
                    ALU.subtract, ALU.mult), deps=[t5, t4b], inc="lns")
            p.op("V", lambda e: e.tensor_scalar(
                s1[:, 0:1024], src_ap, stat[:, c3:c3 + 1], stat[:, c1:c1 + 1],
                ALU.subtract, ALU.mult), deps=[t5, t4b], inc="lns")
            p.op("V", lambda e: e.tensor_mul(s1[:, 0:1024], s1[:, 0:1024],
                                             gain[:]), inc="lns")
            return p.op("V", lambda e: e.tensor_add(dst_ap, s1[:, 0:1024],
                                                    bias[:]), inc="lns")
        return emit_ln

    if "masks" in debug_taps:
        mt = nc.dram_tensor("tap_masks", [4, 128, 512], F32, kind="ExternalOutput").ap()
        taps["masks"] = mt
        def dump_masks(e):
            e.dma_start(out=mt[0], in_=mask_diag[:])
            e.dma_start(out=mt[1], in_=mask_16[:])
            e.dma_start(out=mt[2], in_=mask_32[:])
            return e.dma_start(out=mt[3], in_=mask_64[:])
        for i in range(4):
            p.op("G", lambda e, i=i: e.dma_start(
                out=taps["masks"][i],
                in_=[mask_diag, mask_16, mask_32, mask_64][i][:]),
                deps=[p.tok("dcst"), p.tok("dcs2")], dma="dmask")
        p.barrier()

    def phase1a():
        pctx = ExitStack()

        def pbuf(name, shape, dtype=F32):
            return pctx.enter_context(nc.sbuf_tensor(name, list(shape), dtype))

        Wq = pbuf("Wq", (128, 8 * D), BF16)
        Wk = pbuf("Wk", (128, 8 * D), BF16)
        Wv = pbuf("Wv", (128, 8 * D), BF16)
        Tw1 = pbuf("Tw1", (128, 8 * LORA), BF16)
        Tw2 = pbuf("Tw2", (LORA, 512), BF16)

        for W, nm in [(Wq, "aWq"), (Wk, "aWk"), (Wv, "aWv")]:
            for dc in range(8):
                p.op("G", lambda e, W=W, nm=nm, dc=dc: e.dma_start(
                    out=W[:, dc * D:(dc + 1) * D],
                    in_=dram[nm][dc * 128:(dc + 1) * 128, :]), dma="dw1")
        for dc in range(8):
            p.op("G", lambda e, dc=dc: e.dma_start(
                out=Tw1[:, dc * LORA:(dc + 1) * LORA],
                in_=dram["tw1"][dc * 128:(dc + 1) * 128, :]), dma="dw1")
        p.op("G", lambda e: e.dma_start(out=Tw2[:], in_=dram["tw2"][:, :]), dma="dw1")
        wtok = p.tok("dw1")

        qr = pbuf("qr", (128, 2048), BF16)
        qi = pbuf("qi", (128, 2048), BF16)
        pr = pbuf("pr", (128, 2048), BF16)
        pi = pbuf("pi", (128, 2048), BF16)
        th = pbuf("th", (128, 2048), F32)
        nl = pbuf("nl", (128, 2048), F32)
        s0 = pbuf("s0", (128, 1024), F32)
        s1 = pbuf("s1", (128, 1024), F32)
        s2 = pbuf("s2", (128, 2048), BF16)
        s3 = pbuf("s3", (128, 2048), BF16)
        s4 = pbuf("s4", (128, 2048), BF16)
        xt0 = pbuf("xt0", (128, 2 * 1024), F32)
        h1f = pbuf("h1f", (128, 1024), F32)
        xn = pbuf("xn", (128, 1024), BF16)
        xnT = pbuf("xnT", (128, 4 * 1024), BF16)
        stat = pbuf("stat", (128, 12), F32)
        l1T = pbuf("l1T", (LORA, 512), BF16)
        vtmp = pbuf("vtmp", (128, 1024), BF16)

        emit_ln = make_emit_ln(stat, s0, s1, s2, s3)

        for si in range(n_tiles // ST):
            tts = [ST * si + jj for jj in range(ST)]
            xn_toks = {}
            for b in range(B):
                for j, tt in enumerate(tts):
                    slot = b * 2 + j
                    lx = p.op("S", lambda e, b=b, tt=tt: e.dma_start(
                        out=xt0[:, b * 1024:(b + 1) * 1024],
                        in_=xin[b, tt * 128:(tt + 1) * 128, :]),
                        deps=[p.tok("lns"), p.tok("phi")], dma="dx")
                    src = xt0[:, b * 1024:(b + 1) * 1024]
                    t_h1 = emit_ln(src, h1f[:], 0, dep=[lx])
                    p.op("G", lambda e, b=b, tt=tt: e.dma_start(
                        out=h1S[b, tt * 128:(tt + 1) * 128, :], in_=h1f[:]),
                        deps=[t_h1], dma="dh1")
                    t_xn = emit_ln(h1f[:], xn[:], 1, gain=g1t, bias=b1t,
                                   dep=[t_h1, p.tok("evT"), p.tok("dh1")], bn=6)
                    for r in range(2):
                        tms = []
                        for blk in range(4):
                            dc = r * 4 + blk
                            tm = p.op("T", lambda e, dc=dc, blk=blk: e.transpose(
                                psT[:, blk * 128:(blk + 1) * 128],
                                xn[:, dc * 128:(dc + 1) * 128], iden[:]),
                                deps=[t_xn, p.tok("xnt"), p.tok("evV")], mark=True)
                            tms.append(tm)
                        p.op("V", lambda e, slot=slot, r=r: e.tensor_copy(
                            xnT[:, slot * 1024 + r * 512: slot * 1024 + r * 512 + 512],
                            psT[:]), deps=tms, inc="xnt")
                    xn_toks[(b, j)] = p.tok("xnt")

            dep_all = [xn_toks[(b, j)] for b in range(B) for j in range(ST)] + [wtok]

            def mov_xnT(b, dc):
                return APx(xnT, (b * 2) * 1024 + dc * 128,
                           [[4096, 128], [1024, 2], [1, 128]])

            bank = [0]
            st_toks = []
            for b in range(B):
                for (plane, Wsb, boff, brAP) in [
                    (qr, Wq, 0, bq_r), (qi, Wq, 1, bq_i),
                    (pr, Wk, 0, bk_r), (pi, Wk, 1, bk_i),
                ]:
                    for c in range(4):
                        bk_ = bank[0] % 4
                        bank[0] += 1
                        for dc in range(8):
                            sel = APx(Wsb, dc * D + 256 * c + boff,
                                      [[8 * D, 128], [64, 4], [2, 32]])
                            p.op("T", lambda e, sel=sel, b=b, dc=dc, bk_=bk_: e.matmul(
                                psF[:, bk_, 0:256], sel, mov_xnT(b, dc),
                                skip_group_check=True, start=(dc == 0), stop=(dc == 7)),
                                deps=(dep_all + [p.tok("qkev")]) if dc == 0 else (),
                                mark=(dc == 7))
                        tm = p.tok("evT")
                        tk = p.op("V", lambda e, plane=plane, b=b, c=c, bk_=bk_, brAP=brAP:
                                  e.tensor_scalar_add(
                                      plane[:, b * 1024 + c * 256: b * 1024 + c * 256 + 256],
                                      psF[:, bk_, 0:256], brAP[:, c:c + 1]),
                                  deps=[tm], inc="qkev")

            lo_toks = []
            for b in range(B):
                for dc in range(8):
                    p.op("T", lambda e, b=b, dc=dc: e.matmul(
                        psX[0:48, 0:256], Tw1[:, dc * LORA:(dc + 1) * LORA],
                        mov_xnT(b, dc), skip_group_check=True, start=(dc == 0), stop=(dc == 7)),
                        deps=(dep_all + [p.tok("lo")]) if dc == 0 else (),
                        mark=(dc == 7))
                tm = p.tok("evT")
                tk = p.op("A", lambda e, b=b: e.activation(
                    l1T[0:48, b * 256:(b + 1) * 256], psX[0:48, 0:256], AF.Tanh),
                    deps=[tm], inc="lo")
                lo_toks.append(tk)

            th_toks = []
            for b in range(B):
                for c in range(4):
                    sel = APx(Tw2, 128 * c, [[512, LORA], [32, 4], [1, 32]])
                    p.op("T", lambda e, sel=sel, b=b: e.matmul(
                        psF[:, 4, 0:256], sel, l1T[0:48, b * 256:(b + 1) * 256],
                        skip_group_check=True, start=True, stop=True),
                        deps=lo_toks + [p.tok("thev")], mark=True)
                    tm = p.tok("evT")

                    def theta_ops(e, b=b, c=c):
                        off = b * 1024 + c * 256
                        e.tensor_scalar(th[:, off:off + 256], psF[:, 4, 0:256],
                                        thb_t[:, c:c + 1], PI / 2, ALU.add, ALU.min)
                        e.tensor_scalar_max(th[:, off:off + 256], th[:, off:off + 256],
                                            -PI / 2)
                        e.tensor_tensor(s0[:, 0:256], th[:, off:off + 256],
                                        th[:, off:off + 256], ALU.mult)
                        return e.tensor_scalar(nl[:, off:off + 256], s0[:, 0:256],
                                               etan_t[:, c:c + 1], lmbn_t[:, c:c + 1],
                                               ALU.mult, ALU.add)
                    tk = p.op("V", theta_ops, deps=[tm], inc="thev")
                    th_toks.append(tk)

            for b in range(B):
                for j, tt in enumerate(tts):
                    slot = b * 2 + j
                    for half in range(2):
                        for dc in range(8):
                            p.op("T", lambda e, slot=slot, dc=dc, half=half: e.matmul(
                                psF[:, 5, 0:512],
                                xnT[:, slot * 1024 + dc * 128: slot * 1024 + dc * 128 + 128],
                                Wv[:, dc * D + half * 512: dc * D + half * 512 + 512],
                                skip_group_check=True, start=(dc == 0), stop=False),
                                deps=(dep_all + [p.tok("vev")]) if dc == 0 else ())
                        p.op("T", lambda e, half=half: e.matmul(
                            psF[:, 5, 0:512], ones_row[0:1, :],
                            rows["abv"][0:1, half * 512: half * 512 + 512],
                            skip_group_check=True, start=False, stop=True), deps=[p.tok("evV")], mark=True)
                        tme = p.tok("evT")
                        p.op("V", lambda e, half=half: e.tensor_copy(
                            vtmp[:, half * 512: half * 512 + 512],
                            psF[:, 5, 0:512]), deps=[tme, p.tok("dvS")], inc="vev")
                    p.op("S", lambda e, b=b, tt=tt: e.dma_start(
                        out=vS[b, tt * 128:(tt + 1) * 128, :], in_=vtmp[:]),
                        deps=[p.tok("vev")], dma="dvS")

            phk = []
            for pl in (pr, pi):
                for half in range(2):
                    hs = slice(half * 1024, half * 1024 + 1024)
                    t1 = p.op("A", lambda e, pl=pl, hs=hs: e.activation(
                        s2[:, 0:1024], pl[:, hs], AF.Exp), deps=[p.tok("qkev")] + phk,
                        inc="phi")
                    t2 = p.op("V", lambda e, pl=pl, hs=hs: e.scalar_tensor_tensor(
                        s3[:, 0:1024], pl[:, hs], 1.0, s2[:, 0:1024],
                        ALU.add, ALU.subtract), deps=[t1], inc="phi")
                    t3 = p.op("V", lambda e, pl=pl, hs=hs: e.tensor_single_scalar(
                        s4[:, 0:1024], pl[:, hs], 0.0, ALU.is_gt),
                        deps=[p.tok("qkev")], inc="phi")
                    t4 = p.op("V", lambda e: e.tensor_mul(s3[:, 0:1024], s3[:, 0:1024],
                                                          s4[:, 0:1024]),
                              deps=[t2, t3], inc="phi")
                    t5 = p.op("V", lambda e, pl=pl, hs=hs: e.tensor_add(
                        pl[:, hs], s2[:, 0:1024], s3[:, 0:1024]), deps=[t4], inc="phi")
                    phk = [t5]

            def pl_slice(ten, b, j):
                return APx(ten, b * 1024 + j * 128, [[2048, 128], [256, 4], [1, 128]])

            def th_slice(ten, b, j):
                return APx(ten, b * 1024 + j * 128, [[2048, 128], [256, 4], [1, 128]])

            for b in range(B):
                for j, tt in enumerate(tts):
                    p.op("S", lambda e, b=b, j=j, tt=tt: e.dma_start(
                        out=qrS[b, tt], in_=pl_slice(qr, b, j)), deps=[p.tok("qkev")],
                        dma="dpl")
                    p.op("S", lambda e, b=b, j=j, tt=tt: e.dma_start(
                        out=qiS[b, tt], in_=pl_slice(qi, b, j)), dma="dpl")
                    p.op("S", lambda e, b=b, j=j, tt=tt: e.dma_start(
                        out=prS[b, tt], in_=pl_slice(pr, b, j)), deps=phk, dma="dpl")
                    p.op("S", lambda e, b=b, j=j, tt=tt: e.dma_start(
                        out=piS[b, tt], in_=pl_slice(pi, b, j)), dma="dpl")
                    p.op("S", lambda e, b=b, j=j, tt=tt: e.dma_start(
                        out=thS[b, tt], in_=th_slice(th, b, j)),
                        deps=[p.tok("thev")], dma="dpl")
                    p.op("S", lambda e, b=b, j=j, tt=tt: e.dma_start(
                        out=nlS[b, tt], in_=th_slice(nl, b, j)), dma="dpl")
            p.op("V", lambda e: e.engine_nop(), deps=[p.tok("dpl")], inc="qkev")
            p.op("V", lambda e: e.engine_nop(), deps=[p.tok("dpl")], inc="thev")
            p.op("V", lambda e: e.engine_nop(), deps=[p.tok("dpl")], inc="phi")

        pctx.close()

    phase1a()
    p.barrier()
    if max_phase < 2:
        p.replay()
        ctx.close()
        return nc, taps

    def phase1b():
        pctx = ExitStack()

        def pbuf(name, shape, dtype=F32):
            return pctx.enter_context(nc.sbuf_tensor("b_" + name, list(shape), dtype))

        Wo = pbuf("Wo", (128, 8 * D), BF16)
        for dc in range(8):
            p.op("G", lambda e, dc=dc: e.dma_start(
                out=Wo[:, dc * D:(dc + 1) * D],
                in_=dram["aWo"][dc * 128:(dc + 1) * 128, :]), dma="dwo")
        wtok = p.tok("dwo")

        qr = pbuf("qr", (128, 1024), BF16)
        qi = pbuf("qi", (128, 1024), BF16)
        pr = pbuf("pr", (128, 1024), BF16)
        pi = pbuf("pi", (128, 1024), BF16)
        th = pbuf("th", (128, 1024), F32)
        nl = pbuf("nl", (128, 1024), F32)
        mre = pbuf("mre", (128, 8 * 129), F32)
        mim = pbuf("mim", (128, 8 * 129), F32)
        s0 = pbuf("s0", (128, 1024), F32)
        s1 = pbuf("s1", (128, 1024), F32)
        s2 = pbuf("s2", (128, 1024), BF16)
        s3 = pbuf("s3", (128, 1024), BF16)
        s4 = pbuf("s4", (128, 1024), BF16)
        f16r = pbuf("f16r", (128, 1024), BF16)
        f16i = pbuf("f16i", (128, 1024), BF16)
        fdgc = pbuf("fdgc", (128, 1024), BF16)
        fdgs = pbuf("fdgs", (128, 1024), BF16)
        fqgc = pbuf("fqgc", (128, 1024), BF16)
        fqgs = pbuf("fqgs", (128, 1024), BF16)
        gsm = pbuf("gsm", (128, 30 * 64), F32)
        gsmb = pbuf("gsmb", (128, 32 * 64), BF16)
        LV = ("dg", "l16", "l32", "l64", "pv")
        Aq = {lv: pbuf("Aq_" + lv, (128, 1024), BF16) for lv in LV[:4]}
        Bq = {lv: pbuf("Bq_" + lv, (128, 1024), BF16) for lv in LV[:4]}
        Aqs = pbuf("Aq_st", (128, 1024), BF16)
        Bqs = pbuf("Bq_st", (128, 1024), BF16)
        Ck = {lv: pbuf("Ck_" + lv, (128, 1024), BF16) for lv in LV[:4]}
        Dk = {lv: pbuf("Dk_" + lv, (128, 1024), BF16) for lv in LV[:4]}
        vrg = pbuf("vrg", (128, 4 * 1024), BF16)
        Cst = pbuf("Cst", (128, 4 * 512), BF16)
        Dst = pbuf("Dst", (128, 4 * 512), BF16)
        h1l = pbuf("h1l", (128, 2 * 1024), BF16)
        xt1 = pbuf("xt1", (128, 1024), F32)
        Psb = {lv: pbuf("Psb" + lv, (128, 512), BF16) for lv in LV}
        yT = pbuf("yT", (128, 1024), BF16)
        x1f = pbuf("x1f", (128, 1024), F32)

        tap_y = tap("y", (B, NT, 128, D), BF16)
        tap_x1 = tap("x1", (B, NT, 128, D), F32)

        for ti in range(n_tiles):
            sl = ti % 2
            for b in range(B):
                l_fns = [
                    lambda e, b=b, ti=ti: e.dma_start(out=qr[:, b * 512:(b + 1) * 512],
                                                      in_=qrS[b, ti]),
                    lambda e, b=b, ti=ti: e.dma_start(out=qi[:, b * 512:(b + 1) * 512],
                                                      in_=qiS[b, ti]),
                    lambda e, b=b, ti=ti: e.dma_start(out=pr[:, b * 512:(b + 1) * 512],
                                                      in_=prS[b, ti]),
                    lambda e, b=b, ti=ti: e.dma_start(out=pi[:, b * 512:(b + 1) * 512],
                                                      in_=piS[b, ti]),
                    lambda e, b=b, ti=ti: e.dma_start(out=th[:, b * 512:(b + 1) * 512],
                                                      in_=thS[b, ti]),
                    lambda e, b=b, ti=ti: e.dma_start(out=nl[:, b * 512:(b + 1) * 512],
                                                      in_=nlS[b, ti]),
                    lambda e, b=b, ti=ti: e.dma_start(
                        out=h1l[:, b * 1024:(b + 1) * 1024],
                        in_=h1S[b, ti * 128:(ti + 1) * 128, :]),
                    lambda e, b=b, ti=ti, sl=sl: e.dma_start(
                        out=vrg[:, (b * 2 + sl) * 1024:(b * 2 + sl) * 1024 + 1024],
                        in_=vS[b, ti * 128:(ti + 1) * 128, :]),
                ]
                for i2, fn in enumerate(l_fns):
                    p.op("S", fn, deps=[p.tok("cmb"), p.tok("ymm"), p.tok("x1e")]
                         if i2 == 0 else (), dma="dld")
            ldtok = p.tok("dld")

            def scan_all(e):
                e.memset(APx(mre, 0, [[1032, 128], [129, 8], [1, 1]]), 0.0)
                e.memset(APx(mim, 0, [[1032, 128], [129, 8], [1, 1]]), 0.0)
                for b in range(B):
                    for c in range(4):
                        src = b * 512 + c * 128
                        dst = (b * 4 + c) * 129 + 1
                        e.tensor_tensor_scan(mre[:, dst:dst + 128],
                                             nl[:, src:src + 128], nl[:, src:src + 128],
                                             0.0, ALU.add, ALU.bypass)
                        inst = e.tensor_tensor_scan(mim[:, dst:dst + 128],
                                                    th[:, src:src + 128],
                                                    th[:, src:src + 128],
                                                    0.0, ALU.add, ALU.bypass)
                return inst
            t_scan = p.op("V", scan_all, deps=[ldtok, p.tok("gf"), p.tok("fb"), p.tok("dxo")], inc="msc")

            def mv4(m, off):
                return APx(m, off, [[1032, 128], [129, 8], [16, 8], [1, 16]])

            def mR(m, off):
                return APx(m, off, [[1032, 128], [129, 8], [16, 8], [0, 16]])

            def s_4(s, off=0):
                return APx(s, off, [[1024, 128], [128, 8], [16, 8], [1, 16]])

            def rangered(dep):
                t1 = p.op("V", lambda e: e.tensor_scalar(
                    s1[:], s0[:], 1.0 / (2 * PI), MAGIC, ALU.mult, ALU.add),
                    deps=dep, inc="rr")
                t2 = p.op("V", lambda e: e.tensor_scalar(
                    s1[:], s1[:], MAGIC, 2 * PI, ALU.subtract, ALU.mult), inc="rr")
                t3 = p.op("V", lambda e: e.tensor_sub(s0[:], s0[:], s1[:]), inc="rr")
                t4 = p.op("A", lambda e: e.activation(s3[:], s0[:], AF.Sin),
                          deps=[t3], inc="rr")
                t5 = p.op("A", lambda e: e.activation(s1[:], s0[:], AF.Abs),
                          deps=[t4], inc="rr")
                t6 = p.op("A", lambda e: e.activation(s4[:], s1[:], AF.Sin,
                                                      scale=-1.0, bias=pihalf[:]),
                          deps=[t5], inc="rr")
                return [t4, t6]

            tq1 = p.op("V", lambda e: e.tensor_tensor(s_4(s0), mv4(mre, 1), mR(mre, 0),
                                                      ALU.subtract), deps=[t_scan], inc="fb")
            tq2 = p.op("A", lambda e: e.activation(s2[:], s0[:], AF.Exp), deps=[tq1],
                       inc="fb")
            tq2b = p.op("V", lambda e: e.tensor_copy(f16r[:], s2[:]), deps=[tq2], inc="fb")
            tqe0 = p.op("V", lambda e: e.tensor_add(s1[:], s0[:], alphat[:]),
                        deps=[tq1], inc="fb")
            tqe1 = p.op("A", lambda e: e.activation(fqgc[:], s1[:], AF.Exp),
                        deps=[tqe0], inc="fb")
            tqd0 = p.op("V", lambda e: e.scalar_tensor_tensor(
                s1[:], s0[:], -1.0, alphat[:], ALU.mult, ALU.subtract),
                deps=[tqe1], inc="fb")
            tqd1 = p.op("A", lambda e: e.activation(s4[:], s1[:], AF.Exp),
                        deps=[tqd0], inc="fb")
            tqd2 = p.op("V", lambda e: e.tensor_copy(fdgc[:], s4[:]), deps=[tqd1], inc="fb")
            tq3b = p.op("V", lambda e: e.tensor_tensor(s_4(s0), mv4(mim, 1), mR(mim, 0),
                                                       ALU.subtract),
                        deps=[t_scan, tqd2], inc="fb")
            rr = rangered([tq3b])
            tqd3 = p.op("V", lambda e: e.tensor_mul(fdgs[:], fdgc[:], s3[:]),
                        deps=rr, inc="fb")
            tqd4 = p.op("V", lambda e: e.tensor_mul(fdgc[:], fdgc[:], s4[:]), inc="fb")
            tqe2 = p.op("V", lambda e: e.tensor_mul(fqgs[:], fqgc[:], s3[:]), inc="fb")
            tqe3 = p.op("V", lambda e: e.tensor_mul(fqgc[:], fqgc[:], s4[:]), inc="fb")
            tq4 = p.op("V", lambda e: e.tensor_mul(f16i[:], f16r[:], s3[:]),
                       deps=[tq2b] + rr, inc="fb")
            tq5 = p.op("V", lambda e: e.tensor_mul(f16r[:], f16r[:], s4[:]), inc="fb")
            ta1 = p.op("V", lambda e: e.tensor_mul(s2[:], qr[:], f16r[:]),
                       deps=[tq4, tq5, p.tok("evT")], inc="cmb")
            ta2 = p.op("V", lambda e: e.tensor_mul(s3[:], qi[:], f16i[:]), inc="cmb")
            ta3 = p.op("V", lambda e: e.tensor_add(Aq["l16"][:], s2[:], s3[:]), inc="cmb")
            tb1 = p.op("V", lambda e: e.tensor_mul(s2[:], qi[:], f16r[:]), inc="cmb")
            tb2 = p.op("V", lambda e: e.tensor_mul(s3[:], qr[:], f16i[:]), inc="cmb")
            tb3 = p.op("V", lambda e: e.tensor_sub(Bq["l16"][:], s2[:], s3[:]), inc="cmb")
            td1a = p.op("V", lambda e: e.tensor_mul(s2[:], qr[:], fqgc[:]),
                        deps=[tb3], inc="cmb")
            td1b = p.op("V", lambda e: e.tensor_mul(s3[:], qi[:], fqgs[:]), inc="cmb")
            td1 = p.op("V", lambda e: e.tensor_add(Aq["dg"][:], s2[:], s3[:]), inc="cmb")
            td2a = p.op("V", lambda e: e.tensor_mul(s2[:], qi[:], fqgc[:]), inc="cmb")
            td2b = p.op("V", lambda e: e.tensor_mul(s3[:], qr[:], fqgs[:]), inc="cmb")
            td2 = p.op("V", lambda e: e.tensor_sub(Bq["dg"][:], s2[:], s3[:]), inc="cmb")

            tk1 = p.op("V", lambda e: e.tensor_tensor(s_4(s0), mR(mre, 16), mv4(mre, 1),
                                                      ALU.subtract),
                       deps=[t_scan, td1, td2, tb3], inc="fb")
            tk2 = p.op("A", lambda e: e.activation(s2[:], s0[:], AF.Exp), deps=[tk1],
                       inc="fb")
            tk2b = p.op("V", lambda e: e.tensor_copy(f16r[:], s2[:]), deps=[tk2], inc="fb")
            tk3 = p.op("V", lambda e: e.tensor_tensor(s_4(s0), mR(mim, 16), mv4(mim, 1),
                                                      ALU.subtract), deps=[tk2b], inc="fb")
            rrk = rangered([tk3])
            tk4 = p.op("V", lambda e: e.tensor_mul(f16i[:], f16r[:], s3[:]),
                       deps=[tk2b] + rrk, inc="fb")
            tk5 = p.op("V", lambda e: e.tensor_mul(f16r[:], f16r[:], s4[:]), inc="fb")
            tc1 = p.op("V", lambda e: e.tensor_mul(s2[:], pr[:], f16r[:]),
                       deps=[tk4, tk5, p.tok("evT")], inc="cmb")
            tc2 = p.op("V", lambda e: e.tensor_mul(s3[:], pi[:], f16i[:]), inc="cmb")
            tc3 = p.op("V", lambda e: e.tensor_sub(Ck["l16"][:], s2[:], s3[:]), inc="cmb")
            te1 = p.op("V", lambda e: e.tensor_mul(s2[:], pr[:], f16i[:]), inc="cmb")
            te2 = p.op("V", lambda e: e.tensor_mul(s3[:], pi[:], f16r[:]), inc="cmb")
            te3 = p.op("V", lambda e: e.tensor_add(Dk["l16"][:], s2[:], s3[:]), inc="cmb")
            tf0a = p.op("V", lambda e: e.tensor_mul(s2[:], pr[:], fdgc[:]),
                        deps=[te3, tqd3, tqd4], inc="cmb")
            tf0b = p.op("V", lambda e: e.tensor_mul(s3[:], pi[:], fdgs[:]), inc="cmb")
            tf1 = p.op("V", lambda e: e.tensor_add(Ck["dg"][:], s2[:], s3[:]), inc="cmb")
            tf0c = p.op("V", lambda e: e.tensor_mul(s2[:], pi[:], fdgc[:]), inc="cmb")
            tf0d = p.op("V", lambda e: e.tensor_mul(s3[:], pr[:], fdgs[:]), inc="cmb")
            tf2 = p.op("V", lambda e: e.tensor_sub(Dk["dg"][:], s2[:], s3[:]), inc="cmb")
            base_tok = p.tok("cmb")

            def gsub_ops(kind):
                if kind == "q32":
                    return [(w, [[8, 8], [2, 4]], 16 * w, [[129, 8], [32, 4]],
                             0, [[129, 8], [32, 4]]) for w in range(2)]
                if kind == "k32":
                    return [(w, [[8, 8], [2, 4]], 32, [[129, 8], [32, 4]],
                             16 + 16 * w, [[129, 8], [32, 4]]) for w in range(2)]
                if kind == "q64":
                    return [(2 * v, [[8, 8], [4, 2], [1, 2]],
                             32 * v, [[129, 8], [64, 2], [0, 2]],
                             0, [[129, 8], [64, 2], [0, 2]]) for v in range(2)]
                if kind == "k64":
                    return [(2 * v, [[8, 8], [4, 2], [1, 2]],
                             64, [[129, 8], [64, 2], [0, 2]],
                             32 * v + 32, [[129, 8], [64, 2], [0, 2]]) for v in range(2)]
                if kind == "qst":
                    return [(0, [[8, 8], [4, 2], [1, 4]],
                             0, [[129, 8], [64, 2], [0, 4]],
                             0, [[129, 8], [0, 2], [0, 4]])]
                if kind == "kst":
                    return [(0, [[8, 8], [4, 2], [1, 4]],
                             128, [[129, 8], [0, 2], [0, 4]],
                             64, [[129, 8], [64, 2], [0, 4]])]
                raise ValueError(kind)

            KINDS = [("q32", 0, 1), ("q64", 2, 3), ("qst", 4, 5),
                     ("k32", 6, 7), ("k64", 8, 9), ("kst", 10, 11)]
            NKI = len(KINDS)

            def gsl(ki, j):
                o = (ki * 5 + j) * 64
                return gsm[:, o:o + 64]

            def mk_subs(kind, ki, m):
                def do(e, kind=kind, ki=ki, m=m):
                    inst = None
                    for (doff, dpat, hoff, hpat, loff, lpat) in gsub_ops(kind):
                        dst = APx(gsm, ki * 320 + doff, [[1920, 128]] + dpat)
                        hi = APx(m, hoff, [[1032, 128]] + hpat)
                        lo = APx(m, loff, [[1032, 128]] + lpat)
                        inst = e.tensor_tensor(dst, hi, lo, ALU.subtract)
                    return inst
                return do

            smb = lambda i: gsmb[:, i * 64:(i + 1) * 64]
            u1t, u2t, u3t, u4t, u5t, u6t, u7t, u8t, u10t = ({} for _ in range(9))
            for ki, (kind, slr, sli) in enumerate(KINDS):
                u1t[ki] = p.op("V", mk_subs(kind, ki, mre),
                               deps=([t_scan, p.tok("gf")] if ki == 0 else ()),
                               inc="gf")
            for ki in range(NKI):
                u2t[ki] = p.op("A", lambda e, ki=ki: e.activation(
                    gsl(ki, 1), gsl(ki, 0), AF.Exp), deps=[u1t[ki]], inc="gf")
            for ki, (kind, slr, sli) in enumerate(KINDS):
                u3t[ki] = p.op("V", mk_subs(kind, ki, mim), deps=[u2t[ki]], inc="gf")
            for ki in range(NKI):
                u4t[ki] = p.op("A", lambda e, ki=ki: e.activation(
                    gsl(ki, 2), gsl(ki, 0), AF.Copy, scale=1.0 / (2 * PI),
                    bias=MAGIC), deps=[u3t[ki]], inc="gf")
            for ki in range(NKI):
                u5t[ki] = p.op("A", lambda e, ki=ki: e.activation(
                    gsl(ki, 3), gsl(ki, 0), AF.Copy, scale=1.0 / (2 * PI)),
                    deps=[u3t[ki]], inc="gf")
            for ki in range(NKI):
                u6t[ki] = p.op("V", lambda e, ki=ki: e.scalar_tensor_tensor(
                    gsl(ki, 4), gsl(ki, 2), MAGIC, gsl(ki, 3),
                    ALU.subtract, ALU.subtract), deps=[u4t[ki], u5t[ki]], inc="gf")
            for ki, (kind, slr, sli) in enumerate(KINDS):
                u7t[ki] = p.op("A", lambda e, ki=ki, sli=sli: e.activation(
                    smb(sli), gsl(ki, 4), AF.Sin, scale=-2 * PI),
                    deps=[u6t[ki]] + ([p.tok("ch")] if ki == 0 else []), inc="gf")
            for ki in range(NKI):
                u8t[ki] = p.op("A", lambda e, ki=ki: e.activation(
                    gsl(ki, 2), gsl(ki, 4), AF.Abs), deps=[u6t[ki]], inc="gf")
            for ki, (kind, slr, sli) in enumerate(KINDS):
                u10t[ki] = p.op("A", lambda e, ki=ki, slr=slr: e.activation(
                    smb(slr), gsl(ki, 2), AF.Sin, scale=-2 * PI,
                    bias=pihalf[:]), deps=[u8t[ki]], inc="gf")
            gq_out = {}
            for ki, (kind, slr, sli) in enumerate(KINDS):
                ta = p.op("V", lambda e, ki=ki, sli=sli: e.tensor_mul(
                    smb(sli), smb(sli), gsl(ki, 1)),
                    deps=[u2t[ki], u7t[ki]], inc="gf")
                tb = p.op("V", lambda e, ki=ki, slr=slr: e.tensor_mul(
                    smb(slr), smb(slr), gsl(ki, 1)),
                    deps=[u10t[ki]], inc="gf")
                gq_out[kind] = [ta, tb]
            gq32, gq64, gqst = gq_out["q32"], gq_out["q64"], gq_out["qst"]
            gk32, gk64, gkst = gq_out["k32"], gq_out["k64"], gq_out["kst"]

            def gbb(slot):
                return APx(gsmb, slot * 64, [[2048, 128], [8, 8], [1, 8], [0, 16]])

            def chain_q(Asrc, Bsrc, Adst, Bdst, slr, sli, dep):
                t1 = p.op("V", lambda e: e.tensor_mul(s2[:], Asrc[:], gbb(slr)),
                          deps=list(dep) + [p.tok("evT")], inc="ch")
                t2 = p.op("V", lambda e: e.tensor_mul(s3[:], Bsrc[:], gbb(sli)), inc="ch")
                t3 = p.op("V", lambda e: e.tensor_add(Adst[:], s2[:], s3[:]), inc="ch")
                t4 = p.op("V", lambda e: e.tensor_mul(s2[:], Bsrc[:], gbb(slr)), inc="ch")
                t5 = p.op("V", lambda e: e.tensor_mul(s3[:], Asrc[:], gbb(sli)), inc="ch")
                t6 = p.op("V", lambda e: e.tensor_sub(Bdst[:], s2[:], s3[:]), inc="ch")
                return [t3, t6]

            def chain_k(Csrc, Dsrc, CdstAP, DdstAP, slr, sli, dep):
                t1 = p.op("V", lambda e: e.tensor_mul(s2[:], Csrc[:], gbb(slr)),
                          deps=list(dep) + [p.tok("evT")], inc="ch")
                t2 = p.op("V", lambda e: e.tensor_mul(s3[:], Dsrc[:], gbb(sli)), inc="ch")
                t3 = p.op("V", lambda e: e.tensor_tensor(CdstAP, s2[:], s3[:],
                                                         ALU.subtract), inc="ch")
                t4 = p.op("V", lambda e: e.tensor_mul(s2[:], Dsrc[:], gbb(slr)), inc="ch")
                t5 = p.op("V", lambda e: e.tensor_mul(s3[:], Csrc[:], gbb(sli)), inc="ch")
                t6 = p.op("V", lambda e: e.tensor_tensor(DdstAP, s2[:], s3[:], ALU.add),
                          inc="ch")
                return [t3, t6]

            cq32 = chain_q(Aq["l16"], Bq["l16"], Aq["l32"], Bq["l32"], 0, 1,
                           gq32 + [base_tok])
            cq64 = chain_q(Aq["l32"], Bq["l32"], Aq["l64"], Bq["l64"], 2, 3, gq64 + cq32)
            cqst = chain_q(Aq["l64"], Bq["l64"], Aqs, Bqs, 4, 5, gqst + cq64)
            ck32 = chain_k(Ck["l16"], Dk["l16"], Ck["l32"][:], Dk["l32"][:], 6, 7,
                           gk32 + [base_tok])
            ck64 = chain_k(Ck["l32"], Dk["l32"], Ck["l64"][:], Dk["l64"][:], 8, 9,
                           gk64 + ck32)
            CstV = APx(Cst, sl * 512, [[2048, 128], [1024, 2], [128, 4], [1, 128]])
            DstV = APx(Dst, sl * 512, [[2048, 128], [1024, 2], [128, 4], [1, 128]])
            ckst = chain_k(Ck["l64"], Dk["l64"], CstV, DstV, 10, 11,
                           gkst + ck64 + [p.tok("evT")])

            masks = {"dg": mask_diag, "l16": mask_16, "l32": mask_32, "l64": mask_64}
            lv_ready = {"dg": [td1, td2, tf1, tf2], "l16": [base_tok],
                        "l32": cq32 + ck32, "l64": cq64 + ck64,
                        "pv": cqst + ckst}
            gctr = [0]
            for b in range(B):
                for c in range(4):
                    ev_toks = {}
                    for lv in LV:
                        if lv == "pv" and ti == 0:
                            continue
                        g = gctr[0]
                        gctr[0] += 1
                        colg = (g % 4) * 128
                        foff = b * 512 + c * 128
                        if lv == "pv":
                            soff = (b * 2 + ((ti - 1) % 2)) * 512 + c * 128
                            Ssrc, Msrc = [Cst, Dst], [Aqs, Bqs]
                        else:
                            soff = foff
                            Ssrc, Msrc = [Ck[lv], Dk[lv]], [Aq[lv], Bq[lv]]
                        first = True
                        for pi_ in range(2):
                            for hc in range(4):
                                Ssl = Ssrc[pi_][32 * hc:32 * hc + 32, soff:soff + 128]
                                Msl = Msrc[pi_][32 * hc:32 * hc + 32, foff:foff + 128]
                                dps = [d for d in (lv_ready[lv] + [p.tok("pev"),
                                                                   p.tok("x1e")])
                                       if d is not None] if first and hc == 0 else ()
                                p.op("T", lambda e, Ssl=Ssl, Msl=Msl, hc=hc, colg=colg,
                                     first=first: e.matmul(
                                         psF[:, hc, colg:colg + 128], Ssl, Msl,
                                         skip_group_check=True, start=first, stop=(not first),
                                         tile_position=(32 * hc, 0)),
                                     deps=dps, mark=(pi_ == 1 and hc == 3))
                            first = False
                        tmm = p.tok("evT")
                        if lv == "pv":
                            def evac_pv(e, colg=colg):
                                return e.activation(
                                    APx(Psb["pv"], 0,
                                        [[512, 128], [128, 4], [1, 128]]),
                                    APx(psF, colg,
                                        [[6 * 512, 128], [512, 4], [1, 128]]),
                                    AF.Copy)
                            tk = p.op("A", evac_pv, deps=[tmm, p.tok("ymm")], inc="pev")
                        else:
                            def evac_lv(e, colg=colg, lv=lv):
                                return e.tensor_tensor(
                                    APx(Psb[lv], 0,
                                        [[512, 128], [128, 4], [1, 128]]),
                                    APx(psF, colg,
                                        [[6 * 512, 128], [512, 4], [1, 128]]),
                                    APx(masks[lv], 0,
                                        [[512, 128], [128, 4], [1, 128]]),
                                    ALU.mult)
                            tk = p.op("V", evac_lv, deps=[tmm, p.tok("ymm")], inc="pev")
                        ev_toks[lv] = tk
                    ydeps = [ev_toks.get(lv) for lv in LV if ev_toks.get(lv)]
                    vsl = b * 2 + sl
                    pvs = b * 2 + ((ti - 1) % 2)
                    for hc in range(4):
                        h = 4 * c + hc
                        yb = 4 + (h // 8)
                        ycol = ((h % 8) // 2) * 128
                        pofs = 64 * (h % 2)
                        lvlist = ["dg", "l16", "l32", "l64"] + (["pv"] if ti > 0 else [])
                        for li, lv in enumerate(lvlist):
                            vs = pvs if lv == "pv" else vsl
                            vst = vrg[:, vs * 1024 + 64 * h: vs * 1024 + 64 * h + 64]
                            mv = Psb[lv][:, hc * 128:hc * 128 + 128]
                            p.op("T", lambda e, vst=vst, mv=mv, yb=yb, ycol=ycol,
                                 li=li, pofs=pofs, last=(li == len(lvlist) - 1):
                                 e.matmul(
                                     psF[pofs:pofs + 64, yb, ycol:ycol + 128],
                                     vst, mv, skip_group_check=True, start=(li == 0), stop=last,
                                     tile_position=(0, pofs)),
                                 deps=(ydeps + [ldtok, p.tok("yev")]) if li == 0 else (),
                                 inc="ymm")
                ycht = [p.tok("ymm"), p.tok("evT")]
                te1 = p.op("V", lambda e: e.tensor_copy(yT[:, 0:512], psF[:, 4, :]),
                           deps=ycht, inc="yev")
                te2 = p.op("V", lambda e: e.tensor_copy(yT[:, 512:1024], psF[:, 5, :]),
                           deps=ycht, inc="yev")
                if tap_y is not None:
                    p.op("S", lambda e, b=b, ti=ti: e.dma_start(
                        out=taps["y"][b, ti], in_=yT[:]), deps=[te1, te2], dma="dtap")
                for half in range(2):
                    for dc in range(8):
                        p.op("T", lambda e, dc=dc, half=half: e.matmul(
                            psF[:, half, 0:512],
                            yT[:, dc * 128:(dc + 1) * 128],
                            Wo[:, dc * D + half * 512: dc * D + half * 512 + 512],
                            skip_group_check=True, start=(dc == 0), stop=False),
                            deps=[te1, te2, p.tok("x1e"), wtok] if dc == 0 else ())
                    p.op("T", lambda e, half=half: e.matmul(
                        psF[:, half, 0:512], ones_row[0:1, :],
                        rows["abo"][0:1, half * 512: half * 512 + 512],
                        skip_group_check=True, start=False, stop=True), mark=True)
                two = p.tok("evT")
                tx1 = p.op("V", lambda e, b=b: e.tensor_copy(
                    s1[:], h1l[:, b * 1024:(b + 1) * 1024]),
                    deps=[two, p.tok("dxs"), p.tok("dtap")], inc="x1e")
                tx2 = p.op("V", lambda e: e.tensor_tensor(
                    x1f[:, 0:512], psF[:, 0, 0:512], s1[:, 0:512], ALU.add),
                    deps=[tx1], inc="x1e")
                tx3 = p.op("V", lambda e: e.tensor_tensor(
                    x1f[:, 512:1024], psF[:, 1, 0:512], s1[:, 512:1024], ALU.add),
                    inc="x1e")
                if tap_x1 is not None:
                    p.op("S", lambda e, b=b, ti=ti: e.dma_start(
                        out=taps["x1"][b, ti], in_=x1f[:]), deps=[tx2, tx3], dma="dtap")
                p.op("G", lambda e, b=b, ti=ti: e.dma_start(
                    out=x1s[b, ti * 128:(ti + 1) * 128, :], in_=x1f[:]),
                    deps=[tx2, tx3], dma="dxs")
                lx2 = p.op("S", lambda e, b=b, ti=ti: e.dma_start(
                    out=xt1[:], in_=xin[b, ti * 128:(ti + 1) * 128, :]),
                    deps=[tx2, tx3, p.tok("x1e")], dma="dx2")
                tx4 = p.op("V", lambda e: e.tensor_add(s0[:], x1f[:], xt1[:]),
                           deps=[lx2], inc="x1e")
                p.op("S", lambda e, b=b, ti=ti: e.dma_start(
                    out=xout[b, ti * 128:(ti + 1) * 128, :], in_=s0[:]),
                    deps=[tx4], dma="dxo")

        pctx.close()

    phase1b()
    p.barrier()

    def phase2():
        pctx = ExitStack()

        def pbuf(name, shape, dtype=F32):
            return pctx.enter_context(nc.sbuf_tensor("c_" + name, list(shape), dtype))

        Wf1 = pbuf("Wf1", (128, 8 * FFN), BF16)
        Wf2 = pbuf("Wf2", (128, 32 * D), BF16)
        for jc in range(8):
            p.op("G", lambda e, jc=jc: e.dma_start(
                out=Wf1[:, jc * FFN:(jc + 1) * FFN],
                in_=dram["aWf1"][jc * 128:(jc + 1) * 128, :]), dma="dwf")
        for fc in range(32):
            p.op("G", lambda e, fc=fc: e.dma_start(
                out=Wf2[:, fc * D:(fc + 1) * D],
                in_=dram["aWf2"][fc * 128:(fc + 1) * 128, :]), dma="dwf")
        wtok2 = p.tok("dwf")

        x1l = pbuf("x1l", (128, 1024), F32)
        xol = pbuf("xol", (128, 2 * 1024), F32)
        h2 = pbuf("h2", (128, 1024), BF16)
        h2T = pbuf("h2T", (128, 1024), BF16)
        hg = pbuf("hg", (128, FFN), BF16)
        xs1f = pbuf("xs1f", (128, 1024), F32)
        stat = pbuf("stat", (128, 24), F32)
        s0 = pbuf("s0", (128, 1024), F32)
        s1 = pbuf("s1", (128, 1024), F32)
        eln = make_emit_ln(stat, s0, s1, s1, s1)
        tap_xs1 = tap("xs1", (B, NT, 128, D), F32)
        evq = []
        seq = [(ti, b) for ti in range(n_tiles) for b in range(B)]
        ctxs = {}

        def frontV(i):
            ti, b = seq[i]
            pr = (i % 2) * 1024
            lx = p.op("S", lambda e, b=b, ti=ti: e.dma_start(
                out=x1l[:],
                in_=x1s[b, ti * 128:(ti + 1) * 128, :]),
                deps=[p.tok("lns")], dma="dl2")
            lxo = p.op("S", lambda e, b=b, ti=ti, pr=pr: e.dma_start(
                out=xol[:, pr:pr + 1024],
                in_=xout[b, ti * 128:(ti + 1) * 128, :]),
                deps=[p.tok("x2e")], dma="dl2")
            t_h2 = eln(x1l[:], h2[:], 1, gain=g2t, bias=b2t,
                       dep=[lx] + ctxs.get(i - 1, {}).get("tms", []),
                       bn=12 * (i % 2))
            ctxs[i] = {"t_h2": t_h2, "lxo": lxo, "pr": pr}

        def frontT(i):
            c = ctxs[i]
            tms, ttoks = [], []
            for r in range(2):
                tms_r = []
                for blk in range(4):
                    dc = r * 4 + blk
                    tm = p.op("T", lambda e, dc=dc, blk=blk: e.transpose(
                        psT[:, blk * 128:(blk + 1) * 128],
                        h2[:, dc * 128:(dc + 1) * 128], iden[:]),
                        deps=[c["t_h2"], p.tok("xnt")], mark=True)
                    tms_r.append(tm)
                tms += tms_r
                ttoks.append(p.op("V", lambda e, r=r: e.tensor_copy(
                    h2T[:, r * 512:(r + 1) * 512], psT[:]),
                    deps=tms_r, inc="xnt"))
            c["tms"] = tms
            c["ttoks"] = ttoks

        def back(i):
            ti, b = seq[i]
            c = ctxs[i]
            pr = c["pr"]
            f1toks = {}
            for fb in range(32):
                bank, col = fb % 4, ((fb // 4) % 4) * 128
                for jc in range(8):
                    dps = []
                    if jc == 0:
                        if fb == 0:
                            dps += c["ttoks"] + [wtok2]
                        if len(evq) >= 4:
                            dps.append(evq[-4])
                    p.op("T", lambda e, fb=fb, jc=jc, bank=bank, col=col: e.matmul(
                        psF[:, bank, col:col + 128],
                        Wf1[:, jc * FFN + fb * 128: jc * FFN + fb * 128 + 128],
                        h2T[:, jc * 128:(jc + 1) * 128],
                        skip_group_check=True, start=(jc == 0), stop=(jc == 7)),
                        deps=dps, mark=(jc == 7))
                tmm = p.tok("evT")
                f1toks[fb] = p.op("A", lambda e, fb=fb, bank=bank, col=col:
                                  e.activation(
                                      hg[:, fb * 128:(fb + 1) * 128],
                                      psF[:, bank, col:col + 128],
                                      AF.Gelu_apprx_tanh,
                                      bias=bf1c[:, fb:fb + 1]),
                                  deps=[tmm], inc="f1e")
                evq.append(f1toks[fb])
            for half in range(2):
                for fc in range(32):
                    dps = ([f1toks[fb] for fb in range(32)] + [p.tok("x2e")]) \
                        if fc == 0 else ()
                    p.op("T", lambda e, half=half, fc=fc: e.matmul(
                        psF[:, 4 + half, 0:512],
                        hg[:, fc * 128:(fc + 1) * 128],
                        Wf2[:, fc * D + half * 512: fc * D + half * 512 + 512],
                        skip_group_check=True, start=(fc == 0), stop=False),
                        deps=dps)
                p.op("T", lambda e, half=half: e.matmul(
                    psF[:, 4 + half, 0:512], ones_row[0:1, :],
                    rows["abf2"][0:1, half * 512: half * 512 + 512],
                    skip_group_check=True, start=False, stop=True), mark=True)
            t2w = p.tok("evT")
            tx1 = p.op("V", lambda e, pr=pr: e.tensor_tensor(
                xs1f[:, 0:512], psF[:, 4, 0:512], xol[:, pr:pr + 512],
                ALU.add),
                deps=[t2w, c["lxo"], p.tok("dso"), p.tok("dtap")], inc="x2e")
            tx2 = p.op("V", lambda e, pr=pr: e.tensor_tensor(
                xs1f[:, 512:1024], psF[:, 5, 0:512],
                xol[:, pr + 512:pr + 1024], ALU.add), inc="x2e")
            if tap_xs1 is not None:
                p.op("S", lambda e, b=b, ti=ti: e.dma_start(
                    out=taps["xs1"][b, ti], in_=xs1f[:]),
                    deps=[tx1, tx2], dma="dtap")
            p.op("S", lambda e, b=b, ti=ti: e.dma_start(
                out=xout[b, ti * 128:(ti + 1) * 128, :], in_=xs1f[:]),
                deps=[tx1, tx2, p.tok("dtap")], dma="dso")

        frontV(0)
        frontT(0)
        for i in range(len(seq)):
            if i + 1 < len(seq):
                frontV(i + 1)
            back(i)
            if i + 1 < len(seq):
                frontT(i + 1)
        pctx.close()

    if max_phase >= 2:
        phase2()
        p.barrier()

    def phase0():
        pctx = ExitStack()

        def pbuf(name, shape, dtype=F32):
            return pctx.enter_context(nc.sbuf_tensor("d_" + name, list(shape), dtype))

        wt = pbuf("wt", (128, 2 * FFN), F32)
        wa = pbuf("wa", (128, 2 * FFN), F32)
        acc = pbuf("acc", (128, 40), F32)
        acc2 = pbuf("acc2", (128, 40), F32)
        tn = pbuf("tn", (128, 2 * FFN), BF16)
        tb = pbuf("tb", (128, 2 * FFN), BF16)
        sc2 = pbuf("sc2", (128, 4), F32)
        scB = pbuf("scB", (1, 2), F32)

        def stats(src_name, nrt, ncols, col0):
            atoks, rtoks = [], []
            for rt in range(nrt):
                sl = rt % 2
                la = p.op("S", lambda e, rt=rt, sl=sl, src_name=src_name,
                          ncols=ncols: e.dma_start(
                              out=wt[:, sl * FFN: sl * FFN + ncols],
                              in_=dram[src_name][rt * 128:(rt + 1) * 128, :]),
                          deps=[atoks[rt - 2]] if rt >= 2 else [p.tok("p0a")],
                          dma="dp0l")
                atoks.append(p.op("A", lambda e, ncols=ncols, sl=sl: e.activation(
                    wa[:, sl * FFN: sl * FFN + ncols],
                    wt[:, sl * FFN: sl * FFN + ncols], AF.Abs),
                    deps=[la] + ([rtoks[rt - 2]] if rt >= 2
                                 else [p.tok("p0r")]), inc="p0a"))
                rtoks.append(p.op("V", lambda e, rt=rt, ncols=ncols, col0=col0,
                                  sl=sl:
                                  e.reduce_sum(acc[:, col0 + rt:col0 + rt + 1],
                                               wa[:, sl * FFN: sl * FFN + ncols],
                                               axis=AX.X),
                                  deps=[atoks[rt]], inc="p0r"))

        stats("fw1", 32, D, 0)
        stats("fw2", 8, FFN, 32)

        def scalar_pipeline(col0, n, wcol, pc, si):
            ca = p.op("A", lambda e, col0=col0, n=n: e.activation(
                acc2[:, col0:col0 + n], acc[:, col0:col0 + n], AF.Copy),
                deps=[p.tok("p0r")], inc="p0s")
            cv = p.op("V", lambda e, col0=col0, n=n, si=si: e.reduce_sum(
                sc2[:, si:si + 1], acc2[:, col0:col0 + n], axis=AX.X),
                deps=[ca], inc="p0s")
            tm1 = p.op("T", lambda e, pc=pc, si=si: e.matmul(
                psX[0:1, pc:pc + 1], sc2[:, si:si + 1], onec[:],
                skip_group_check=True, start=True, stop=True),
                deps=[cv, p.tok("p0s")], mark=True)
            cc1 = p.op("V", lambda e, pc=pc, si=si: e.tensor_copy(
                scB[0:1, si // 2:si // 2 + 1], psX[0:1, pc:pc + 1]),
                deps=[tm1], inc="p0s")
            tm2 = p.op("T", lambda e, pc=pc, si=si: e.matmul(
                psX[:, pc + 8:pc + 9], onef[0:1, :],
                scB[0:1, si // 2:si // 2 + 1],
                skip_group_check=True, start=True, stop=True), deps=[cc1], mark=True)
            cc2 = p.op("A", lambda e, pc=pc, si=si: e.activation(
                sc2[:, si:si + 1], psX[:, pc + 8:pc + 9], AF.Copy,
                scale=1.0 / (D * FFN)), deps=[tm2], inc="p0s")
            cm = p.op("V", lambda e, wcol=wcol, si=si: e.tensor_scalar_max(
                wsc[:, wcol:wcol + 1], sc2[:, si:si + 1], 1e-5),
                deps=[cc2], inc="p0s")
            ch = p.op("A", lambda e, wcol=wcol, si=si: e.activation(
                sc2[:, si + 1:si + 2], wsc[:, wcol:wcol + 1], AF.Copy),
                deps=[cm], inc="p0s")
            p.op("V", lambda e, wcol=wcol, si=si: e.reciprocal(
                wsc[:, wcol + 1:wcol + 2], sc2[:, si + 1:si + 2]),
                deps=[ch], inc="p0s")

        scalar_pipeline(0, 32, 0, 0, 0)
        scalar_pipeline(32, 8, 2, 4, 2)
        sc_tok = p.tok("p0s")

        def quant_store(src_name, nrt, ncols, wcol, dstT, vtoks, ttoks, dtoks):
            nblk = ncols // 128
            for rt in range(nrt):
                sl = rt % 2
                la = p.op("S", lambda e, rt=rt, sl=sl, src_name=src_name,
                          ncols=ncols: e.dma_start(
                              out=wt[:, sl * FFN: sl * FFN + ncols],
                              in_=dram[src_name][rt * 128:(rt + 1) * 128, :]),
                          deps=[vtoks[-2]] if len(vtoks) >= 2
                          else [p.tok("p0q")], dma="dp0l")
                v1 = p.op("V", lambda e, sl=sl, ncols=ncols, wcol=wcol:
                          e.tensor_scalar(wa[:, sl * FFN: sl * FFN + ncols],
                                          wt[:, sl * FFN: sl * FFN + ncols],
                                          wsc[:, wcol + 1:wcol + 2], MAGIC,
                                          ALU.mult, ALU.add),
                          deps=[la, sc_tok] + ([ttoks[-2]] if len(ttoks) >= 2
                                               else [p.tok("evT")]), inc="p0q")
                vtoks.append(v1)
                p.op("V", lambda e, sl=sl, ncols=ncols: e.tensor_scalar(
                    wa[:, sl * FFN: sl * FFN + ncols],
                    wa[:, sl * FFN: sl * FFN + ncols], MAGIC, 1.0,
                    ALU.subtract, ALU.min), inc="p0q")
                v3 = p.op("V", lambda e, sl=sl, ncols=ncols: e.tensor_scalar_max(
                    tn[:, sl * FFN: sl * FFN + ncols],
                    wa[:, sl * FFN: sl * FFN + ncols], -1.0), inc="p0q")
                tmlast = None
                for dc in range(nblk):
                    blk = dc % 4
                    tm = p.op("T", lambda e, dc=dc, blk=blk, sl=sl: e.transpose(
                        psT[:, blk * 128:(blk + 1) * 128],
                        tn[:, sl * FFN + dc * 128: sl * FFN + (dc + 1) * 128],
                        iden[:]),
                        deps=[v3, p.tok("xnt")], mark=True)
                    cv = p.op("V", lambda e, dc=dc, blk=blk, sl=sl: e.tensor_copy(
                        tb[:, sl * FFN + dc * 128: sl * FFN + (dc + 1) * 128],
                        psT[:, blk * 128:(blk + 1) * 128]),
                        deps=[tm] + ([dtoks[-2]] if len(dtoks) >= 2 and dc == 0
                                     else []), inc="xnt")
                    dt_ = p.op("G", lambda e, dc=dc, rt=rt, dstT=dstT, sl=sl:
                               e.dma_start(
                                   out=dstT[dc * 128:(dc + 1) * 128,
                                            rt * 128:(rt + 1) * 128],
                                   in_=tb[:, sl * FFN + dc * 128:
                                          sl * FFN + (dc + 1) * 128]),
                               deps=[cv], dma="dp0")
                    tmlast = tm
                ttoks.append(tmlast)
                dtoks.append(dt_)

        vtoks, ttoks, dtoks = [], [], []
        quant_store("fw1", 32, D, 0, w1tq, vtoks, ttoks, dtoks)
        quant_store("fw2", 8, FFN, 2, w2tq, vtoks, ttoks, dtoks)

        for wi, nm in enumerate(("bWq", "bWk", "bWv", "bWo")):
            for rt in range(8):
                gi = wi * 8 + rt
                sl = gi % 2
                lb = p.op("G", lambda e, nm=nm, rt=rt, sl=sl: e.dma_start(
                    out=tn[:, sl * FFN: sl * FFN + 1024],
                    in_=dram[nm][rt * 128:(rt + 1) * 128, :]),
                    deps=[ttoks[-2]] if len(ttoks) >= 2 else [p.tok("evT")],
                    dma="dp0b")
                tmlast = None
                for dc in range(8):
                    blk = dc % 4
                    tm = p.op("T", lambda e, dc=dc, blk=blk, sl=sl: e.transpose(
                        psT[:, blk * 128:(blk + 1) * 128],
                        tn[:, sl * FFN + dc * 128: sl * FFN + (dc + 1) * 128],
                        iden[:]),
                        deps=[lb, p.tok("xnt")], mark=True)
                    cv = p.op("V", lambda e, dc=dc, blk=blk, sl=sl: e.tensor_copy(
                        tb[:, sl * FFN + dc * 128: sl * FFN + (dc + 1) * 128],
                        psT[:, blk * 128:(blk + 1) * 128]),
                        deps=[tm] + ([dtoks[-2]] if len(dtoks) >= 2 and dc == 0
                                     else []), inc="xnt")
                    dt_ = p.op("G", lambda e, dc=dc, rt=rt, nm=nm, sl=sl:
                               e.dma_start(
                                   out=bWT[nm][dc * 128:(dc + 1) * 128,
                                               rt * 128:(rt + 1) * 128],
                                   in_=tb[:, sl * FFN + dc * 128:
                                          sl * FFN + (dc + 1) * 128]),
                               deps=[cv], dma="dp0")
                    tmlast = tm
                ttoks.append(tmlast)
                dtoks.append(dt_)
        pctx.close()

    def phase_bitffn(tapname):
        pctx = ExitStack()

        def pbuf(name, shape, dtype=F32):
            return pctx.enter_context(
                nc.sbuf_tensor(tapname + "_" + name, list(shape), dtype))

        W1 = pbuf("W1", (128, 8 * FFN), BF16)
        W2 = pbuf("W2", (128, 32 * D), BF16)
        for jc in range(8):
            p.op("S", lambda e, jc=jc: e.dma_start(
                out=W1[:, jc * FFN:(jc + 1) * FFN],
                in_=w1tq[jc * 128:(jc + 1) * 128, :]), dma="dwb")
        for fc in range(32):
            p.op("S", lambda e, fc=fc: e.dma_start(
                out=W2[:, fc * D:(fc + 1) * D],
                in_=w2tq[fc * 128:(fc + 1) * 128, :]), dma="dwb")
        wtokb = p.tok("dwb")

        xol = pbuf("xol", (128, 2 * 1024), F32)
        zf = pbuf("zf", (128, 1024), F32)
        zq = pbuf("zq", (128, 1024), BF16)
        zqT = pbuf("zqT", (128, 1024), BF16)
        hg = pbuf("hg", (128, FFN), BF16)
        z2q = pbuf("z2q", (128, 1024), BF16)
        z2T = pbuf("z2T", (128, 1024), BF16)
        stat = pbuf("stat", (128, 64), F32)
        s0 = pbuf("s0", (128, 1024), F32)
        s1 = pbuf("s1", (128, 1024), F32)
        eln = make_emit_ln(stat, s0, s0, s0, s0)
        tap_xs = tap(tapname, (B, NT, 128, D), F32)
        evroll = []
        seq = [(ti, b) for ti in range(n_tiles) for b in range(B)]
        ctxs = {}

        def frontV(i):
            ti, b = seq[i]
            pr = (i % 2) * 1024
            fp = (i % 2) * 20
            lxo = p.op("S", lambda e, b=b, ti=ti, pr=pr: e.dma_start(
                out=xol[:, pr:pr + 1024],
                in_=xout[b, ti * 128:(ti + 1) * 128, :]),
                deps=[p.tok("bfe"), p.tok("lns"), p.tok("dso2")], dma="dl3")
            t_z = eln(xol[:, pr:pr + 1024], zf[:], 0,
                      dep=[lxo, p.tok("bfq")], bn=fp)
            q1 = p.op("A", lambda e: e.activation(
                s0[:], zf[:], AF.Abs), deps=[t_z], inc="bfq")
            q2 = p.op("V", lambda e, fp=fp: e.reduce_max(
                stat[:, fp + 6:fp + 7], s0[:], axis=AX.X), deps=[q1], inc="bfq")
            q3 = p.op("A", lambda e, fp=fp: e.activation(
                stat[:, fp + 7:fp + 8], stat[:, fp + 6:fp + 7], AF.Copy,
                scale=1.0 / 127.0), deps=[q2], inc="bfq")
            q4 = p.op("V", lambda e, fp=fp: e.tensor_scalar_max(
                stat[:, fp + 8:fp + 9], stat[:, fp + 7:fp + 8], 1e-5 / 127.0),
                deps=[q3], inc="bfq")
            q5 = p.op("A", lambda e, fp=fp: e.activation(
                stat[:, fp + 9:fp + 10], stat[:, fp + 8:fp + 9], AF.Copy),
                deps=[q4], inc="bfq")
            q6 = p.op("V", lambda e, fp=fp: e.reciprocal(
                stat[:, fp + 10:fp + 11], stat[:, fp + 9:fp + 10]),
                deps=[q5], inc="bfq")
            q7 = p.op("A", lambda e, fp=fp: e.activation(
                stat[:, fp + 11:fp + 12], stat[:, fp + 10:fp + 11], AF.Copy),
                deps=[q6], inc="bfq")
            se1 = p.op("V", lambda e, fp=fp: e.tensor_scalar_mul(
                stat[:, fp + 12:fp + 13], stat[:, fp + 9:fp + 10],
                wsc[:, 0:1]), deps=[q5, p.tok("f1e")], inc="bfq")
            v1 = p.op("V", lambda e, fp=fp: e.tensor_scalar(
                s0[:], zf[:], stat[:, fp + 11:fp + 12], MAGIC,
                ALU.mult, ALU.add),
                deps=[q7] + ctxs.get(i - 1, {}).get("ztm", []), inc="bfq")
            p.op("V", lambda e: e.tensor_scalar(
                s0[:], s0[:], MAGIC, 127.0, ALU.subtract, ALU.min), inc="bfq")
            v3 = p.op("V", lambda e: e.tensor_scalar_max(
                zq[:], s0[:], -128.0), inc="bfq")
            ctxs[i] = {"lxo": lxo, "q2": q2, "se1": se1, "v3": v3,
                       "pr": pr, "fp": fp}

        def frontT(i):
            c = ctxs[i]
            zt, ztm = [], []
            for dc in range(8):
                blk = dc % 4
                tm = p.op("T", lambda e, dc=dc, blk=blk: e.transpose(
                    psT[:, blk * 128:(blk + 1) * 128],
                    zq[:, dc * 128:(dc + 1) * 128], iden[:]),
                    deps=[c["v3"], p.tok("xnt")], mark=True)
                ztm.append(tm)
                zt.append(p.op("V", lambda e, dc=dc, blk=blk: e.tensor_copy(
                    zqT[:, dc * 128:(dc + 1) * 128],
                    psT[:, blk * 128:(blk + 1) * 128]),
                    deps=[tm], inc="xnt"))
            c["zt"] = zt
            c["ztm"] = ztm

        def back(i):
            ti, b = seq[i]
            c = ctxs[i]
            pr, fp = c["pr"], c["fp"]
            qtoks = []
            rounds_prev = ctxs.get(i - 1, {}).get("hgr", [])
            for qk in range(4):
                bks = (0, 1) if qk % 2 == 0 else (2, 3)
                etoks = []
                for bi, bank in enumerate(bks):
                    fcol0 = qk * 1024 + bi * 512
                    for jc in range(8):
                        dps = []
                        if jc == 0:
                            if qk == 0 and bi == 0:
                                dps += c["zt"] + [wtokb]
                            if len(evroll) >= 4:
                                dps.append(evroll[-4])
                        p.op("T", lambda e, jc=jc, bank=bank, fcol0=fcol0:
                             e.matmul(
                                 psF[:, bank, 0:512],
                                 zqT[:, jc * 128:(jc + 1) * 128],
                                 W1[:, jc * FFN + fcol0:
                                    jc * FFN + fcol0 + 512],
                                 skip_group_check=True, start=(jc == 0),
                                 stop=(jc == 7)),
                             deps=dps, mark=(jc == 7))
                    tmm = p.tok("evT")
                    etoks.append(p.op("A", lambda e, bank=bank, fcol0=fcol0,
                                      fp=fp: e.activation(
                                          hg[:, fcol0:fcol0 + 512],
                                          psF[:, bank, 0:512], AF.Gelu,
                                          scale=stat[:, fp + 12:fp + 13]),
                                      deps=[tmm, c["se1"]] + rounds_prev,
                                      inc="f1e"))
                    evroll.append(etoks[-1])
                qa = p.op("A", lambda e, qk=qk: e.activation(
                    s1[:], hg[:, qk * 1024:(qk + 1) * 1024], AF.Abs),
                    deps=etoks + [p.tok("bfe")] + (qtoks[-1:] if qtoks else []),
                    inc="f1e")
                qtoks.append(p.op("V", lambda e, qk=qk: e.reduce_max(
                    stat[:, 40 + qk:41 + qk], s1[:], axis=AX.X),
                    deps=[qa], inc="bfq"))
            hA = p.op("A", lambda e: e.activation(stat[:, 44:48],
                                                  stat[:, 40:44], AF.Copy),
                      deps=qtoks, inc="bfq")
            hV = p.op("V", lambda e: e.reduce_max(stat[:, 48:49],
                                                  stat[:, 44:48], axis=AX.X),
                      deps=[hA], inc="bfq")
            hB = p.op("A", lambda e: e.activation(stat[:, 49:50],
                                                  stat[:, 48:49], AF.Copy,
                                                  scale=1.0 / 127.0),
                      deps=[hV], inc="bfq")
            hC = p.op("V", lambda e: e.tensor_scalar_max(
                stat[:, 50:51], stat[:, 49:50], 1e-5 / 127.0),
                deps=[hB], inc="bfq")
            hD = p.op("A", lambda e: e.activation(stat[:, 51:52],
                                                  stat[:, 50:51], AF.Copy),
                      deps=[hC], inc="bfq")
            hE = p.op("V", lambda e: e.reciprocal(stat[:, 52:53],
                                                  stat[:, 51:52]),
                      deps=[hD], inc="bfq")
            hF = p.op("A", lambda e: e.activation(stat[:, 53:54],
                                                  stat[:, 52:53], AF.Copy),
                      deps=[hE], inc="bfq")
            se2 = p.op("V", lambda e: e.tensor_scalar_mul(
                stat[:, 54:55], stat[:, 51:52], wsc[:, 2:3]),
                deps=[hD], inc="bfq")
            hgr = []
            for qk in range(4):
                w1q = p.op("V", lambda e, qk=qk: e.tensor_scalar(
                    s1[:], hg[:, qk * 1024:(qk + 1) * 1024],
                    stat[:, 53:54], MAGIC, ALU.mult, ALU.add),
                    deps=[hF, p.tok("evT")], inc="bfq")
                p.op("V", lambda e: e.tensor_scalar(
                    s1[:], s1[:], MAGIC, 127.0, ALU.subtract, ALU.min),
                    inc="bfq")
                w3q = p.op("V", lambda e: e.tensor_scalar_max(
                    z2q[:], s1[:], -128.0), inc="bfq")
                hgr.append(w3q)
                z2toks = []
                for dc in range(8):
                    blk = dc % 4
                    tm = p.op("T", lambda e, dc=dc, blk=blk: e.transpose(
                        psT[:, blk * 128:(blk + 1) * 128],
                        z2q[:, dc * 128:(dc + 1) * 128], iden[:]),
                        deps=[w3q, p.tok("xnt")], mark=True)
                    z2toks.append(p.op("V", lambda e, dc=dc, blk=blk:
                                       e.tensor_copy(
                                           z2T[:, dc * 128:(dc + 1) * 128],
                                           psT[:, blk * 128:(blk + 1) * 128]),
                                       deps=[tm], inc="xnt"))
                for half in range(2):
                    for dci in range(8):
                        fc = qk * 8 + dci
                        dps = (z2toks + [p.tok("bfe")]) if fc == 0 else \
                            (z2toks if dci == 0 else ())
                        p.op("T", lambda e, half=half, fc=fc, dci=dci:
                             e.matmul(
                                 psF[:, 4 + half, 0:512],
                                 z2T[:, dci * 128:(dci + 1) * 128],
                                 W2[:, fc * D + half * 512:
                                    fc * D + half * 512 + 512],
                                 skip_group_check=True, start=(fc == 0),
                                 stop=(fc == 31)),
                             deps=dps, mark=(fc == 31 and half == 1))
            c["hgr"] = hgr
            t2w = p.tok("evT")
            ev = []
            for half in range(2):
                p.op("V", lambda e, half=half: e.tensor_scalar_mul(
                    s1[:, half * 512:(half + 1) * 512],
                    psF[:, 4 + half, 0:512], stat[:, 54:55]),
                    deps=[t2w, se2, c["lxo"], p.tok("dso2")], inc="bfe")
                ev.append(p.op("V", lambda e, half=half, pr=pr: e.tensor_tensor(
                    s0[:, half * 512:(half + 1) * 512],
                    s1[:, half * 512:(half + 1) * 512],
                    xol[:, pr + half * 512:pr + (half + 1) * 512], ALU.add),
                    inc="bfe"))
            if tap_xs is not None:
                p.op("S", lambda e, b=b, ti=ti: e.dma_start(
                    out=taps[tapname][b, ti], in_=s0[:]),
                    deps=ev, dma="dtap")
            p.op("S", lambda e, b=b, ti=ti: e.dma_start(
                out=xout[b, ti * 128:(ti + 1) * 128, :], in_=s0[:]),
                deps=ev + [p.tok("dtap")], dma="dso2")

        frontV(0)
        frontT(0)
        for i in range(len(seq)):
            if i + 1 < len(seq):
                frontV(i + 1)
            back(i)
            if i + 1 < len(seq):
                frontT(i + 1)
        pctx.close()

    def phase4():
        pctx = ExitStack()

        def pbuf(name, shape, dtype=F32):
            return pctx.enter_context(nc.sbuf_tensor("e_" + name, list(shape), dtype))

        WqT = pbuf("WqT", (128, 8 * D), BF16)
        WkT = pbuf("WkT", (128, 8 * D), BF16)
        WvT = pbuf("WvT", (128, 8 * D), BF16)
        for W, nm in [(WqT, "bWq"), (WkT, "bWk"), (WvT, "bWv")]:
            for jc in range(8):
                p.op("S", lambda e, W=W, nm=nm, jc=jc: e.dma_start(
                    out=W[:, jc * D:(jc + 1) * D],
                    in_=bWT[nm][jc * 128:(jc + 1) * 128, :]), dma="dw4")
        wtok4 = p.tok("dw4")
        xol = pbuf("xol", (128, 1024), F32)
        zb = pbuf("zb", (128, 1024), BF16)
        zbT = pbuf("zbT", (128, 1024), BF16)
        QTt = pbuf("QTt", (128, 1024), BF16)
        KTt = pbuf("KTt", (128, 1024), BF16)
        Vt = pbuf("Vt", (128, 1024), BF16)
        stat = pbuf("stat", (128, 12), F32)
        s0 = pbuf("s0", (128, 1024), F32)
        eln = make_emit_ln(stat, s0, s0, s0, s0)
        evq4 = []

        cut = globals().get("P4_CUT", 99)
        for ti in range(n_tiles):
            for b in range(B):
                lxo = p.op("S", lambda e, b=b, ti=ti: e.dma_start(
                    out=xol[:], in_=xout[b, ti * 128:(ti + 1) * 128, :]),
                    deps=[p.tok("lns")], dma="dl4")
                t_z = eln(xol[:], zb[:], 0, dep=[lxo, p.tok("evT")], bn=0)
                if cut < 1:
                    continue
                zt = []
                for dc in range(8):
                    blk = dc % 4
                    tm = p.op("T", lambda e, dc=dc, blk=blk: e.transpose(
                        psT[:, blk * 128:(blk + 1) * 128],
                        zb[:, dc * 128:(dc + 1) * 128], iden[:]),
                        deps=[t_z, p.tok("xnt")], mark=True)
                    zt.append(p.op("V", lambda e, dc=dc, blk=blk: e.tensor_copy(
                        zbT[:, dc * 128:(dc + 1) * 128],
                        psT[:, blk * 128:(blk + 1) * 128]),
                        deps=[tm], inc="xnt"))
                if cut < 2:
                    continue
                ev4 = []
                for gi, (W, dstT, bcol) in enumerate(
                        [(WqT, QTt, bqcol)] * 8 + [(WkT, KTt, bkcol)] * 8):
                    ic = gi % 8
                    if True:
                        bank = gi % 4
                        col = ((gi // 4) % 4) * 128
                        for jc in range(8):
                            dps = []
                            if jc == 0:
                                if gi == 0:
                                    dps += zt + [wtok4]
                                if len(evq4) >= 4:
                                    dps.append(evq4[-4])
                            p.op("T", lambda e, W=W, ic=ic, jc=jc, bank=bank,
                                 col=col: e.matmul(
                                     psF[:, bank, col:col + 128],
                                     W[:, jc * D + ic * 128:
                                       jc * D + ic * 128 + 128],
                                     zbT[:, jc * 128:(jc + 1) * 128],
                                     skip_group_check=True, start=(jc == 0), stop=(jc == 7)),
                                 deps=dps, mark=(jc == 7))
                        tmm = p.tok("evT")
                        ev4.append(p.op("V", lambda e, dstT=dstT, bcol=bcol,
                                        ic=ic, bank=bank, col=col:
                                        e.tensor_scalar_add(
                                            dstT[:, ic * 128:(ic + 1) * 128],
                                            psF[:, bank, col:col + 128],
                                            bcol[:, ic:ic + 1]),
                                        deps=[tmm, p.tok("ds4")], inc="q4e"))
                        evq4.append(ev4[-1])
                if cut < 3:
                    continue
                for half in range(2):
                    for jc in range(8):
                        dps = (zt + [p.tok("q4e")]) if jc == 0 else ()
                        p.op("T", lambda e, half=half, jc=jc: e.matmul(
                            psF[:, 4 + half, 0:512],
                            zbT[:, jc * 128:(jc + 1) * 128],
                            WvT[:, jc * D + half * 512:
                                jc * D + half * 512 + 512],
                            skip_group_check=True, start=(jc == 0), stop=False), deps=dps)
                    p.op("T", lambda e, half=half: e.matmul(
                        psF[:, 4 + half, 0:512], ones_row[0:1, :],
                        rows["bbv"][0:1, half * 512: half * 512 + 512],
                        skip_group_check=True, start=False, stop=True), mark=True)
                tvm = p.tok("evT")
                ev4.append(p.op("V", lambda e: e.tensor_copy(
                    Vt[:, 0:512], psF[:, 4, 0:512]),
                    deps=[tvm, p.tok("ds4")], inc="q4e"))
                ev4.append(p.op("V", lambda e: e.tensor_copy(
                    Vt[:, 512:1024], psF[:, 5, 0:512]),
                    deps=[tvm], inc="q4e"))
                if cut < 4:
                    continue
                p.op("G", lambda e, b=b, ti=ti: e.dma_start(
                    out=qbS[b, ti], in_=QTt[:]), deps=ev4, dma="ds4")
                for ic in range(8):
                    p.op("G", lambda e, b=b, ti=ti, ic=ic: e.dma_start(
                        out=kbS[b, ic, :, ti * 128:(ti + 1) * 128],
                        in_=KTt[:, ic * 128:(ic + 1) * 128]), dma="ds4")
                p.op("G", lambda e, b=b, ti=ti: e.dma_start(
                    out=vbS[b, ti * 128:(ti + 1) * 128, :], in_=Vt[:]),
                    dma="ds4")
        pctx.close()

    def phase5():
        pctx = ExitStack()

        def pbuf(name, shape, dtype=F32):
            return pctx.enter_context(nc.sbuf_tensor("f_" + name, list(shape), dtype))

        Kpl = pbuf("Kpl", (128, 8 * T), BF16)
        Vpl = pbuf("Vpl", (128, NT * 1024), BF16)
        WoT = pbuf("WoT", (128, 8 * D), BF16)
        for jc in range(8):
            p.op("S", lambda e, jc=jc: e.dma_start(
                out=WoT[:, jc * D:(jc + 1) * D],
                in_=bWT["bWo"][jc * 128:(jc + 1) * 128, :]), dma="dw5")
        wtok5 = p.tok("dw5")
        Qt = pbuf("Qt", (128, 1024), BF16)
        Pex = pbuf("Pex", (128, 2048), BF16)
        PnT = pbuf("PnT", (128, 2048), BF16)
        avT = pbuf("avT", (128, 1024), BF16)
        xol = pbuf("xol", (128, 1024), F32)
        xsf = pbuf("xsf", (128, 1024), F32)
        stat = pbuf("stat", (128, 16), F32)
        tap_xs3 = tap("xs3", (B, NT, 128, D), F32)
        RS = 1.0 / math.sqrt(512.0)
        Ts = n_tiles * 128
        nsb = (Ts + 511) // 512

        for b in range(B):
            kvdeps = [p.tok("evT")]
            for ic in range(8):
                p.op("S", lambda e, b=b, ic=ic: e.dma_start(
                    out=Kpl[:, ic * T:ic * T + Ts], in_=kbS[b, ic, :, 0:Ts]),
                    deps=kvdeps if ic == 0 else (), dma="dkv")
            for st in range(n_tiles):
                p.op("S", lambda e, b=b, st=st: e.dma_start(
                    out=Vpl[:, st * 1024:(st + 1) * 1024],
                    in_=vbS[b, st * 128:(st + 1) * 128, :]), dma="dkv")
            kvtok = p.tok("dkv")
            for ti in range(n_tiles):
                lq = p.op("S", lambda e, b=b, ti=ti: e.dma_start(
                    out=Qt[:], in_=qbS[b, ti]),
                    deps=[p.tok("evT")], dma="dq5")
                lxo = p.op("S", lambda e, b=b, ti=ti: e.dma_start(
                    out=xol[:], in_=xout[b, ti * 128:(ti + 1) * 128, :]),
                    deps=[p.tok("x5e")], dma="dq5")
                for sb in range(nsb):
                    Nn = min(512, Ts - sb * 512)
                    for ic in range(8):
                        dps = ([lq, kvtok, p.tok("sfe")]
                               if (sb == 0 and ic == 0) else ())
                        p.op("T", lambda e, sb=sb, ic=ic, Nn=Nn: e.matmul(
                            psF[:, sb, 0:Nn],
                            Qt[:, ic * 128:(ic + 1) * 128],
                            Kpl[:, ic * T + sb * 512: ic * T + sb * 512 + Nn],
                            skip_group_check=True, start=(ic == 0), stop=(ic == 7)),
                            deps=dps, mark=(ic == 7))
                smk = p.tok("evT")
                mt = []
                for sb in range(nsb):
                    Nn = min(512, Ts - sb * 512)
                    mt.append(p.op("V", lambda e, sb=sb, Nn=Nn: e.reduce_max(
                        stat[:, sb:sb + 1], psF[:, sb, 0:Nn], axis=AX.X),
                        deps=[smk] if sb == 0 else (), inc="sfq"))
                mA = p.op("A", lambda e: e.activation(stat[:, 4:4 + nsb],
                                                      stat[:, 0:nsb], AF.Copy),
                          deps=mt, inc="sfq")
                mV = p.op("V", lambda e: e.reduce_max(stat[:, 8:9],
                                                      stat[:, 4:4 + nsb],
                                                      axis=AX.X),
                          deps=[mA], inc="sfq")
                mN = p.op("A", lambda e: e.activation(stat[:, 9:10],
                                                      stat[:, 8:9], AF.Copy,
                                                      scale=-RS),
                          deps=[mV], inc="sfq")
                mH = p.op("V", lambda e: e.tensor_copy(stat[:, 10:11],
                                                       stat[:, 9:10]),
                          deps=[mN], inc="sfq")
                ext = []
                for sb in range(nsb):
                    Nn = min(512, Ts - sb * 512)
                    ext.append(p.op("A", lambda e, sb=sb, Nn=Nn: e.activation(
                        Pex[:, sb * 512:sb * 512 + Nn], psF[:, sb, 0:Nn],
                        AF.Exp, scale=RS, bias=stat[:, 10:11]),
                        deps=[mH, p.tok("sfq")] if sb == 0 else (), inc="sfe"))
                r1 = p.op("V", lambda e: e.reduce_sum(stat[:, 11:12],
                                                      Pex[:, 0:Ts], axis=AX.X),
                          deps=ext, inc="sfq")
                r2 = p.op("A", lambda e: e.activation(stat[:, 12:13],
                                                      stat[:, 11:12], AF.Copy),
                          deps=[r1], inc="sfq")
                r3 = p.op("V", lambda e: e.reciprocal(stat[:, 13:14],
                                                      stat[:, 12:13]),
                          deps=[r2], inc="sfq")
                r4 = p.op("A", lambda e: e.activation(stat[:, 14:15],
                                                      stat[:, 13:14], AF.Copy),
                          deps=[r3], inc="sfq")
                nrm = p.op("V", lambda e: e.tensor_scalar_mul(
                    Pex[:, 0:Ts], Pex[:, 0:Ts], stat[:, 14:15]),
                    deps=[r4], inc="sfq")
                ptoks = []
                for st in range(n_tiles):
                    blk = st % 4
                    tm = p.op("T", lambda e, st=st, blk=blk: e.transpose(
                        psT[:, blk * 128:(blk + 1) * 128],
                        Pex[:, st * 128:(st + 1) * 128], iden[:]),
                        deps=[nrm, p.tok("xnt")], mark=True)
                    ptoks.append(p.op("V", lambda e, st=st, blk=blk:
                                      e.tensor_copy(
                                          PnT[:, st * 128:(st + 1) * 128],
                                          psT[:, blk * 128:(blk + 1) * 128]),
                                      deps=[tm], inc="xnt"))
                for vc in range(8):
                    bank = 4 + vc // 4
                    col = (vc % 4) * 128
                    for st in range(n_tiles):
                        dps = (ptoks + [p.tok("ave")]) \
                            if (vc == 0 and st == 0) else ()
                        p.op("T", lambda e, vc=vc, st=st, bank=bank, col=col:
                             e.matmul(
                                 psF[:, bank, col:col + 128],
                                 Vpl[:, st * 1024 + vc * 128:
                                     st * 1024 + vc * 128 + 128],
                                 PnT[:, st * 128:(st + 1) * 128],
                                 skip_group_check=True, start=(st == 0),
                                 stop=(st == n_tiles - 1)),
                             deps=dps, mark=(st == n_tiles - 1))
                avk = p.tok("evT")
                avt = []
                for vc in range(8):
                    bank = 4 + vc // 4
                    col = (vc % 4) * 128
                    avt.append(p.op("V", lambda e, vc=vc, bank=bank, col=col:
                                    e.tensor_copy(
                                        avT[:, vc * 128:(vc + 1) * 128],
                                        psF[:, bank, col:col + 128]),
                                    deps=[avk] if vc == 0 else (), inc="ave"))
                for half in range(2):
                    for vc in range(8):
                        dps = (avt + ext + [wtok5, p.tok("x5e")]) \
                            if (half == 0 and vc == 0) else ()
                        p.op("T", lambda e, half=half, vc=vc: e.matmul(
                            psF[:, half, 0:512],
                            avT[:, vc * 128:(vc + 1) * 128],
                            WoT[:, vc * D + half * 512:
                                vc * D + half * 512 + 512],
                            skip_group_check=True, start=(vc == 0), stop=False), deps=dps)
                    p.op("T", lambda e, half=half: e.matmul(
                        psF[:, half, 0:512], ones_row[0:1, :],
                        rows["bbo"][0:1, half * 512: half * 512 + 512],
                        skip_group_check=True, start=False, stop=True), mark=True)
                t5w = p.tok("evT")
                ex1 = p.op("V", lambda e: e.tensor_tensor(
                    xsf[:, 0:512], psF[:, 0, 0:512], xol[:, 0:512], ALU.add),
                    deps=[t5w, lxo, p.tok("dso5"), p.tok("dtap")], inc="x5e")
                ex2 = p.op("V", lambda e: e.tensor_tensor(
                    xsf[:, 512:1024], psF[:, 1, 0:512], xol[:, 512:1024],
                    ALU.add), inc="x5e")
                if tap_xs3 is not None:
                    p.op("S", lambda e, b=b, ti=ti: e.dma_start(
                        out=taps["xs3"][b, ti], in_=xsf[:]),
                        deps=[ex1, ex2], dma="dtap")
                p.op("S", lambda e, b=b, ti=ti: e.dma_start(
                    out=xout[b, ti * 128:(ti + 1) * 128, :], in_=xsf[:]),
                    deps=[ex1, ex2, p.tok("dtap")], dma="dso5")
        pctx.close()

    if max_phase >= 3:
        phase0()
        p.barrier()
        phase_bitffn("xs2")
        p.barrier()
    if max_phase >= 4:
        phase4()
        p.barrier()
    if max_phase >= 5:
        phase5()
        p.barrier()
    if max_phase >= 6:
        phase_bitffn("xs4")
        p.barrier()

    p.replay()
    ctx.close()
    return nc, taps


NMAP = {
    "xin": "L0_attn_a_x",
    "aWq": "L0_attn_a_Wq", "abq": "L0_attn_a_bq", "aWk": "L0_attn_a_Wk",
    "abk": "L0_attn_a_bk", "aWv": "L0_attn_a_Wv", "abv": "L0_attn_a_bv",
    "aWo": "L0_attn_a_Wo", "abo": "L0_attn_a_bo",
    "n1g": "L0_attn_a_n1g", "n1b": "L0_attn_a_n1b",
    "n2g": "L0_attn_a_n2g", "n2b": "L0_attn_a_n2b",
    "thb": "L0_attn_a_theta_base", "tw1": "L0_attn_a_theta_w1",
    "tw2": "L0_attn_a_theta_w2", "lmb": "L0_attn_a_lambda_base",
    "eta": "L0_attn_a_eta",
    "aWf1": "L0_attn_a_Wf1", "abf1": "L0_attn_a_bf1",
    "aWf2": "L0_attn_a_Wf2", "abf2": "L0_attn_a_bf2",
    "bWq": "L0_attn_b_Wq", "bbq": "L0_attn_b_bq", "bWk": "L0_attn_b_Wk",
    "bbk": "L0_attn_b_bk", "bWv": "L0_attn_b_Wv", "bbv": "L0_attn_b_bv",
    "bWo": "L0_attn_b_Wo", "bbo": "L0_attn_b_bo",
    "fw1": "L0_ffn_w1", "fw2": "L0_ffn_w2",
}


def make_in_maps(inputs):
    x = np.ascontiguousarray(inputs[NMAP["xin"]], dtype=np.float32)
    in_maps = []
    cst = host_constants()
    for core in range(NCORES):
        m = {"xin": x[core * B:(core + 1) * B], "cst": cst}
        for k2, v2 in NMAP.items():
            if k2 == "xin":
                continue
            m[k2] = np.ascontiguousarray(inputs[v2], dtype=np.float32)
        in_maps.append(m)
    return in_maps


def kernel(**inputs):
    nc, _ = build_program()
    in_maps = make_in_maps(inputs)
    res = run_bass_kernel_spmd(nc, in_maps, core_ids=list(range(NCORES)))
    return np.concatenate([res.results[i]["out"] for i in range(NCORES)], axis=0)
